# version 07
# baseline (speedup 1.0000x reference)
import math

import numpy as np

import concourse.bass as bass
import concourse.mybir as mybir
from concourse.bass_utils import run_bass_kernel_spmd
import concourse.library_config as libcfg

F32 = mybir.dt.float32
BF16 = mybir.dt.bfloat16
AF = mybir.ActivationFunctionType
ALU = mybir.AluOpType
AX = mybir.AxisListType

T = 512
NEG = -1.0e30
PI = math.pi
LN1E4 = math.log(10000.0)

ENGS = ("pe", "dve", "act", "pool", "sp")
_BLOCKSECT = {"pe": "tensor", "dve": "vector", "act": "scalar",
              "pool": "gpsimd", "sp": "sync"}

PROXY_LIB = [l for l in libcfg.all_libraries if l.name == "proxy"][0]


class Chan:
    __slots__ = ("name", "sem", "issued")

    def __init__(self, name, sem):
        self.name = name
        self.sem = sem
        self.issued = 0


class Bld:

    def __init__(self, nc):
        self.nc = nc
        self.ops = {e: [] for e in ENGS}
        self.sem = {e: nc.alloc_semaphore("tk_" + e) for e in ENGS}
        self.n = {e: 0 for e in ENGS}
        self.wm = {e: {} for e in ENGS}
        self.zw = {}
        self.zr = {}
        self.chans = []

    def chan(self, name):
        c = Chan(name, self.nc.alloc_semaphore("ch_" + name))
        self.chans.append(c)
        return c

    def _wait(self, e, key, val):
        if isinstance(key, Chan):
            val = 16 * key.issued
        if val <= 0:
            return
        if self.wm[e].get(key, 0) >= val:
            return
        self.wm[e][key] = val
        semh = key.sem if isinstance(key, Chan) else self.sem[key]
        self.ops[e].append(("w", semh, val))

    def _dep(self, e, zones, war):
        for z in zones:
            lw = self.zw.get(z)
            if lw is not None:
                self._wait(e, lw[0], lw[1])
            if war:
                for k, v in self.zr.get(z, {}).items():
                    self._wait(e, k, v)

    def emit(self, e, fn, reads=(), writes=()):
        self._dep(e, reads, False)
        self._dep(e, writes, True)
        self.ops[e].append(("i", fn, self.sem[e]))
        self.n[e] += 1
        t = self.n[e]
        for z in reads:
            self.zr.setdefault(z, {})[e] = t
        for z in writes:
            self.zw[z] = (e, t)
            self.zr[z] = {}

    def dma(self, ch, out_ap, in_ap, reads=(), writes=(), eng="sp"):
        self._dep(eng, reads, False)
        self._dep(eng, writes, True)
        self.ops[eng].append(("d", out_ap, in_ap, ch.sem))
        ch.issued += 1
        for z in reads:
            self.zr.setdefault(z, {})[ch] = 0
        for z in writes:
            self.zw[z] = (ch, 0)
            self.zr[z] = {}

    def barrier(self):
        for e1 in ENGS:
            for e2 in ENGS:
                if e1 != e2:
                    self._wait(e1, e2, self.n[e2])

    def wait_all_chans(self, e="sp"):
        for c in self.chans:
            self._wait(e, c, 1)

    def replay(self):
        with self.nc.Block() as block:
            for e in ENGS:
                ops = self.ops[e]

                def body(eng, ops=ops):
                    for op in ops:
                        if op[0] == "w":
                            eng.wait_ge(op[1], op[2])
                        elif op[0] == "i":
                            r = op[1](eng)
                            if isinstance(r, list):
                                r = r[-1]
                            r.then_inc(op[2], 1)
                        else:
                            eng.dma_start(op[1], op[2]).then_inc(op[3], 16)
                getattr(block, _BLOCKSECT[e])(body)


def _mmg(parts):
    def fn(eng):
        last = None
        for (o, l, r, s, st, tp) in parts:
            if tp is None:
                last = eng.matmul(o, l, r, start=s, stop=st)
            else:
                last = eng.matmul(o, l, r, start=s, stop=st, tile_position=tp)
        return last
    return fn


def build_core_program(NT=4096):
    NC = NT // T
    nc = bass.Bass("TRN2", target_bir_lowering=False)

    xT = nc.dram_tensor("xT", [1024, NT], F32, kind="ExternalInput").ap()
    wqkvA = nc.dram_tensor("wqkvA", [24, 128 * 8 * 128], F32, kind="ExternalInput").ap()
    woutA = nc.dram_tensor("woutA", [8, 128 * 8 * 128], F32, kind="ExternalInput").ap()
    wqkvF = [nc.dram_tensor(f"wqkvF{l}", [12, 128 * 8 * 128], F32, kind="ExternalInput").ap() for l in range(2)]
    woF = [nc.dram_tensor(f"woF{l}", [8, 128 * 4 * 128], F32, kind="ExternalInput").ap() for l in range(2)]
    wfus = [nc.dram_tensor(f"wfus{l}", [22, 128 * 24 * 128], F32, kind="ExternalInput").ap() for l in range(2)]
    b1a_d = [nc.dram_tensor(f"b1a{l}", [128, 22], F32, kind="ExternalInput").ap() for l in range(2)]
    b1g_d = [nc.dram_tensor(f"b1g{l}", [128, 22], F32, kind="ExternalInput").ap() for l in range(2)]
    b2_d = [nc.dram_tensor(f"b2{l}", [128, 8], F32, kind="ExternalInput").ap() for l in range(2)]
    lng_d = [nc.dram_tensor(f"lng{l}", [128, 8], F32, kind="ExternalInput").ap() for l in range(2)]
    lnb_d = [nc.dram_tensor(f"lnb{l}", [128, 8], F32, kind="ExternalInput").ap() for l in range(2)]
    qsc_d = [nc.dram_tensor(f"qsc{l}", [128, 2], F32, kind="ExternalInput").ap() for l in range(2)]
    ksc_d = [nc.dram_tensor(f"ksc{l}", [128, 2], F32, kind="ExternalInput").ap() for l in range(2)]
    outT = nc.dram_tensor("outT", [1024, NT], F32, kind="ExternalOutput").ap()
    acc_s = nc.dram_tensor("acc_s", [1024, NT], F32, kind="Internal").ap()
    y1_s = nc.dram_tensor("y1_s", [1024, NT], F32, kind="Internal").ap()
    rott = [nc.dram_tensor(f"rott{l}", [4 * 128, NT], F32, kind="Internal").ap() for l in range(2)]

    A = nc.alloc_sbuf_tensor
    XN = A("XN", [128, 8 * 513], F32)
    X = A("Xt", [128, 8 * 512], F32)
    BIGQ = A("BIGQ", [128, 6152], F32)
    X1OFF = 2048
    ACC = A("ACCt", [128, 8 * 512], F32)
    O = A("Ot", [128, 8 * 512], F32)
    KR = A("KRt", [128, 6144], F32)
    VT = A("VTt", [128, 6144], F32)
    WS = [A("WS0", [128, 24 * 128], F32), A("WS1", [128, 24 * 128], F32)]
    P = A("Pt", [128, 1024], F32)
    PTS = A("PTSt", [128, 1024], F32)
    Mt = A("Mt", [128, 512], F32)
    MG = A("MGt", [128, 256], F32)
    QR = A("QRt", [128, 512], F32)
    VS = A("VSt", [128, 512], F32)
    SQA = A("SQAt", [128, 512], F32)
    SQB = A("SQBt", [128, 512], F32)
    MB = A("MBt", [128, 512], F32)
    RB = A("RBt", [128, 512], F32)
    RT = A("RTt", [128, 2048], F32)
    MA = A("MAt", [128, 1024], BF16)
    MF = A("MFt", [128, 4096], BF16)
    SM = A("SMt", [128, 96], F32)
    NMT = A("NMTt", [128, 16], F32)
    SMD = A("SMDt", [1, 256], F32)
    RBC = A("RBCt", [128, 128], F32)
    IDENT = A("IDENTt", [128, 128], F32)
    ONES = A("ONESt", [128, 1], F32)
    ONES2 = A("ONES2t", [128, 2], F32)
    GXN = A("GXNt", [128, 8], F32)
    GX1 = A("GX1t", [128, 8], F32)
    RSS = A("RSSt", [1, 512], F32)
    RROT = A("RROTt", [128, 128], F32)
    BONE = A("BONEt", [1, 128], F32)
    CEPS = A("CEPSt", [128, 1], F32)
    IOD = A("IODt", [128, 4], F32)
    B1A = [A(f"B1A{l}", [128, 22], F32) for l in range(2)]
    B1G = [A(f"B1G{l}", [128, 22], F32) for l in range(2)]
    B2 = [A(f"B2{l}", [128, 8], F32) for l in range(2)]
    LNG = [A(f"LNG{l}", [128, 8], F32) for l in range(2)]
    LNB = [A(f"LNB{l}", [128, 8], F32) for l in range(2)]
    QSC = [A(f"QSC{l}", [128, 2], F32) for l in range(2)]
    KSC = [A(f"KSC{l}", [128, 2], F32) for l in range(2)]

    def Qa(a, b_, p0=0, p1=128):
        return BIGQ[p0:p1, a:b_]

    def X1s(a, b_):
        return BIGQ[:, X1OFF + a:X1OFF + b_]

    PS = [nc.alloc_psum_tensor(f"PS{i}", [128, 512], F32) for i in range(8)]

    b = Bld(nc)
    CX = b.chan("cx")
    CW = [b.chan("cw0"), b.chan("cw1")]
    CRT = b.chan("crt")
    CB = b.chan("cb")
    CO = b.chan("co")
    CTW = b.chan("ctw")

    psz = [f"ps{i}" for i in range(8)]

    b.emit("dve", lambda e: e.memset(BONE[:, :], 1.0), writes=("BONE",))

    b.emit("dve", lambda e: e.memset(IDENT[:, :], 1.0), writes=("IDENT",))
    b.emit("pool", lambda e: e.affine_select(IDENT[:, :], IDENT[:, :], [[1, 128]],
                                             ALU.is_equal, 0.0, base=0,
                                             channel_multiplier=-1),
           reads=("lib",), writes=("IDENT",))
    b.emit("dve", lambda e: e.memset(ONES[:, :], 1.0), writes=("ONES",))
    b.emit("dve", lambda e: e.memset(CEPS[:, :], 1e-5), writes=("CEPS",))
    b.emit("dve", lambda e: e.memset(ONES2[:, :], 1.0), writes=("ONES2",))
    b.emit("pool", lambda e: e.affine_select(ONES2[:, 0:1], ONES2[:, 0:1], [[1, 1]],
                                             ALU.is_ge, 0.0, base=63,
                                             channel_multiplier=-1),
           reads=("lib",), writes=("ONES2",))
    b.emit("pool", lambda e: e.affine_select(ONES2[:, 1:2], ONES2[:, 1:2], [[1, 1]],
                                             ALU.is_ge, 0.0, base=-64,
                                             channel_multiplier=1),
           reads=("lib",), writes=("ONES2",))
    for (c0, sgn, kb) in ((0, -1.0, -32), (32, 1.0, 0), (64, -1.0, -96), (96, 1.0, -64)):
        sl = RROT[:, c0:c0 + 32]
        b.emit("dve", lambda e, sl=sl, sgn=sgn: e.memset(sl, sgn), writes=("RROT",))
        b.emit("pool", lambda e, sl=sl, kb=kb: e.affine_select(
            sl, sl, [[-1, 32]], ALU.is_equal, 0.0, base=kb,
            channel_multiplier=1), reads=("lib",), writes=("RROT",))
    b.emit("dve", lambda e: e.memset(MA[:, :], 0.0), writes=("MA",))
    b.emit("dve", lambda e: e.memset(MF[:, :], 0.0), writes=("MF",))
    for qs in range(2):
        sl = MA[:, qs * 512:(qs + 1) * 512]
        b.emit("pool", lambda e, sl=sl, qs=qs: e.affine_select(
            sl, sl, [[1, 512]], ALU.is_ge, NEG, base=-(128 * qs) - 1,
            channel_multiplier=-1), reads=("lib",), writes=("MA",))
        b.emit("pool", lambda e, sl=sl, qs=qs: e.affine_select(
            sl, sl, [[-1, 512]], ALU.is_ge, NEG, base=(128 * qs) + 256,
            channel_multiplier=1), reads=("lib",), writes=("MA",))
    for qs in range(4):
        sl = MF[:, qs * 1024:(qs + 1) * 1024]
        b.emit("pool", lambda e, sl=sl, qs=qs: e.affine_select(
            sl, sl, [[1, 1024]], ALU.is_ge, NEG, base=-(128 * qs),
            channel_multiplier=-1), reads=("lib",), writes=("MF",))
        b.emit("pool", lambda e, sl=sl, qs=qs: e.affine_select(
            sl, sl, [[-1, 1024]], ALU.is_ge, NEG, base=(128 * qs) + 512,
            channel_multiplier=1), reads=("lib",), writes=("MF",))
    for l in range(2):
        b.dma(CB, B1A[l][:, :], b1a_d[l], writes=("BI",))
        b.dma(CB, B1G[l][:, :], b1g_d[l], writes=("BI",))
        b.dma(CB, B2[l][:, :], b2_d[l], writes=("BI",))
        b.dma(CB, LNG[l][:, :], lng_d[l], writes=("BI",))
        b.dma(CB, LNB[l][:, :], lnb_d[l], writes=("BI",))
        b.dma(CB, QSC[l][:, :], qsc_d[l], writes=("BI",))
        b.dma(CB, KSC[l][:, :], ksc_d[l], writes=("BI",))

    for qtr in range(4):
        b.emit("pool", lambda e, q=qtr: e.iota(
            IOD[q * 32:(q + 1) * 32, 0:1], [[1, 1]], base=0, channel_multiplier=1,
            allow_small_or_imprecise_dtypes=True), reads=("lib",), writes=("IOD",))
    b.emit("act", lambda e: e.activation(IOD[:, 1:2], IOD[:, 0:1], AF.Exp,
                                         scale=-2.0 * LN1E4 / 64.0),
           reads=("IOD",), writes=("IOD",))
    b.emit("dve", lambda e: e.tensor_scalar(IOD[:, 2:3], IOD[:, 0:1],
                                            2.0 / 89.6, 25.6 / 89.6,
                                            ALU.mult, ALU.add),
           reads=("IOD",), writes=("IOD",))
    b.emit("act", lambda e: e.activation(IOD[:, 3:4], IOD[:, 2:3], AF.Ln),
           reads=("IOD",), writes=("IOD",))
    for l in range(2):
        for cb in range(NC):
            t0 = cb * 512
            b.emit("pool", lambda e, t0=t0: e.iota(
                MB[:, :], [[1, 512]], base=t0, channel_multiplier=0,
                allow_small_or_imprecise_dtypes=True),
                reads=("lib",), writes=("MB",))
            b.emit("dve", lambda e: e.tensor_scalar_mul(RB[:, :], MB[:, :], IOD[:, 1:2]),
                   reads=("MB", "IOD"), writes=("RB",))
            b.emit("dve", lambda e: e.tensor_scalar_add(P[:, 0:512], RB[:, :], PI),
                   reads=("RB",), writes=("P",))
            for j in range(9, -1, -1):
                cj = 2.0 * PI * (1 << j)
                b.emit("dve", lambda e, cj=cj: e.tensor_single_scalar(
                    VS[:, :], P[:, 0:512], cj, ALU.is_ge),
                    reads=("P",), writes=("VS",))
                b.emit("dve", lambda e, cj=cj: e.scalar_tensor_tensor(
                    P[:, 0:512], VS[:, :], -cj, P[:, 0:512], ALU.mult, ALU.add),
                    reads=("VS",), writes=("P",))
            b.emit("dve", lambda e: e.tensor_scalar_add(P[:, 0:512], P[:, 0:512], -PI),
                   reads=(), writes=("P",))
            b.emit("dve", lambda e: e.tensor_scalar_add(P[:, 512:1024], P[:, 0:512], PI / 2),
                   reads=(), writes=("P",))
            b.emit("dve", lambda e: e.tensor_single_scalar(
                VS[:, :], P[:, 512:1024], PI, ALU.is_ge),
                reads=("P",), writes=("VS",))
            b.emit("dve", lambda e: e.scalar_tensor_tensor(
                P[:, 512:1024], VS[:, :], -2.0 * PI, P[:, 512:1024],
                ALU.mult, ALU.add),
                reads=("VS",), writes=("P",))
            b.emit("act", lambda e: e.activation(SQB[:, :], P[:, 0:512], AF.Sin),
                   reads=("P",), writes=("SQB",))
            b.emit("act", lambda e: e.activation(VS[:, :], P[:, 512:1024], AF.Sin),
                   reads=("P",), writes=("VS",))
            b.emit("pool", lambda e, t0=t0: e.iota(
                QR[:, :], [[1, 512]], base=t0 - 2048, channel_multiplier=0,
                allow_small_or_imprecise_dtypes=True),
                reads=("lib",), writes=("QR",))
            b.emit("dve", lambda e: e.tensor_scalar_mul(QR[:, :], QR[:, :], IOD[:, 3:4]),
                   reads=("IOD",), writes=("QR",))
            b.emit("act", lambda e: e.activation(MB[:, :], QR[:, :], AF.Exp,
                                                 scale=1.0 / 256.0),
                   reads=("QR",), writes=("MB",))
            b.emit("act", lambda e: e.activation(RB[:, :], QR[:, :], AF.Exp,
                                                 scale=-1.0 / 256.0),
                   reads=("QR",), writes=("RB",))
            tabs = [(0, VS, MB, QSC[l][:, 0:1]),
                    (512, SQB, MB, QSC[l][:, 1:2]),
                    (1024, VS, RB, KSC[l][:, 0:1]),
                    (1536, SQB, RB, KSC[l][:, 1:2])]
            for (off, t1, t2, sc) in tabs:
                b.emit("dve", lambda e, off=off, t1=t1, t2=t2: e.tensor_mul(
                    RT[:, off:off + 512], t1[:, :], t2[:, :]),
                    reads=("VS", "SQB", "MB", "RB"), writes=("RT",))
                b.emit("dve", lambda e, off=off, sc=sc: e.tensor_scalar_mul(
                    RT[:, off:off + 512], RT[:, off:off + 512], sc),
                    reads=("BI",), writes=("RT",))
            for i in range(4):
                b.dma(CTW, rott[l][i * 128:(i + 1) * 128, t0:t0 + 512],
                      RT[:, i * 512:(i + 1) * 512],
                      reads=("RT",), writes=(f"d:rott{l}",))

    b.barrier()


    def emit_ln_stats_and_apply(src_tile, src_cols, dst_writes, dst_slices,
                                affine=None):
        for kk in range(8):
            sq = SQA if kk % 2 == 0 else SQB
            sqz = "SQA" if kk % 2 == 0 else "SQB"
            src = src_cols(kk)
            b.emit("act", lambda e, sq=sq, src=src: e.activation(sq[:, :], src, AF.Square),
                   reads=(src_tile,), writes=(sqz,))
            b.emit("pe", _mmg([(PS[6][0:1, 0:512], ONES[:, 0:1], src,
                                kk == 0, kk == 7, None)]),
                   reads=(src_tile, "ONES"), writes=("ps6",))
            b.emit("pe", _mmg([(PS[7][0:1, 0:512], ONES[:, 0:1], sq[:, :],
                                kk == 0, kk == 7, None)]),
                   reads=(sqz, "ONES"), writes=("ps7",))
        b.emit("dve", lambda e: e.tensor_scalar_mul(MB[0:1, :], PS[6][0:1, 0:512], 1.0 / 1024.0),
               reads=("ps6",), writes=("MB",))
        b.emit("dve", lambda e: e.tensor_scalar_mul(RB[0:1, :], PS[7][0:1, 0:512], 1.0 / 1024.0),
               reads=("ps7",), writes=("RB",))
        b.emit("act", lambda e: e.activation(RSS[0:1, :], MB[0:1, :], AF.Square),
               reads=("MB",), writes=("RSS",))
        b.emit("dve", lambda e: e.tensor_sub(RB[0:1, :], RB[0:1, :], RSS[0:1, :]),
               reads=("RSS",), writes=("RB",))
        b.emit("act", lambda e: e.activation(RSS[0:1, :], RB[0:1, :], AF.Ln,
                                             bias=CEPS[0:1, 0:1]),
               reads=("RB", "CEPS"), writes=("RSS",))
        b.emit("act", lambda e: e.activation(RB[0:1, :], RSS[0:1, :], AF.Exp, scale=-0.5),
               reads=("RSS",), writes=("RB",))
        b.emit("pe", _mmg([(PS[6][:, 0:512], BONE[0:1, 0:128], MB[0:1, :],
                            True, True, None)]),
               reads=("MB", "BONE"), writes=("ps6",))
        b.emit("pe", _mmg([(PS[7][:, 0:512], BONE[0:1, 0:128], RB[0:1, :],
                            True, True, None)]),
               reads=("RB", "BONE"), writes=("ps7",))
        for kk in range(8):
            src = src_cols(kk)
            dst = dst_slices(kk)
            b.emit("dve", lambda e, d=dst, s=src: e.tensor_sub(d, s, PS[6][:, 0:512]),
                   reads=(src_tile, "ps6"), writes=dst_writes)
            b.emit("dve", lambda e, d=dst: e.tensor_mul(d, d, PS[7][:, 0:512]),
                   reads=("ps7",), writes=dst_writes)
            if affine is not None:
                g_ap, bt_ap = affine
                b.emit("dve", lambda e, d=dst, g=g_ap(kk), bb=bt_ap(kk): e.tensor_scalar(
                    d, d, g, bb, ALU.mult, ALU.add),
                    reads=("BI",), writes=dst_writes)

    def wslot_load(widx, dram_ap, j, nblk):
        s = widx % 2
        b.dma(CW[s], WS[s][:, 0:nblk * 128], dram_ap[j:j + 1, :],
              writes=(f"WS{s}",))
        return s

    for c in range(NC):
        ccol = c * 512
        for kk in range(8):
            b.dma(CX, X[:, kk * 512:(kk + 1) * 512],
                  xT[kk * 128:(kk + 1) * 128, ccol:ccol + 512], writes=("X",))
        emit_ln_stats_and_apply(
            "X", lambda kk: X[:, kk * 512:(kk + 1) * 512], ("XN",),
            dst_slices=lambda kk: XN[:, kk * 513 + 1:kk * 513 + 513])

        s0 = (2 * c) % 3
        s1 = (2 * c + 1) % 3

        def a1_qkv_group(m):
            s = m % 2
            parts = [(PS[s][:, 0:512], WS[s][:, kk * 128:(kk + 1) * 128],
                      XN[:, kk * 513 + 1:kk * 513 + 513], kk == 0, kk == 7, None)
                     for kk in range(8)]
            b.emit("pe", _mmg(parts), reads=(f"WS{s}", "XN"), writes=(psz[s],))

        def a1_qkv_evac(m):
            s = m % 2
            if m < 8:
                b.emit("act", lambda e, m=m, s=s: e.copy(Qa(m * 512, (m + 1) * 512),
                                                         PS[s][:, 0:512]),
                       reads=(psz[s],), writes=("Q",))
            elif m < 16:
                j = m - 8
                d0 = KR[:, j * 768 + s0 * 256:j * 768 + s0 * 256 + 256]
                d1 = KR[:, j * 768 + s1 * 256:j * 768 + s1 * 256 + 256]
                b.emit("act", lambda e, d=d0, s=s: e.copy(d, PS[s][:, 0:256]),
                       reads=(psz[s],), writes=(f"KR{s0}",))
                b.emit("act", lambda e, d=d1, s=s: e.copy(d, PS[s][:, 256:512]),
                       reads=(psz[s],), writes=(f"KR{s1}",))
            else:
                j = m - 16
                b.emit("act", lambda e, s=s: e.copy(VS[:, :], PS[s][:, 0:512]),
                       reads=(psz[s],), writes=("VS",))
                for u in range(4):
                    b.emit("pe", lambda e, u=u: e.transpose(
                        PS[2][:, u * 128:(u + 1) * 128],
                        VS[:, u * 128:(u + 1) * 128], IDENT[:, :]),
                        reads=("VS", "IDENT"), writes=("ps2",))
                for half in range(2):
                    sl = s0 if half == 0 else s1
                    src = PS[2][:, half * 256:half * 256 + 256]
                    dst = bass.AP(VT, sl * 2048 + j * 128,
                                  [[6144, 128], [1024, 2], [1, 128]])
                    b.emit("act", lambda e, d=dst, sp=src: e.copy(d, sp),
                           reads=("ps2",), writes=(f"VT{sl}",))

        wslot_load(0, wqkvA, 0, 8)
        wslot_load(1, wqkvA, 1, 8)
        a1_qkv_group(0)
        for m in range(1, 24):
            if m + 1 < 24:
                wslot_load(m + 1, wqkvA, m + 1, 8)
            a1_qkv_group(m)
            a1_qkv_evac(m - 1)
        a1_qkv_evac(23)

        aunits = [(b_loc, hp, qs, h) for b_loc in range(2)
                  for hp in range(8) for qs in range(2) for h in range(2)]
        A1_SBANK = (3, 4, 0, 1)

        def a1_env(u, ui):
            b_loc, hp, qs, h = u
            blk = 2 * c + b_loc
            return dict(b_loc=b_loc, hp=hp, qs=qs, h=h,
                        spv=(blk - 1) % 3, scv=blk % 3,
                        first=(blk == 0),
                        o_off=256 if blk == 0 else 0,
                        qcols=hp * 512 + b_loc * 256 + qs * 128,
                        cidx=((b_loc * 8 + hp) * 2 + qs) * 2,
                        nb=2 * (ui % 4),
                        sbk=A1_SBANK[ui % 4])

        def a1_part1(u, ui):
            v = a1_env(u, ui)
            hp, qs, h = v["hp"], v["qs"], v["h"]
            spv, scv, first = v["spv"], v["scv"], v["first"]
            o_off, qcols, nb, sbk = v["o_off"], v["qcols"], v["nb"], v["sbk"]
            parts = []
            if not first:
                parts.append((PS[sbk][:, 0:256],
                              Qa(qcols, qcols + 128, 64 * h, 64 * h + 64),
                              KR[64 * h:64 * h + 64,
                                 hp * 768 + spv * 256:hp * 768 + spv * 256 + 256],
                              True, False, None))
            parts.append((PS[sbk][:, 256:512],
                          Qa(qcols, qcols + 128, 64 * h, 64 * h + 64),
                          KR[64 * h:64 * h + 64,
                             hp * 768 + scv * 256:hp * 768 + scv * 256 + 256],
                          first, True, None))
            rd = ("Q", f"KR{scv}") if first else ("Q", f"KR{spv}", f"KR{scv}")
            b.emit("pe", _mmg(parts), reads=rd, writes=(psz[sbk],))
            b.emit("dve", lambda e, sbk=sbk, o=o_off, qs=qs: e.tensor_add(
                PS[sbk][:, o:512], PS[sbk][:, o:512],
                MA[:, qs * 512 + o:qs * 512 + 512]),
                reads=("MA",), writes=(psz[sbk],))
            b.emit("dve", lambda e, sbk=sbk, o=o_off, nb=nb: e.reduce_max(
                NMT[:, nb:nb + 1], PS[sbk][:, o:512], axis=AX.X, negate=True),
                reads=(psz[sbk],), writes=("NMT",))
            b.emit("dve", lambda e, nb=nb: e.tensor_scalar_mul(
                NMT[:, nb + 1:nb + 2], NMT[:, nb:nb + 1], 0.125),
                reads=(), writes=("NMT",))

        def a1_part2(u, ui):
            v = a1_env(u, ui)
            hp, qs, h = v["hp"], v["qs"], v["h"]
            spv, scv, first = v["spv"], v["scv"], v["first"]
            o_off, qcols, cidx, nb, sbk = (v["o_off"], v["qcols"], v["cidx"],
                                           v["nb"], v["sbk"])
            b.emit("act", lambda e, sbk=sbk, o=o_off, h=h, ci=cidx, nb=nb: e.activation(
                P[:, h * 512 + o:h * 512 + 512], PS[sbk][:, o:512], AF.Exp,
                bias=NMT[:, nb + 1:nb + 2], scale=0.125,
                accum_out=SM[:, ci + h:ci + h + 1]),
                reads=(psz[sbk], "NMT"), writes=("P", "SM"))
            jl = [2, 3] if first else [0, 1, 2, 3]
            nj = len(jl)
            ptb = 5 if h == 0 else 7
            if h == 1:
                b.emit("pe", lambda e, ci=cidx: [
                    e.transpose(PS[6][0:1, 0:128], SM[:, ci:ci + 1], IDENT[:, :]),
                    e.transpose(PS[6][0:1, 128:256], SM[:, ci + 1:ci + 2],
                                IDENT[:, :])],
                    reads=("SM", "IDENT"), writes=("ps6",))
                b.emit("dve", lambda e: e.reciprocal(SMD[0:1, :], PS[6][0:1, 0:256]),
                       reads=("ps6",), writes=("SMD",))
            for jj, j in enumerate(jl):
                b.emit("pe", lambda e, h=h, j=j, jj=jj, ptb=ptb: e.transpose(
                    PS[ptb][:, jj * 128:(jj + 1) * 128],
                    P[:, h * 512 + j * 128:h * 512 + (j + 1) * 128],
                    IDENT[:, :]),
                    reads=("P", "IDENT"), writes=(psz[ptb],))
            if h == 1:
                b.emit("pe", _mmg([(PS[6][0:64, 0:128], BONE[0:1, 0:64],
                                    SMD[0:1, 0:128], True, True, (0, 0)),
                                   (PS[6][64:128, 0:128], BONE[0:1, 0:64],
                                    SMD[0:1, 128:256], True, True, (0, 64))]),
                       reads=("SMD", "BONE"), writes=("ps6",))
                b.emit("act", lambda e: e.copy(RBC[:, :], PS[6][:, 0:128]),
                       reads=("ps6",), writes=("RBC",))
            b.emit("act", lambda e, h=h, nj=nj, ptb=ptb: e.copy(
                PTS[:, h * 512:h * 512 + nj * 128],
                PS[ptb][:, 0:nj * 128]),
                reads=(psz[ptb],), writes=("PTS",))
            parts = []
            vread = set()
            for jj, j in enumerate(jl):
                sl = spv if j < 2 else scv
                ts = j % 2
                vread.add(f"VT{sl}")
                base = sl * 2048 + ts * 1024 + (2 * hp + h) * 64
                parts.append((PS[2][64 * h:64 * h + 64, 0:128],
                              VT[:, base:base + 64],
                              PTS[:, h * 512 + jj * 128:h * 512 + (jj + 1) * 128],
                              jj == 0, jj == nj - 1, (0, 64 * h)))
            b.emit("pe", _mmg(parts), reads=tuple(vread) + ("PTS",),
                   writes=("ps2",))
            if h == 1:
                b.emit("dve", lambda e, qc=qcols: e.tensor_mul(
                    O[:, qc:qc + 128], PS[2][:, 0:128], RBC[:, :]),
                    reads=("ps2", "RBC"), writes=("O",))

        a1_part1(aunits[0], 0)
        a1_part1(aunits[1], 1)
        for ui in range(2, len(aunits)):
            a1_part1(aunits[ui], ui)
            a1_part2(aunits[ui - 2], ui - 2)
        a1_part2(aunits[-2], len(aunits) - 2)
        a1_part2(aunits[-1], len(aunits) - 1)


        def a1_wo_group(fo):
            s = fo % 2
            parts = [(PS[s][:, 0:512], WS[s][:, kk * 128:(kk + 1) * 128],
                      O[:, kk * 512:(kk + 1) * 512], kk == 0, kk == 7, None)
                     for kk in range(8)]
            b.emit("pe", _mmg(parts), reads=(f"WS{s}", "O"), writes=(psz[s],))

        def a1_wo_evac(fo):
            s = fo % 2
            b.emit("dve", lambda e, fo=fo, s=s: e.tensor_add(
                ACC[:, fo * 512:(fo + 1) * 512], PS[s][:, 0:512],
                X[:, fo * 512:(fo + 1) * 512]),
                reads=(psz[s], "X"), writes=(f"ACC{fo}",))
            b.dma(CO, acc_s[fo * 128:(fo + 1) * 128, ccol:ccol + 512],
                  ACC[:, fo * 512:(fo + 1) * 512],
                  reads=(f"ACC{fo}",), writes=("d:acc",))

        wslot_load(0, woutA, 0, 8)
        wslot_load(1, woutA, 1, 8)
        a1_wo_group(0)
        for fo in range(1, 8):
            if fo + 1 < 8:
                wslot_load(fo + 1, woutA, fo + 1, 8)
            a1_wo_group(fo)
            a1_wo_evac(fo - 1)
        a1_wo_evac(7)

    b.barrier()

    def sweep_ffn(l, src, dst):
        b.emit("dve", lambda e: e.memset(GXN[:, :], 0.0), writes=("GXN",))
        b.emit("dve", lambda e: e.memset(GX1[:, :], 0.0), writes=("GX1",))
        for c in range(NC):
            ccol = c * 512
            spv = (c - 1) % 2
            scv = c % 2
            first = (c == 0)
            if src == "x":
                for kk in range(8):
                    b.dma(CX, X[:, kk * 512:(kk + 1) * 512],
                          xT[kk * 128:(kk + 1) * 128, ccol:ccol + 512], writes=("X",))
                for kk in range(8):
                    b.emit("dve", lambda e, kk=kk: e.tensor_copy(
                        XN[:, kk * 513:kk * 513 + 1], GXN[:, kk:kk + 1]),
                        reads=("GXN",), writes=("XN",))
                emit_ln_stats_and_apply(
                    "X", lambda kk: X[:, kk * 512:(kk + 1) * 512], ("XN",),
                    dst_slices=lambda kk: XN[:, kk * 513 + 1:kk * 513 + 513])
                for kk in range(8):
                    b.emit("dve", lambda e, kk=kk: e.tensor_copy(
                        GXN[:, kk:kk + 1], XN[:, kk * 513 + 512:kk * 513 + 513]),
                        reads=("XN",), writes=("GXN",))
            else:
                for kk in range(8):
                    b.dma(CX, X[:, kk * 512:(kk + 1) * 512],
                          acc_s[kk * 128:(kk + 1) * 128, ccol:ccol + 512],
                          reads=("d:acc",), writes=("X",))
                if first:
                    for kk in range(8):
                        b.emit("dve", lambda e, kk=kk: e.memset(
                            XN[:, kk * 513:kk * 513 + 1], 0.0), writes=("XN",))
                        b.dma(CX, XN[:, kk * 513 + 1:kk * 513 + 513],
                              y1_s[kk * 128:(kk + 1) * 128, 0:512],
                              reads=("d:y1",), writes=("XN",))
                else:
                    for kk in range(8):
                        b.dma(CX, XN[:, kk * 513:kk * 513 + 513],
                              y1_s[kk * 128:(kk + 1) * 128, ccol - 1:ccol + 512],
                              reads=("d:y1",), writes=("XN",))
            for i in range(4):
                b.dma(CRT, RT[:, i * 512:(i + 1) * 512],
                      rott[l][i * 128:(i + 1) * 128, ccol:ccol + 512],
                      reads=(f"d:rott{l}",), writes=("RT",))

            def hshift(kk, off=0, w=512):
                base = kk * 513 + (1 if kk < 4 else 0) + off
                return XN[:, base:base + w]

            def f_qkv_group(m):
                s = m % 2
                parts = [(PS[s][:, 0:512], WS[s][:, kk * 128:(kk + 1) * 128],
                          hshift(kk), kk == 0, kk == 7, None) for kk in range(8)]
                b.emit("pe", _mmg(parts), reads=(f"WS{s}", "XN"), writes=(psz[s],))

            def f_qkv_evac(m):
                s = m % 2
                if m < 8:
                    b.emit("act", lambda e, s=s: e.activation(SQA[:, :], PS[s][:, 0:512],
                                                              AF.Square),
                           reads=(psz[s],), writes=("SQA",))
                    b.emit("pe", _mmg([(PS[5][0:1, 0:512], ONES2[:, 0:1], SQA[:, :],
                                        True, True, None),
                                       (PS[6][0:1, 0:512], ONES2[:, 1:2], SQA[:, :],
                                        True, True, None)]),
                           reads=("SQA", "ONES2"), writes=("ps5", "ps6"))
                    b.emit("dve", lambda e: e.tensor_scalar_max(RSS[0:1, :],
                                                                PS[5][0:1, 0:512], 1e-24),
                           reads=("ps5",), writes=("RSS",))
                    b.emit("dve", lambda e: e.tensor_scalar_max(P[0:1, 0:512],
                                                                PS[6][0:1, 0:512], 1e-24),
                           reads=("ps6",), writes=("P",))
                    b.emit("act", lambda e: e.activation(RSS[0:1, :], RSS[0:1, :], AF.Ln),
                           reads=(), writes=("RSS",))
                    b.emit("act", lambda e: e.activation(RSS[0:1, :], RSS[0:1, :], AF.Exp,
                                                         scale=-0.5),
                           reads=(), writes=("RSS",))
                    b.emit("act", lambda e: e.activation(P[0:1, 0:512], P[0:1, 0:512], AF.Ln),
                           reads=(), writes=("P",))
                    b.emit("act", lambda e: e.activation(P[0:1, 0:512], P[0:1, 0:512], AF.Exp,
                                                         scale=-0.5),
                           reads=(), writes=("P",))
                    b.emit("pe", _mmg([(PS[6][0:64, 0:512], BONE[0:1, 0:64],
                                        RSS[0:1, :], True, True, (0, 0)),
                                       (PS[6][64:128, 0:512], BONE[0:1, 0:64],
                                        P[0:1, 0:512], True, True, (0, 64))]),
                           reads=("RSS", "P", "BONE"), writes=("ps6",))
                    b.emit("act", lambda e: e.copy(MB[:, :], PS[6][:, 0:512]),
                           reads=("ps6",), writes=("MB",))
                    b.emit("dve", lambda e, s=s: e.tensor_mul(QR[:, :], PS[s][:, 0:512],
                                                              MB[:, :]),
                           reads=(psz[s], "MB"), writes=("QR",))
                    tb = 0 if m < 4 else 1024
                    if m < 4:
                        dzone = "Q"
                        dcol0, dcol1 = m * 512, (m + 1) * 512

                        def dsl(p0, p1, a=dcol0, bcol=dcol1):
                            return BIGQ[p0:p1, a:bcol]
                    else:
                        j = m - 4
                        dzone = f"KR{scv}"
                        dcol0 = j * 1024 + scv * 512
                        dcol1 = dcol0 + 512

                        def dsl(p0, p1, a=dcol0, bcol=dcol1):
                            return KR[p0:p1, a:bcol]
                    b.emit("pe", _mmg([(PS[5][:, 0:512], RROT[:, :], QR[:, :],
                                        True, True, None)]),
                           reads=("RROT", "QR"), writes=("ps5",))
                    b.emit("dve", lambda e, tb=tb, d=dsl(0, 128): e.tensor_mul(
                        d, QR[:, :], RT[:, tb:tb + 512]),
                        reads=("QR", "RT"), writes=(dzone,))
                    b.emit("dve", lambda e, tb=tb: e.tensor_mul(
                        VS[:, :], PS[5][:, 0:512], RT[:, tb + 512:tb + 1024]),
                        reads=("ps5", "RT"), writes=("VS",))
                    b.emit("dve", lambda e, d=dsl(0, 128): e.tensor_add(d, d, VS[:, :]),
                           reads=("VS",), writes=(dzone,))
                else:
                    j = m - 8
                    b.emit("act", lambda e, s=s: e.copy(VS[:, :], PS[s][:, 0:512]),
                           reads=(psz[s],), writes=("VS",))
                    for u in range(4):
                        b.emit("pe", lambda e, u=u: e.transpose(
                            PS[2][:, u * 128:(u + 1) * 128],
                            VS[:, u * 128:(u + 1) * 128], IDENT[:, :]),
                            reads=("VS", "IDENT"), writes=("ps2",))
                    dstv = bass.AP(VT, scv * 2048 + j * 128,
                                   [[6144, 128], [512, 4], [1, 128]])
                    b.emit("act", lambda e, d=dstv: e.copy(d, PS[2][:, 0:512]),
                           reads=("ps2",), writes=(f"VT{scv}",))

            wslot_load(0, wqkvF[l], 0, 8)
            wslot_load(1, wqkvF[l], 1, 8)
            f_qkv_group(0)
            for m in range(1, 12):
                if m + 1 < 12:
                    wslot_load(m + 1, wqkvF[l], m + 1, 8)
                f_qkv_group(m)
                f_qkv_evac(m - 1)
            f_qkv_evac(11)

            funits = [(hp, qs, h) for hp in range(4) for qs in range(4)
                      for h in range(2)]

            def f_env(u, ui):
                hp, qs, h = u
                return dict(hp=hp, qs=qs, h=h,
                            qcols=hp * 512 + qs * 128,
                            scidx=(hp * 4 + qs) * 2 + h,
                            nbh=4 * (ui % 2),
                            bkA=2 + 2 * (ui % 2), bkB=3 + 2 * (ui % 2))

            def f_part1(u, ui):
                v = f_env(u, ui)
                hp, qs, h = v["hp"], v["qs"], v["h"]
                nbh, bkA, bkB = v["nbh"], v["bkA"], v["bkB"]
                if not first:
                    b.emit("pe", _mmg([(PS[bkA][:, 0:512],
                                        Qa(v["qcols"], v["qcols"] + 128, 64 * h, 64 * h + 64),
                                        KR[64 * h:64 * h + 64,
                                           hp * 1024 + spv * 512:hp * 1024 + spv * 512 + 512],
                                        True, True, None)]),
                           reads=("Q", f"KR{spv}"), writes=(psz[bkA],))
                b.emit("pe", _mmg([(PS[bkB][:, 0:512],
                                    Qa(v["qcols"], v["qcols"] + 128, 64 * h, 64 * h + 64),
                                    KR[64 * h:64 * h + 64,
                                       hp * 1024 + scv * 512:hp * 1024 + scv * 512 + 512],
                                    True, True, None)]),
                       reads=("Q", f"KR{scv}"), writes=(psz[bkB],))
                if not first:
                    b.emit("dve", lambda e, bk=bkA, qs=qs: e.tensor_add(
                        PS[bk][:, 0:512], PS[bk][:, 0:512],
                        MF[:, qs * 1024:qs * 1024 + 512]),
                        reads=("MF",), writes=(psz[bkA],))
                b.emit("dve", lambda e, bk=bkB, qs=qs: e.tensor_add(
                    PS[bk][:, 0:512], PS[bk][:, 0:512],
                    MF[:, qs * 1024 + 512:qs * 1024 + 1024]),
                    reads=("MF",), writes=(psz[bkB],))
                if not first:
                    b.emit("dve", lambda e, bk=bkA, nbh=nbh: e.reduce_max(
                        NMT[:, nbh:nbh + 1], PS[bk][:, 0:512], axis=AX.X, negate=True),
                        reads=(psz[bkA],), writes=("NMT",))
                b.emit("dve", lambda e, bk=bkB, nbh=nbh: e.reduce_max(
                    NMT[:, nbh + 1:nbh + 2], PS[bk][:, 0:512], axis=AX.X, negate=True),
                    reads=(psz[bkB],), writes=("NMT",))
                if not first:
                    b.emit("dve", lambda e, nbh=nbh: e.tensor_tensor(
                        NMT[:, nbh + 2:nbh + 3], NMT[:, nbh:nbh + 1],
                        NMT[:, nbh + 1:nbh + 2], ALU.min),
                        reads=(), writes=("NMT",))
                    b.emit("dve", lambda e, nbh=nbh: e.tensor_scalar_mul(
                        NMT[:, nbh + 3:nbh + 4], NMT[:, nbh + 2:nbh + 3], 8.0),
                        reads=(), writes=("NMT",))
                else:
                    b.emit("dve", lambda e, nbh=nbh: e.tensor_scalar_mul(
                        NMT[:, nbh + 3:nbh + 4], NMT[:, nbh + 1:nbh + 2], 8.0),
                        reads=(), writes=("NMT",))

            def f_part2(u, ui):
                v = f_env(u, ui)
                hp, qs, h = v["hp"], v["qs"], v["h"]
                nbh, bkA, bkB = v["nbh"], v["bkA"], v["bkB"]
                scidx = v["scidx"]
                qcols = v["qcols"]
                if not first:
                    b.emit("act", lambda e, bk=bkA, si=scidx, nbh=nbh: e.activation(
                        P[:, 0:512], PS[bk][:, 0:512], AF.Exp,
                        bias=NMT[:, nbh + 3:nbh + 4], scale=8.0,
                        accum_out=SM[:, 2 * si:2 * si + 1]),
                        reads=(psz[bkA], "NMT"), writes=("P", "SM"))
                    b.emit("act", lambda e, bk=bkB, si=scidx, nbh=nbh: e.activation(
                        P[:, 512:1024], PS[bk][:, 0:512], AF.Exp,
                        bias=NMT[:, nbh + 3:nbh + 4], scale=8.0,
                        accum_out=SM[:, 2 * si + 1:2 * si + 2]),
                        reads=(psz[bkB], "NMT"), writes=("P", "SM"))
                    b.emit("dve", lambda e, si=scidx: e.tensor_add(
                        SM[:, 64 + si:65 + si], SM[:, 2 * si:2 * si + 1],
                        SM[:, 2 * si + 1:2 * si + 2]),
                        reads=(), writes=("SM",))
                else:
                    b.emit("act", lambda e, bk=bkB, si=scidx, nbh=nbh: e.activation(
                        P[:, 512:1024], PS[bk][:, 0:512], AF.Exp,
                        bias=NMT[:, nbh + 3:nbh + 4], scale=8.0,
                        accum_out=SM[:, 64 + si:65 + si]),
                        reads=(psz[bkB], "NMT"), writes=("P", "SM"))
                last_pair = (h == 1)
                if last_pair:
                    si0 = (hp * 4 + qs) * 2
                    b.emit("pe", lambda e, si=si0: [
                        e.transpose(PS[1][0:1, 0:128], SM[:, 64 + si:65 + si],
                                    IDENT[:, :]),
                        e.transpose(PS[1][0:1, 128:256], SM[:, 65 + si:66 + si],
                                    IDENT[:, :])],
                        reads=("SM", "IDENT"), writes=("ps1",))
                    b.emit("dve", lambda e: e.reciprocal(SMD[0:1, :], PS[1][0:1, 0:256]),
                           reads=("ps1",), writes=("SMD",))
                jl = [4, 5, 6, 7] if first else list(range(8))
                for j in jl:
                    pb = 6 + j // 4
                    b.emit("pe", lambda e, j=j, pb=pb: e.transpose(
                        PS[pb][:, (j % 4) * 128:(j % 4) * 128 + 128],
                        P[:, j * 128:(j + 1) * 128], IDENT[:, :]),
                        reads=("P", "IDENT"), writes=(psz[pb],))
                if last_pair:
                    b.emit("pe", _mmg([(PS[1][0:64, 0:128], BONE[0:1, 0:64],
                                        SMD[0:1, 0:128], True, True, (0, 0)),
                                       (PS[1][64:128, 0:128], BONE[0:1, 0:64],
                                        SMD[0:1, 128:256], True, True, (0, 64))]),
                           reads=("SMD", "BONE"), writes=("ps1",))
                    b.emit("act", lambda e: e.copy(RBC[:, :], PS[1][:, 0:128]),
                           reads=("ps1",), writes=("RBC",))
                if not first:
                    b.emit("act", lambda e: e.copy(PTS[:, 0:512], PS[6][:, 0:512]),
                           reads=("ps6",), writes=("PTS",))
                b.emit("act", lambda e: e.copy(PTS[:, 512:1024], PS[7][:, 0:512]),
                       reads=("ps7",), writes=("PTS",))
                parts = []
                vread = set()
                for j in jl:
                    sl = spv if j < 4 else scv
                    ts = j % 4
                    vread.add(f"VT{sl}")
                    base = sl * 2048 + ts * 512 + (2 * hp + h) * 64
                    parts.append((PS[0][64 * h:64 * h + 64, 0:128],
                                  VT[:, base:base + 64],
                                  PTS[:, j * 128:(j + 1) * 128],
                                  j == jl[0], j == jl[-1], (0, 64 * h)))
                b.emit("pe", _mmg(parts), reads=tuple(vread) + ("PTS",),
                       writes=("ps0",))
                if last_pair:
                    b.emit("dve", lambda e, qc=qcols: e.tensor_mul(
                        O[:, qc:qc + 128], PS[0][:, 0:128], RBC[:, :]),
                        reads=("ps0", "RBC"), writes=("O",))

            f_part1(funits[0], 0)
            for ui in range(1, len(funits)):
                f_part1(funits[ui], ui)
                f_part2(funits[ui - 1], ui - 1)
            f_part2(funits[-1], len(funits) - 1)

            def f_wo_group(fo):
                s = fo % 2
                parts = [(PS[s][:, 0:512], WS[s][:, kk * 128:(kk + 1) * 128],
                          O[:, kk * 512:(kk + 1) * 512], kk == 0, kk == 3, None)
                         for kk in range(4)]
                b.emit("pe", _mmg(parts), reads=(f"WS{s}", "O"), writes=(psz[s],))

            def f_wo_evac(fo):
                s = fo % 2
                b.emit("dve", lambda e, fo=fo, s=s: e.tensor_add(
                    X1s(fo * 513 + 1, fo * 513 + 513), PS[s][:, 0:512],
                    XN[:, fo * 513 + 1:fo * 513 + 513]),
                    reads=(psz[s], "XN"), writes=("X1",))

            wslot_load(0, woF[l], 0, 4)
            wslot_load(1, woF[l], 1, 4)
            f_wo_group(0)
            for fo in range(1, 8):
                if fo + 1 < 8:
                    wslot_load(fo + 1, woF[l], fo + 1, 4)
                f_wo_group(fo)
                f_wo_evac(fo - 1)
            f_wo_evac(7)
            for kk in range(8):
                b.emit("dve", lambda e, kk=kk: e.tensor_copy(
                    X1s(kk * 513, kk * 513 + 1), GX1[:, kk:kk + 1]),
                    reads=("GX1",), writes=("X1",))
            for kk in range(8):
                b.emit("dve", lambda e, kk=kk: e.tensor_copy(
                    GX1[:, kk:kk + 1], X1s(kk * 513 + 512, kk * 513 + 513)),
                    reads=("X1",), writes=("GX1",))

            def h2slice(kk):
                base = kk * 513 + (1 if kk < 4 else 0)
                return X1s(base, base + 512)

            emit_ln_stats_and_apply(
                "X1", h2slice, ("XN",),
                dst_slices=lambda kk: XN[:, kk * 513 + 1:kk * 513 + 513],
                affine=(lambda kk: LNG[l][:, kk:kk + 1], lambda kk: LNB[l][:, kk:kk + 1]))

            for ts2 in range(2):
                tw = ts2 * 256

                def w1_ag_group(m, tw=tw):
                    s = (ts2 * 22 + m) % 2
                    sa = m % 2
                    parts = [(PS[sa][:, 0:256], WS[s][:, kk * 128:(kk + 1) * 128],
                              XN[:, kk * 513 + 1 + tw:kk * 513 + 1 + tw + 256],
                              kk == 0, False, None) for kk in range(8)]
                    parts += [(PS[sa][:, 256:512], WS[s][:, (8 + kk) * 128:(9 + kk) * 128],
                               XN[:, kk * 513 + 1 + tw:kk * 513 + 1 + tw + 256],
                               False, kk == 7, None) for kk in range(8)]
                    b.emit("pe", _mmg(parts), reads=(f"WS{s}", "XN"), writes=(psz[sa],))

                def w1_w2_chain(m, tw=tw):
                    s = (ts2 * 22 + m) % 2
                    sa = m % 2
                    mz = f"M{m % 2}"
                    mc = (m % 2) * 256
                    b.emit("act", lambda e, sa=sa, m=m: e.activation(
                        MG[:, :], PS[sa][:, 256:512], AF.Gelu,
                        bias=B1G[l][:, m:m + 1]),
                        reads=(psz[sa], "BI"), writes=("MG",))
                    b.emit("dve", lambda e, sa=sa, m=m, mc=mc: e.scalar_tensor_tensor(
                        Mt[:, mc:mc + 256], PS[sa][:, 0:256], B1A[l][:, m:m + 1],
                        MG[:, :], ALU.add, ALU.mult),
                        reads=(psz[sa], "MG", "BI"), writes=(mz,))
                    parts = [(PS[2 + fo // 2][:, (fo % 2) * 256:(fo % 2) * 256 + 256],
                              WS[s][:, (16 + fo) * 128:(17 + fo) * 128],
                              Mt[:, mc:mc + 256], (m == 0 and fo % 2 == 0),
                              (m == 21 and fo % 2 == 1), None)
                             for fo in range(8)]
                    b.emit("pe", _mmg(parts), reads=(f"WS{s}", mz),
                           writes=("ps2", "ps3", "ps4", "ps5"))

                wslot_load(ts2 * 22, wfus[l], 0, 24)
                wslot_load(ts2 * 22 + 1, wfus[l], 1, 24)
                w1_ag_group(0)
                for m in range(1, 22):
                    w1_ag_group(m)
                    w1_w2_chain(m - 1)
                    if m + 1 < 22:
                        wslot_load(ts2 * 22 + m + 1, wfus[l], m + 1, 24)
                w1_w2_chain(21)
                for fo in range(8):
                    w2v = PS[2 + fo // 2][:, (fo % 2) * 256:(fo % 2) * 256 + 256]
                    b.emit("dve", lambda e, fo=fo, w2v=w2v, tw=tw: e.scalar_tensor_tensor(
                        ACC[:, fo * 512 + tw:fo * 512 + tw + 256], w2v,
                        B2[l][:, fo:fo + 1],
                        X1s(fo * 513 + 1 + tw, fo * 513 + 1 + tw + 256),
                        ALU.add, ALU.add),
                        reads=("ps" + str(2 + fo // 2), "X1", "BI"),
                        writes=(f"ACC{fo}",))
                    if dst == "out":
                        b.emit("dve", lambda e, fo=fo, tw=tw: e.tensor_add(
                            ACC[:, fo * 512 + tw:fo * 512 + tw + 256],
                            ACC[:, fo * 512 + tw:fo * 512 + tw + 256],
                            X[:, fo * 512 + tw:fo * 512 + tw + 256]),
                            reads=("X",), writes=(f"ACC{fo}",))
            dstd = y1_s if dst == "y1" else outT
            dzone = "d:y1" if dst == "y1" else "d:out"
            for fo in range(8):
                b.dma(CO, dstd[fo * 128:(fo + 1) * 128, ccol:ccol + 512],
                      ACC[:, fo * 512:(fo + 1) * 512],
                      reads=(f"ACC{fo}",), writes=(dzone,))

    sweep_ffn(0, "x", "y1")
    b.barrier()
    sweep_ffn(1, "y1", "out")

    b.wait_all_chans("sp")
    b.replay()
    nc.finalize()
    return nc



def _tile_w(w, nk, nm):
    K, M = w.shape
    assert K == nk * 128 and M == nm * 128
    t = w.reshape(nk, 128, nm, 128).transpose(2, 1, 0, 3)
    return np.ascontiguousarray(t).reshape(nm, 128 * nk * 128)


def _pad_cols(w, M):
    out = np.zeros((w.shape[0], M), dtype=w.dtype)
    out[:, :w.shape[1]] = w
    return out


def _pad_rows(w, K):
    out = np.zeros((K, w.shape[1]), dtype=w.dtype)
    out[:w.shape[0]] = w
    return out


def stage_inputs(L0_attn_w_qkv, L0_attn_w_out, L0_ffn_wqkv, L0_ffn_q_scale,
                 L0_ffn_k_scale, L0_ffn_wo, L0_ffn_ln_g, L0_ffn_ln_b,
                 L0_ffn_w1, L0_ffn_b1, L0_ffn_w2, L0_ffn_b2):
    f = np.float32
    d = {}
    d["wqkvA"] = _tile_w(np.asarray(L0_attn_w_qkv, f), 8, 24)
    d["woutA"] = _tile_w(np.asarray(L0_attn_w_out, f), 8, 8)
    for l in range(2):
        d[f"wqkvF{l}"] = _tile_w(np.asarray(L0_ffn_wqkv[l], f), 8, 12)
        d[f"woF{l}"] = _tile_w(np.asarray(L0_ffn_wo[l], f), 4, 8)
        w1 = np.asarray(L0_ffn_w1[l], f)
        wa = _pad_cols(w1[:, :2730], 2816).reshape(8, 128, 22, 128).transpose(2, 1, 0, 3)
        wg = _pad_cols(w1[:, 2730:], 2816).reshape(8, 128, 22, 128).transpose(2, 1, 0, 3)
        w2 = _pad_rows(np.asarray(L0_ffn_w2[l], f), 2816).reshape(22, 128, 8, 128)
        d[f"wfus{l}"] = np.ascontiguousarray(
            np.concatenate([wa, wg, w2], axis=2)).reshape(22, 128 * 24 * 128)
        b1 = np.asarray(L0_ffn_b1[l], f)
        d[f"b1a{l}"] = np.ascontiguousarray(
            np.concatenate([b1[:2730], np.zeros(86, f)]).reshape(22, 128).T)
        d[f"b1g{l}"] = np.ascontiguousarray(
            np.concatenate([b1[2730:], np.zeros(86, f)]).reshape(22, 128).T)
        d[f"b2{l}"] = np.ascontiguousarray(np.asarray(L0_ffn_b2[l], f).reshape(8, 128).T)
        d[f"lng{l}"] = np.ascontiguousarray(np.asarray(L0_ffn_ln_g[l], f).reshape(8, 128).T)
        d[f"lnb{l}"] = np.ascontiguousarray(np.asarray(L0_ffn_ln_b[l], f).reshape(8, 128).T)
        qs = np.asarray(L0_ffn_q_scale[l], f)
        ks = np.asarray(L0_ffn_k_scale[l], f)
        d[f"qsc{l}"] = np.ascontiguousarray(
            np.stack([np.tile(qs, 2), np.tile(np.roll(qs, -32), 2)], axis=1))
        d[f"ksc{l}"] = np.ascontiguousarray(
            np.stack([np.tile(ks, 2), np.tile(np.roll(ks, -32), 2)], axis=1))
    return d


def kernel(L0_attn_x, L0_attn_w_qkv, L0_attn_w_out, L0_ffn_wqkv, L0_ffn_q_scale,
           L0_ffn_k_scale, L0_ffn_wo, L0_ffn_ln_g, L0_ffn_ln_b, L0_ffn_w1,
           L0_ffn_b1, L0_ffn_w2, L0_ffn_b2):
    x = np.asarray(L0_attn_x, np.float32)
    B, NT, D = x.shape
    nc = build_core_program(NT)
    wd = stage_inputs(L0_attn_w_qkv, L0_attn_w_out, L0_ffn_wqkv, L0_ffn_q_scale,
                      L0_ffn_k_scale, L0_ffn_wo, L0_ffn_ln_g, L0_ffn_ln_b,
                      L0_ffn_w1, L0_ffn_b1, L0_ffn_w2, L0_ffn_b2)
    in_maps = []
    for bb in range(B):
        m = {"xT": np.ascontiguousarray(x[bb].T)}
        m.update(wd)
        in_maps.append(m)
    res = run_bass_kernel_spmd(nc, in_maps, core_ids=list(range(B)))
    out = np.stack([np.asarray(r["outT"]).T for r in res.results], axis=0)
    return out.astype(np.float32)


# version 08
# speedup vs baseline: 1.2148x; 1.2148x over previous
import math

import numpy as np

import concourse.bass as bass
import concourse.mybir as mybir
from concourse.bass_utils import run_bass_kernel_spmd
import concourse.library_config as libcfg

F32 = mybir.dt.float32
BF16 = mybir.dt.bfloat16
AF = mybir.ActivationFunctionType
ALU = mybir.AluOpType
AX = mybir.AxisListType

T = 512
NEG = -1.0e30
PI = math.pi
LN1E4 = math.log(10000.0)

ENGS = ("pe", "dve", "act", "pool", "sp")
_BLOCKSECT = {"pe": "tensor", "dve": "vector", "act": "scalar",
              "pool": "gpsimd", "sp": "sync"}

PROXY_LIB = [l for l in libcfg.all_libraries if l.name == "proxy"][0]


class Chan:
    __slots__ = ("name", "sem", "issued")

    def __init__(self, name, sem):
        self.name = name
        self.sem = sem
        self.issued = 0


class Bld:

    def __init__(self, nc):
        self.nc = nc
        self.ops = {e: [] for e in ENGS}
        self.sem = {e: nc.alloc_semaphore("tk_" + e) for e in ENGS}
        self.n = {e: 0 for e in ENGS}
        self.wm = {e: {} for e in ENGS}
        self.zw = {}
        self.zr = {}
        self.chans = []

    def chan(self, name):
        c = Chan(name, self.nc.alloc_semaphore("ch_" + name))
        self.chans.append(c)
        return c

    def _wait(self, e, key, val):
        if isinstance(key, Chan):
            val = 16 * key.issued
        if val <= 0:
            return
        if self.wm[e].get(key, 0) >= val:
            return
        self.wm[e][key] = val
        semh = key.sem if isinstance(key, Chan) else self.sem[key]
        self.ops[e].append(("w", semh, val))

    def _dep(self, e, zones, war):
        for z in zones:
            lw = self.zw.get(z)
            if lw is not None:
                self._wait(e, lw[0], lw[1])
            if war:
                for k, v in self.zr.get(z, {}).items():
                    self._wait(e, k, v)

    def emit(self, e, fn, reads=(), writes=()):
        self._dep(e, reads, False)
        self._dep(e, writes, True)
        self.ops[e].append(("i", fn, self.sem[e]))
        self.n[e] += 1
        t = self.n[e]
        for z in reads:
            self.zr.setdefault(z, {})[e] = t
        for z in writes:
            self.zw[z] = (e, t)
            self.zr[z] = {}

    def dma(self, ch, out_ap, in_ap, reads=(), writes=(), eng="sp"):
        self._dep(eng, reads, False)
        self._dep(eng, writes, True)
        self.ops[eng].append(("d", out_ap, in_ap, ch.sem))
        ch.issued += 1
        for z in reads:
            self.zr.setdefault(z, {})[ch] = 0
        for z in writes:
            self.zw[z] = (ch, 0)
            self.zr[z] = {}

    def barrier(self):
        for e1 in ENGS:
            for e2 in ENGS:
                if e1 != e2:
                    self._wait(e1, e2, self.n[e2])

    def wait_all_chans(self, e="sp"):
        for c in self.chans:
            self._wait(e, c, 1)

    def replay(self):
        with self.nc.Block() as block:
            for e in ENGS:
                ops = self.ops[e]

                def body(eng, ops=ops):
                    for op in ops:
                        if op[0] == "w":
                            eng.wait_ge(op[1], op[2])
                        elif op[0] == "i":
                            r = op[1](eng)
                            if isinstance(r, list):
                                r = r[-1]
                            r.then_inc(op[2], 1)
                        else:
                            eng.dma_start(op[1], op[2]).then_inc(op[3], 16)
                getattr(block, _BLOCKSECT[e])(body)


def _mmg(parts):
    def fn(eng):
        last = None
        for (o, l, r, s, st, tp) in parts:
            if tp is None:
                last = eng.matmul(o, l, r, start=s, stop=st)
            else:
                last = eng.matmul(o, l, r, start=s, stop=st, tile_position=tp)
        return last
    return fn


def build_core_program(NT=4096):
    NC = NT // T
    nc = bass.Bass("TRN2", target_bir_lowering=False)

    xT = nc.dram_tensor("xT", [1024, NT], F32, kind="ExternalInput").ap()
    wqkvA = nc.dram_tensor("wqkvA", [24, 128 * 8 * 128], F32, kind="ExternalInput").ap()
    woutA = nc.dram_tensor("woutA", [8, 128 * 8 * 128], F32, kind="ExternalInput").ap()
    wqkvF = [nc.dram_tensor(f"wqkvF{l}", [12, 128 * 8 * 128], F32, kind="ExternalInput").ap() for l in range(2)]
    woF = [nc.dram_tensor(f"woF{l}", [8, 128 * 4 * 128], F32, kind="ExternalInput").ap() for l in range(2)]
    wfus_h = [nc.dram_tensor(f"wfus{l}", [22, 128 * 24 * 128], F32, kind="ExternalInput") for l in range(2)]
    b1a_d = [nc.dram_tensor(f"b1a{l}", [128, 22], F32, kind="ExternalInput").ap() for l in range(2)]
    b1g_d = [nc.dram_tensor(f"b1g{l}", [128, 22], F32, kind="ExternalInput").ap() for l in range(2)]
    b2_d = [nc.dram_tensor(f"b2{l}", [128, 8], F32, kind="ExternalInput").ap() for l in range(2)]
    lng_d = [nc.dram_tensor(f"lng{l}", [128, 8], F32, kind="ExternalInput").ap() for l in range(2)]
    lnb_d = [nc.dram_tensor(f"lnb{l}", [128, 8], F32, kind="ExternalInput").ap() for l in range(2)]
    qsc_d = [nc.dram_tensor(f"qsc{l}", [128, 2], F32, kind="ExternalInput").ap() for l in range(2)]
    ksc_d = [nc.dram_tensor(f"ksc{l}", [128, 2], F32, kind="ExternalInput").ap() for l in range(2)]
    outT = nc.dram_tensor("outT", [1024, NT], F32, kind="ExternalOutput").ap()
    acc_s = nc.dram_tensor("acc_s", [1024, NT], F32, kind="Internal").ap()
    y1_s = nc.dram_tensor("y1_s", [1024, NT], F32, kind="Internal").ap()
    rott = [nc.dram_tensor(f"rott{l}", [4 * 128, NT], F32, kind="Internal").ap() for l in range(2)]

    A = nc.alloc_sbuf_tensor
    XN = A("XN", [128, 8 * 513], F32)
    X = A("Xt", [128, 8 * 512], F32)
    BIGQ = A("BIGQ", [128, 6152], F32)
    X1OFF = 2048
    ACC = A("ACCt", [128, 8 * 512], F32)
    O = A("Ot", [128, 8 * 512], F32)
    KR = A("KRt", [128, 6144], F32)
    VT = A("VTt", [128, 6144], F32)
    WS = [A("WS0", [128, 16 * 128], F32), A("WS1", [128, 16 * 128], F32)]
    W2S = [A("W2S0", [128, 8 * 128], F32), A("W2S1", [128, 8 * 128], F32)]
    P = A("Pt", [128, 1024], F32)
    PTS = A("PTSt", [128, 1024], F32)
    Mt = A("Mt", [128, 512], F32)
    MG = A("MGt", [128, 256], F32)
    QR = A("QRt", [128, 512], F32)
    VS = A("VSt", [128, 512], F32)
    SQA = A("SQAt", [128, 512], F32)
    SQB = A("SQBt", [128, 512], F32)
    MB = A("MBt", [128, 512], F32)
    RB = A("RBt", [128, 512], F32)
    RT = A("RTt", [128, 2048], F32)
    MA = A("MAt", [128, 1024], BF16)
    MF = A("MFt", [128, 4096], BF16)
    SM = A("SMt", [128, 96], F32)
    NMT = A("NMTt", [128, 16], F32)
    SMD = A("SMDt", [1, 256], F32)
    RBC = A("RBCt", [128, 128], F32)
    IDENT = A("IDENTt", [128, 128], F32)
    ONES = A("ONESt", [128, 1], F32)
    ONES2 = A("ONES2t", [128, 2], F32)
    GXN = A("GXNt", [128, 8], F32)
    GX1 = A("GX1t", [128, 8], F32)
    RSS = A("RSSt", [1, 512], F32)
    RROT = A("RROTt", [128, 128], F32)
    BONE = A("BONEt", [1, 128], F32)
    CEPS = A("CEPSt", [128, 1], F32)
    IOD = A("IODt", [128, 4], F32)
    B1A = [A(f"B1A{l}", [128, 22], F32) for l in range(2)]
    B1G = [A(f"B1G{l}", [128, 22], F32) for l in range(2)]
    B2 = [A(f"B2{l}", [128, 8], F32) for l in range(2)]
    LNG = [A(f"LNG{l}", [128, 8], F32) for l in range(2)]
    LNB = [A(f"LNB{l}", [128, 8], F32) for l in range(2)]
    QSC = [A(f"QSC{l}", [128, 2], F32) for l in range(2)]
    KSC = [A(f"KSC{l}", [128, 2], F32) for l in range(2)]

    def Qa(a, b_, p0=0, p1=128):
        return BIGQ[p0:p1, a:b_]

    def X1s(a, b_):
        return BIGQ[:, X1OFF + a:X1OFF + b_]

    PS = [nc.alloc_psum_tensor(f"PS{i}", [128, 512], F32) for i in range(8)]

    b = Bld(nc)
    CX = b.chan("cx")
    CW = [b.chan("cw0"), b.chan("cw1")]
    CW2 = [b.chan("cw2a"), b.chan("cw2b")]
    CRT = b.chan("crt")
    CB = b.chan("cb")
    CO = b.chan("co")
    CTW = b.chan("ctw")

    psz = [f"ps{i}" for i in range(8)]

    b.emit("dve", lambda e: e.memset(BONE[:, :], 1.0), writes=("BONE",))

    b.emit("dve", lambda e: e.memset(IDENT[:, :], 1.0), writes=("IDENT",))
    b.emit("pool", lambda e: e.affine_select(IDENT[:, :], IDENT[:, :], [[1, 128]],
                                             ALU.is_equal, 0.0, base=0,
                                             channel_multiplier=-1),
           reads=("lib",), writes=("IDENT",))
    b.emit("dve", lambda e: e.memset(ONES[:, :], 1.0), writes=("ONES",))
    b.emit("dve", lambda e: e.memset(CEPS[:, :], 1e-5), writes=("CEPS",))
    b.emit("dve", lambda e: e.memset(ONES2[:, :], 1.0), writes=("ONES2",))
    b.emit("pool", lambda e: e.affine_select(ONES2[:, 0:1], ONES2[:, 0:1], [[1, 1]],
                                             ALU.is_ge, 0.0, base=63,
                                             channel_multiplier=-1),
           reads=("lib",), writes=("ONES2",))
    b.emit("pool", lambda e: e.affine_select(ONES2[:, 1:2], ONES2[:, 1:2], [[1, 1]],
                                             ALU.is_ge, 0.0, base=-64,
                                             channel_multiplier=1),
           reads=("lib",), writes=("ONES2",))
    for (c0, sgn, kb) in ((0, -1.0, -32), (32, 1.0, 0), (64, -1.0, -96), (96, 1.0, -64)):
        sl = RROT[:, c0:c0 + 32]
        b.emit("dve", lambda e, sl=sl, sgn=sgn: e.memset(sl, sgn), writes=("RROT",))
        b.emit("pool", lambda e, sl=sl, kb=kb: e.affine_select(
            sl, sl, [[-1, 32]], ALU.is_equal, 0.0, base=kb,
            channel_multiplier=1), reads=("lib",), writes=("RROT",))
    b.emit("dve", lambda e: e.memset(MA[:, :], 0.0), writes=("MA",))
    b.emit("dve", lambda e: e.memset(MF[:, :], 0.0), writes=("MF",))
    for qs in range(2):
        sl = MA[:, qs * 512:(qs + 1) * 512]
        b.emit("pool", lambda e, sl=sl, qs=qs: e.affine_select(
            sl, sl, [[1, 512]], ALU.is_ge, NEG, base=-(128 * qs) - 1,
            channel_multiplier=-1), reads=("lib",), writes=("MA",))
        b.emit("pool", lambda e, sl=sl, qs=qs: e.affine_select(
            sl, sl, [[-1, 512]], ALU.is_ge, NEG, base=(128 * qs) + 256,
            channel_multiplier=1), reads=("lib",), writes=("MA",))
    for qs in range(4):
        sl = MF[:, qs * 1024:(qs + 1) * 1024]
        b.emit("pool", lambda e, sl=sl, qs=qs: e.affine_select(
            sl, sl, [[1, 1024]], ALU.is_ge, NEG, base=-(128 * qs),
            channel_multiplier=-1), reads=("lib",), writes=("MF",))
        b.emit("pool", lambda e, sl=sl, qs=qs: e.affine_select(
            sl, sl, [[-1, 1024]], ALU.is_ge, NEG, base=(128 * qs) + 512,
            channel_multiplier=1), reads=("lib",), writes=("MF",))
    for l in range(2):
        b.dma(CB, B1A[l][:, :], b1a_d[l], writes=("BI",))
        b.dma(CB, B1G[l][:, :], b1g_d[l], writes=("BI",))
        b.dma(CB, B2[l][:, :], b2_d[l], writes=("BI",))
        b.dma(CB, LNG[l][:, :], lng_d[l], writes=("BI",))
        b.dma(CB, LNB[l][:, :], lnb_d[l], writes=("BI",))
        b.dma(CB, QSC[l][:, :], qsc_d[l], writes=("BI",))
        b.dma(CB, KSC[l][:, :], ksc_d[l], writes=("BI",))

    for qtr in range(4):
        b.emit("pool", lambda e, q=qtr: e.iota(
            IOD[q * 32:(q + 1) * 32, 0:1], [[1, 1]], base=0, channel_multiplier=1,
            allow_small_or_imprecise_dtypes=True), reads=("lib",), writes=("IOD",))
    b.emit("act", lambda e: e.activation(IOD[:, 1:2], IOD[:, 0:1], AF.Exp,
                                         scale=-2.0 * LN1E4 / 64.0),
           reads=("IOD",), writes=("IOD",))
    b.emit("dve", lambda e: e.tensor_scalar(IOD[:, 2:3], IOD[:, 0:1],
                                            2.0 / 89.6, 25.6 / 89.6,
                                            ALU.mult, ALU.add),
           reads=("IOD",), writes=("IOD",))
    b.emit("act", lambda e: e.activation(IOD[:, 3:4], IOD[:, 2:3], AF.Ln),
           reads=("IOD",), writes=("IOD",))
    for l in range(2):
        for cb in range(NC):
            t0 = cb * 512
            b.emit("pool", lambda e, t0=t0: e.iota(
                MB[:, :], [[1, 512]], base=t0, channel_multiplier=0,
                allow_small_or_imprecise_dtypes=True),
                reads=("lib",), writes=("MB",))
            b.emit("dve", lambda e: e.tensor_scalar_mul(RB[:, :], MB[:, :], IOD[:, 1:2]),
                   reads=("MB", "IOD"), writes=("RB",))
            b.emit("dve", lambda e: e.tensor_scalar_add(P[:, 0:512], RB[:, :], PI),
                   reads=("RB",), writes=("P",))
            for j in range(9, -1, -1):
                cj = 2.0 * PI * (1 << j)
                b.emit("dve", lambda e, cj=cj: e.tensor_single_scalar(
                    VS[:, :], P[:, 0:512], cj, ALU.is_ge),
                    reads=("P",), writes=("VS",))
                b.emit("dve", lambda e, cj=cj: e.scalar_tensor_tensor(
                    P[:, 0:512], VS[:, :], -cj, P[:, 0:512], ALU.mult, ALU.add),
                    reads=("VS",), writes=("P",))
            b.emit("dve", lambda e: e.tensor_scalar_add(P[:, 0:512], P[:, 0:512], -PI),
                   reads=(), writes=("P",))
            b.emit("dve", lambda e: e.tensor_scalar_add(P[:, 512:1024], P[:, 0:512], PI / 2),
                   reads=(), writes=("P",))
            b.emit("dve", lambda e: e.tensor_single_scalar(
                VS[:, :], P[:, 512:1024], PI, ALU.is_ge),
                reads=("P",), writes=("VS",))
            b.emit("dve", lambda e: e.scalar_tensor_tensor(
                P[:, 512:1024], VS[:, :], -2.0 * PI, P[:, 512:1024],
                ALU.mult, ALU.add),
                reads=("VS",), writes=("P",))
            b.emit("act", lambda e: e.activation(SQB[:, :], P[:, 0:512], AF.Sin),
                   reads=("P",), writes=("SQB",))
            b.emit("act", lambda e: e.activation(VS[:, :], P[:, 512:1024], AF.Sin),
                   reads=("P",), writes=("VS",))
            b.emit("pool", lambda e, t0=t0: e.iota(
                QR[:, :], [[1, 512]], base=t0 - 2048, channel_multiplier=0,
                allow_small_or_imprecise_dtypes=True),
                reads=("lib",), writes=("QR",))
            b.emit("dve", lambda e: e.tensor_scalar_mul(QR[:, :], QR[:, :], IOD[:, 3:4]),
                   reads=("IOD",), writes=("QR",))
            b.emit("act", lambda e: e.activation(MB[:, :], QR[:, :], AF.Exp,
                                                 scale=1.0 / 256.0),
                   reads=("QR",), writes=("MB",))
            b.emit("act", lambda e: e.activation(RB[:, :], QR[:, :], AF.Exp,
                                                 scale=-1.0 / 256.0),
                   reads=("QR",), writes=("RB",))
            tabs = [(0, VS, MB, QSC[l][:, 0:1]),
                    (512, SQB, MB, QSC[l][:, 1:2]),
                    (1024, VS, RB, KSC[l][:, 0:1]),
                    (1536, SQB, RB, KSC[l][:, 1:2])]
            for (off, t1, t2, sc) in tabs:
                b.emit("dve", lambda e, off=off, t1=t1, t2=t2: e.tensor_mul(
                    RT[:, off:off + 512], t1[:, :], t2[:, :]),
                    reads=("VS", "SQB", "MB", "RB"), writes=("RT",))
                b.emit("dve", lambda e, off=off, sc=sc: e.tensor_scalar_mul(
                    RT[:, off:off + 512], RT[:, off:off + 512], sc),
                    reads=("BI",), writes=("RT",))
            for i in range(4):
                b.dma(CTW, rott[l][i * 128:(i + 1) * 128, t0:t0 + 512],
                      RT[:, i * 512:(i + 1) * 512],
                      reads=("RT",), writes=(f"d:rott{l}",))

    b.barrier()


    def emit_ln_stats_and_apply(src_tile, src_cols, dst_writes, dst_slices,
                                affine=None):
        for kk in range(8):
            sq = SQA if kk % 2 == 0 else SQB
            sqz = "SQA" if kk % 2 == 0 else "SQB"
            src = src_cols(kk)
            b.emit("act", lambda e, sq=sq, src=src: e.activation(sq[:, :], src, AF.Square),
                   reads=(src_tile,), writes=(sqz,))
            b.emit("pe", _mmg([(PS[6][0:1, 0:512], ONES[:, 0:1], src,
                                kk == 0, kk == 7, None)]),
                   reads=(src_tile, "ONES"), writes=("ps6",))
            b.emit("pe", _mmg([(PS[7][0:1, 0:512], ONES[:, 0:1], sq[:, :],
                                kk == 0, kk == 7, None)]),
                   reads=(sqz, "ONES"), writes=("ps7",))
        b.emit("dve", lambda e: e.tensor_scalar_mul(MB[0:1, :], PS[6][0:1, 0:512], 1.0 / 1024.0),
               reads=("ps6",), writes=("MB",))
        b.emit("dve", lambda e: e.tensor_scalar_mul(RB[0:1, :], PS[7][0:1, 0:512], 1.0 / 1024.0),
               reads=("ps7",), writes=("RB",))
        b.emit("act", lambda e: e.activation(RSS[0:1, :], MB[0:1, :], AF.Square),
               reads=("MB",), writes=("RSS",))
        b.emit("dve", lambda e: e.tensor_sub(RB[0:1, :], RB[0:1, :], RSS[0:1, :]),
               reads=("RSS",), writes=("RB",))
        b.emit("act", lambda e: e.activation(RSS[0:1, :], RB[0:1, :], AF.Ln,
                                             bias=CEPS[0:1, 0:1]),
               reads=("RB", "CEPS"), writes=("RSS",))
        b.emit("act", lambda e: e.activation(RB[0:1, :], RSS[0:1, :], AF.Exp, scale=-0.5),
               reads=("RSS",), writes=("RB",))
        b.emit("pe", _mmg([(PS[6][:, 0:512], BONE[0:1, 0:128], MB[0:1, :],
                            True, True, None)]),
               reads=("MB", "BONE"), writes=("ps6",))
        b.emit("pe", _mmg([(PS[7][:, 0:512], BONE[0:1, 0:128], RB[0:1, :],
                            True, True, None)]),
               reads=("RB", "BONE"), writes=("ps7",))
        for kk in range(8):
            src = src_cols(kk)
            dst = dst_slices(kk)
            b.emit("dve", lambda e, d=dst, s=src: e.tensor_sub(d, s, PS[6][:, 0:512]),
                   reads=(src_tile, "ps6"), writes=dst_writes)
            b.emit("dve", lambda e, d=dst: e.tensor_mul(d, d, PS[7][:, 0:512]),
                   reads=("ps7",), writes=dst_writes)
            if affine is not None:
                g_ap, bt_ap = affine
                b.emit("dve", lambda e, d=dst, g=g_ap(kk), bb=bt_ap(kk): e.tensor_scalar(
                    d, d, g, bb, ALU.mult, ALU.add),
                    reads=("BI",), writes=dst_writes)

    def wslot_load(widx, dram_ap, j, nblk):
        s = widx % 2
        b.dma(CW[s], WS[s][:, 0:nblk * 128], dram_ap[j:j + 1, :],
              writes=(f"WS{s}",))
        return s

    def wfus_load_ag(widx, l, j):
        s = widx % 2
        src_ap = bass.AP(wfus_h[l], j * (128 * 24 * 128),
                         [[24 * 128, 128], [1, 16 * 128]])
        b.dma(CW[s], WS[s][:, 0:16 * 128], src_ap, writes=(f"WS{s}",))

    def wfus_load_w2(widx, l, j):
        s = widx % 2
        src_ap = bass.AP(wfus_h[l], j * (128 * 24 * 128) + 16 * 128,
                         [[24 * 128, 128], [1, 8 * 128]])
        b.dma(CW2[s], W2S[s][:, 0:8 * 128], src_ap, writes=(f"W2S{s}",))

    for c in range(NC):
        ccol = c * 512
        for kk in range(8):
            b.dma(CX, X[:, kk * 512:(kk + 1) * 512],
                  xT[kk * 128:(kk + 1) * 128, ccol:ccol + 512], writes=("X",))
        emit_ln_stats_and_apply(
            "X", lambda kk: X[:, kk * 512:(kk + 1) * 512], ("XN",),
            dst_slices=lambda kk: XN[:, kk * 513 + 1:kk * 513 + 513])

        s0 = (2 * c) % 3
        s1 = (2 * c + 1) % 3

        def a1_qkv_group(m):
            s = m % 2
            parts = [(PS[s][:, 0:512], WS[s][:, kk * 128:(kk + 1) * 128],
                      XN[:, kk * 513 + 1:kk * 513 + 513], kk == 0, kk == 7, None)
                     for kk in range(8)]
            b.emit("pe", _mmg(parts), reads=(f"WS{s}", "XN"), writes=(psz[s],))

        def a1_qkv_evac(m):
            s = m % 2
            if m < 8:
                b.emit("act", lambda e, m=m, s=s: e.copy(Qa(m * 512, (m + 1) * 512),
                                                         PS[s][:, 0:512]),
                       reads=(psz[s],), writes=("Q",))
            elif m < 16:
                j = m - 8
                d0 = KR[:, j * 768 + s0 * 256:j * 768 + s0 * 256 + 256]
                d1 = KR[:, j * 768 + s1 * 256:j * 768 + s1 * 256 + 256]
                b.emit("act", lambda e, d=d0, s=s: e.copy(d, PS[s][:, 0:256]),
                       reads=(psz[s],), writes=(f"KR{s0}",))
                b.emit("act", lambda e, d=d1, s=s: e.copy(d, PS[s][:, 256:512]),
                       reads=(psz[s],), writes=(f"KR{s1}",))
            else:
                j = m - 16
                b.emit("act", lambda e, s=s: e.copy(VS[:, :], PS[s][:, 0:512]),
                       reads=(psz[s],), writes=("VS",))
                for u in range(4):
                    b.emit("pe", lambda e, u=u: e.transpose(
                        PS[2][:, u * 128:(u + 1) * 128],
                        VS[:, u * 128:(u + 1) * 128], IDENT[:, :]),
                        reads=("VS", "IDENT"), writes=("ps2",))
                for half in range(2):
                    sl = s0 if half == 0 else s1
                    src = PS[2][:, half * 256:half * 256 + 256]
                    dst = bass.AP(VT, sl * 2048 + j * 128,
                                  [[6144, 128], [1024, 2], [1, 128]])
                    b.emit("act", lambda e, d=dst, sp=src: e.copy(d, sp),
                           reads=("ps2",), writes=(f"VT{sl}",))

        wslot_load(0, wqkvA, 0, 8)
        wslot_load(1, wqkvA, 1, 8)
        a1_qkv_group(0)
        for m in range(1, 24):
            if m + 1 < 24:
                wslot_load(m + 1, wqkvA, m + 1, 8)
            a1_qkv_group(m)
            a1_qkv_evac(m - 1)
        a1_qkv_evac(23)

        aunits = [(b_loc, hp, qs, h) for b_loc in range(2)
                  for hp in range(8) for qs in range(2) for h in range(2)]
        A1_SBANK = (3, 4, 0, 1)

        def a1_env(u, ui):
            b_loc, hp, qs, h = u
            blk = 2 * c + b_loc
            return dict(b_loc=b_loc, hp=hp, qs=qs, h=h,
                        spv=(blk - 1) % 3, scv=blk % 3,
                        first=(blk == 0),
                        o_off=256 if blk == 0 else 0,
                        qcols=hp * 512 + b_loc * 256 + qs * 128,
                        cidx=((b_loc * 8 + hp) * 2 + qs) * 2,
                        nb=2 * (ui % 4),
                        sbk=A1_SBANK[ui % 4])

        def a1_part1(u, ui):
            v = a1_env(u, ui)
            hp, qs, h = v["hp"], v["qs"], v["h"]
            spv, scv, first = v["spv"], v["scv"], v["first"]
            o_off, qcols, nb, sbk = v["o_off"], v["qcols"], v["nb"], v["sbk"]
            parts = []
            if not first:
                parts.append((PS[sbk][:, 0:256],
                              Qa(qcols, qcols + 128, 64 * h, 64 * h + 64),
                              KR[64 * h:64 * h + 64,
                                 hp * 768 + spv * 256:hp * 768 + spv * 256 + 256],
                              True, False, None))
            parts.append((PS[sbk][:, 256:512],
                          Qa(qcols, qcols + 128, 64 * h, 64 * h + 64),
                          KR[64 * h:64 * h + 64,
                             hp * 768 + scv * 256:hp * 768 + scv * 256 + 256],
                          first, True, None))
            rd = ("Q", f"KR{scv}") if first else ("Q", f"KR{spv}", f"KR{scv}")
            b.emit("pe", _mmg(parts), reads=rd, writes=(psz[sbk],))
            b.emit("dve", lambda e, sbk=sbk, o=o_off, qs=qs: e.tensor_add(
                PS[sbk][:, o:512], PS[sbk][:, o:512],
                MA[:, qs * 512 + o:qs * 512 + 512]),
                reads=("MA",), writes=(psz[sbk],))
            b.emit("dve", lambda e, sbk=sbk, o=o_off, nb=nb: e.reduce_max(
                NMT[:, nb:nb + 1], PS[sbk][:, o:512], axis=AX.X, negate=True),
                reads=(psz[sbk],), writes=("NMT",))
            b.emit("dve", lambda e, nb=nb: e.tensor_scalar_mul(
                NMT[:, nb + 1:nb + 2], NMT[:, nb:nb + 1], 0.125),
                reads=(), writes=("NMT",))

        def a1_part2(u, ui):
            v = a1_env(u, ui)
            hp, qs, h = v["hp"], v["qs"], v["h"]
            spv, scv, first = v["spv"], v["scv"], v["first"]
            o_off, qcols, cidx, nb, sbk = (v["o_off"], v["qcols"], v["cidx"],
                                           v["nb"], v["sbk"])
            b.emit("act", lambda e, sbk=sbk, o=o_off, h=h, ci=cidx, nb=nb: e.activation(
                P[:, h * 512 + o:h * 512 + 512], PS[sbk][:, o:512], AF.Exp,
                bias=NMT[:, nb + 1:nb + 2], scale=0.125,
                accum_out=SM[:, ci + h:ci + h + 1]),
                reads=(psz[sbk], "NMT"), writes=("P", "SM"))
            jl = [2, 3] if first else [0, 1, 2, 3]
            nj = len(jl)
            ptb = 5 if h == 0 else 7
            if h == 1:
                b.emit("pe", lambda e, ci=cidx: [
                    e.transpose(PS[6][0:1, 0:128], SM[:, ci:ci + 1], IDENT[:, :]),
                    e.transpose(PS[6][0:1, 128:256], SM[:, ci + 1:ci + 2],
                                IDENT[:, :])],
                    reads=("SM", "IDENT"), writes=("ps6",))
                b.emit("dve", lambda e: e.reciprocal(SMD[0:1, :], PS[6][0:1, 0:256]),
                       reads=("ps6",), writes=("SMD",))
            for jj, j in enumerate(jl):
                b.emit("pe", lambda e, h=h, j=j, jj=jj, ptb=ptb: e.transpose(
                    PS[ptb][:, jj * 128:(jj + 1) * 128],
                    P[:, h * 512 + j * 128:h * 512 + (j + 1) * 128],
                    IDENT[:, :]),
                    reads=("P", "IDENT"), writes=(psz[ptb],))
            if h == 1:
                b.emit("pe", _mmg([(PS[6][0:64, 0:128], BONE[0:1, 0:64],
                                    SMD[0:1, 0:128], True, True, (0, 0)),
                                   (PS[6][64:128, 0:128], BONE[0:1, 0:64],
                                    SMD[0:1, 128:256], True, True, (0, 64))]),
                       reads=("SMD", "BONE"), writes=("ps6",))
                b.emit("act", lambda e: e.copy(RBC[:, :], PS[6][:, 0:128]),
                       reads=("ps6",), writes=("RBC",))
            b.emit("act", lambda e, h=h, nj=nj, ptb=ptb: e.copy(
                PTS[:, h * 512:h * 512 + nj * 128],
                PS[ptb][:, 0:nj * 128]),
                reads=(psz[ptb],), writes=("PTS",))
            parts = []
            vread = set()
            for jj, j in enumerate(jl):
                sl = spv if j < 2 else scv
                ts = j % 2
                vread.add(f"VT{sl}")
                base = sl * 2048 + ts * 1024 + (2 * hp + h) * 64
                parts.append((PS[2][64 * h:64 * h + 64, 0:128],
                              VT[:, base:base + 64],
                              PTS[:, h * 512 + jj * 128:h * 512 + (jj + 1) * 128],
                              jj == 0, jj == nj - 1, (0, 64 * h)))
            b.emit("pe", _mmg(parts), reads=tuple(vread) + ("PTS",),
                   writes=("ps2",))
            if h == 1:
                b.emit("dve", lambda e, qc=qcols: e.tensor_mul(
                    O[:, qc:qc + 128], PS[2][:, 0:128], RBC[:, :]),
                    reads=("ps2", "RBC"), writes=("O",))

        a1_part1(aunits[0], 0)
        a1_part1(aunits[1], 1)
        for ui in range(2, len(aunits)):
            a1_part1(aunits[ui], ui)
            a1_part2(aunits[ui - 2], ui - 2)
        a1_part2(aunits[-2], len(aunits) - 2)
        a1_part2(aunits[-1], len(aunits) - 1)


        def a1_wo_group(fo):
            s = fo % 2
            parts = [(PS[s][:, 0:512], WS[s][:, kk * 128:(kk + 1) * 128],
                      O[:, kk * 512:(kk + 1) * 512], kk == 0, kk == 7, None)
                     for kk in range(8)]
            b.emit("pe", _mmg(parts), reads=(f"WS{s}", "O"), writes=(psz[s],))

        def a1_wo_evac(fo):
            s = fo % 2
            b.emit("dve", lambda e, fo=fo, s=s: e.tensor_add(
                ACC[:, fo * 512:(fo + 1) * 512], PS[s][:, 0:512],
                X[:, fo * 512:(fo + 1) * 512]),
                reads=(psz[s], "X"), writes=(f"ACC{fo}",))
            b.dma(CO, acc_s[fo * 128:(fo + 1) * 128, ccol:ccol + 512],
                  ACC[:, fo * 512:(fo + 1) * 512],
                  reads=(f"ACC{fo}",), writes=("d:acc",))

        wslot_load(0, woutA, 0, 8)
        wslot_load(1, woutA, 1, 8)
        a1_wo_group(0)
        for fo in range(1, 8):
            if fo + 1 < 8:
                wslot_load(fo + 1, woutA, fo + 1, 8)
            a1_wo_group(fo)
            a1_wo_evac(fo - 1)
        a1_wo_evac(7)

    b.barrier()

    def sweep_ffn(l, src, dst):
        b.emit("dve", lambda e: e.memset(GXN[:, :], 0.0), writes=("GXN",))
        b.emit("dve", lambda e: e.memset(GX1[:, :], 0.0), writes=("GX1",))
        for c in range(NC):
            ccol = c * 512
            spv = (c - 1) % 2
            scv = c % 2
            first = (c == 0)
            if src == "x":
                for kk in range(8):
                    b.dma(CX, X[:, kk * 512:(kk + 1) * 512],
                          xT[kk * 128:(kk + 1) * 128, ccol:ccol + 512], writes=("X",))
                for kk in range(8):
                    b.emit("dve", lambda e, kk=kk: e.tensor_copy(
                        XN[:, kk * 513:kk * 513 + 1], GXN[:, kk:kk + 1]),
                        reads=("GXN",), writes=("XN",))
                emit_ln_stats_and_apply(
                    "X", lambda kk: X[:, kk * 512:(kk + 1) * 512], ("XN",),
                    dst_slices=lambda kk: XN[:, kk * 513 + 1:kk * 513 + 513])
                for kk in range(8):
                    b.emit("dve", lambda e, kk=kk: e.tensor_copy(
                        GXN[:, kk:kk + 1], XN[:, kk * 513 + 512:kk * 513 + 513]),
                        reads=("XN",), writes=("GXN",))
            else:
                for kk in range(8):
                    b.dma(CX, X[:, kk * 512:(kk + 1) * 512],
                          acc_s[kk * 128:(kk + 1) * 128, ccol:ccol + 512],
                          reads=("d:acc",), writes=("X",))
                if first:
                    for kk in range(8):
                        b.emit("dve", lambda e, kk=kk: e.memset(
                            XN[:, kk * 513:kk * 513 + 1], 0.0), writes=("XN",))
                        b.dma(CX, XN[:, kk * 513 + 1:kk * 513 + 513],
                              y1_s[kk * 128:(kk + 1) * 128, 0:512],
                              reads=("d:y1",), writes=("XN",))
                else:
                    for kk in range(8):
                        b.dma(CX, XN[:, kk * 513:kk * 513 + 513],
                              y1_s[kk * 128:(kk + 1) * 128, ccol - 1:ccol + 512],
                              reads=("d:y1",), writes=("XN",))
            for i in range(4):
                b.dma(CRT, RT[:, i * 512:(i + 1) * 512],
                      rott[l][i * 128:(i + 1) * 128, ccol:ccol + 512],
                      reads=(f"d:rott{l}",), writes=("RT",))

            def hshift(kk, off=0, w=512):
                base = kk * 513 + (1 if kk < 4 else 0) + off
                return XN[:, base:base + w]

            def f_qkv_group(m):
                s = m % 2
                parts = [(PS[s][:, 0:512], WS[s][:, kk * 128:(kk + 1) * 128],
                          hshift(kk), kk == 0, kk == 7, None) for kk in range(8)]
                b.emit("pe", _mmg(parts), reads=(f"WS{s}", "XN"), writes=(psz[s],))

            def f_qkv_evac(m):
                s = m % 2
                if m < 8:
                    b.emit("act", lambda e, s=s: e.activation(SQA[:, :], PS[s][:, 0:512],
                                                              AF.Square),
                           reads=(psz[s],), writes=("SQA",))
                    b.emit("pe", _mmg([(PS[5][0:1, 0:512], ONES2[:, 0:1], SQA[:, :],
                                        True, True, None),
                                       (PS[6][0:1, 0:512], ONES2[:, 1:2], SQA[:, :],
                                        True, True, None)]),
                           reads=("SQA", "ONES2"), writes=("ps5", "ps6"))
                    b.emit("dve", lambda e: e.tensor_scalar_max(RSS[0:1, :],
                                                                PS[5][0:1, 0:512], 1e-24),
                           reads=("ps5",), writes=("RSS",))
                    b.emit("dve", lambda e: e.tensor_scalar_max(P[0:1, 0:512],
                                                                PS[6][0:1, 0:512], 1e-24),
                           reads=("ps6",), writes=("P",))
                    b.emit("act", lambda e: e.activation(RSS[0:1, :], RSS[0:1, :], AF.Ln),
                           reads=(), writes=("RSS",))
                    b.emit("act", lambda e: e.activation(RSS[0:1, :], RSS[0:1, :], AF.Exp,
                                                         scale=-0.5),
                           reads=(), writes=("RSS",))
                    b.emit("act", lambda e: e.activation(P[0:1, 0:512], P[0:1, 0:512], AF.Ln),
                           reads=(), writes=("P",))
                    b.emit("act", lambda e: e.activation(P[0:1, 0:512], P[0:1, 0:512], AF.Exp,
                                                         scale=-0.5),
                           reads=(), writes=("P",))
                    b.emit("pe", _mmg([(PS[6][0:64, 0:512], BONE[0:1, 0:64],
                                        RSS[0:1, :], True, True, (0, 0)),
                                       (PS[6][64:128, 0:512], BONE[0:1, 0:64],
                                        P[0:1, 0:512], True, True, (0, 64))]),
                           reads=("RSS", "P", "BONE"), writes=("ps6",))
                    b.emit("act", lambda e: e.copy(MB[:, :], PS[6][:, 0:512]),
                           reads=("ps6",), writes=("MB",))
                    b.emit("dve", lambda e, s=s: e.tensor_mul(QR[:, :], PS[s][:, 0:512],
                                                              MB[:, :]),
                           reads=(psz[s], "MB"), writes=("QR",))
                    tb = 0 if m < 4 else 1024
                    if m < 4:
                        dzone = "Q"
                        dcol0, dcol1 = m * 512, (m + 1) * 512

                        def dsl(p0, p1, a=dcol0, bcol=dcol1):
                            return BIGQ[p0:p1, a:bcol]
                    else:
                        j = m - 4
                        dzone = f"KR{scv}"
                        dcol0 = j * 1024 + scv * 512
                        dcol1 = dcol0 + 512

                        def dsl(p0, p1, a=dcol0, bcol=dcol1):
                            return KR[p0:p1, a:bcol]
                    b.emit("pe", _mmg([(PS[5][:, 0:512], RROT[:, :], QR[:, :],
                                        True, True, None)]),
                           reads=("RROT", "QR"), writes=("ps5",))
                    b.emit("dve", lambda e, tb=tb, d=dsl(0, 128): e.tensor_mul(
                        d, QR[:, :], RT[:, tb:tb + 512]),
                        reads=("QR", "RT"), writes=(dzone,))
                    b.emit("dve", lambda e, tb=tb: e.tensor_mul(
                        VS[:, :], PS[5][:, 0:512], RT[:, tb + 512:tb + 1024]),
                        reads=("ps5", "RT"), writes=("VS",))
                    b.emit("dve", lambda e, d=dsl(0, 128): e.tensor_add(d, d, VS[:, :]),
                           reads=("VS",), writes=(dzone,))
                else:
                    j = m - 8
                    b.emit("act", lambda e, s=s: e.copy(VS[:, :], PS[s][:, 0:512]),
                           reads=(psz[s],), writes=("VS",))
                    for u in range(4):
                        b.emit("pe", lambda e, u=u: e.transpose(
                            PS[2][:, u * 128:(u + 1) * 128],
                            VS[:, u * 128:(u + 1) * 128], IDENT[:, :]),
                            reads=("VS", "IDENT"), writes=("ps2",))
                    dstv = bass.AP(VT, scv * 2048 + j * 128,
                                   [[6144, 128], [512, 4], [1, 128]])
                    b.emit("act", lambda e, d=dstv: e.copy(d, PS[2][:, 0:512]),
                           reads=("ps2",), writes=(f"VT{scv}",))

            wslot_load(0, wqkvF[l], 0, 8)
            wslot_load(1, wqkvF[l], 1, 8)
            f_qkv_group(0)
            for m in range(1, 12):
                if m + 1 < 12:
                    wslot_load(m + 1, wqkvF[l], m + 1, 8)
                f_qkv_group(m)
                f_qkv_evac(m - 1)
            f_qkv_evac(11)

            funits = [(hp, qs, h) for hp in range(4) for qs in range(4)
                      for h in range(2)]

            def f_env(u, ui):
                hp, qs, h = u
                return dict(hp=hp, qs=qs, h=h,
                            qcols=hp * 512 + qs * 128,
                            scidx=(hp * 4 + qs) * 2 + h,
                            nbh=4 * (ui % 2),
                            bkA=2 + 2 * (ui % 2), bkB=3 + 2 * (ui % 2))

            def f_part1(u, ui):
                v = f_env(u, ui)
                hp, qs, h = v["hp"], v["qs"], v["h"]
                nbh, bkA, bkB = v["nbh"], v["bkA"], v["bkB"]
                if not first:
                    b.emit("pe", _mmg([(PS[bkA][:, 0:512],
                                        Qa(v["qcols"], v["qcols"] + 128, 64 * h, 64 * h + 64),
                                        KR[64 * h:64 * h + 64,
                                           hp * 1024 + spv * 512:hp * 1024 + spv * 512 + 512],
                                        True, True, None)]),
                           reads=("Q", f"KR{spv}"), writes=(psz[bkA],))
                b.emit("pe", _mmg([(PS[bkB][:, 0:512],
                                    Qa(v["qcols"], v["qcols"] + 128, 64 * h, 64 * h + 64),
                                    KR[64 * h:64 * h + 64,
                                       hp * 1024 + scv * 512:hp * 1024 + scv * 512 + 512],
                                    True, True, None)]),
                       reads=("Q", f"KR{scv}"), writes=(psz[bkB],))
                if not first:
                    b.emit("dve", lambda e, bk=bkA, qs=qs: e.tensor_add(
                        PS[bk][:, 0:512], PS[bk][:, 0:512],
                        MF[:, qs * 1024:qs * 1024 + 512]),
                        reads=("MF",), writes=(psz[bkA],))
                b.emit("dve", lambda e, bk=bkB, qs=qs: e.tensor_add(
                    PS[bk][:, 0:512], PS[bk][:, 0:512],
                    MF[:, qs * 1024 + 512:qs * 1024 + 1024]),
                    reads=("MF",), writes=(psz[bkB],))
                if not first:
                    b.emit("dve", lambda e, bk=bkA, nbh=nbh: e.reduce_max(
                        NMT[:, nbh:nbh + 1], PS[bk][:, 0:512], axis=AX.X, negate=True),
                        reads=(psz[bkA],), writes=("NMT",))
                b.emit("dve", lambda e, bk=bkB, nbh=nbh: e.reduce_max(
                    NMT[:, nbh + 1:nbh + 2], PS[bk][:, 0:512], axis=AX.X, negate=True),
                    reads=(psz[bkB],), writes=("NMT",))
                if not first:
                    b.emit("dve", lambda e, nbh=nbh: e.tensor_tensor(
                        NMT[:, nbh + 2:nbh + 3], NMT[:, nbh:nbh + 1],
                        NMT[:, nbh + 1:nbh + 2], ALU.min),
                        reads=(), writes=("NMT",))
                    b.emit("dve", lambda e, nbh=nbh: e.tensor_scalar_mul(
                        NMT[:, nbh + 3:nbh + 4], NMT[:, nbh + 2:nbh + 3], 8.0),
                        reads=(), writes=("NMT",))
                else:
                    b.emit("dve", lambda e, nbh=nbh: e.tensor_scalar_mul(
                        NMT[:, nbh + 3:nbh + 4], NMT[:, nbh + 1:nbh + 2], 8.0),
                        reads=(), writes=("NMT",))

            def f_part2(u, ui):
                v = f_env(u, ui)
                hp, qs, h = v["hp"], v["qs"], v["h"]
                nbh, bkA, bkB = v["nbh"], v["bkA"], v["bkB"]
                scidx = v["scidx"]
                qcols = v["qcols"]
                if not first:
                    b.emit("act", lambda e, bk=bkA, si=scidx, nbh=nbh: e.activation(
                        P[:, 0:512], PS[bk][:, 0:512], AF.Exp,
                        bias=NMT[:, nbh + 3:nbh + 4], scale=8.0,
                        accum_out=SM[:, 2 * si:2 * si + 1]),
                        reads=(psz[bkA], "NMT"), writes=("P", "SM"))
                    b.emit("act", lambda e, bk=bkB, si=scidx, nbh=nbh: e.activation(
                        P[:, 512:1024], PS[bk][:, 0:512], AF.Exp,
                        bias=NMT[:, nbh + 3:nbh + 4], scale=8.0,
                        accum_out=SM[:, 2 * si + 1:2 * si + 2]),
                        reads=(psz[bkB], "NMT"), writes=("P", "SM"))
                    b.emit("dve", lambda e, si=scidx: e.tensor_add(
                        SM[:, 64 + si:65 + si], SM[:, 2 * si:2 * si + 1],
                        SM[:, 2 * si + 1:2 * si + 2]),
                        reads=(), writes=("SM",))
                else:
                    b.emit("act", lambda e, bk=bkB, si=scidx, nbh=nbh: e.activation(
                        P[:, 512:1024], PS[bk][:, 0:512], AF.Exp,
                        bias=NMT[:, nbh + 3:nbh + 4], scale=8.0,
                        accum_out=SM[:, 64 + si:65 + si]),
                        reads=(psz[bkB], "NMT"), writes=("P", "SM"))
                last_pair = (h == 1)
                if last_pair:
                    si0 = (hp * 4 + qs) * 2
                    b.emit("pe", lambda e, si=si0: [
                        e.transpose(PS[1][0:1, 0:128], SM[:, 64 + si:65 + si],
                                    IDENT[:, :]),
                        e.transpose(PS[1][0:1, 128:256], SM[:, 65 + si:66 + si],
                                    IDENT[:, :])],
                        reads=("SM", "IDENT"), writes=("ps1",))
                    b.emit("dve", lambda e: e.reciprocal(SMD[0:1, :], PS[1][0:1, 0:256]),
                           reads=("ps1",), writes=("SMD",))
                jl = [4, 5, 6, 7] if first else list(range(8))
                for j in jl:
                    pb = 6 + j // 4
                    b.emit("pe", lambda e, j=j, pb=pb: e.transpose(
                        PS[pb][:, (j % 4) * 128:(j % 4) * 128 + 128],
                        P[:, j * 128:(j + 1) * 128], IDENT[:, :]),
                        reads=("P", "IDENT"), writes=(psz[pb],))
                if last_pair:
                    b.emit("pe", _mmg([(PS[1][0:64, 0:128], BONE[0:1, 0:64],
                                        SMD[0:1, 0:128], True, True, (0, 0)),
                                       (PS[1][64:128, 0:128], BONE[0:1, 0:64],
                                        SMD[0:1, 128:256], True, True, (0, 64))]),
                           reads=("SMD", "BONE"), writes=("ps1",))
                    b.emit("act", lambda e: e.copy(RBC[:, :], PS[1][:, 0:128]),
                           reads=("ps1",), writes=("RBC",))
                if not first:
                    b.emit("act", lambda e: e.copy(PTS[:, 0:512], PS[6][:, 0:512]),
                           reads=("ps6",), writes=("PTS",))
                b.emit("act", lambda e: e.copy(PTS[:, 512:1024], PS[7][:, 0:512]),
                       reads=("ps7",), writes=("PTS",))
                parts = []
                vread = set()
                for j in jl:
                    sl = spv if j < 4 else scv
                    ts = j % 4
                    vread.add(f"VT{sl}")
                    base = sl * 2048 + ts * 512 + (2 * hp + h) * 64
                    parts.append((PS[0][64 * h:64 * h + 64, 0:128],
                                  VT[:, base:base + 64],
                                  PTS[:, j * 128:(j + 1) * 128],
                                  j == jl[0], j == jl[-1], (0, 64 * h)))
                b.emit("pe", _mmg(parts), reads=tuple(vread) + ("PTS",),
                       writes=("ps0",))
                if last_pair:
                    b.emit("dve", lambda e, qc=qcols: e.tensor_mul(
                        O[:, qc:qc + 128], PS[0][:, 0:128], RBC[:, :]),
                        reads=("ps0", "RBC"), writes=("O",))

            f_part1(funits[0], 0)
            for ui in range(1, len(funits)):
                f_part1(funits[ui], ui)
                f_part2(funits[ui - 1], ui - 1)
            f_part2(funits[-1], len(funits) - 1)

            def f_wo_group(fo):
                s = fo % 2
                parts = [(PS[s][:, 0:512], WS[s][:, kk * 128:(kk + 1) * 128],
                          O[:, kk * 512:(kk + 1) * 512], kk == 0, kk == 3, None)
                         for kk in range(4)]
                b.emit("pe", _mmg(parts), reads=(f"WS{s}", "O"), writes=(psz[s],))

            def f_wo_evac(fo):
                s = fo % 2
                b.emit("dve", lambda e, fo=fo, s=s: e.tensor_add(
                    X1s(fo * 513 + 1, fo * 513 + 513), PS[s][:, 0:512],
                    XN[:, fo * 513 + 1:fo * 513 + 513]),
                    reads=(psz[s], "XN"), writes=("X1",))

            wslot_load(0, woF[l], 0, 4)
            wslot_load(1, woF[l], 1, 4)
            f_wo_group(0)
            for fo in range(1, 8):
                if fo + 1 < 8:
                    wslot_load(fo + 1, woF[l], fo + 1, 4)
                f_wo_group(fo)
                f_wo_evac(fo - 1)
            f_wo_evac(7)
            for kk in range(8):
                b.emit("dve", lambda e, kk=kk: e.tensor_copy(
                    X1s(kk * 513, kk * 513 + 1), GX1[:, kk:kk + 1]),
                    reads=("GX1",), writes=("X1",))
            for kk in range(8):
                b.emit("dve", lambda e, kk=kk: e.tensor_copy(
                    GX1[:, kk:kk + 1], X1s(kk * 513 + 512, kk * 513 + 513)),
                    reads=("X1",), writes=("GX1",))

            def h2slice(kk):
                base = kk * 513 + (1 if kk < 4 else 0)
                return X1s(base, base + 512)

            emit_ln_stats_and_apply(
                "X1", h2slice, ("XN",),
                dst_slices=lambda kk: XN[:, kk * 513 + 1:kk * 513 + 513],
                affine=(lambda kk: LNG[l][:, kk:kk + 1], lambda kk: LNB[l][:, kk:kk + 1]))

            for ts2 in range(2):
                tw = ts2 * 256

                def w1_ag_group(m, tw=tw):
                    s = (ts2 * 22 + m) % 2
                    sa = m % 2
                    parts = [(PS[sa][:, 0:256], WS[s][:, kk * 128:(kk + 1) * 128],
                              XN[:, kk * 513 + 1 + tw:kk * 513 + 1 + tw + 256],
                              kk == 0, False, None) for kk in range(8)]
                    parts += [(PS[sa][:, 256:512], WS[s][:, (8 + kk) * 128:(9 + kk) * 128],
                               XN[:, kk * 513 + 1 + tw:kk * 513 + 1 + tw + 256],
                               False, kk == 7, None) for kk in range(8)]
                    b.emit("pe", _mmg(parts), reads=(f"WS{s}", "XN"), writes=(psz[sa],))

                def w1_w2_chain(m, tw=tw):
                    s = (ts2 * 22 + m) % 2
                    sa = m % 2
                    mz = f"M{m % 2}"
                    mc = (m % 2) * 256
                    b.emit("act", lambda e, sa=sa, m=m: e.activation(
                        MG[:, :], PS[sa][:, 256:512], AF.Gelu,
                        bias=B1G[l][:, m:m + 1]),
                        reads=(psz[sa], "BI"), writes=("MG",))
                    b.emit("dve", lambda e, sa=sa, m=m, mc=mc: e.scalar_tensor_tensor(
                        Mt[:, mc:mc + 256], PS[sa][:, 0:256], B1A[l][:, m:m + 1],
                        MG[:, :], ALU.add, ALU.mult),
                        reads=(psz[sa], "MG", "BI"), writes=(mz,))
                    parts = [(PS[2 + fo // 2][:, (fo % 2) * 256:(fo % 2) * 256 + 256],
                              W2S[s][:, fo * 128:(fo + 1) * 128],
                              Mt[:, mc:mc + 256], (m == 0 and fo % 2 == 0),
                              (m == 21 and fo % 2 == 1), None)
                             for fo in range(8)]
                    b.emit("pe", _mmg(parts), reads=(f"W2S{s}", mz),
                           writes=("ps2", "ps3", "ps4", "ps5"))

                wfus_load_ag(ts2 * 22, l, 0)
                wfus_load_ag(ts2 * 22 + 1, l, 1)
                wfus_load_w2(ts2 * 22, l, 0)
                wfus_load_w2(ts2 * 22 + 1, l, 1)
                w1_ag_group(0)
                for m in range(1, 22):
                    if m + 1 < 22:
                        wfus_load_ag(ts2 * 22 + m + 1, l, m + 1)
                    w1_ag_group(m)
                    w1_w2_chain(m - 1)
                    if m + 1 < 22:
                        wfus_load_w2(ts2 * 22 + m + 1, l, m + 1)
                w1_w2_chain(21)
                for fo in range(8):
                    w2v = PS[2 + fo // 2][:, (fo % 2) * 256:(fo % 2) * 256 + 256]
                    b.emit("dve", lambda e, fo=fo, w2v=w2v, tw=tw: e.scalar_tensor_tensor(
                        ACC[:, fo * 512 + tw:fo * 512 + tw + 256], w2v,
                        B2[l][:, fo:fo + 1],
                        X1s(fo * 513 + 1 + tw, fo * 513 + 1 + tw + 256),
                        ALU.add, ALU.add),
                        reads=("ps" + str(2 + fo // 2), "X1", "BI"),
                        writes=(f"ACC{fo}",))
                    if dst == "out":
                        b.emit("dve", lambda e, fo=fo, tw=tw: e.tensor_add(
                            ACC[:, fo * 512 + tw:fo * 512 + tw + 256],
                            ACC[:, fo * 512 + tw:fo * 512 + tw + 256],
                            X[:, fo * 512 + tw:fo * 512 + tw + 256]),
                            reads=("X",), writes=(f"ACC{fo}",))
            dstd = y1_s if dst == "y1" else outT
            dzone = "d:y1" if dst == "y1" else "d:out"
            for fo in range(8):
                b.dma(CO, dstd[fo * 128:(fo + 1) * 128, ccol:ccol + 512],
                      ACC[:, fo * 512:(fo + 1) * 512],
                      reads=(f"ACC{fo}",), writes=(dzone,))

    sweep_ffn(0, "x", "y1")
    b.barrier()
    sweep_ffn(1, "y1", "out")

    b.wait_all_chans("sp")
    b.replay()
    nc.finalize()
    return nc



def _tile_w(w, nk, nm):
    K, M = w.shape
    assert K == nk * 128 and M == nm * 128
    t = w.reshape(nk, 128, nm, 128).transpose(2, 1, 0, 3)
    return np.ascontiguousarray(t).reshape(nm, 128 * nk * 128)


def _pad_cols(w, M):
    out = np.zeros((w.shape[0], M), dtype=w.dtype)
    out[:, :w.shape[1]] = w
    return out


def _pad_rows(w, K):
    out = np.zeros((K, w.shape[1]), dtype=w.dtype)
    out[:w.shape[0]] = w
    return out


def stage_inputs(L0_attn_w_qkv, L0_attn_w_out, L0_ffn_wqkv, L0_ffn_q_scale,
                 L0_ffn_k_scale, L0_ffn_wo, L0_ffn_ln_g, L0_ffn_ln_b,
                 L0_ffn_w1, L0_ffn_b1, L0_ffn_w2, L0_ffn_b2):
    f = np.float32
    d = {}
    d["wqkvA"] = _tile_w(np.asarray(L0_attn_w_qkv, f), 8, 24)
    d["woutA"] = _tile_w(np.asarray(L0_attn_w_out, f), 8, 8)
    for l in range(2):
        d[f"wqkvF{l}"] = _tile_w(np.asarray(L0_ffn_wqkv[l], f), 8, 12)
        d[f"woF{l}"] = _tile_w(np.asarray(L0_ffn_wo[l], f), 4, 8)
        w1 = np.asarray(L0_ffn_w1[l], f)
        wa = _pad_cols(w1[:, :2730], 2816).reshape(8, 128, 22, 128).transpose(2, 1, 0, 3)
        wg = _pad_cols(w1[:, 2730:], 2816).reshape(8, 128, 22, 128).transpose(2, 1, 0, 3)
        w2 = _pad_rows(np.asarray(L0_ffn_w2[l], f), 2816).reshape(22, 128, 8, 128)
        d[f"wfus{l}"] = np.ascontiguousarray(
            np.concatenate([wa, wg, w2], axis=2)).reshape(22, 128 * 24 * 128)
        b1 = np.asarray(L0_ffn_b1[l], f)
        d[f"b1a{l}"] = np.ascontiguousarray(
            np.concatenate([b1[:2730], np.zeros(86, f)]).reshape(22, 128).T)
        d[f"b1g{l}"] = np.ascontiguousarray(
            np.concatenate([b1[2730:], np.zeros(86, f)]).reshape(22, 128).T)
        d[f"b2{l}"] = np.ascontiguousarray(np.asarray(L0_ffn_b2[l], f).reshape(8, 128).T)
        d[f"lng{l}"] = np.ascontiguousarray(np.asarray(L0_ffn_ln_g[l], f).reshape(8, 128).T)
        d[f"lnb{l}"] = np.ascontiguousarray(np.asarray(L0_ffn_ln_b[l], f).reshape(8, 128).T)
        qs = np.asarray(L0_ffn_q_scale[l], f)
        ks = np.asarray(L0_ffn_k_scale[l], f)
        d[f"qsc{l}"] = np.ascontiguousarray(
            np.stack([np.tile(qs, 2), np.tile(np.roll(qs, -32), 2)], axis=1))
        d[f"ksc{l}"] = np.ascontiguousarray(
            np.stack([np.tile(ks, 2), np.tile(np.roll(ks, -32), 2)], axis=1))
    return d


def kernel(L0_attn_x, L0_attn_w_qkv, L0_attn_w_out, L0_ffn_wqkv, L0_ffn_q_scale,
           L0_ffn_k_scale, L0_ffn_wo, L0_ffn_ln_g, L0_ffn_ln_b, L0_ffn_w1,
           L0_ffn_b1, L0_ffn_w2, L0_ffn_b2):
    x = np.asarray(L0_attn_x, np.float32)
    B, NT, D = x.shape
    nc = build_core_program(NT)
    wd = stage_inputs(L0_attn_w_qkv, L0_attn_w_out, L0_ffn_wqkv, L0_ffn_q_scale,
                      L0_ffn_k_scale, L0_ffn_wo, L0_ffn_ln_g, L0_ffn_ln_b,
                      L0_ffn_w1, L0_ffn_b1, L0_ffn_w2, L0_ffn_b2)
    in_maps = []
    for bb in range(B):
        m = {"xT": np.ascontiguousarray(x[bb].T)}
        m.update(wd)
        in_maps.append(m)
    res = run_bass_kernel_spmd(nc, in_maps, core_ids=list(range(B)))
    out = np.stack([np.asarray(r["outT"]).T for r in res.results], axis=0)
    return out.astype(np.float32)


# version 09
# speedup vs baseline: 1.2184x; 1.0030x over previous
import math

import numpy as np

import concourse.bass as bass
import concourse.mybir as mybir
from concourse.bass_utils import run_bass_kernel_spmd
import concourse.library_config as libcfg

F32 = mybir.dt.float32
BF16 = mybir.dt.bfloat16
AF = mybir.ActivationFunctionType
ALU = mybir.AluOpType
AX = mybir.AxisListType

T = 512
NEG = -1.0e30
PI = math.pi
LN1E4 = math.log(10000.0)

ENGS = ("pe", "dve", "act", "pool", "sp")
_BLOCKSECT = {"pe": "tensor", "dve": "vector", "act": "scalar",
              "pool": "gpsimd", "sp": "sync"}

PROXY_LIB = [l for l in libcfg.all_libraries if l.name == "proxy"][0]


class Chan:
    __slots__ = ("name", "sem", "issued")

    def __init__(self, name, sem):
        self.name = name
        self.sem = sem
        self.issued = 0


class Bld:

    def __init__(self, nc):
        self.nc = nc
        self.ops = {e: [] for e in ENGS}
        self.sem = {e: nc.alloc_semaphore("tk_" + e) for e in ENGS}
        self.n = {e: 0 for e in ENGS}
        self.wm = {e: {} for e in ENGS}
        self.zw = {}
        self.zr = {}
        self.chans = []

    def chan(self, name):
        c = Chan(name, self.nc.alloc_semaphore("ch_" + name))
        self.chans.append(c)
        return c

    def _wait(self, e, key, val):
        if isinstance(key, Chan):
            val = 16 * key.issued
        if val <= 0:
            return
        if self.wm[e].get(key, 0) >= val:
            return
        self.wm[e][key] = val
        semh = key.sem if isinstance(key, Chan) else self.sem[key]
        self.ops[e].append(("w", semh, val))

    def _dep(self, e, zones, war):
        for z in zones:
            lw = self.zw.get(z)
            if lw is not None:
                self._wait(e, lw[0], lw[1])
            if war:
                for k, v in self.zr.get(z, {}).items():
                    self._wait(e, k, v)

    def emit(self, e, fn, reads=(), writes=()):
        self._dep(e, reads, False)
        self._dep(e, writes, True)
        self.ops[e].append(("i", fn, self.sem[e]))
        self.n[e] += 1
        t = self.n[e]
        for z in reads:
            self.zr.setdefault(z, {})[e] = t
        for z in writes:
            self.zw[z] = (e, t)
            self.zr[z] = {}

    def dma(self, ch, out_ap, in_ap, reads=(), writes=(), eng="sp"):
        self._dep(eng, reads, False)
        self._dep(eng, writes, True)
        self.ops[eng].append(("d", out_ap, in_ap, ch.sem))
        ch.issued += 1
        for z in reads:
            self.zr.setdefault(z, {})[ch] = 0
        for z in writes:
            self.zw[z] = (ch, 0)
            self.zr[z] = {}

    def barrier(self):
        for e1 in ENGS:
            for e2 in ENGS:
                if e1 != e2:
                    self._wait(e1, e2, self.n[e2])

    def wait_all_chans(self, e="sp"):
        for c in self.chans:
            self._wait(e, c, 1)

    def replay(self):
        with self.nc.Block() as block:
            for e in ENGS:
                ops = self.ops[e]

                def body(eng, ops=ops):
                    for op in ops:
                        if op[0] == "w":
                            eng.wait_ge(op[1], op[2])
                        elif op[0] == "i":
                            r = op[1](eng)
                            if isinstance(r, list):
                                r = r[-1]
                            r.then_inc(op[2], 1)
                        else:
                            eng.dma_start(op[1], op[2]).then_inc(op[3], 16)
                getattr(block, _BLOCKSECT[e])(body)


def _mmg(parts):
    def fn(eng):
        last = None
        for (o, l, r, s, st, tp) in parts:
            if tp is None:
                last = eng.matmul(o, l, r, start=s, stop=st)
            else:
                last = eng.matmul(o, l, r, start=s, stop=st, tile_position=tp)
        return last
    return fn


def build_core_program(NT=4096):
    NC = NT // T
    nc = bass.Bass("TRN2", target_bir_lowering=False)

    xT = nc.dram_tensor("xT", [1024, NT], F32, kind="ExternalInput").ap()
    wqkvA = nc.dram_tensor("wqkvA", [24, 128 * 8 * 128], F32, kind="ExternalInput").ap()
    woutA = nc.dram_tensor("woutA", [8, 128 * 8 * 128], F32, kind="ExternalInput").ap()
    wqkvF = [nc.dram_tensor(f"wqkvF{l}", [12, 128 * 8 * 128], F32, kind="ExternalInput").ap() for l in range(2)]
    woF = [nc.dram_tensor(f"woF{l}", [8, 128 * 4 * 128], F32, kind="ExternalInput").ap() for l in range(2)]
    wfus_h = [nc.dram_tensor(f"wfus{l}", [22, 128 * 24 * 128], F32, kind="ExternalInput") for l in range(2)]
    b1a_d = [nc.dram_tensor(f"b1a{l}", [128, 22], F32, kind="ExternalInput").ap() for l in range(2)]
    b1g_d = [nc.dram_tensor(f"b1g{l}", [128, 22], F32, kind="ExternalInput").ap() for l in range(2)]
    b2_d = [nc.dram_tensor(f"b2{l}", [128, 8], F32, kind="ExternalInput").ap() for l in range(2)]
    lng_d = [nc.dram_tensor(f"lng{l}", [128, 8], F32, kind="ExternalInput").ap() for l in range(2)]
    lnb_d = [nc.dram_tensor(f"lnb{l}", [128, 8], F32, kind="ExternalInput").ap() for l in range(2)]
    qsc_d = [nc.dram_tensor(f"qsc{l}", [128, 2], F32, kind="ExternalInput").ap() for l in range(2)]
    ksc_d = [nc.dram_tensor(f"ksc{l}", [128, 2], F32, kind="ExternalInput").ap() for l in range(2)]
    outT = nc.dram_tensor("outT", [1024, NT], F32, kind="ExternalOutput").ap()
    acc_s = nc.dram_tensor("acc_s", [1024, NT], F32, kind="Internal").ap()
    y1_s = nc.dram_tensor("y1_s", [1024, NT], F32, kind="Internal").ap()
    rott = [nc.dram_tensor(f"rott{l}", [4 * 128, NT], F32, kind="Internal").ap() for l in range(2)]

    A = nc.alloc_sbuf_tensor
    XN = A("XN", [128, 8 * 513], F32)
    X = A("Xt", [128, 8 * 512], F32)
    BIGQ = A("BIGQ", [128, 6152], F32)
    X1OFF = 2048
    ACC = A("ACCt", [128, 8 * 512], F32)
    O = A("Ot", [128, 8 * 512], F32)
    KR = A("KRt", [128, 6144], F32)
    VT = A("VTt", [128, 6144], F32)
    WS = [A("WS0", [128, 16 * 128], F32), A("WS1", [128, 16 * 128], F32)]
    W2S = [A("W2S0", [128, 8 * 128], F32), A("W2S1", [128, 8 * 128], F32)]
    P = A("Pt", [128, 1024], F32)
    PTS = A("PTSt", [128, 1024], F32)
    Mt = A("Mt", [128, 512], F32)
    MG = A("MGt", [128, 256], F32)
    QR = A("QRt", [128, 512], F32)
    VS = A("VSt", [128, 512], F32)
    SQA = A("SQAt", [128, 512], F32)
    SQB = A("SQBt", [128, 512], F32)
    MB = A("MBt", [128, 512], F32)
    RB = A("RBt", [128, 512], F32)
    RT = A("RTt", [128, 2048], F32)
    MA = A("MAt", [128, 1024], BF16)
    MF = A("MFt", [128, 4096], BF16)
    SM = A("SMt", [128, 96], F32)
    NMT = A("NMTt", [128, 16], F32)
    SMD = A("SMDt", [1, 256], F32)
    RBC = A("RBCt", [128, 128], F32)
    IDENT = A("IDENTt", [128, 128], F32)
    ONES = A("ONESt", [128, 1], F32)
    ONES2 = A("ONES2t", [128, 2], F32)
    GXN = A("GXNt", [128, 8], F32)
    GX1 = A("GX1t", [128, 8], F32)
    RSS = A("RSSt", [1, 512], F32)
    RROT = A("RROTt", [128, 128], F32)
    BONE = A("BONEt", [1, 128], F32)
    CEPS = A("CEPSt", [128, 1], F32)
    IOD = A("IODt", [128, 4], F32)
    B1A = [A(f"B1A{l}", [128, 22], F32) for l in range(2)]
    B1G = [A(f"B1G{l}", [128, 22], F32) for l in range(2)]
    B2 = [A(f"B2{l}", [128, 8], F32) for l in range(2)]
    LNG = [A(f"LNG{l}", [128, 8], F32) for l in range(2)]
    LNB = [A(f"LNB{l}", [128, 8], F32) for l in range(2)]
    QSC = [A(f"QSC{l}", [128, 2], F32) for l in range(2)]
    KSC = [A(f"KSC{l}", [128, 2], F32) for l in range(2)]

    def Qa(a, b_, p0=0, p1=128):
        return BIGQ[p0:p1, a:b_]

    def X1s(a, b_):
        return BIGQ[:, X1OFF + a:X1OFF + b_]

    PS = [nc.alloc_psum_tensor(f"PS{i}", [128, 512], F32) for i in range(8)]

    b = Bld(nc)
    CX = b.chan("cx")
    CW = [b.chan("cw0"), b.chan("cw1")]
    CW2 = [b.chan("cw2a"), b.chan("cw2b")]
    CRT = b.chan("crt")
    CB = b.chan("cb")
    CO = b.chan("co")
    CTW = b.chan("ctw")

    psz = [f"ps{i}" for i in range(8)]

    b.emit("dve", lambda e: e.memset(BONE[:, :], 1.0), writes=("BONE",))

    b.emit("dve", lambda e: e.memset(IDENT[:, :], 1.0), writes=("IDENT",))
    b.emit("pool", lambda e: e.affine_select(IDENT[:, :], IDENT[:, :], [[1, 128]],
                                             ALU.is_equal, 0.0, base=0,
                                             channel_multiplier=-1),
           reads=("lib",), writes=("IDENT",))
    b.emit("dve", lambda e: e.memset(ONES[:, :], 1.0), writes=("ONES",))
    b.emit("dve", lambda e: e.memset(CEPS[:, :], 1e-5), writes=("CEPS",))
    b.emit("dve", lambda e: e.memset(ONES2[:, :], 1.0), writes=("ONES2",))
    b.emit("pool", lambda e: e.affine_select(ONES2[:, 0:1], ONES2[:, 0:1], [[1, 1]],
                                             ALU.is_ge, 0.0, base=63,
                                             channel_multiplier=-1),
           reads=("lib",), writes=("ONES2",))
    b.emit("pool", lambda e: e.affine_select(ONES2[:, 1:2], ONES2[:, 1:2], [[1, 1]],
                                             ALU.is_ge, 0.0, base=-64,
                                             channel_multiplier=1),
           reads=("lib",), writes=("ONES2",))
    for (c0, sgn, kb) in ((0, -1.0, -32), (32, 1.0, 0), (64, -1.0, -96), (96, 1.0, -64)):
        sl = RROT[:, c0:c0 + 32]
        b.emit("dve", lambda e, sl=sl, sgn=sgn: e.memset(sl, sgn), writes=("RROT",))
        b.emit("pool", lambda e, sl=sl, kb=kb: e.affine_select(
            sl, sl, [[-1, 32]], ALU.is_equal, 0.0, base=kb,
            channel_multiplier=1), reads=("lib",), writes=("RROT",))
    b.emit("dve", lambda e: e.memset(MA[:, :], 0.0), writes=("MA",))
    b.emit("dve", lambda e: e.memset(MF[:, :], 0.0), writes=("MF",))
    for qs in range(2):
        sl = MA[:, qs * 512:(qs + 1) * 512]
        b.emit("pool", lambda e, sl=sl, qs=qs: e.affine_select(
            sl, sl, [[1, 512]], ALU.is_ge, NEG, base=-(128 * qs) - 1,
            channel_multiplier=-1), reads=("lib",), writes=("MA",))
        b.emit("pool", lambda e, sl=sl, qs=qs: e.affine_select(
            sl, sl, [[-1, 512]], ALU.is_ge, NEG, base=(128 * qs) + 256,
            channel_multiplier=1), reads=("lib",), writes=("MA",))
    for qs in range(4):
        sl = MF[:, qs * 1024:(qs + 1) * 1024]
        b.emit("pool", lambda e, sl=sl, qs=qs: e.affine_select(
            sl, sl, [[1, 1024]], ALU.is_ge, NEG, base=-(128 * qs),
            channel_multiplier=-1), reads=("lib",), writes=("MF",))
        b.emit("pool", lambda e, sl=sl, qs=qs: e.affine_select(
            sl, sl, [[-1, 1024]], ALU.is_ge, NEG, base=(128 * qs) + 512,
            channel_multiplier=1), reads=("lib",), writes=("MF",))
    for l in range(2):
        b.dma(CB, B1A[l][:, :], b1a_d[l], writes=("BI",))
        b.dma(CB, B1G[l][:, :], b1g_d[l], writes=("BI",))
        b.dma(CB, B2[l][:, :], b2_d[l], writes=("BI",))
        b.dma(CB, LNG[l][:, :], lng_d[l], writes=("BI",))
        b.dma(CB, LNB[l][:, :], lnb_d[l], writes=("BI",))
        b.dma(CB, QSC[l][:, :], qsc_d[l], writes=("BI",))
        b.dma(CB, KSC[l][:, :], ksc_d[l], writes=("BI",))

    for qtr in range(4):
        b.emit("pool", lambda e, q=qtr: e.iota(
            IOD[q * 32:(q + 1) * 32, 0:1], [[1, 1]], base=0, channel_multiplier=1,
            allow_small_or_imprecise_dtypes=True), reads=("lib",), writes=("IOD",))
    b.emit("act", lambda e: e.activation(IOD[:, 1:2], IOD[:, 0:1], AF.Exp,
                                         scale=-2.0 * LN1E4 / 64.0),
           reads=("IOD",), writes=("IOD",))
    b.emit("dve", lambda e: e.tensor_scalar(IOD[:, 2:3], IOD[:, 0:1],
                                            2.0 / 89.6, 25.6 / 89.6,
                                            ALU.mult, ALU.add),
           reads=("IOD",), writes=("IOD",))
    b.emit("act", lambda e: e.activation(IOD[:, 3:4], IOD[:, 2:3], AF.Ln),
           reads=("IOD",), writes=("IOD",))
    for l in range(2):
        for cb in range(NC):
            t0 = cb * 512
            b.emit("pool", lambda e, t0=t0: e.iota(
                MB[:, :], [[1, 512]], base=t0, channel_multiplier=0,
                allow_small_or_imprecise_dtypes=True),
                reads=("lib",), writes=("MB",))
            b.emit("dve", lambda e: e.tensor_scalar_mul(RB[:, :], MB[:, :], IOD[:, 1:2]),
                   reads=("MB", "IOD"), writes=("RB",))
            b.emit("dve", lambda e: e.tensor_scalar_add(P[:, 0:512], RB[:, :], PI),
                   reads=("RB",), writes=("P",))
            for j in range(9, -1, -1):
                cj = 2.0 * PI * (1 << j)
                b.emit("dve", lambda e, cj=cj: e.tensor_single_scalar(
                    VS[:, :], P[:, 0:512], cj, ALU.is_ge),
                    reads=("P",), writes=("VS",))
                b.emit("dve", lambda e, cj=cj: e.scalar_tensor_tensor(
                    P[:, 0:512], VS[:, :], -cj, P[:, 0:512], ALU.mult, ALU.add),
                    reads=("VS",), writes=("P",))
            b.emit("dve", lambda e: e.tensor_scalar_add(P[:, 0:512], P[:, 0:512], -PI),
                   reads=(), writes=("P",))
            b.emit("dve", lambda e: e.tensor_scalar_add(P[:, 512:1024], P[:, 0:512], PI / 2),
                   reads=(), writes=("P",))
            b.emit("dve", lambda e: e.tensor_single_scalar(
                VS[:, :], P[:, 512:1024], PI, ALU.is_ge),
                reads=("P",), writes=("VS",))
            b.emit("dve", lambda e: e.scalar_tensor_tensor(
                P[:, 512:1024], VS[:, :], -2.0 * PI, P[:, 512:1024],
                ALU.mult, ALU.add),
                reads=("VS",), writes=("P",))
            b.emit("act", lambda e: e.activation(SQB[:, :], P[:, 0:512], AF.Sin),
                   reads=("P",), writes=("SQB",))
            b.emit("act", lambda e: e.activation(VS[:, :], P[:, 512:1024], AF.Sin),
                   reads=("P",), writes=("VS",))
            b.emit("pool", lambda e, t0=t0: e.iota(
                QR[:, :], [[1, 512]], base=t0 - 2048, channel_multiplier=0,
                allow_small_or_imprecise_dtypes=True),
                reads=("lib",), writes=("QR",))
            b.emit("dve", lambda e: e.tensor_scalar_mul(QR[:, :], QR[:, :], IOD[:, 3:4]),
                   reads=("IOD",), writes=("QR",))
            b.emit("act", lambda e: e.activation(MB[:, :], QR[:, :], AF.Exp,
                                                 scale=1.0 / 256.0),
                   reads=("QR",), writes=("MB",))
            b.emit("act", lambda e: e.activation(RB[:, :], QR[:, :], AF.Exp,
                                                 scale=-1.0 / 256.0),
                   reads=("QR",), writes=("RB",))
            tabs = [(0, VS, MB, QSC[l][:, 0:1]),
                    (512, SQB, MB, QSC[l][:, 1:2]),
                    (1024, VS, RB, KSC[l][:, 0:1]),
                    (1536, SQB, RB, KSC[l][:, 1:2])]
            for (off, t1, t2, sc) in tabs:
                b.emit("dve", lambda e, off=off, t1=t1, t2=t2: e.tensor_mul(
                    RT[:, off:off + 512], t1[:, :], t2[:, :]),
                    reads=("VS", "SQB", "MB", "RB"), writes=("RT",))
                b.emit("dve", lambda e, off=off, sc=sc: e.tensor_scalar_mul(
                    RT[:, off:off + 512], RT[:, off:off + 512], sc),
                    reads=("BI",), writes=("RT",))
            for i in range(4):
                b.dma(CTW, rott[l][i * 128:(i + 1) * 128, t0:t0 + 512],
                      RT[:, i * 512:(i + 1) * 512],
                      reads=("RT",), writes=(f"d:rott{l}",))

    b.barrier()


    def emit_ln_stats_and_apply(src_tile, src_cols, dst_writes, dst_slices,
                                affine=None):
        for kk in range(8):
            sq = SQA if kk % 2 == 0 else SQB
            sqz = "SQA" if kk % 2 == 0 else "SQB"
            src = src_cols(kk)
            b.emit("act", lambda e, sq=sq, src=src: e.activation(sq[:, :], src, AF.Square),
                   reads=(src_tile,), writes=(sqz,))
            b.emit("pe", _mmg([(PS[6][0:1, 0:512], ONES[:, 0:1], src,
                                kk == 0, kk == 7, None)]),
                   reads=(src_tile, "ONES"), writes=("ps6",))
            b.emit("pe", _mmg([(PS[7][0:1, 0:512], ONES[:, 0:1], sq[:, :],
                                kk == 0, kk == 7, None)]),
                   reads=(sqz, "ONES"), writes=("ps7",))
        b.emit("dve", lambda e: e.tensor_scalar_mul(MB[0:1, :], PS[6][0:1, 0:512], 1.0 / 1024.0),
               reads=("ps6",), writes=("MB",))
        b.emit("dve", lambda e: e.tensor_scalar_mul(RB[0:1, :], PS[7][0:1, 0:512], 1.0 / 1024.0),
               reads=("ps7",), writes=("RB",))
        b.emit("act", lambda e: e.activation(RSS[0:1, :], MB[0:1, :], AF.Square),
               reads=("MB",), writes=("RSS",))
        b.emit("dve", lambda e: e.tensor_sub(RB[0:1, :], RB[0:1, :], RSS[0:1, :]),
               reads=("RSS",), writes=("RB",))
        b.emit("act", lambda e: e.activation(RSS[0:1, :], RB[0:1, :], AF.Ln,
                                             bias=CEPS[0:1, 0:1]),
               reads=("RB", "CEPS"), writes=("RSS",))
        b.emit("act", lambda e: e.activation(RB[0:1, :], RSS[0:1, :], AF.Exp, scale=-0.5),
               reads=("RSS",), writes=("RB",))
        b.emit("pe", _mmg([(PS[6][:, 0:512], BONE[0:1, 0:128], MB[0:1, :],
                            True, True, None)]),
               reads=("MB", "BONE"), writes=("ps6",))
        b.emit("pe", _mmg([(PS[7][:, 0:512], BONE[0:1, 0:128], RB[0:1, :],
                            True, True, None)]),
               reads=("RB", "BONE"), writes=("ps7",))
        for kk in range(8):
            src = src_cols(kk)
            dst = dst_slices(kk)
            b.emit("dve", lambda e, d=dst, s=src: e.tensor_sub(d, s, PS[6][:, 0:512]),
                   reads=(src_tile, "ps6"), writes=dst_writes)
            b.emit("dve", lambda e, d=dst: e.tensor_mul(d, d, PS[7][:, 0:512]),
                   reads=("ps7",), writes=dst_writes)
            if affine is not None:
                g_ap, bt_ap = affine
                b.emit("dve", lambda e, d=dst, g=g_ap(kk), bb=bt_ap(kk): e.tensor_scalar(
                    d, d, g, bb, ALU.mult, ALU.add),
                    reads=("BI",), writes=dst_writes)

    def wslot_load(widx, dram_ap, j, nblk):
        s = widx % 2
        b.dma(CW[s], WS[s][:, 0:nblk * 128], dram_ap[j:j + 1, :],
              writes=(f"WS{s}",))
        return s

    def wfus_load_ag(widx, l, j):
        s = widx % 2
        src_ap = bass.AP(wfus_h[l], j * (128 * 24 * 128),
                         [[24 * 128, 128], [1, 16 * 128]])
        b.dma(CW[s], WS[s][:, 0:16 * 128], src_ap, writes=(f"WS{s}",))

    def wfus_load_w2(widx, l, j):
        s = widx % 2
        src_ap = bass.AP(wfus_h[l], j * (128 * 24 * 128) + 16 * 128,
                         [[24 * 128, 128], [1, 8 * 128]])
        b.dma(CW2[s], W2S[s][:, 0:8 * 128], src_ap, writes=(f"W2S{s}",))

    for c in range(NC):
        ccol = c * 512
        for kk in range(8):
            b.dma(CX, X[:, kk * 512:(kk + 1) * 512],
                  xT[kk * 128:(kk + 1) * 128, ccol:ccol + 512], writes=("X",))
        emit_ln_stats_and_apply(
            "X", lambda kk: X[:, kk * 512:(kk + 1) * 512], ("XN",),
            dst_slices=lambda kk: XN[:, kk * 513 + 1:kk * 513 + 513])

        s0 = (2 * c) % 3
        s1 = (2 * c + 1) % 3

        def a1_qkv_group(m):
            s = m % 2
            parts = [(PS[s][:, 0:512], WS[s][:, kk * 128:(kk + 1) * 128],
                      XN[:, kk * 513 + 1:kk * 513 + 513], kk == 0, kk == 7, None)
                     for kk in range(8)]
            b.emit("pe", _mmg(parts), reads=(f"WS{s}", "XN"), writes=(psz[s],))

        def a1_qkv_evac(m):
            s = m % 2
            if m < 8:
                b.emit("act", lambda e, m=m, s=s: e.copy(Qa(m * 512, (m + 1) * 512),
                                                         PS[s][:, 0:512]),
                       reads=(psz[s],), writes=("Q",))
            elif m < 16:
                j = m - 8
                d0 = KR[:, j * 768 + s0 * 256:j * 768 + s0 * 256 + 256]
                d1 = KR[:, j * 768 + s1 * 256:j * 768 + s1 * 256 + 256]
                b.emit("act", lambda e, d=d0, s=s: e.copy(d, PS[s][:, 0:256]),
                       reads=(psz[s],), writes=(f"KR{s0}",))
                b.emit("act", lambda e, d=d1, s=s: e.copy(d, PS[s][:, 256:512]),
                       reads=(psz[s],), writes=(f"KR{s1}",))
            else:
                j = m - 16
                b.emit("act", lambda e, s=s: e.copy(VS[:, :], PS[s][:, 0:512]),
                       reads=(psz[s],), writes=("VS",))
                for u in range(4):
                    b.emit("pe", lambda e, u=u: e.transpose(
                        PS[2][:, u * 128:(u + 1) * 128],
                        VS[:, u * 128:(u + 1) * 128], IDENT[:, :]),
                        reads=("VS", "IDENT"), writes=("ps2",))
                for half in range(2):
                    sl = s0 if half == 0 else s1
                    src = PS[2][:, half * 256:half * 256 + 256]
                    dst = bass.AP(VT, sl * 2048 + j * 128,
                                  [[6144, 128], [1024, 2], [1, 128]])
                    b.emit("act", lambda e, d=dst, sp=src: e.copy(d, sp),
                           reads=("ps2",), writes=(f"VT{sl}",))

        wslot_load(0, wqkvA, 0, 8)
        wslot_load(1, wqkvA, 1, 8)
        a1_qkv_group(0)
        for m in range(1, 24):
            if m + 1 < 24:
                wslot_load(m + 1, wqkvA, m + 1, 8)
            a1_qkv_group(m)
            a1_qkv_evac(m - 1)
        a1_qkv_evac(23)

        aunits = [(b_loc, hp, qs, h) for b_loc in range(2)
                  for hp in range(8) for qs in range(2) for h in range(2)]
        A1_SBANK = (3, 4, 0, 1)

        def a1_env(u, ui):
            b_loc, hp, qs, h = u
            blk = 2 * c + b_loc
            return dict(b_loc=b_loc, hp=hp, qs=qs, h=h,
                        spv=(blk - 1) % 3, scv=blk % 3,
                        first=(blk == 0),
                        o_off=256 if blk == 0 else 0,
                        qcols=hp * 512 + b_loc * 256 + qs * 128,
                        cidx=((b_loc * 8 + hp) * 2 + qs) * 2,
                        nb=2 * (ui % 4),
                        sbk=A1_SBANK[ui % 4])

        def a1_part1(u, ui):
            v = a1_env(u, ui)
            hp, qs, h = v["hp"], v["qs"], v["h"]
            spv, scv, first = v["spv"], v["scv"], v["first"]
            o_off, qcols, nb, sbk = v["o_off"], v["qcols"], v["nb"], v["sbk"]
            parts = []
            if not first:
                parts.append((PS[sbk][:, 0:256],
                              Qa(qcols, qcols + 128, 64 * h, 64 * h + 64),
                              KR[64 * h:64 * h + 64,
                                 hp * 768 + spv * 256:hp * 768 + spv * 256 + 256],
                              True, False, None))
            parts.append((PS[sbk][:, 256:512],
                          Qa(qcols, qcols + 128, 64 * h, 64 * h + 64),
                          KR[64 * h:64 * h + 64,
                             hp * 768 + scv * 256:hp * 768 + scv * 256 + 256],
                          first, True, None))
            rd = ("Q", f"KR{scv}") if first else ("Q", f"KR{spv}", f"KR{scv}")
            b.emit("pe", _mmg(parts), reads=rd, writes=(psz[sbk],))
            b.emit("dve", lambda e, sbk=sbk, o=o_off, qs=qs: e.tensor_add(
                PS[sbk][:, o:512], PS[sbk][:, o:512],
                MA[:, qs * 512 + o:qs * 512 + 512]),
                reads=("MA",), writes=(psz[sbk],))
            b.emit("dve", lambda e, sbk=sbk, o=o_off, nb=nb: e.reduce_max(
                NMT[:, nb:nb + 1], PS[sbk][:, o:512], axis=AX.X, negate=True),
                reads=(psz[sbk],), writes=("NMT",))
            b.emit("dve", lambda e, nb=nb: e.tensor_scalar_mul(
                NMT[:, nb + 1:nb + 2], NMT[:, nb:nb + 1], 0.125),
                reads=(), writes=("NMT",))

        def a1_part2(u, ui):
            v = a1_env(u, ui)
            hp, qs, h = v["hp"], v["qs"], v["h"]
            spv, scv, first = v["spv"], v["scv"], v["first"]
            o_off, qcols, cidx, nb, sbk = (v["o_off"], v["qcols"], v["cidx"],
                                           v["nb"], v["sbk"])
            b.emit("act", lambda e, sbk=sbk, o=o_off, h=h, ci=cidx, nb=nb: e.activation(
                P[:, h * 512 + o:h * 512 + 512], PS[sbk][:, o:512], AF.Exp,
                bias=NMT[:, nb + 1:nb + 2], scale=0.125,
                accum_out=SM[:, ci + h:ci + h + 1]),
                reads=(psz[sbk], "NMT"), writes=("P", "SM"))
            jl = [2, 3] if first else [0, 1, 2, 3]
            nj = len(jl)
            ptb = 5 if h == 0 else 7
            if h == 1:
                b.emit("pe", lambda e, ci=cidx: [
                    e.transpose(PS[6][0:1, 0:128], SM[:, ci:ci + 1], IDENT[:, :]),
                    e.transpose(PS[6][0:1, 128:256], SM[:, ci + 1:ci + 2],
                                IDENT[:, :])],
                    reads=("SM", "IDENT"), writes=("ps6",))
                b.emit("dve", lambda e: e.reciprocal(SMD[0:1, :], PS[6][0:1, 0:256]),
                       reads=("ps6",), writes=("SMD",))
            for jj, j in enumerate(jl):
                b.emit("pe", lambda e, h=h, j=j, jj=jj, ptb=ptb: e.transpose(
                    PS[ptb][:, jj * 128:(jj + 1) * 128],
                    P[:, h * 512 + j * 128:h * 512 + (j + 1) * 128],
                    IDENT[:, :]),
                    reads=("P", "IDENT"), writes=(psz[ptb],))
            if h == 1:
                b.emit("pe", _mmg([(PS[6][0:64, 0:128], BONE[0:1, 0:64],
                                    SMD[0:1, 0:128], True, True, (0, 0)),
                                   (PS[6][64:128, 0:128], BONE[0:1, 0:64],
                                    SMD[0:1, 128:256], True, True, (0, 64))]),
                       reads=("SMD", "BONE"), writes=("ps6",))
                b.emit("act", lambda e: e.copy(RBC[:, :], PS[6][:, 0:128]),
                       reads=("ps6",), writes=("RBC",))
            b.emit("act", lambda e, h=h, nj=nj, ptb=ptb: e.copy(
                PTS[:, h * 512:h * 512 + nj * 128],
                PS[ptb][:, 0:nj * 128]),
                reads=(psz[ptb],), writes=("PTS",))
            parts = []
            vread = set()
            for jj, j in enumerate(jl):
                sl = spv if j < 2 else scv
                ts = j % 2
                vread.add(f"VT{sl}")
                base = sl * 2048 + ts * 1024 + (2 * hp + h) * 64
                parts.append((PS[2][64 * h:64 * h + 64, 0:128],
                              VT[:, base:base + 64],
                              PTS[:, h * 512 + jj * 128:h * 512 + (jj + 1) * 128],
                              jj == 0, jj == nj - 1, (0, 64 * h)))
            b.emit("pe", _mmg(parts), reads=tuple(vread) + ("PTS",),
                   writes=("ps2",))
            if h == 1:
                b.emit("dve", lambda e, qc=qcols: e.tensor_mul(
                    O[:, qc:qc + 128], PS[2][:, 0:128], RBC[:, :]),
                    reads=("ps2", "RBC"), writes=("O",))

        a1_part1(aunits[0], 0)
        a1_part1(aunits[1], 1)
        for ui in range(2, len(aunits)):
            a1_part1(aunits[ui], ui)
            a1_part2(aunits[ui - 2], ui - 2)
        a1_part2(aunits[-2], len(aunits) - 2)
        a1_part2(aunits[-1], len(aunits) - 1)


        def a1_wo_group(fo):
            s = fo % 2
            parts = [(PS[s][:, 0:512], WS[s][:, kk * 128:(kk + 1) * 128],
                      O[:, kk * 512:(kk + 1) * 512], kk == 0, kk == 7, None)
                     for kk in range(8)]
            b.emit("pe", _mmg(parts), reads=(f"WS{s}", "O"), writes=(psz[s],))

        def a1_wo_evac(fo):
            s = fo % 2
            b.emit("dve", lambda e, fo=fo, s=s: e.tensor_add(
                ACC[:, fo * 512:(fo + 1) * 512], PS[s][:, 0:512],
                X[:, fo * 512:(fo + 1) * 512]),
                reads=(psz[s], "X"), writes=(f"ACC{fo}",))
            b.dma(CO, acc_s[fo * 128:(fo + 1) * 128, ccol:ccol + 512],
                  ACC[:, fo * 512:(fo + 1) * 512],
                  reads=(f"ACC{fo}",), writes=("d:acc",))

        wslot_load(0, woutA, 0, 8)
        wslot_load(1, woutA, 1, 8)
        a1_wo_group(0)
        for fo in range(1, 8):
            if fo + 1 < 8:
                wslot_load(fo + 1, woutA, fo + 1, 8)
            a1_wo_group(fo)
            a1_wo_evac(fo - 1)
        a1_wo_evac(7)

    b.barrier()

    def sweep_ffn(l, src, dst):
        b.emit("dve", lambda e: e.memset(GXN[:, :], 0.0), writes=("GXN",))
        b.emit("dve", lambda e: e.memset(GX1[:, :], 0.0), writes=("GX1",))
        for c in range(NC):
            ccol = c * 512
            spv = (c - 1) % 2
            scv = c % 2
            first = (c == 0)
            if src == "x":
                for kk in range(8):
                    b.dma(CX, X[:, kk * 512:(kk + 1) * 512],
                          xT[kk * 128:(kk + 1) * 128, ccol:ccol + 512], writes=("X",))
                for kk in range(8):
                    b.emit("dve", lambda e, kk=kk: e.tensor_copy(
                        XN[:, kk * 513:kk * 513 + 1], GXN[:, kk:kk + 1]),
                        reads=("GXN",), writes=("XN",))
                emit_ln_stats_and_apply(
                    "X", lambda kk: X[:, kk * 512:(kk + 1) * 512], ("XN",),
                    dst_slices=lambda kk: XN[:, kk * 513 + 1:kk * 513 + 513])
                for kk in range(8):
                    b.emit("dve", lambda e, kk=kk: e.tensor_copy(
                        GXN[:, kk:kk + 1], XN[:, kk * 513 + 512:kk * 513 + 513]),
                        reads=("XN",), writes=("GXN",))
            else:
                for kk in range(8):
                    b.dma(CX, X[:, kk * 512:(kk + 1) * 512],
                          acc_s[kk * 128:(kk + 1) * 128, ccol:ccol + 512],
                          reads=("d:acc",), writes=("X",))
                if first:
                    for kk in range(8):
                        b.emit("dve", lambda e, kk=kk: e.memset(
                            XN[:, kk * 513:kk * 513 + 1], 0.0), writes=("XN",))
                        b.dma(CX, XN[:, kk * 513 + 1:kk * 513 + 513],
                              y1_s[kk * 128:(kk + 1) * 128, 0:512],
                              reads=("d:y1",), writes=("XN",))
                else:
                    for kk in range(8):
                        b.dma(CX, XN[:, kk * 513:kk * 513 + 513],
                              y1_s[kk * 128:(kk + 1) * 128, ccol - 1:ccol + 512],
                              reads=("d:y1",), writes=("XN",))
            for i in range(4):
                b.dma(CRT, RT[:, i * 512:(i + 1) * 512],
                      rott[l][i * 128:(i + 1) * 128, ccol:ccol + 512],
                      reads=(f"d:rott{l}",), writes=("RT",))

            def hshift(kk, off=0, w=512):
                base = kk * 513 + (1 if kk < 4 else 0) + off
                return XN[:, base:base + w]

            def f_qkv_group(m):
                s = m % 2
                parts = [(PS[s][:, 0:512], WS[s][:, kk * 128:(kk + 1) * 128],
                          hshift(kk), kk == 0, kk == 7, None) for kk in range(8)]
                b.emit("pe", _mmg(parts), reads=(f"WS{s}", "XN"), writes=(psz[s],))

            def f_qkv_evac(m):
                s = m % 2
                if m < 4:
                    b.emit("act", lambda e, m=m, s=s: e.copy(
                        Qa(m * 512, (m + 1) * 512), PS[s][:, 0:512]),
                        reads=(psz[s],), writes=("Q",))
                elif m < 8:
                    j = m - 4
                    b.emit("act", lambda e, j=j, s=s: e.copy(
                        X1s(j * 512, (j + 1) * 512), PS[s][:, 0:512]),
                        reads=(psz[s],), writes=("KRAW",))
                else:
                    j = m - 8
                    b.emit("act", lambda e, s=s: e.copy(VS[:, :], PS[s][:, 0:512]),
                           reads=(psz[s],), writes=("VS",))
                    for u in range(4):
                        b.emit("pe", lambda e, u=u: e.transpose(
                            PS[2][:, u * 128:(u + 1) * 128],
                            VS[:, u * 128:(u + 1) * 128], IDENT[:, :]),
                            reads=("VS", "IDENT"), writes=("ps2",))
                    dstv = bass.AP(VT, scv * 2048 + j * 128,
                                   [[6144, 128], [512, 4], [1, 128]])
                    b.emit("act", lambda e, d=dstv: e.copy(d, PS[2][:, 0:512]),
                           reads=("ps2",), writes=(f"VT{scv}",))

            def f_qk_proc(m):
                if m < 4:
                    raw = Qa(m * 512, (m + 1) * 512)
                    rawz = "Q"
                else:
                    raw = X1s((m - 4) * 512, (m - 3) * 512)
                    rawz = "KRAW"
                b.emit("act", lambda e, raw=raw: e.activation(SQA[:, :], raw, AF.Square),
                       reads=(rawz,), writes=("SQA",))
                b.emit("pe", _mmg([(PS[5][0:1, 0:512], ONES2[:, 0:1], SQA[:, :],
                                    True, True, None),
                                   (PS[6][0:1, 0:512], ONES2[:, 1:2], SQA[:, :],
                                    True, True, None)]),
                       reads=("SQA", "ONES2"), writes=("ps5", "ps6"))
                b.emit("dve", lambda e: e.tensor_scalar_max(RSS[0:1, :],
                                                            PS[5][0:1, 0:512], 1e-24),
                       reads=("ps5",), writes=("RSS",))
                b.emit("dve", lambda e: e.tensor_scalar_max(P[0:1, 0:512],
                                                            PS[6][0:1, 0:512], 1e-24),
                       reads=("ps6",), writes=("P",))
                b.emit("act", lambda e: e.activation(RSS[0:1, :], RSS[0:1, :], AF.Ln),
                       reads=(), writes=("RSS",))
                b.emit("act", lambda e: e.activation(RSS[0:1, :], RSS[0:1, :], AF.Exp,
                                                     scale=-0.5),
                       reads=(), writes=("RSS",))
                b.emit("act", lambda e: e.activation(P[0:1, 0:512], P[0:1, 0:512], AF.Ln),
                       reads=(), writes=("P",))
                b.emit("act", lambda e: e.activation(P[0:1, 0:512], P[0:1, 0:512], AF.Exp,
                                                     scale=-0.5),
                       reads=(), writes=("P",))
                b.emit("pe", _mmg([(PS[6][0:64, 0:512], BONE[0:1, 0:64],
                                    RSS[0:1, :], True, True, (0, 0)),
                                   (PS[6][64:128, 0:512], BONE[0:1, 0:64],
                                    P[0:1, 0:512], True, True, (0, 64))]),
                       reads=("RSS", "P", "BONE"), writes=("ps6",))
                b.emit("act", lambda e: e.copy(MB[:, :], PS[6][:, 0:512]),
                       reads=("ps6",), writes=("MB",))
                b.emit("dve", lambda e, raw=raw: e.tensor_mul(QR[:, :], raw, MB[:, :]),
                       reads=(rawz, "MB"), writes=("QR",))
                tb = 0 if m < 4 else 1024
                if m < 4:
                    dzone = "Q"
                    dst = Qa(m * 512, (m + 1) * 512)
                else:
                    j = m - 4
                    dzone = f"KR{scv}"
                    dst = KR[:, j * 1024 + scv * 512:j * 1024 + scv * 512 + 512]
                b.emit("pe", _mmg([(PS[7][:, 0:512], RROT[:, :], QR[:, :],
                                    True, True, None)]),
                       reads=("RROT", "QR"), writes=("ps7",))
                b.emit("dve", lambda e, tb=tb, d=dst: e.tensor_mul(
                    d, QR[:, :], RT[:, tb:tb + 512]),
                    reads=("QR", "RT"), writes=(dzone,))
                b.emit("dve", lambda e, tb=tb: e.tensor_mul(
                    SQB[:, :], PS[7][:, 0:512], RT[:, tb + 512:tb + 1024]),
                    reads=("ps7", "RT"), writes=("SQB",))
                b.emit("dve", lambda e, d=dst: e.tensor_add(d, d, SQB[:, :]),
                       reads=("SQB",), writes=(dzone,))

            wslot_load(0, wqkvF[l], 0, 8)
            wslot_load(1, wqkvF[l], 1, 8)
            f_qkv_group(0)
            for m in range(1, 12):
                if m + 1 < 12:
                    wslot_load(m + 1, wqkvF[l], m + 1, 8)
                f_qkv_group(m)
                f_qkv_evac(m - 1)
                if m >= 4:
                    f_qk_proc(m - 4)
            f_qkv_evac(11)

            funits = [(hp, qs, h) for hp in range(4) for qs in range(4)
                      for h in range(2)]

            def f_env(u, ui):
                hp, qs, h = u
                return dict(hp=hp, qs=qs, h=h,
                            qcols=hp * 512 + qs * 128,
                            scidx=(hp * 4 + qs) * 2 + h,
                            nbh=4 * (ui % 2),
                            bkA=2 + 2 * (ui % 2), bkB=3 + 2 * (ui % 2))

            def f_part1(u, ui):
                v = f_env(u, ui)
                hp, qs, h = v["hp"], v["qs"], v["h"]
                nbh, bkA, bkB = v["nbh"], v["bkA"], v["bkB"]
                if not first:
                    b.emit("pe", _mmg([(PS[bkA][:, 0:512],
                                        Qa(v["qcols"], v["qcols"] + 128, 64 * h, 64 * h + 64),
                                        KR[64 * h:64 * h + 64,
                                           hp * 1024 + spv * 512:hp * 1024 + spv * 512 + 512],
                                        True, True, None)]),
                           reads=("Q", f"KR{spv}"), writes=(psz[bkA],))
                b.emit("pe", _mmg([(PS[bkB][:, 0:512],
                                    Qa(v["qcols"], v["qcols"] + 128, 64 * h, 64 * h + 64),
                                    KR[64 * h:64 * h + 64,
                                       hp * 1024 + scv * 512:hp * 1024 + scv * 512 + 512],
                                    True, True, None)]),
                       reads=("Q", f"KR{scv}"), writes=(psz[bkB],))
                if not first:
                    b.emit("dve", lambda e, bk=bkA, qs=qs: e.tensor_add(
                        PS[bk][:, 0:512], PS[bk][:, 0:512],
                        MF[:, qs * 1024:qs * 1024 + 512]),
                        reads=("MF",), writes=(psz[bkA],))
                b.emit("dve", lambda e, bk=bkB, qs=qs: e.tensor_add(
                    PS[bk][:, 0:512], PS[bk][:, 0:512],
                    MF[:, qs * 1024 + 512:qs * 1024 + 1024]),
                    reads=("MF",), writes=(psz[bkB],))
                if not first:
                    b.emit("dve", lambda e, bk=bkA, nbh=nbh: e.reduce_max(
                        NMT[:, nbh:nbh + 1], PS[bk][:, 0:512], axis=AX.X, negate=True),
                        reads=(psz[bkA],), writes=("NMT",))
                b.emit("dve", lambda e, bk=bkB, nbh=nbh: e.reduce_max(
                    NMT[:, nbh + 1:nbh + 2], PS[bk][:, 0:512], axis=AX.X, negate=True),
                    reads=(psz[bkB],), writes=("NMT",))
                if not first:
                    b.emit("dve", lambda e, nbh=nbh: e.tensor_tensor(
                        NMT[:, nbh + 2:nbh + 3], NMT[:, nbh:nbh + 1],
                        NMT[:, nbh + 1:nbh + 2], ALU.min),
                        reads=(), writes=("NMT",))
                    b.emit("dve", lambda e, nbh=nbh: e.tensor_scalar_mul(
                        NMT[:, nbh + 3:nbh + 4], NMT[:, nbh + 2:nbh + 3], 8.0),
                        reads=(), writes=("NMT",))
                else:
                    b.emit("dve", lambda e, nbh=nbh: e.tensor_scalar_mul(
                        NMT[:, nbh + 3:nbh + 4], NMT[:, nbh + 1:nbh + 2], 8.0),
                        reads=(), writes=("NMT",))

            def f_part2(u, ui):
                v = f_env(u, ui)
                hp, qs, h = v["hp"], v["qs"], v["h"]
                nbh, bkA, bkB = v["nbh"], v["bkA"], v["bkB"]
                scidx = v["scidx"]
                qcols = v["qcols"]
                if not first:
                    b.emit("act", lambda e, bk=bkA, si=scidx, nbh=nbh: e.activation(
                        P[:, 0:512], PS[bk][:, 0:512], AF.Exp,
                        bias=NMT[:, nbh + 3:nbh + 4], scale=8.0,
                        accum_out=SM[:, 2 * si:2 * si + 1]),
                        reads=(psz[bkA], "NMT"), writes=("P", "SM"))
                    b.emit("act", lambda e, bk=bkB, si=scidx, nbh=nbh: e.activation(
                        P[:, 512:1024], PS[bk][:, 0:512], AF.Exp,
                        bias=NMT[:, nbh + 3:nbh + 4], scale=8.0,
                        accum_out=SM[:, 2 * si + 1:2 * si + 2]),
                        reads=(psz[bkB], "NMT"), writes=("P", "SM"))
                    b.emit("dve", lambda e, si=scidx: e.tensor_add(
                        SM[:, 64 + si:65 + si], SM[:, 2 * si:2 * si + 1],
                        SM[:, 2 * si + 1:2 * si + 2]),
                        reads=(), writes=("SM",))
                else:
                    b.emit("act", lambda e, bk=bkB, si=scidx, nbh=nbh: e.activation(
                        P[:, 512:1024], PS[bk][:, 0:512], AF.Exp,
                        bias=NMT[:, nbh + 3:nbh + 4], scale=8.0,
                        accum_out=SM[:, 64 + si:65 + si]),
                        reads=(psz[bkB], "NMT"), writes=("P", "SM"))
                last_pair = (h == 1)
                if last_pair:
                    si0 = (hp * 4 + qs) * 2
                    b.emit("pe", lambda e, si=si0: [
                        e.transpose(PS[1][0:1, 0:128], SM[:, 64 + si:65 + si],
                                    IDENT[:, :]),
                        e.transpose(PS[1][0:1, 128:256], SM[:, 65 + si:66 + si],
                                    IDENT[:, :])],
                        reads=("SM", "IDENT"), writes=("ps1",))
                    b.emit("dve", lambda e: e.reciprocal(SMD[0:1, :], PS[1][0:1, 0:256]),
                           reads=("ps1",), writes=("SMD",))
                jl = [4, 5, 6, 7] if first else list(range(8))
                for j in jl:
                    pb = 6 + j // 4
                    b.emit("pe", lambda e, j=j, pb=pb: e.transpose(
                        PS[pb][:, (j % 4) * 128:(j % 4) * 128 + 128],
                        P[:, j * 128:(j + 1) * 128], IDENT[:, :]),
                        reads=("P", "IDENT"), writes=(psz[pb],))
                if last_pair:
                    b.emit("pe", _mmg([(PS[1][0:64, 0:128], BONE[0:1, 0:64],
                                        SMD[0:1, 0:128], True, True, (0, 0)),
                                       (PS[1][64:128, 0:128], BONE[0:1, 0:64],
                                        SMD[0:1, 128:256], True, True, (0, 64))]),
                           reads=("SMD", "BONE"), writes=("ps1",))
                    b.emit("act", lambda e: e.copy(RBC[:, :], PS[1][:, 0:128]),
                           reads=("ps1",), writes=("RBC",))
                if not first:
                    b.emit("act", lambda e: e.copy(PTS[:, 0:512], PS[6][:, 0:512]),
                           reads=("ps6",), writes=("PTS",))
                b.emit("act", lambda e: e.copy(PTS[:, 512:1024], PS[7][:, 0:512]),
                       reads=("ps7",), writes=("PTS",))
                parts = []
                vread = set()
                for j in jl:
                    sl = spv if j < 4 else scv
                    ts = j % 4
                    vread.add(f"VT{sl}")
                    base = sl * 2048 + ts * 512 + (2 * hp + h) * 64
                    parts.append((PS[0][64 * h:64 * h + 64, 0:128],
                                  VT[:, base:base + 64],
                                  PTS[:, j * 128:(j + 1) * 128],
                                  j == jl[0], j == jl[-1], (0, 64 * h)))
                b.emit("pe", _mmg(parts), reads=tuple(vread) + ("PTS",),
                       writes=("ps0",))
                if last_pair:
                    b.emit("dve", lambda e, qc=qcols: e.tensor_mul(
                        O[:, qc:qc + 128], PS[0][:, 0:128], RBC[:, :]),
                        reads=("ps0", "RBC"), writes=("O",))

            f_part1(funits[0], 0)
            for ui in range(1, len(funits)):
                f_part1(funits[ui], ui)
                f_part2(funits[ui - 1], ui - 1)
            f_part2(funits[-1], len(funits) - 1)

            def f_wo_group(fo):
                s = fo % 2
                parts = [(PS[s][:, 0:512], WS[s][:, kk * 128:(kk + 1) * 128],
                          O[:, kk * 512:(kk + 1) * 512], kk == 0, kk == 3, None)
                         for kk in range(4)]
                b.emit("pe", _mmg(parts), reads=(f"WS{s}", "O"), writes=(psz[s],))

            def f_wo_evac(fo):
                s = fo % 2
                b.emit("dve", lambda e, fo=fo, s=s: e.tensor_add(
                    X1s(fo * 513 + 1, fo * 513 + 513), PS[s][:, 0:512],
                    XN[:, fo * 513 + 1:fo * 513 + 513]),
                    reads=(psz[s], "XN"), writes=("X1",))

            wslot_load(0, woF[l], 0, 4)
            wslot_load(1, woF[l], 1, 4)
            f_wo_group(0)
            for fo in range(1, 8):
                if fo + 1 < 8:
                    wslot_load(fo + 1, woF[l], fo + 1, 4)
                f_wo_group(fo)
                f_wo_evac(fo - 1)
            f_wo_evac(7)
            for kk in range(8):
                b.emit("dve", lambda e, kk=kk: e.tensor_copy(
                    X1s(kk * 513, kk * 513 + 1), GX1[:, kk:kk + 1]),
                    reads=("GX1",), writes=("X1",))
            for kk in range(8):
                b.emit("dve", lambda e, kk=kk: e.tensor_copy(
                    GX1[:, kk:kk + 1], X1s(kk * 513 + 512, kk * 513 + 513)),
                    reads=("X1",), writes=("GX1",))

            def h2slice(kk):
                base = kk * 513 + (1 if kk < 4 else 0)
                return X1s(base, base + 512)

            emit_ln_stats_and_apply(
                "X1", h2slice, ("XN",),
                dst_slices=lambda kk: XN[:, kk * 513 + 1:kk * 513 + 513],
                affine=(lambda kk: LNG[l][:, kk:kk + 1], lambda kk: LNB[l][:, kk:kk + 1]))

            for ts2 in range(2):
                tw = ts2 * 256

                def w1_ag_group(m, tw=tw):
                    s = (ts2 * 22 + m) % 2
                    sa = m % 2
                    parts = [(PS[sa][:, 0:256], WS[s][:, kk * 128:(kk + 1) * 128],
                              XN[:, kk * 513 + 1 + tw:kk * 513 + 1 + tw + 256],
                              kk == 0, False, None) for kk in range(8)]
                    parts += [(PS[sa][:, 256:512], WS[s][:, (8 + kk) * 128:(9 + kk) * 128],
                               XN[:, kk * 513 + 1 + tw:kk * 513 + 1 + tw + 256],
                               False, kk == 7, None) for kk in range(8)]
                    b.emit("pe", _mmg(parts), reads=(f"WS{s}", "XN"), writes=(psz[sa],))

                def w1_w2_chain(m, tw=tw):
                    s = (ts2 * 22 + m) % 2
                    sa = m % 2
                    mz = f"M{m % 2}"
                    mc = (m % 2) * 256
                    b.emit("act", lambda e, sa=sa, m=m: e.activation(
                        MG[:, :], PS[sa][:, 256:512], AF.Gelu,
                        bias=B1G[l][:, m:m + 1]),
                        reads=(psz[sa], "BI"), writes=("MG",))
                    b.emit("dve", lambda e, sa=sa, m=m, mc=mc: e.scalar_tensor_tensor(
                        Mt[:, mc:mc + 256], PS[sa][:, 0:256], B1A[l][:, m:m + 1],
                        MG[:, :], ALU.add, ALU.mult),
                        reads=(psz[sa], "MG", "BI"), writes=(mz,))
                    parts = [(PS[2 + fo // 2][:, (fo % 2) * 256:(fo % 2) * 256 + 256],
                              W2S[s][:, fo * 128:(fo + 1) * 128],
                              Mt[:, mc:mc + 256], (m == 0 and fo % 2 == 0),
                              (m == 21 and fo % 2 == 1), None)
                             for fo in range(8)]
                    b.emit("pe", _mmg(parts), reads=(f"W2S{s}", mz),
                           writes=("ps2", "ps3", "ps4", "ps5"))

                wfus_load_ag(ts2 * 22, l, 0)
                wfus_load_ag(ts2 * 22 + 1, l, 1)
                wfus_load_w2(ts2 * 22, l, 0)
                wfus_load_w2(ts2 * 22 + 1, l, 1)
                w1_ag_group(0)
                for m in range(1, 22):
                    if m + 1 < 22:
                        wfus_load_ag(ts2 * 22 + m + 1, l, m + 1)
                    w1_ag_group(m)
                    w1_w2_chain(m - 1)
                    if m + 1 < 22:
                        wfus_load_w2(ts2 * 22 + m + 1, l, m + 1)
                w1_w2_chain(21)
                for fo in range(8):
                    w2v = PS[2 + fo // 2][:, (fo % 2) * 256:(fo % 2) * 256 + 256]
                    b.emit("dve", lambda e, fo=fo, w2v=w2v, tw=tw: e.scalar_tensor_tensor(
                        ACC[:, fo * 512 + tw:fo * 512 + tw + 256], w2v,
                        B2[l][:, fo:fo + 1],
                        X1s(fo * 513 + 1 + tw, fo * 513 + 1 + tw + 256),
                        ALU.add, ALU.add),
                        reads=("ps" + str(2 + fo // 2), "X1", "BI"),
                        writes=(f"ACC{fo}",))
                    if dst == "out":
                        b.emit("dve", lambda e, fo=fo, tw=tw: e.tensor_add(
                            ACC[:, fo * 512 + tw:fo * 512 + tw + 256],
                            ACC[:, fo * 512 + tw:fo * 512 + tw + 256],
                            X[:, fo * 512 + tw:fo * 512 + tw + 256]),
                            reads=("X",), writes=(f"ACC{fo}",))
            dstd = y1_s if dst == "y1" else outT
            dzone = "d:y1" if dst == "y1" else "d:out"
            for fo in range(8):
                b.dma(CO, dstd[fo * 128:(fo + 1) * 128, ccol:ccol + 512],
                      ACC[:, fo * 512:(fo + 1) * 512],
                      reads=(f"ACC{fo}",), writes=(dzone,))

    sweep_ffn(0, "x", "y1")
    b.barrier()
    sweep_ffn(1, "y1", "out")

    b.wait_all_chans("sp")
    b.replay()
    nc.finalize()
    return nc



def _tile_w(w, nk, nm):
    K, M = w.shape
    assert K == nk * 128 and M == nm * 128
    t = w.reshape(nk, 128, nm, 128).transpose(2, 1, 0, 3)
    return np.ascontiguousarray(t).reshape(nm, 128 * nk * 128)


def _pad_cols(w, M):
    out = np.zeros((w.shape[0], M), dtype=w.dtype)
    out[:, :w.shape[1]] = w
    return out


def _pad_rows(w, K):
    out = np.zeros((K, w.shape[1]), dtype=w.dtype)
    out[:w.shape[0]] = w
    return out


def stage_inputs(L0_attn_w_qkv, L0_attn_w_out, L0_ffn_wqkv, L0_ffn_q_scale,
                 L0_ffn_k_scale, L0_ffn_wo, L0_ffn_ln_g, L0_ffn_ln_b,
                 L0_ffn_w1, L0_ffn_b1, L0_ffn_w2, L0_ffn_b2):
    f = np.float32
    d = {}
    d["wqkvA"] = _tile_w(np.asarray(L0_attn_w_qkv, f), 8, 24)
    d["woutA"] = _tile_w(np.asarray(L0_attn_w_out, f), 8, 8)
    for l in range(2):
        d[f"wqkvF{l}"] = _tile_w(np.asarray(L0_ffn_wqkv[l], f), 8, 12)
        d[f"woF{l}"] = _tile_w(np.asarray(L0_ffn_wo[l], f), 4, 8)
        w1 = np.asarray(L0_ffn_w1[l], f)
        wa = _pad_cols(w1[:, :2730], 2816).reshape(8, 128, 22, 128).transpose(2, 1, 0, 3)
        wg = _pad_cols(w1[:, 2730:], 2816).reshape(8, 128, 22, 128).transpose(2, 1, 0, 3)
        w2 = _pad_rows(np.asarray(L0_ffn_w2[l], f), 2816).reshape(22, 128, 8, 128)
        d[f"wfus{l}"] = np.ascontiguousarray(
            np.concatenate([wa, wg, w2], axis=2)).reshape(22, 128 * 24 * 128)
        b1 = np.asarray(L0_ffn_b1[l], f)
        d[f"b1a{l}"] = np.ascontiguousarray(
            np.concatenate([b1[:2730], np.zeros(86, f)]).reshape(22, 128).T)
        d[f"b1g{l}"] = np.ascontiguousarray(
            np.concatenate([b1[2730:], np.zeros(86, f)]).reshape(22, 128).T)
        d[f"b2{l}"] = np.ascontiguousarray(np.asarray(L0_ffn_b2[l], f).reshape(8, 128).T)
        d[f"lng{l}"] = np.ascontiguousarray(np.asarray(L0_ffn_ln_g[l], f).reshape(8, 128).T)
        d[f"lnb{l}"] = np.ascontiguousarray(np.asarray(L0_ffn_ln_b[l], f).reshape(8, 128).T)
        qs = np.asarray(L0_ffn_q_scale[l], f)
        ks = np.asarray(L0_ffn_k_scale[l], f)
        d[f"qsc{l}"] = np.ascontiguousarray(
            np.stack([np.tile(qs, 2), np.tile(np.roll(qs, -32), 2)], axis=1))
        d[f"ksc{l}"] = np.ascontiguousarray(
            np.stack([np.tile(ks, 2), np.tile(np.roll(ks, -32), 2)], axis=1))
    return d


def kernel(L0_attn_x, L0_attn_w_qkv, L0_attn_w_out, L0_ffn_wqkv, L0_ffn_q_scale,
           L0_ffn_k_scale, L0_ffn_wo, L0_ffn_ln_g, L0_ffn_ln_b, L0_ffn_w1,
           L0_ffn_b1, L0_ffn_w2, L0_ffn_b2):
    x = np.asarray(L0_attn_x, np.float32)
    B, NT, D = x.shape
    nc = build_core_program(NT)
    wd = stage_inputs(L0_attn_w_qkv, L0_attn_w_out, L0_ffn_wqkv, L0_ffn_q_scale,
                      L0_ffn_k_scale, L0_ffn_wo, L0_ffn_ln_g, L0_ffn_ln_b,
                      L0_ffn_w1, L0_ffn_b1, L0_ffn_w2, L0_ffn_b2)
    in_maps = []
    for bb in range(B):
        m = {"xT": np.ascontiguousarray(x[bb].T)}
        m.update(wd)
        in_maps.append(m)
    res = run_bass_kernel_spmd(nc, in_maps, core_ids=list(range(B)))
    out = np.stack([np.asarray(r["outT"]).T for r in res.results], axis=0)
    return out.astype(np.float32)


# version 11
# speedup vs baseline: 1.2907x; 1.0594x over previous
import math

import numpy as np

import concourse.bass as bass
import concourse.mybir as mybir
from concourse.bass_utils import run_bass_kernel_spmd
import concourse.library_config as libcfg

F32 = mybir.dt.float32
BF16 = mybir.dt.bfloat16
AF = mybir.ActivationFunctionType
ALU = mybir.AluOpType
AX = mybir.AxisListType

T = 512
NEG = -1.0e30
PI = math.pi
LN1E4 = math.log(10000.0)

ENGS = ("pe", "dve", "act", "pool", "sp")
_BLOCKSECT = {"pe": "tensor", "dve": "vector", "act": "scalar",
              "pool": "gpsimd", "sp": "sync"}

PROXY_LIB = [l for l in libcfg.all_libraries if l.name == "proxy"][0]


class Chan:
    __slots__ = ("name", "sem", "issued")

    def __init__(self, name, sem):
        self.name = name
        self.sem = sem
        self.issued = 0


class Bld:

    def __init__(self, nc):
        self.nc = nc
        self.ops = {e: [] for e in ENGS}
        self.sem = {e: nc.alloc_semaphore("tk_" + e) for e in ENGS}
        self.n = {e: 0 for e in ENGS}
        self.wm = {e: {} for e in ENGS}
        self.zw = {}
        self.zr = {}
        self.chans = []

    def chan(self, name):
        c = Chan(name, self.nc.alloc_semaphore("ch_" + name))
        self.chans.append(c)
        return c

    def _wait(self, e, key, val):
        if isinstance(key, Chan):
            val = 16 * key.issued
        if val <= 0:
            return
        if self.wm[e].get(key, 0) >= val:
            return
        self.wm[e][key] = val
        semh = key.sem if isinstance(key, Chan) else self.sem[key]
        self.ops[e].append(("w", semh, val))

    def _dep(self, e, zones, war):
        for z in zones:
            lw = self.zw.get(z)
            if lw is not None:
                self._wait(e, lw[0], lw[1])
            if war:
                for k, v in self.zr.get(z, {}).items():
                    self._wait(e, k, v)

    def emit(self, e, fn, reads=(), writes=()):
        self._dep(e, reads, False)
        self._dep(e, writes, True)
        self.ops[e].append(("i", fn, self.sem[e]))
        self.n[e] += 1
        t = self.n[e]
        for z in reads:
            self.zr.setdefault(z, {})[e] = t
        for z in writes:
            self.zw[z] = (e, t)
            self.zr[z] = {}

    def dma(self, ch, out_ap, in_ap, reads=(), writes=(), eng="sp"):
        self._dep(eng, reads, False)
        self._dep(eng, writes, True)
        self.ops[eng].append(("d", out_ap, in_ap, ch.sem))
        ch.issued += 1
        for z in reads:
            self.zr.setdefault(z, {})[ch] = 0
        for z in writes:
            self.zw[z] = (ch, 0)
            self.zr[z] = {}

    def barrier(self):
        for e1 in ENGS:
            for e2 in ENGS:
                if e1 != e2:
                    self._wait(e1, e2, self.n[e2])

    def wait_all_chans(self, e="sp"):
        for c in self.chans:
            self._wait(e, c, 1)

    def replay(self):
        with self.nc.Block() as block:
            for e in ENGS:
                ops = self.ops[e]

                def body(eng, ops=ops):
                    for op in ops:
                        if op[0] == "w":
                            eng.wait_ge(op[1], op[2])
                        elif op[0] == "i":
                            r = op[1](eng)
                            if isinstance(r, list):
                                r = r[-1]
                            r.then_inc(op[2], 1)
                        else:
                            eng.dma_start(op[1], op[2]).then_inc(op[3], 16)
                getattr(block, _BLOCKSECT[e])(body)


def _mmg(parts):
    def fn(eng):
        last = None
        for (o, l, r, s, st, tp) in parts:
            if tp is None:
                last = eng.matmul(o, l, r, start=s, stop=st)
            else:
                last = eng.matmul(o, l, r, start=s, stop=st, tile_position=tp)
        return last
    return fn


def build_core_program(NT=4096):
    NC = NT // T
    nc = bass.Bass("TRN2", target_bir_lowering=False)

    xT = nc.dram_tensor("xT", [1024, NT], F32, kind="ExternalInput").ap()
    wqkvA = nc.dram_tensor("wqkvA", [24, 128 * 8 * 128], F32, kind="ExternalInput").ap()
    woutA = nc.dram_tensor("woutA", [8, 128 * 8 * 128], F32, kind="ExternalInput").ap()
    wqkvF = [nc.dram_tensor(f"wqkvF{l}", [12, 128 * 8 * 128], F32, kind="ExternalInput").ap() for l in range(2)]
    woF = [nc.dram_tensor(f"woF{l}", [8, 128 * 4 * 128], F32, kind="ExternalInput").ap() for l in range(2)]
    wfus_h = [nc.dram_tensor(f"wfus{l}", [22, 128 * 24 * 128], F32, kind="ExternalInput") for l in range(2)]
    b1a_d = [nc.dram_tensor(f"b1a{l}", [128, 22], F32, kind="ExternalInput").ap() for l in range(2)]
    b1g_d = [nc.dram_tensor(f"b1g{l}", [128, 22], F32, kind="ExternalInput").ap() for l in range(2)]
    b2_d = [nc.dram_tensor(f"b2{l}", [128, 8], F32, kind="ExternalInput").ap() for l in range(2)]
    lng_d = [nc.dram_tensor(f"lng{l}", [128, 8], F32, kind="ExternalInput").ap() for l in range(2)]
    lnb_d = [nc.dram_tensor(f"lnb{l}", [128, 8], F32, kind="ExternalInput").ap() for l in range(2)]
    qsc_d = [nc.dram_tensor(f"qsc{l}", [128, 2], F32, kind="ExternalInput").ap() for l in range(2)]
    ksc_d = [nc.dram_tensor(f"ksc{l}", [128, 2], F32, kind="ExternalInput").ap() for l in range(2)]
    outT = nc.dram_tensor("outT", [1024, NT], F32, kind="ExternalOutput").ap()
    acc_s = nc.dram_tensor("acc_s", [1024, NT], F32, kind="Internal").ap()
    y1_s = nc.dram_tensor("y1_s", [1024, NT], F32, kind="Internal").ap()
    xn_s = nc.dram_tensor("xn_s", [1024, NT], F32, kind="Internal").ap()
    rott = [nc.dram_tensor(f"rott{l}", [4 * 128, NT], F32, kind="Internal").ap() for l in range(2)]

    A = nc.alloc_sbuf_tensor
    XN = A("XN", [128, 8 * 513], F32)
    X = A("Xt", [128, 8 * 512], F32)
    BIGQ = A("BIGQ", [128, 6152], F32)
    X1OFF = 2048
    ACC = A("ACCt", [128, 8 * 512], F32)
    O = A("Ot", [128, 8 * 512], F32)
    KR = A("KRt", [128, 6144], F32)
    VT = A("VTt", [128, 6144], F32)
    WS = [A("WS0", [128, 16 * 128], F32), A("WS1", [128, 16 * 128], F32)]
    W2S = [A("W2S0", [128, 8 * 128], F32), A("W2S1", [128, 8 * 128], F32)]
    P = A("Pt", [128, 1024], F32)
    PTS = A("PTSt", [128, 1024], F32)
    Mt = A("Mt", [128, 512], F32)
    MG = A("MGt", [128, 256], F32)
    QR = A("QRt", [128, 512], F32)
    VS = A("VSt", [128, 512], F32)
    SQA = A("SQAt", [128, 512], F32)
    SQB = A("SQBt", [128, 512], F32)
    MB = A("MBt", [128, 512], F32)
    RB = A("RBt", [128, 512], F32)
    RT = A("RTt", [128, 2048], F32)
    MA = A("MAt", [128, 1024], BF16)
    MF = A("MFt", [128, 4096], BF16)
    SM = A("SMt", [128, 96], F32)
    NMT = A("NMTt", [128, 16], F32)
    SMD = A("SMDt", [1, 256], F32)
    RBC = A("RBCt", [128, 128], F32)
    IDENT = A("IDENTt", [128, 128], F32)
    ONES = A("ONESt", [128, 1], F32)
    ONES2 = A("ONES2t", [128, 2], F32)
    GXN = A("GXNt", [128, 8], F32)
    GX1 = A("GX1t", [128, 8], F32)
    RSS = A("RSSt", [1, 512], F32)
    RROT = A("RROTt", [128, 128], F32)
    BONE = A("BONEt", [1, 128], F32)
    CEPS = A("CEPSt", [128, 1], F32)
    IOD = A("IODt", [128, 4], F32)
    B1A = [A(f"B1A{l}", [128, 22], F32) for l in range(2)]
    B1G = [A(f"B1G{l}", [128, 22], F32) for l in range(2)]
    B2 = [A(f"B2{l}", [128, 8], F32) for l in range(2)]
    LNG = [A(f"LNG{l}", [128, 8], F32) for l in range(2)]
    LNB = [A(f"LNB{l}", [128, 8], F32) for l in range(2)]
    QSC = [A(f"QSC{l}", [128, 2], F32) for l in range(2)]
    KSC = [A(f"KSC{l}", [128, 2], F32) for l in range(2)]

    def Qa(a, b_, p0=0, p1=128):
        return BIGQ[p0:p1, a:b_]

    def X1s(a, b_):
        return BIGQ[:, X1OFF + a:X1OFF + b_]

    PS = [nc.alloc_psum_tensor(f"PS{i}", [128, 512], F32) for i in range(8)]

    b = Bld(nc)
    CX = b.chan("cx")
    CW = [b.chan("cw0"), b.chan("cw1")]
    CW2 = [b.chan("cw2a"), b.chan("cw2b")]
    CRT = b.chan("crt")
    CB = b.chan("cb")
    CO = b.chan("co")
    CTW = b.chan("ctw")

    psz = [f"ps{i}" for i in range(8)]

    b.emit("dve", lambda e: e.memset(BONE[:, :], 1.0), writes=("BONE",))

    b.emit("dve", lambda e: e.memset(IDENT[:, :], 1.0), writes=("IDENT",))
    b.emit("pool", lambda e: e.affine_select(IDENT[:, :], IDENT[:, :], [[1, 128]],
                                             ALU.is_equal, 0.0, base=0,
                                             channel_multiplier=-1),
           reads=("lib",), writes=("IDENT",))
    b.emit("dve", lambda e: e.memset(ONES[:, :], 1.0), writes=("ONES",))
    b.emit("dve", lambda e: e.memset(CEPS[:, :], 1e-5), writes=("CEPS",))
    b.emit("dve", lambda e: e.memset(ONES2[:, :], 1.0), writes=("ONES2",))
    b.emit("pool", lambda e: e.affine_select(ONES2[:, 0:1], ONES2[:, 0:1], [[1, 1]],
                                             ALU.is_ge, 0.0, base=63,
                                             channel_multiplier=-1),
           reads=("lib",), writes=("ONES2",))
    b.emit("pool", lambda e: e.affine_select(ONES2[:, 1:2], ONES2[:, 1:2], [[1, 1]],
                                             ALU.is_ge, 0.0, base=-64,
                                             channel_multiplier=1),
           reads=("lib",), writes=("ONES2",))
    for (c0, sgn, kb) in ((0, -1.0, -32), (32, 1.0, 0), (64, -1.0, -96), (96, 1.0, -64)):
        sl = RROT[:, c0:c0 + 32]
        b.emit("dve", lambda e, sl=sl, sgn=sgn: e.memset(sl, sgn), writes=("RROT",))
        b.emit("pool", lambda e, sl=sl, kb=kb: e.affine_select(
            sl, sl, [[-1, 32]], ALU.is_equal, 0.0, base=kb,
            channel_multiplier=1), reads=("lib",), writes=("RROT",))
    b.emit("dve", lambda e: e.memset(MA[:, :], 0.0), writes=("MA",))
    b.emit("dve", lambda e: e.memset(MF[:, :], 0.0), writes=("MF",))
    for qs in range(2):
        sl = MA[:, qs * 512:(qs + 1) * 512]
        b.emit("pool", lambda e, sl=sl, qs=qs: e.affine_select(
            sl, sl, [[1, 512]], ALU.is_ge, NEG, base=-(128 * qs) - 1,
            channel_multiplier=-1), reads=("lib",), writes=("MA",))
        b.emit("pool", lambda e, sl=sl, qs=qs: e.affine_select(
            sl, sl, [[-1, 512]], ALU.is_ge, NEG, base=(128 * qs) + 256,
            channel_multiplier=1), reads=("lib",), writes=("MA",))
    for qs in range(4):
        sl = MF[:, qs * 1024:(qs + 1) * 1024]
        b.emit("pool", lambda e, sl=sl, qs=qs: e.affine_select(
            sl, sl, [[1, 1024]], ALU.is_ge, NEG, base=-(128 * qs),
            channel_multiplier=-1), reads=("lib",), writes=("MF",))
        b.emit("pool", lambda e, sl=sl, qs=qs: e.affine_select(
            sl, sl, [[-1, 1024]], ALU.is_ge, NEG, base=(128 * qs) + 512,
            channel_multiplier=1), reads=("lib",), writes=("MF",))
    for l in range(2):
        b.dma(CB, B1A[l][:, :], b1a_d[l], writes=("BI",))
        b.dma(CB, B1G[l][:, :], b1g_d[l], writes=("BI",))
        b.dma(CB, B2[l][:, :], b2_d[l], writes=("BI",))
        b.dma(CB, LNG[l][:, :], lng_d[l], writes=("BI",))
        b.dma(CB, LNB[l][:, :], lnb_d[l], writes=("BI",))
        b.dma(CB, QSC[l][:, :], qsc_d[l], writes=("BI",))
        b.dma(CB, KSC[l][:, :], ksc_d[l], writes=("BI",))

    for qtr in range(4):
        b.emit("pool", lambda e, q=qtr: e.iota(
            IOD[q * 32:(q + 1) * 32, 0:1], [[1, 1]], base=0, channel_multiplier=1,
            allow_small_or_imprecise_dtypes=True), reads=("lib",), writes=("IOD",))
    b.emit("act", lambda e: e.activation(IOD[:, 1:2], IOD[:, 0:1], AF.Exp,
                                         scale=-2.0 * LN1E4 / 64.0),
           reads=("IOD",), writes=("IOD",))
    b.emit("dve", lambda e: e.tensor_scalar(IOD[:, 2:3], IOD[:, 0:1],
                                            2.0 / 89.6, 25.6 / 89.6,
                                            ALU.mult, ALU.add),
           reads=("IOD",), writes=("IOD",))
    b.emit("act", lambda e: e.activation(IOD[:, 3:4], IOD[:, 2:3], AF.Ln),
           reads=("IOD",), writes=("IOD",))
    for l in range(2):
        for cb in range(NC):
            t0 = cb * 512
            b.emit("pool", lambda e, t0=t0: e.iota(
                MB[:, :], [[1, 512]], base=t0, channel_multiplier=0,
                allow_small_or_imprecise_dtypes=True),
                reads=("lib",), writes=("MB",))
            b.emit("dve", lambda e: e.tensor_scalar_mul(RB[:, :], MB[:, :], IOD[:, 1:2]),
                   reads=("MB", "IOD"), writes=("RB",))
            b.emit("dve", lambda e: e.tensor_scalar_add(P[:, 0:512], RB[:, :], PI),
                   reads=("RB",), writes=("P",))
            for j in range(9, -1, -1):
                cj = 2.0 * PI * (1 << j)
                b.emit("dve", lambda e, cj=cj: e.tensor_single_scalar(
                    VS[:, :], P[:, 0:512], cj, ALU.is_ge),
                    reads=("P",), writes=("VS",))
                b.emit("dve", lambda e, cj=cj: e.scalar_tensor_tensor(
                    P[:, 0:512], VS[:, :], -cj, P[:, 0:512], ALU.mult, ALU.add),
                    reads=("VS",), writes=("P",))
            b.emit("dve", lambda e: e.tensor_scalar_add(P[:, 0:512], P[:, 0:512], -PI),
                   reads=(), writes=("P",))
            b.emit("dve", lambda e: e.tensor_scalar_add(P[:, 512:1024], P[:, 0:512], PI / 2),
                   reads=(), writes=("P",))
            b.emit("dve", lambda e: e.tensor_single_scalar(
                VS[:, :], P[:, 512:1024], PI, ALU.is_ge),
                reads=("P",), writes=("VS",))
            b.emit("dve", lambda e: e.scalar_tensor_tensor(
                P[:, 512:1024], VS[:, :], -2.0 * PI, P[:, 512:1024],
                ALU.mult, ALU.add),
                reads=("VS",), writes=("P",))
            b.emit("act", lambda e: e.activation(SQB[:, :], P[:, 0:512], AF.Sin),
                   reads=("P",), writes=("SQB",))
            b.emit("act", lambda e: e.activation(VS[:, :], P[:, 512:1024], AF.Sin),
                   reads=("P",), writes=("VS",))
            b.emit("pool", lambda e, t0=t0: e.iota(
                QR[:, :], [[1, 512]], base=t0 - 2048, channel_multiplier=0,
                allow_small_or_imprecise_dtypes=True),
                reads=("lib",), writes=("QR",))
            b.emit("dve", lambda e: e.tensor_scalar_mul(QR[:, :], QR[:, :], IOD[:, 3:4]),
                   reads=("IOD",), writes=("QR",))
            b.emit("act", lambda e: e.activation(MB[:, :], QR[:, :], AF.Exp,
                                                 scale=1.0 / 256.0),
                   reads=("QR",), writes=("MB",))
            b.emit("act", lambda e: e.activation(RB[:, :], QR[:, :], AF.Exp,
                                                 scale=-1.0 / 256.0),
                   reads=("QR",), writes=("RB",))
            tabs = [(0, VS, MB, QSC[l][:, 0:1]),
                    (512, SQB, MB, QSC[l][:, 1:2]),
                    (1024, VS, RB, KSC[l][:, 0:1]),
                    (1536, SQB, RB, KSC[l][:, 1:2])]
            for (off, t1, t2, sc) in tabs:
                b.emit("dve", lambda e, off=off, t1=t1, t2=t2: e.tensor_mul(
                    RT[:, off:off + 512], t1[:, :], t2[:, :]),
                    reads=("VS", "SQB", "MB", "RB"), writes=("RT",))
                b.emit("dve", lambda e, off=off, sc=sc: e.tensor_scalar_mul(
                    RT[:, off:off + 512], RT[:, off:off + 512], sc),
                    reads=("BI",), writes=("RT",))
            for i in range(4):
                b.dma(CTW, rott[l][i * 128:(i + 1) * 128, t0:t0 + 512],
                      RT[:, i * 512:(i + 1) * 512],
                      reads=("RT",), writes=(f"d:rott{l}",))

    b.barrier()


    def emit_ln_stats_and_apply(src_tile, src_cols, dst_writes, dst_slices,
                                affine=None):
        for kk in range(8):
            sq = SQA if kk % 2 == 0 else SQB
            sqz = "SQA" if kk % 2 == 0 else "SQB"
            src = src_cols(kk)
            b.emit("act", lambda e, sq=sq, src=src: e.activation(sq[:, :], src, AF.Square),
                   reads=(src_tile,), writes=(sqz,))
            b.emit("pe", _mmg([(PS[6][0:1, 0:512], ONES[:, 0:1], src,
                                kk == 0, kk == 7, None)]),
                   reads=(src_tile, "ONES"), writes=("ps6",))
            b.emit("pe", _mmg([(PS[7][0:1, 0:512], ONES[:, 0:1], sq[:, :],
                                kk == 0, kk == 7, None)]),
                   reads=(sqz, "ONES"), writes=("ps7",))
        b.emit("dve", lambda e: e.tensor_scalar_mul(MB[0:1, :], PS[6][0:1, 0:512], 1.0 / 1024.0),
               reads=("ps6",), writes=("MB",))
        b.emit("dve", lambda e: e.tensor_scalar_mul(RB[0:1, :], PS[7][0:1, 0:512], 1.0 / 1024.0),
               reads=("ps7",), writes=("RB",))
        b.emit("act", lambda e: e.activation(RSS[0:1, :], MB[0:1, :], AF.Square),
               reads=("MB",), writes=("RSS",))
        b.emit("dve", lambda e: e.tensor_sub(RB[0:1, :], RB[0:1, :], RSS[0:1, :]),
               reads=("RSS",), writes=("RB",))
        b.emit("act", lambda e: e.activation(RSS[0:1, :], RB[0:1, :], AF.Ln,
                                             bias=CEPS[0:1, 0:1]),
               reads=("RB", "CEPS"), writes=("RSS",))
        b.emit("act", lambda e: e.activation(RB[0:1, :], RSS[0:1, :], AF.Exp, scale=-0.5),
               reads=("RSS",), writes=("RB",))
        b.emit("pe", _mmg([(PS[6][:, 0:512], BONE[0:1, 0:128], MB[0:1, :],
                            True, True, None)]),
               reads=("MB", "BONE"), writes=("ps6",))
        b.emit("pe", _mmg([(PS[7][:, 0:512], BONE[0:1, 0:128], RB[0:1, :],
                            True, True, None)]),
               reads=("RB", "BONE"), writes=("ps7",))
        for kk in range(8):
            src = src_cols(kk)
            dst = dst_slices(kk)
            b.emit("dve", lambda e, d=dst, s=src: e.tensor_sub(d, s, PS[6][:, 0:512]),
                   reads=(src_tile, "ps6"), writes=dst_writes)
            b.emit("dve", lambda e, d=dst: e.tensor_mul(d, d, PS[7][:, 0:512]),
                   reads=("ps7",), writes=dst_writes)
            if affine is not None:
                g_ap, bt_ap = affine
                b.emit("dve", lambda e, d=dst, g=g_ap(kk), bb=bt_ap(kk): e.tensor_scalar(
                    d, d, g, bb, ALU.mult, ALU.add),
                    reads=("BI",), writes=dst_writes)

    def wslot_load(widx, dram_ap, j, nblk):
        s = widx % 2
        b.dma(CW[s], WS[s][:, 0:nblk * 128], dram_ap[j:j + 1, :],
              writes=(f"WS{s}",))
        return s

    def wfus_load_ag(widx, l, j):
        s = widx % 2
        src_ap = bass.AP(wfus_h[l], j * (128 * 24 * 128),
                         [[24 * 128, 128], [1, 16 * 128]])
        b.dma(CW[s], WS[s][:, 0:16 * 128], src_ap, writes=(f"WS{s}",))

    def wfus_load_w2(widx, l, j):
        s = widx % 2
        src_ap = bass.AP(wfus_h[l], j * (128 * 24 * 128) + 16 * 128,
                         [[24 * 128, 128], [1, 8 * 128]])
        b.dma(CW2[s], W2S[s][:, 0:8 * 128], src_ap, writes=(f"W2S{s}",))

    for c in range(NC):
        ccol = c * 512
        for kk in range(8):
            b.dma(CX, X[:, kk * 512:(kk + 1) * 512],
                  xT[kk * 128:(kk + 1) * 128, ccol:ccol + 512], writes=("X",))
        emit_ln_stats_and_apply(
            "X", lambda kk: X[:, kk * 512:(kk + 1) * 512], ("XN",),
            dst_slices=lambda kk: XN[:, kk * 513 + 1:kk * 513 + 513])
        for kk in range(8):
            b.dma(CO, xn_s[kk * 128:(kk + 1) * 128, ccol:ccol + 512],
                  XN[:, kk * 513 + 1:kk * 513 + 513],
                  reads=("XN",), writes=("d:xn",))

        s0 = (2 * c) % 3
        s1 = (2 * c + 1) % 3

        def a1_qkv_group(m):
            s = m % 2
            parts = [(PS[s][:, 0:512], WS[s][:, kk * 128:(kk + 1) * 128],
                      XN[:, kk * 513 + 1:kk * 513 + 513], kk == 0, kk == 7, None)
                     for kk in range(8)]
            b.emit("pe", _mmg(parts), reads=(f"WS{s}", "XN"), writes=(psz[s],))

        def a1_qkv_evac(m):
            s = m % 2
            if m < 8:
                b.emit("act", lambda e, m=m, s=s: e.copy(Qa(m * 512, (m + 1) * 512),
                                                         PS[s][:, 0:512]),
                       reads=(psz[s],), writes=("Q",))
            elif m < 16:
                j = m - 8
                d0 = KR[:, j * 768 + s0 * 256:j * 768 + s0 * 256 + 256]
                d1 = KR[:, j * 768 + s1 * 256:j * 768 + s1 * 256 + 256]
                b.emit("act", lambda e, d=d0, s=s: e.copy(d, PS[s][:, 0:256]),
                       reads=(psz[s],), writes=(f"KR{s0}",))
                b.emit("act", lambda e, d=d1, s=s: e.copy(d, PS[s][:, 256:512]),
                       reads=(psz[s],), writes=(f"KR{s1}",))
            else:
                j = m - 16
                b.emit("act", lambda e, s=s: e.copy(VS[:, :], PS[s][:, 0:512]),
                       reads=(psz[s],), writes=("VS",))
                for u in range(4):
                    b.emit("pe", lambda e, u=u: e.transpose(
                        PS[2][:, u * 128:(u + 1) * 128],
                        VS[:, u * 128:(u + 1) * 128], IDENT[:, :]),
                        reads=("VS", "IDENT"), writes=("ps2",))
                for half in range(2):
                    sl = s0 if half == 0 else s1
                    src = PS[2][:, half * 256:half * 256 + 256]
                    dst = bass.AP(VT, sl * 2048 + j * 128,
                                  [[6144, 128], [1024, 2], [1, 128]])
                    b.emit("act", lambda e, d=dst, sp=src: e.copy(d, sp),
                           reads=("ps2",), writes=(f"VT{sl}",))

        wslot_load(0, wqkvA, 0, 8)
        wslot_load(1, wqkvA, 1, 8)
        a1_qkv_group(0)
        for m in range(1, 24):
            if m + 1 < 24:
                wslot_load(m + 1, wqkvA, m + 1, 8)
            a1_qkv_group(m)
            a1_qkv_evac(m - 1)
        a1_qkv_evac(23)

        aunits = [(b_loc, hp, qs, h) for b_loc in range(2)
                  for hp in range(8) for qs in range(2) for h in range(2)]
        A1_SBANK = (3, 4, 0, 1)

        def a1_env(u, ui):
            b_loc, hp, qs, h = u
            blk = 2 * c + b_loc
            first = (blk == 0)
            jl = [j for j in (qs, qs + 1, qs + 2) if (not first or j >= 2)]
            return dict(b_loc=b_loc, hp=hp, qs=qs, h=h,
                        spv=(blk - 1) % 3, scv=blk % 3,
                        first=first, jl=jl,
                        w0=jl[0] * 128, w1=(jl[-1] + 1) * 128,
                        qcols=hp * 512 + b_loc * 256 + qs * 128,
                        cidx=((b_loc * 8 + hp) * 2 + qs) * 2,
                        nb=2 * (ui % 4),
                        sbk=A1_SBANK[ui % 4])

        def a1_part1(u, ui):
            v = a1_env(u, ui)
            hp, qs, h = v["hp"], v["qs"], v["h"]
            spv, scv, first = v["spv"], v["scv"], v["first"]
            w0, w1, qcols, nb, sbk = v["w0"], v["w1"], v["qcols"], v["nb"], v["sbk"]
            parts = []
            pa1 = min(w1, 256)
            if w0 < pa1:
                parts.append((PS[sbk][:, w0:pa1],
                              Qa(qcols, qcols + 128, 64 * h, 64 * h + 64),
                              KR[64 * h:64 * h + 64,
                                 hp * 768 + spv * 256 + w0:hp * 768 + spv * 256 + pa1],
                              True, False, None))
            cb0 = max(w0, 256)
            if cb0 < w1:
                parts.append((PS[sbk][:, cb0:w1],
                              Qa(qcols, qcols + 128, 64 * h, 64 * h + 64),
                              KR[64 * h:64 * h + 64,
                                 hp * 768 + scv * 256 + cb0 - 256:
                                 hp * 768 + scv * 256 + w1 - 256],
                              len(parts) == 0, True, None))
            parts[-1] = parts[-1][:4] + (True, parts[-1][5])
            rd = ("Q", f"KR{scv}") if first else ("Q", f"KR{spv}", f"KR{scv}")
            b.emit("pe", _mmg(parts), reads=rd, writes=(psz[sbk],))
            b.emit("dve", lambda e, sbk=sbk, w0=w0, w1=w1, qs=qs: e.tensor_add(
                PS[sbk][:, w0:w1], PS[sbk][:, w0:w1],
                MA[:, qs * 512 + w0:qs * 512 + w1]),
                reads=("MA",), writes=(psz[sbk],))
            b.emit("dve", lambda e, sbk=sbk, w0=w0, w1=w1, nb=nb: e.reduce_max(
                NMT[:, nb:nb + 1], PS[sbk][:, w0:w1], axis=AX.X,
                negate=True),
                reads=(psz[sbk],), writes=("NMT",))
            b.emit("dve", lambda e, nb=nb: e.tensor_scalar_mul(
                NMT[:, nb + 1:nb + 2], NMT[:, nb:nb + 1], 0.125),
                reads=(), writes=("NMT",))

        def a1_part2(u, ui):
            v = a1_env(u, ui)
            hp, qs, h = v["hp"], v["qs"], v["h"]
            spv, scv, first, jl = v["spv"], v["scv"], v["first"], v["jl"]
            w0, w1, qcols, cidx, nb, sbk = (v["w0"], v["w1"], v["qcols"],
                                            v["cidx"], v["nb"], v["sbk"])
            b.emit("act", lambda e, sbk=sbk, w0=w0, w1=w1, h=h, ci=cidx, nb=nb:
                   e.activation(
                       P[:, h * 512 + w0:h * 512 + w1], PS[sbk][:, w0:w1], AF.Exp,
                       bias=NMT[:, nb + 1:nb + 2], scale=0.125,
                       accum_out=SM[:, ci + h:ci + h + 1]),
                   reads=(psz[sbk], "NMT"), writes=("P", "SM"))
            nj = len(jl)
            ptb = 5 if h == 0 else 7
            if h == 1:
                b.emit("pe", lambda e, ci=cidx: [
                    e.transpose(PS[6][0:1, 0:128], SM[:, ci:ci + 1], IDENT[:, :]),
                    e.transpose(PS[6][0:1, 128:256], SM[:, ci + 1:ci + 2],
                                IDENT[:, :])],
                    reads=("SM", "IDENT"), writes=("ps6",))
                b.emit("dve", lambda e: e.reciprocal(SMD[0:1, :], PS[6][0:1, 0:256]),
                       reads=("ps6",), writes=("SMD",))
            for jj, j in enumerate(jl):
                b.emit("pe", lambda e, h=h, j=j, jj=jj, ptb=ptb: e.transpose(
                    PS[ptb][:, jj * 128:(jj + 1) * 128],
                    P[:, h * 512 + j * 128:h * 512 + (j + 1) * 128],
                    IDENT[:, :]),
                    reads=("P", "IDENT"), writes=(psz[ptb],))
            if h == 1:
                b.emit("pe", _mmg([(PS[6][0:64, 0:128], BONE[0:1, 0:64],
                                    SMD[0:1, 0:128], True, True, (0, 0)),
                                   (PS[6][64:128, 0:128], BONE[0:1, 0:64],
                                    SMD[0:1, 128:256], True, True, (0, 64))]),
                       reads=("SMD", "BONE"), writes=("ps6",))
                b.emit("act", lambda e: e.copy(RBC[:, :], PS[6][:, 0:128]),
                       reads=("ps6",), writes=("RBC",))
            b.emit("act", lambda e, h=h, nj=nj, ptb=ptb: e.copy(
                PTS[:, h * 512:h * 512 + nj * 128],
                PS[ptb][:, 0:nj * 128]),
                reads=(psz[ptb],), writes=("PTS",))
            parts = []
            vread = set()
            for jj, j in enumerate(jl):
                sl = spv if j < 2 else scv
                ts = j % 2
                vread.add(f"VT{sl}")
                base = sl * 2048 + ts * 1024 + (2 * hp + h) * 64
                parts.append((PS[2][64 * h:64 * h + 64, 0:128],
                              VT[:, base:base + 64],
                              PTS[:, h * 512 + jj * 128:h * 512 + (jj + 1) * 128],
                              jj == 0, jj == nj - 1, (0, 64 * h)))
            b.emit("pe", _mmg(parts), reads=tuple(vread) + ("PTS",),
                   writes=("ps2",))
            if h == 1:
                b.emit("dve", lambda e, qc=qcols: e.tensor_mul(
                    O[:, qc:qc + 128], PS[2][:, 0:128], RBC[:, :]),
                    reads=("ps2", "RBC"), writes=("O",))

        a1_part1(aunits[0], 0)
        a1_part1(aunits[1], 1)
        for ui in range(2, len(aunits)):
            a1_part1(aunits[ui], ui)
            a1_part2(aunits[ui - 2], ui - 2)
        a1_part2(aunits[-2], len(aunits) - 2)
        a1_part2(aunits[-1], len(aunits) - 1)


        def a1_wo_group(fo):
            s = fo % 2
            parts = [(PS[s][:, 0:512], WS[s][:, kk * 128:(kk + 1) * 128],
                      O[:, kk * 512:(kk + 1) * 512], kk == 0, kk == 7, None)
                     for kk in range(8)]
            b.emit("pe", _mmg(parts), reads=(f"WS{s}", "O"), writes=(psz[s],))

        def a1_wo_evac(fo):
            s = fo % 2
            b.emit("dve", lambda e, fo=fo, s=s: e.tensor_add(
                ACC[:, fo * 512:(fo + 1) * 512], PS[s][:, 0:512],
                X[:, fo * 512:(fo + 1) * 512]),
                reads=(psz[s], "X"), writes=(f"ACC{fo}",))
            b.dma(CO, acc_s[fo * 128:(fo + 1) * 128, ccol:ccol + 512],
                  ACC[:, fo * 512:(fo + 1) * 512],
                  reads=(f"ACC{fo}",), writes=("d:acc",))

        wslot_load(0, woutA, 0, 8)
        wslot_load(1, woutA, 1, 8)
        a1_wo_group(0)
        for fo in range(1, 8):
            if fo + 1 < 8:
                wslot_load(fo + 1, woutA, fo + 1, 8)
            a1_wo_group(fo)
            a1_wo_evac(fo - 1)
        a1_wo_evac(7)

    b.barrier()

    def sweep_ffn(l, src, dst):
        b.emit("dve", lambda e: e.memset(GXN[:, :], 0.0), writes=("GXN",))
        b.emit("dve", lambda e: e.memset(GX1[:, :], 0.0), writes=("GX1",))
        for c in range(NC):
            ccol = c * 512
            spv = (c - 1) % 2
            scv = c % 2
            first = (c == 0)
            if src == "x":
                if first:
                    for kk in range(8):
                        b.emit("dve", lambda e, kk=kk: e.memset(
                            XN[:, kk * 513:kk * 513 + 1], 0.0), writes=("XN",))
                        b.dma(CX, XN[:, kk * 513 + 1:kk * 513 + 513],
                              xn_s[kk * 128:(kk + 1) * 128, 0:512],
                              reads=("d:xn",), writes=("XN",))
                else:
                    for kk in range(8):
                        b.dma(CX, XN[:, kk * 513:kk * 513 + 513],
                              xn_s[kk * 128:(kk + 1) * 128, ccol - 1:ccol + 512],
                              reads=("d:xn",), writes=("XN",))
            else:
                for kk in range(8):
                    b.dma(CX, X[:, kk * 512:(kk + 1) * 512],
                          acc_s[kk * 128:(kk + 1) * 128, ccol:ccol + 512],
                          reads=("d:acc",), writes=("X",))
                if first:
                    for kk in range(8):
                        b.emit("dve", lambda e, kk=kk: e.memset(
                            XN[:, kk * 513:kk * 513 + 1], 0.0), writes=("XN",))
                        b.dma(CX, XN[:, kk * 513 + 1:kk * 513 + 513],
                              y1_s[kk * 128:(kk + 1) * 128, 0:512],
                              reads=("d:y1",), writes=("XN",))
                else:
                    for kk in range(8):
                        b.dma(CX, XN[:, kk * 513:kk * 513 + 513],
                              y1_s[kk * 128:(kk + 1) * 128, ccol - 1:ccol + 512],
                              reads=("d:y1",), writes=("XN",))
            for i in range(4):
                b.dma(CRT, RT[:, i * 512:(i + 1) * 512],
                      rott[l][i * 128:(i + 1) * 128, ccol:ccol + 512],
                      reads=(f"d:rott{l}",), writes=("RT",))

            def hshift(kk, off=0, w=512):
                base = kk * 513 + (1 if kk < 4 else 0) + off
                return XN[:, base:base + w]

            def f_qkv_group(m):
                s = m % 2
                parts = [(PS[s][:, 0:512], WS[s][:, kk * 128:(kk + 1) * 128],
                          hshift(kk), kk == 0, kk == 7, None) for kk in range(8)]
                b.emit("pe", _mmg(parts), reads=(f"WS{s}", "XN"), writes=(psz[s],))

            def f_qkv_evac(m):
                s = m % 2
                if m < 4:
                    b.emit("act", lambda e, m=m, s=s: e.copy(
                        Qa(m * 512, (m + 1) * 512), PS[s][:, 0:512]),
                        reads=(psz[s],), writes=("Q",))
                elif m < 8:
                    j = m - 4
                    b.emit("act", lambda e, j=j, s=s: e.copy(
                        X1s(j * 512, (j + 1) * 512), PS[s][:, 0:512]),
                        reads=(psz[s],), writes=("KRAW",))
                else:
                    j = m - 8
                    b.emit("act", lambda e, s=s: e.copy(VS[:, :], PS[s][:, 0:512]),
                           reads=(psz[s],), writes=("VS",))
                    for u in range(4):
                        b.emit("pe", lambda e, u=u: e.transpose(
                            PS[2][:, u * 128:(u + 1) * 128],
                            VS[:, u * 128:(u + 1) * 128], IDENT[:, :]),
                            reads=("VS", "IDENT"), writes=("ps2",))
                    dstv = bass.AP(VT, scv * 2048 + j * 128,
                                   [[6144, 128], [512, 4], [1, 128]])
                    b.emit("act", lambda e, d=dstv: e.copy(d, PS[2][:, 0:512]),
                           reads=("ps2",), writes=(f"VT{scv}",))

            def f_qk_proc(m):
                if m < 4:
                    raw = Qa(m * 512, (m + 1) * 512)
                    rawz = "Q"
                else:
                    raw = X1s((m - 4) * 512, (m - 3) * 512)
                    rawz = "KRAW"
                b.emit("act", lambda e, raw=raw: e.activation(SQA[:, :], raw, AF.Square),
                       reads=(rawz,), writes=("SQA",))
                b.emit("pe", _mmg([(PS[5][0:1, 0:512], ONES2[:, 0:1], SQA[:, :],
                                    True, True, None),
                                   (PS[6][0:1, 0:512], ONES2[:, 1:2], SQA[:, :],
                                    True, True, None)]),
                       reads=("SQA", "ONES2"), writes=("ps5", "ps6"))
                b.emit("dve", lambda e: e.tensor_scalar_max(RSS[0:1, :],
                                                            PS[5][0:1, 0:512], 1e-24),
                       reads=("ps5",), writes=("RSS",))
                b.emit("dve", lambda e: e.tensor_scalar_max(P[0:1, 0:512],
                                                            PS[6][0:1, 0:512], 1e-24),
                       reads=("ps6",), writes=("P",))
                b.emit("act", lambda e: e.activation(RSS[0:1, :], RSS[0:1, :], AF.Ln),
                       reads=(), writes=("RSS",))
                b.emit("act", lambda e: e.activation(RSS[0:1, :], RSS[0:1, :], AF.Exp,
                                                     scale=-0.5),
                       reads=(), writes=("RSS",))
                b.emit("act", lambda e: e.activation(P[0:1, 0:512], P[0:1, 0:512], AF.Ln),
                       reads=(), writes=("P",))
                b.emit("act", lambda e: e.activation(P[0:1, 0:512], P[0:1, 0:512], AF.Exp,
                                                     scale=-0.5),
                       reads=(), writes=("P",))
                b.emit("pe", _mmg([(PS[6][0:64, 0:512], BONE[0:1, 0:64],
                                    RSS[0:1, :], True, True, (0, 0)),
                                   (PS[6][64:128, 0:512], BONE[0:1, 0:64],
                                    P[0:1, 0:512], True, True, (0, 64))]),
                       reads=("RSS", "P", "BONE"), writes=("ps6",))
                b.emit("act", lambda e: e.copy(MB[:, :], PS[6][:, 0:512]),
                       reads=("ps6",), writes=("MB",))
                b.emit("dve", lambda e, raw=raw: e.tensor_mul(QR[:, :], raw, MB[:, :]),
                       reads=(rawz, "MB"), writes=("QR",))
                tb = 0 if m < 4 else 1024
                if m < 4:
                    dzone = "Q"
                    dst = Qa(m * 512, (m + 1) * 512)
                else:
                    j = m - 4
                    dzone = f"KR{scv}"
                    dst = KR[:, j * 1024 + scv * 512:j * 1024 + scv * 512 + 512]
                b.emit("pe", _mmg([(PS[7][:, 0:512], RROT[:, :], QR[:, :],
                                    True, True, None)]),
                       reads=("RROT", "QR"), writes=("ps7",))
                b.emit("dve", lambda e, tb=tb, d=dst: e.tensor_mul(
                    d, QR[:, :], RT[:, tb:tb + 512]),
                    reads=("QR", "RT"), writes=(dzone,))
                b.emit("dve", lambda e, tb=tb: e.tensor_mul(
                    SQB[:, :], PS[7][:, 0:512], RT[:, tb + 512:tb + 1024]),
                    reads=("ps7", "RT"), writes=("SQB",))
                b.emit("dve", lambda e, d=dst: e.tensor_add(d, d, SQB[:, :]),
                       reads=("SQB",), writes=(dzone,))

            wslot_load(0, wqkvF[l], 0, 8)
            wslot_load(1, wqkvF[l], 1, 8)
            f_qkv_group(0)
            for m in range(1, 12):
                if m + 1 < 12:
                    wslot_load(m + 1, wqkvF[l], m + 1, 8)
                f_qkv_group(m)
                f_qkv_evac(m - 1)
                if m >= 4:
                    f_qk_proc(m - 4)
            f_qkv_evac(11)

            funits = [(hp, qs, h) for hp in range(4) for qs in range(4)
                      for h in range(2)]

            def f_env(u, ui):
                hp, qs, h = u
                jl = [j for j in range(qs, qs + 5) if (not first or j >= 4)]
                return dict(hp=hp, qs=qs, h=h, jl=jl,
                            w0=jl[0] * 128, w1=(jl[-1] + 1) * 128,
                            qcols=hp * 512 + qs * 128,
                            scidx=(hp * 4 + qs) * 2 + h,
                            nbh=4 * (ui % 2),
                            bkA=2 + 2 * (ui % 2), bkB=3 + 2 * (ui % 2))

            def f_part1(u, ui):
                v = f_env(u, ui)
                hp, qs, h = v["hp"], v["qs"], v["h"]
                w0, w1 = v["w0"], v["w1"]
                nbh, bkA, bkB = v["nbh"], v["bkA"], v["bkB"]
                aa1 = min(w1, 512)
                hasA = w0 < aa1
                bb0 = max(w0, 512) - 512
                bb1 = w1 - 512
                if hasA:
                    b.emit("pe", _mmg([(PS[bkA][:, w0:aa1],
                                        Qa(v["qcols"], v["qcols"] + 128, 64 * h, 64 * h + 64),
                                        KR[64 * h:64 * h + 64,
                                           hp * 1024 + spv * 512 + w0:
                                           hp * 1024 + spv * 512 + aa1],
                                        True, True, None)]),
                           reads=("Q", f"KR{spv}"), writes=(psz[bkA],))
                b.emit("pe", _mmg([(PS[bkB][:, bb0:bb1],
                                    Qa(v["qcols"], v["qcols"] + 128, 64 * h, 64 * h + 64),
                                    KR[64 * h:64 * h + 64,
                                       hp * 1024 + scv * 512 + bb0:
                                       hp * 1024 + scv * 512 + bb1],
                                    True, True, None)]),
                       reads=("Q", f"KR{scv}"), writes=(psz[bkB],))
                if hasA:
                    b.emit("dve", lambda e, bk=bkA, qs=qs, w0=w0, aa1=aa1: e.tensor_add(
                        PS[bk][:, w0:aa1], PS[bk][:, w0:aa1],
                        MF[:, qs * 1024 + w0:qs * 1024 + aa1]),
                        reads=("MF",), writes=(psz[bkA],))
                b.emit("dve", lambda e, bk=bkB, qs=qs, bb0=bb0, bb1=bb1: e.tensor_add(
                    PS[bk][:, bb0:bb1], PS[bk][:, bb0:bb1],
                    MF[:, qs * 1024 + 512 + bb0:qs * 1024 + 512 + bb1]),
                    reads=("MF",), writes=(psz[bkB],))
                if hasA:
                    b.emit("dve", lambda e, bk=bkA, w0=w0, aa1=aa1, nbh=nbh: e.reduce_max(
                        NMT[:, nbh:nbh + 1], PS[bk][:, w0:aa1], axis=AX.X, negate=True),
                        reads=(psz[bkA],), writes=("NMT",))
                b.emit("dve", lambda e, bk=bkB, bb0=bb0, bb1=bb1, nbh=nbh: e.reduce_max(
                    NMT[:, nbh + 1:nbh + 2], PS[bk][:, bb0:bb1], axis=AX.X, negate=True),
                    reads=(psz[bkB],), writes=("NMT",))
                if hasA:
                    b.emit("dve", lambda e, nbh=nbh: e.tensor_tensor(
                        NMT[:, nbh + 2:nbh + 3], NMT[:, nbh:nbh + 1],
                        NMT[:, nbh + 1:nbh + 2], ALU.min),
                        reads=(), writes=("NMT",))
                    b.emit("dve", lambda e, nbh=nbh: e.tensor_scalar_mul(
                        NMT[:, nbh + 3:nbh + 4], NMT[:, nbh + 2:nbh + 3], 8.0),
                        reads=(), writes=("NMT",))
                else:
                    b.emit("dve", lambda e, nbh=nbh: e.tensor_scalar_mul(
                        NMT[:, nbh + 3:nbh + 4], NMT[:, nbh + 1:nbh + 2], 8.0),
                        reads=(), writes=("NMT",))

            def f_part2(u, ui):
                v = f_env(u, ui)
                hp, qs, h = v["hp"], v["qs"], v["h"]
                jl, w0, w1 = v["jl"], v["w0"], v["w1"]
                nbh, bkA, bkB = v["nbh"], v["bkA"], v["bkB"]
                scidx = v["scidx"]
                qcols = v["qcols"]
                aa1 = min(w1, 512)
                hasA = w0 < aa1
                bb0 = max(w0, 512) - 512
                bb1 = w1 - 512
                if hasA:
                    b.emit("act", lambda e, bk=bkA, si=scidx, nbh=nbh, w0=w0, aa1=aa1:
                           e.activation(
                               P[:, w0:aa1], PS[bk][:, w0:aa1], AF.Exp,
                               bias=NMT[:, nbh + 3:nbh + 4], scale=8.0,
                               accum_out=SM[:, 2 * si:2 * si + 1]),
                           reads=(psz[bkA], "NMT"), writes=("P", "SM"))
                    b.emit("act", lambda e, bk=bkB, si=scidx, nbh=nbh, bb0=bb0, bb1=bb1:
                           e.activation(
                               P[:, 512 + bb0:512 + bb1], PS[bk][:, bb0:bb1], AF.Exp,
                               bias=NMT[:, nbh + 3:nbh + 4], scale=8.0,
                               accum_out=SM[:, 2 * si + 1:2 * si + 2]),
                           reads=(psz[bkB], "NMT"), writes=("P", "SM"))
                    b.emit("dve", lambda e, si=scidx: e.tensor_add(
                        SM[:, 64 + si:65 + si], SM[:, 2 * si:2 * si + 1],
                        SM[:, 2 * si + 1:2 * si + 2]),
                        reads=(), writes=("SM",))
                else:
                    b.emit("act", lambda e, bk=bkB, si=scidx, nbh=nbh, bb0=bb0, bb1=bb1:
                           e.activation(
                               P[:, 512 + bb0:512 + bb1], PS[bk][:, bb0:bb1], AF.Exp,
                               bias=NMT[:, nbh + 3:nbh + 4], scale=8.0,
                               accum_out=SM[:, 64 + si:65 + si]),
                           reads=(psz[bkB], "NMT"), writes=("P", "SM"))
                last_pair = (h == 1)
                if last_pair:
                    si0 = (hp * 4 + qs) * 2
                    b.emit("pe", lambda e, si=si0: [
                        e.transpose(PS[1][0:1, 0:128], SM[:, 64 + si:65 + si],
                                    IDENT[:, :]),
                        e.transpose(PS[1][0:1, 128:256], SM[:, 65 + si:66 + si],
                                    IDENT[:, :])],
                        reads=("SM", "IDENT"), writes=("ps1",))
                    b.emit("dve", lambda e: e.reciprocal(SMD[0:1, :], PS[1][0:1, 0:256]),
                           reads=("ps1",), writes=("SMD",))
                for j in jl:
                    pb = 6 + j // 4
                    b.emit("pe", lambda e, j=j, pb=pb: e.transpose(
                        PS[pb][:, (j % 4) * 128:(j % 4) * 128 + 128],
                        P[:, j * 128:(j + 1) * 128], IDENT[:, :]),
                        reads=("P", "IDENT"), writes=(psz[pb],))
                if last_pair:
                    b.emit("pe", _mmg([(PS[1][0:64, 0:128], BONE[0:1, 0:64],
                                        SMD[0:1, 0:128], True, True, (0, 0)),
                                       (PS[1][64:128, 0:128], BONE[0:1, 0:64],
                                        SMD[0:1, 128:256], True, True, (0, 64))]),
                           reads=("SMD", "BONE"), writes=("ps1",))
                    b.emit("act", lambda e: e.copy(RBC[:, :], PS[1][:, 0:128]),
                           reads=("ps1",), writes=("RBC",))
                jA = [j for j in jl if j < 4]
                jB = [j for j in jl if j >= 4]
                if jA:
                    b.emit("act", lambda e, j0=jA[0], j1=jA[-1]: e.copy(
                        PTS[:, j0 * 128:(j1 + 1) * 128],
                        PS[6][:, (j0 % 4) * 128:((j1 % 4) + 1) * 128]),
                        reads=("ps6",), writes=("PTS",))
                b.emit("act", lambda e, j0=jB[0], j1=jB[-1]: e.copy(
                    PTS[:, j0 * 128:(j1 + 1) * 128],
                    PS[7][:, (j0 % 4) * 128:((j1 % 4) + 1) * 128]),
                    reads=("ps7",), writes=("PTS",))
                parts = []
                vread = set()
                for jj, j in enumerate(jl):
                    sl = spv if j < 4 else scv
                    ts = j % 4
                    vread.add(f"VT{sl}")
                    base = sl * 2048 + ts * 512 + (2 * hp + h) * 64
                    parts.append((PS[0][64 * h:64 * h + 64, 0:128],
                                  VT[:, base:base + 64],
                                  PTS[:, j * 128:(j + 1) * 128],
                                  jj == 0, jj == len(jl) - 1, (0, 64 * h)))
                b.emit("pe", _mmg(parts), reads=tuple(vread) + ("PTS",),
                       writes=("ps0",))
                if last_pair:
                    b.emit("dve", lambda e, qc=qcols: e.tensor_mul(
                        O[:, qc:qc + 128], PS[0][:, 0:128], RBC[:, :]),
                        reads=("ps0", "RBC"), writes=("O",))

            f_part1(funits[0], 0)
            for ui in range(1, len(funits)):
                f_part1(funits[ui], ui)
                f_part2(funits[ui - 1], ui - 1)
            f_part2(funits[-1], len(funits) - 1)

            def f_wo_group(fo):
                s = fo % 2
                parts = [(PS[s][:, 0:512], WS[s][:, kk * 128:(kk + 1) * 128],
                          O[:, kk * 512:(kk + 1) * 512], kk == 0, kk == 3, None)
                         for kk in range(4)]
                b.emit("pe", _mmg(parts), reads=(f"WS{s}", "O"), writes=(psz[s],))

            def f_wo_evac(fo):
                s = fo % 2
                b.emit("dve", lambda e, fo=fo, s=s: e.tensor_add(
                    X1s(fo * 513 + 1, fo * 513 + 513), PS[s][:, 0:512],
                    XN[:, fo * 513 + 1:fo * 513 + 513]),
                    reads=(psz[s], "XN"), writes=("X1",))

            wslot_load(0, woF[l], 0, 4)
            wslot_load(1, woF[l], 1, 4)
            f_wo_group(0)
            for fo in range(1, 8):
                if fo + 1 < 8:
                    wslot_load(fo + 1, woF[l], fo + 1, 4)
                f_wo_group(fo)
                f_wo_evac(fo - 1)
            f_wo_evac(7)
            for kk in range(8):
                b.emit("dve", lambda e, kk=kk: e.tensor_copy(
                    X1s(kk * 513, kk * 513 + 1), GX1[:, kk:kk + 1]),
                    reads=("GX1",), writes=("X1",))
            for kk in range(8):
                b.emit("dve", lambda e, kk=kk: e.tensor_copy(
                    GX1[:, kk:kk + 1], X1s(kk * 513 + 512, kk * 513 + 513)),
                    reads=("X1",), writes=("GX1",))

            def h2slice(kk):
                base = kk * 513 + (1 if kk < 4 else 0)
                return X1s(base, base + 512)

            emit_ln_stats_and_apply(
                "X1", h2slice, ("XN",),
                dst_slices=lambda kk: XN[:, kk * 513 + 1:kk * 513 + 513],
                affine=(lambda kk: LNG[l][:, kk:kk + 1], lambda kk: LNB[l][:, kk:kk + 1]))

            for ts2 in range(2):
                tw = ts2 * 256

                def w1_ag_group(m, tw=tw):
                    s = (ts2 * 22 + m) % 2
                    sa = m % 2
                    parts = [(PS[sa][:, 0:256], WS[s][:, kk * 128:(kk + 1) * 128],
                              XN[:, kk * 513 + 1 + tw:kk * 513 + 1 + tw + 256],
                              kk == 0, False, None) for kk in range(8)]
                    parts += [(PS[sa][:, 256:512], WS[s][:, (8 + kk) * 128:(9 + kk) * 128],
                               XN[:, kk * 513 + 1 + tw:kk * 513 + 1 + tw + 256],
                               False, kk == 7, None) for kk in range(8)]
                    b.emit("pe", _mmg(parts), reads=(f"WS{s}", "XN"), writes=(psz[sa],))

                def w1_w2_chain(m, tw=tw):
                    s = (ts2 * 22 + m) % 2
                    sa = m % 2
                    mz = f"M{m % 2}"
                    mc = (m % 2) * 256
                    b.emit("act", lambda e, sa=sa, m=m: e.activation(
                        MG[:, :], PS[sa][:, 256:512], AF.Gelu,
                        bias=B1G[l][:, m:m + 1]),
                        reads=(psz[sa], "BI"), writes=("MG",))
                    b.emit("dve", lambda e, sa=sa, m=m, mc=mc: e.scalar_tensor_tensor(
                        Mt[:, mc:mc + 256], PS[sa][:, 0:256], B1A[l][:, m:m + 1],
                        MG[:, :], ALU.add, ALU.mult),
                        reads=(psz[sa], "MG", "BI"), writes=(mz,))
                    parts = [(PS[2 + fo // 2][:, (fo % 2) * 256:(fo % 2) * 256 + 256],
                              W2S[s][:, fo * 128:(fo + 1) * 128],
                              Mt[:, mc:mc + 256], (m == 0 and fo % 2 == 0),
                              (m == 21 and fo % 2 == 1), None)
                             for fo in range(8)]
                    b.emit("pe", _mmg(parts), reads=(f"W2S{s}", mz),
                           writes=("ps2", "ps3", "ps4", "ps5"))

                wfus_load_ag(ts2 * 22, l, 0)
                wfus_load_ag(ts2 * 22 + 1, l, 1)
                wfus_load_w2(ts2 * 22, l, 0)
                wfus_load_w2(ts2 * 22 + 1, l, 1)
                w1_ag_group(0)
                for m in range(1, 22):
                    if m + 1 < 22:
                        wfus_load_ag(ts2 * 22 + m + 1, l, m + 1)
                    w1_ag_group(m)
                    w1_w2_chain(m - 1)
                    if m + 1 < 22:
                        wfus_load_w2(ts2 * 22 + m + 1, l, m + 1)
                w1_w2_chain(21)
                for fo in range(8):
                    w2v = PS[2 + fo // 2][:, (fo % 2) * 256:(fo % 2) * 256 + 256]
                    b.emit("dve", lambda e, fo=fo, w2v=w2v, tw=tw: e.scalar_tensor_tensor(
                        ACC[:, fo * 512 + tw:fo * 512 + tw + 256], w2v,
                        B2[l][:, fo:fo + 1],
                        X1s(fo * 513 + 1 + tw, fo * 513 + 1 + tw + 256),
                        ALU.add, ALU.add),
                        reads=("ps" + str(2 + fo // 2), "X1", "BI"),
                        writes=(f"ACC{fo}",))
                    if dst == "out":
                        b.emit("dve", lambda e, fo=fo, tw=tw: e.tensor_add(
                            ACC[:, fo * 512 + tw:fo * 512 + tw + 256],
                            ACC[:, fo * 512 + tw:fo * 512 + tw + 256],
                            X[:, fo * 512 + tw:fo * 512 + tw + 256]),
                            reads=("X",), writes=(f"ACC{fo}",))
            dstd = y1_s if dst == "y1" else outT
            dzone = "d:y1" if dst == "y1" else "d:out"
            for fo in range(8):
                b.dma(CO, dstd[fo * 128:(fo + 1) * 128, ccol:ccol + 512],
                      ACC[:, fo * 512:(fo + 1) * 512],
                      reads=(f"ACC{fo}",), writes=(dzone,))

    sweep_ffn(0, "x", "y1")
    b.barrier()
    sweep_ffn(1, "y1", "out")

    b.wait_all_chans("sp")
    b.replay()
    nc.finalize()
    return nc



def _tile_w(w, nk, nm):
    K, M = w.shape
    assert K == nk * 128 and M == nm * 128
    t = w.reshape(nk, 128, nm, 128).transpose(2, 1, 0, 3)
    return np.ascontiguousarray(t).reshape(nm, 128 * nk * 128)


def _pad_cols(w, M):
    out = np.zeros((w.shape[0], M), dtype=w.dtype)
    out[:, :w.shape[1]] = w
    return out


def _pad_rows(w, K):
    out = np.zeros((K, w.shape[1]), dtype=w.dtype)
    out[:w.shape[0]] = w
    return out


def stage_inputs(L0_attn_w_qkv, L0_attn_w_out, L0_ffn_wqkv, L0_ffn_q_scale,
                 L0_ffn_k_scale, L0_ffn_wo, L0_ffn_ln_g, L0_ffn_ln_b,
                 L0_ffn_w1, L0_ffn_b1, L0_ffn_w2, L0_ffn_b2):
    f = np.float32
    d = {}
    d["wqkvA"] = _tile_w(np.asarray(L0_attn_w_qkv, f), 8, 24)
    d["woutA"] = _tile_w(np.asarray(L0_attn_w_out, f), 8, 8)
    for l in range(2):
        d[f"wqkvF{l}"] = _tile_w(np.asarray(L0_ffn_wqkv[l], f), 8, 12)
        d[f"woF{l}"] = _tile_w(np.asarray(L0_ffn_wo[l], f), 4, 8)
        w1 = np.asarray(L0_ffn_w1[l], f)
        wa = _pad_cols(w1[:, :2730], 2816).reshape(8, 128, 22, 128).transpose(2, 1, 0, 3)
        wg = _pad_cols(w1[:, 2730:], 2816).reshape(8, 128, 22, 128).transpose(2, 1, 0, 3)
        w2 = _pad_rows(np.asarray(L0_ffn_w2[l], f), 2816).reshape(22, 128, 8, 128)
        d[f"wfus{l}"] = np.ascontiguousarray(
            np.concatenate([wa, wg, w2], axis=2)).reshape(22, 128 * 24 * 128)
        b1 = np.asarray(L0_ffn_b1[l], f)
        d[f"b1a{l}"] = np.ascontiguousarray(
            np.concatenate([b1[:2730], np.zeros(86, f)]).reshape(22, 128).T)
        d[f"b1g{l}"] = np.ascontiguousarray(
            np.concatenate([b1[2730:], np.zeros(86, f)]).reshape(22, 128).T)
        d[f"b2{l}"] = np.ascontiguousarray(np.asarray(L0_ffn_b2[l], f).reshape(8, 128).T)
        d[f"lng{l}"] = np.ascontiguousarray(np.asarray(L0_ffn_ln_g[l], f).reshape(8, 128).T)
        d[f"lnb{l}"] = np.ascontiguousarray(np.asarray(L0_ffn_ln_b[l], f).reshape(8, 128).T)
        qs = np.asarray(L0_ffn_q_scale[l], f)
        ks = np.asarray(L0_ffn_k_scale[l], f)
        d[f"qsc{l}"] = np.ascontiguousarray(
            np.stack([np.tile(qs, 2), np.tile(np.roll(qs, -32), 2)], axis=1))
        d[f"ksc{l}"] = np.ascontiguousarray(
            np.stack([np.tile(ks, 2), np.tile(np.roll(ks, -32), 2)], axis=1))
    return d


def kernel(L0_attn_x, L0_attn_w_qkv, L0_attn_w_out, L0_ffn_wqkv, L0_ffn_q_scale,
           L0_ffn_k_scale, L0_ffn_wo, L0_ffn_ln_g, L0_ffn_ln_b, L0_ffn_w1,
           L0_ffn_b1, L0_ffn_w2, L0_ffn_b2):
    x = np.asarray(L0_attn_x, np.float32)
    B, NT, D = x.shape
    nc = build_core_program(NT)
    wd = stage_inputs(L0_attn_w_qkv, L0_attn_w_out, L0_ffn_wqkv, L0_ffn_q_scale,
                      L0_ffn_k_scale, L0_ffn_wo, L0_ffn_ln_g, L0_ffn_ln_b,
                      L0_ffn_w1, L0_ffn_b1, L0_ffn_w2, L0_ffn_b2)
    in_maps = []
    for bb in range(B):
        m = {"xT": np.ascontiguousarray(x[bb].T)}
        m.update(wd)
        in_maps.append(m)
    res = run_bass_kernel_spmd(nc, in_maps, core_ids=list(range(B)))
    out = np.stack([np.asarray(r["outT"]).T for r in res.results], axis=0)
    return out.astype(np.float32)


# version 12
# speedup vs baseline: 1.4572x; 1.1290x over previous
import math

import numpy as np

import concourse.bass as bass
import concourse.mybir as mybir
from concourse.bass_utils import run_bass_kernel_spmd
import concourse.library_config as libcfg

F32 = mybir.dt.float32
BF16 = mybir.dt.bfloat16
AF = mybir.ActivationFunctionType
ALU = mybir.AluOpType
AX = mybir.AxisListType

T = 512
NEG = -1.0e30
PI = math.pi
LN1E4 = math.log(10000.0)

ENGS = ("pe", "dve", "act", "pool", "sp")
_BLOCKSECT = {"pe": "tensor", "dve": "vector", "act": "scalar",
              "pool": "gpsimd", "sp": "sync"}

PROXY_LIB = [l for l in libcfg.all_libraries if l.name == "proxy"][0]


class Chan:
    __slots__ = ("name", "sem", "issued")

    def __init__(self, name, sem):
        self.name = name
        self.sem = sem
        self.issued = 0


class Bld:

    def __init__(self, nc):
        self.nc = nc
        self.ops = {e: [] for e in ENGS}
        self.sem = {e: nc.alloc_semaphore("tk_" + e) for e in ENGS}
        self.n = {e: 0 for e in ENGS}
        self.wm = {e: {} for e in ENGS}
        self.zw = {}
        self.zr = {}
        self.chans = []

    def chan(self, name):
        c = Chan(name, self.nc.alloc_semaphore("ch_" + name))
        self.chans.append(c)
        return c

    def _wait(self, e, key, val):
        if isinstance(key, Chan):
            val = 16 * key.issued
        if val <= 0:
            return
        if self.wm[e].get(key, 0) >= val:
            return
        self.wm[e][key] = val
        semh = key.sem if isinstance(key, Chan) else self.sem[key]
        self.ops[e].append(("w", semh, val))

    def _dep(self, e, zones, war):
        for z in zones:
            lw = self.zw.get(z)
            if lw is not None:
                self._wait(e, lw[0], lw[1])
            if war:
                for k, v in self.zr.get(z, {}).items():
                    self._wait(e, k, v)

    def emit(self, e, fn, reads=(), writes=()):
        self._dep(e, reads, False)
        self._dep(e, writes, True)
        self.ops[e].append(("i", fn, self.sem[e]))
        self.n[e] += 1
        t = self.n[e]
        for z in reads:
            self.zr.setdefault(z, {})[e] = t
        for z in writes:
            self.zw[z] = (e, t)
            self.zr[z] = {}

    def dma(self, ch, out_ap, in_ap, reads=(), writes=(), eng="sp"):
        self._dep(eng, reads, False)
        self._dep(eng, writes, True)
        self.ops[eng].append(("d", out_ap, in_ap, ch.sem))
        ch.issued += 1
        for z in reads:
            self.zr.setdefault(z, {})[ch] = 0
        for z in writes:
            self.zw[z] = (ch, 0)
            self.zr[z] = {}

    def barrier(self):
        for e1 in ENGS:
            for e2 in ENGS:
                if e1 != e2:
                    self._wait(e1, e2, self.n[e2])

    def wait_all_chans(self, e="sp"):
        for c in self.chans:
            self._wait(e, c, 1)

    def replay(self):
        with self.nc.Block() as block:
            for e in ENGS:
                ops = self.ops[e]

                def body(eng, ops=ops):
                    for op in ops:
                        if op[0] == "w":
                            eng.wait_ge(op[1], op[2])
                        elif op[0] == "i":
                            r = op[1](eng)
                            if isinstance(r, list):
                                r = r[-1]
                            r.then_inc(op[2], 1)
                        else:
                            eng.dma_start(op[1], op[2]).then_inc(op[3], 16)
                getattr(block, _BLOCKSECT[e])(body)


def _mmg(parts):
    def fn(eng):
        last = None
        for (o, l, r, s, st, tp) in parts:
            if tp is None:
                last = eng.matmul(o, l, r, start=s, stop=st)
            else:
                last = eng.matmul(o, l, r, start=s, stop=st, tile_position=tp)
        return last
    return fn


def build_core_program(NT=4096):
    NC = NT // T
    nc = bass.Bass("TRN2", target_bir_lowering=False)

    xT = nc.dram_tensor("xT", [1024, NT], F32, kind="ExternalInput").ap()
    wqkvA = nc.dram_tensor("wqkvA", [24, 128 * 8 * 128], F32, kind="ExternalInput").ap()
    woutA = nc.dram_tensor("woutA", [8, 128 * 8 * 128], F32, kind="ExternalInput").ap()
    wqkvF = [nc.dram_tensor(f"wqkvF{l}", [12, 128 * 8 * 128], F32, kind="ExternalInput").ap() for l in range(2)]
    woF = [nc.dram_tensor(f"woF{l}", [8, 128 * 4 * 128], F32, kind="ExternalInput").ap() for l in range(2)]
    wfus_h = [nc.dram_tensor(f"wfus{l}", [22, 128 * 24 * 128], F32, kind="ExternalInput") for l in range(2)]
    b1a_d = [nc.dram_tensor(f"b1a{l}", [128, 22], F32, kind="ExternalInput").ap() for l in range(2)]
    b1g_d = [nc.dram_tensor(f"b1g{l}", [128, 22], F32, kind="ExternalInput").ap() for l in range(2)]
    b2_d = [nc.dram_tensor(f"b2{l}", [128, 8], F32, kind="ExternalInput").ap() for l in range(2)]
    lng_d = [nc.dram_tensor(f"lng{l}", [128, 8], F32, kind="ExternalInput").ap() for l in range(2)]
    lnb_d = [nc.dram_tensor(f"lnb{l}", [128, 8], F32, kind="ExternalInput").ap() for l in range(2)]
    qsc_d = [nc.dram_tensor(f"qsc{l}", [128, 2], F32, kind="ExternalInput").ap() for l in range(2)]
    ksc_d = [nc.dram_tensor(f"ksc{l}", [128, 2], F32, kind="ExternalInput").ap() for l in range(2)]
    outT = nc.dram_tensor("outT", [1024, NT], F32, kind="ExternalOutput").ap()
    acc_s = nc.dram_tensor("acc_s", [1024, NT], F32, kind="Internal").ap()
    y1_s = nc.dram_tensor("y1_s", [1024, NT], F32, kind="Internal").ap()
    xn_s = nc.dram_tensor("xn_s", [1024, NT], F32, kind="Internal").ap()
    rott = [nc.dram_tensor(f"rott{l}", [4 * 128, NT], F32, kind="Internal").ap() for l in range(2)]

    A = nc.alloc_sbuf_tensor
    XN = A("XN", [128, 8 * 513], F32)
    X = A("Xt", [128, 8 * 512], F32)
    BIGQ = A("BIGQ", [128, 6152], F32)
    X1OFF = 2048
    ACC = A("ACCt", [128, 8 * 512], F32)
    O = A("Ot", [128, 8 * 512], F32)
    KR = A("KRt", [128, 6144], F32)
    VT = A("VTt", [128, 6144], F32)
    WS = [A("WS0", [128, 16 * 128], F32), A("WS1", [128, 16 * 128], F32)]
    W2S = [A("W2S0", [128, 8 * 128], F32), A("W2S1", [128, 8 * 128], F32)]
    P = A("Pt", [128, 1024], F32)
    PTS = A("PTSt", [128, 1024], F32)
    Mt = A("Mt", [128, 512], F32)
    MG = A("MGt", [128, 256], F32)
    QR = A("QRt", [128, 512], F32)
    VS = A("VSt", [128, 512], F32)
    SQA = A("SQAt", [128, 512], F32)
    SQB = A("SQBt", [128, 512], F32)
    MB = A("MBt", [128, 512], F32)
    RB = A("RBt", [128, 512], F32)
    RT = A("RTt", [128, 2048], F32)
    MA = A("MAt", [128, 768], BF16)
    MF = A("MFt", [128, 4096], BF16)
    SM = A("SMt", [128, 96], F32)
    NMT = A("NMTt", [128, 16], F32)
    SMD = A("SMDt", [1, 256], F32)
    RBC = A("RBCt", [128, 256], F32)
    IDENT = A("IDENTt", [128, 128], F32)
    ONES = A("ONESt", [128, 1], F32)
    ONES2 = A("ONES2t", [128, 2], F32)
    GXN = A("GXNt", [128, 8], F32)
    GX1 = A("GX1t", [128, 8], F32)
    RSS = A("RSSt", [1, 512], F32)
    RROT = A("RROTt", [128, 128], F32)
    BONE = A("BONEt", [1, 128], F32)
    CEPS = A("CEPSt", [128, 1], F32)
    C15 = A("C15t", [128, 1], F32)
    IOD = A("IODt", [128, 4], F32)
    B1A = [A(f"B1A{l}", [128, 22], F32) for l in range(2)]
    B1G = [A(f"B1G{l}", [128, 22], F32) for l in range(2)]
    B2 = [A(f"B2{l}", [128, 8], F32) for l in range(2)]
    LNG = [A(f"LNG{l}", [128, 8], F32) for l in range(2)]
    LNB = [A(f"LNB{l}", [128, 8], F32) for l in range(2)]
    QSC = [A(f"QSC{l}", [128, 2], F32) for l in range(2)]
    KSC = [A(f"KSC{l}", [128, 2], F32) for l in range(2)]

    def Qa(a, b_, p0=0, p1=128):
        return BIGQ[p0:p1, a:b_]

    def X1s(a, b_):
        return BIGQ[:, X1OFF + a:X1OFF + b_]

    PS = [nc.alloc_psum_tensor(f"PS{i}", [128, 512], F32) for i in range(8)]

    b = Bld(nc)
    CX = b.chan("cx")
    CW = [b.chan("cw0"), b.chan("cw1")]
    CW2 = [b.chan("cw2a"), b.chan("cw2b")]
    CRT = b.chan("crt")
    CB = b.chan("cb")
    CO = b.chan("co")
    CTW = b.chan("ctw")

    psz = [f"ps{i}" for i in range(8)]

    b.emit("dve", lambda e: e.memset(BONE[:, :], 1.0), writes=("BONE",))

    b.emit("dve", lambda e: e.memset(IDENT[:, :], 1.0), writes=("IDENT",))
    b.emit("pool", lambda e: e.affine_select(IDENT[:, :], IDENT[:, :], [[1, 128]],
                                             ALU.is_equal, 0.0, base=0,
                                             channel_multiplier=-1),
           reads=("lib",), writes=("IDENT",))
    b.emit("dve", lambda e: e.memset(ONES[:, :], 1.0), writes=("ONES",))
    b.emit("dve", lambda e: e.memset(CEPS[:, :], 1e-5), writes=("CEPS",))
    b.emit("dve", lambda e: e.memset(C15[:, :], -15.0), writes=("C15",))
    b.emit("dve", lambda e: e.memset(ONES2[:, :], 1.0), writes=("ONES2",))
    b.emit("pool", lambda e: e.affine_select(ONES2[:, 0:1], ONES2[:, 0:1], [[1, 1]],
                                             ALU.is_ge, 0.0, base=63,
                                             channel_multiplier=-1),
           reads=("lib",), writes=("ONES2",))
    b.emit("pool", lambda e: e.affine_select(ONES2[:, 1:2], ONES2[:, 1:2], [[1, 1]],
                                             ALU.is_ge, 0.0, base=-64,
                                             channel_multiplier=1),
           reads=("lib",), writes=("ONES2",))
    for (c0, sgn, kb) in ((0, -1.0, -32), (32, 1.0, 0), (64, -1.0, -96), (96, 1.0, -64)):
        sl = RROT[:, c0:c0 + 32]
        b.emit("dve", lambda e, sl=sl, sgn=sgn: e.memset(sl, sgn), writes=("RROT",))
        b.emit("pool", lambda e, sl=sl, kb=kb: e.affine_select(
            sl, sl, [[-1, 32]], ALU.is_equal, 0.0, base=kb,
            channel_multiplier=1), reads=("lib",), writes=("RROT",))
    b.emit("dve", lambda e: e.memset(MA[:, :], 0.0), writes=("MA",))
    b.emit("dve", lambda e: e.memset(MF[:, :], 0.0), writes=("MF",))
    A1_MCOL = {(0, 0): 0, (1, 0): 128, (1, 1): 256, (2, 0): 384,
               (2, 1): 512, (3, 1): 640}
    for (j, qs), c0 in A1_MCOL.items():
        sl = MA[:, c0:c0 + 128]
        dj = (j - qs) * 128
        b.emit("pool", lambda e, sl=sl, dj=dj: e.affine_select(
            sl, sl, [[-1, 128]], ALU.is_ge, NEG, base=dj - 1,
            channel_multiplier=1), reads=("lib",), writes=("MA",))
        b.emit("pool", lambda e, sl=sl, dj=dj: e.affine_select(
            sl, sl, [[1, 128]], ALU.is_ge, NEG, base=256 - dj,
            channel_multiplier=-1), reads=("lib",), writes=("MA",))
    for qs in range(4):
        sl = MF[:, qs * 1024:(qs + 1) * 1024]
        b.emit("pool", lambda e, sl=sl, qs=qs: e.affine_select(
            sl, sl, [[1, 1024]], ALU.is_ge, NEG, base=-(128 * qs),
            channel_multiplier=-1), reads=("lib",), writes=("MF",))
        b.emit("pool", lambda e, sl=sl, qs=qs: e.affine_select(
            sl, sl, [[-1, 1024]], ALU.is_ge, NEG, base=(128 * qs) + 512,
            channel_multiplier=1), reads=("lib",), writes=("MF",))
    for l in range(2):
        b.dma(CB, B1A[l][:, :], b1a_d[l], writes=("BI",))
        b.dma(CB, B1G[l][:, :], b1g_d[l], writes=("BI",))
        b.dma(CB, B2[l][:, :], b2_d[l], writes=("BI",))
        b.dma(CB, LNG[l][:, :], lng_d[l], writes=("BI",))
        b.dma(CB, LNB[l][:, :], lnb_d[l], writes=("BI",))
        b.dma(CB, QSC[l][:, :], qsc_d[l], writes=("BI",))
        b.dma(CB, KSC[l][:, :], ksc_d[l], writes=("BI",))

    for qtr in range(4):
        b.emit("pool", lambda e, q=qtr: e.iota(
            IOD[q * 32:(q + 1) * 32, 0:1], [[1, 1]], base=0, channel_multiplier=1,
            allow_small_or_imprecise_dtypes=True), reads=("lib",), writes=("IOD",))
    b.emit("act", lambda e: e.activation(IOD[:, 1:2], IOD[:, 0:1], AF.Exp,
                                         scale=-2.0 * LN1E4 / 64.0),
           reads=("IOD",), writes=("IOD",))
    b.emit("dve", lambda e: e.tensor_scalar(IOD[:, 2:3], IOD[:, 0:1],
                                            2.0 / 89.6, 25.6 / 89.6,
                                            ALU.mult, ALU.add),
           reads=("IOD",), writes=("IOD",))
    b.emit("act", lambda e: e.activation(IOD[:, 3:4], IOD[:, 2:3], AF.Ln),
           reads=("IOD",), writes=("IOD",))
    for l in range(2):
        for cb in range(NC):
            t0 = cb * 512
            b.emit("pool", lambda e, t0=t0: e.iota(
                MB[:, :], [[1, 512]], base=t0, channel_multiplier=0,
                allow_small_or_imprecise_dtypes=True),
                reads=("lib",), writes=("MB",))
            b.emit("dve", lambda e: e.tensor_scalar_mul(RB[:, :], MB[:, :], IOD[:, 1:2]),
                   reads=("MB", "IOD"), writes=("RB",))
            b.emit("dve", lambda e: e.tensor_scalar_add(P[:, 0:512], RB[:, :], PI),
                   reads=("RB",), writes=("P",))
            for j in range(9, -1, -1):
                cj = 2.0 * PI * (1 << j)
                b.emit("dve", lambda e, cj=cj: e.tensor_single_scalar(
                    VS[:, :], P[:, 0:512], cj, ALU.is_ge),
                    reads=("P",), writes=("VS",))
                b.emit("dve", lambda e, cj=cj: e.scalar_tensor_tensor(
                    P[:, 0:512], VS[:, :], -cj, P[:, 0:512], ALU.mult, ALU.add),
                    reads=("VS",), writes=("P",))
            b.emit("dve", lambda e: e.tensor_scalar_add(P[:, 0:512], P[:, 0:512], -PI),
                   reads=(), writes=("P",))
            b.emit("dve", lambda e: e.tensor_scalar_add(P[:, 512:1024], P[:, 0:512], PI / 2),
                   reads=(), writes=("P",))
            b.emit("dve", lambda e: e.tensor_single_scalar(
                VS[:, :], P[:, 512:1024], PI, ALU.is_ge),
                reads=("P",), writes=("VS",))
            b.emit("dve", lambda e: e.scalar_tensor_tensor(
                P[:, 512:1024], VS[:, :], -2.0 * PI, P[:, 512:1024],
                ALU.mult, ALU.add),
                reads=("VS",), writes=("P",))
            b.emit("act", lambda e: e.activation(SQB[:, :], P[:, 0:512], AF.Sin),
                   reads=("P",), writes=("SQB",))
            b.emit("act", lambda e: e.activation(VS[:, :], P[:, 512:1024], AF.Sin),
                   reads=("P",), writes=("VS",))
            b.emit("pool", lambda e, t0=t0: e.iota(
                QR[:, :], [[1, 512]], base=t0 - 2048, channel_multiplier=0,
                allow_small_or_imprecise_dtypes=True),
                reads=("lib",), writes=("QR",))
            b.emit("dve", lambda e: e.tensor_scalar_mul(QR[:, :], QR[:, :], IOD[:, 3:4]),
                   reads=("IOD",), writes=("QR",))
            b.emit("act", lambda e: e.activation(MB[:, :], QR[:, :], AF.Exp,
                                                 scale=1.0 / 256.0),
                   reads=("QR",), writes=("MB",))
            b.emit("act", lambda e: e.activation(RB[:, :], QR[:, :], AF.Exp,
                                                 scale=-1.0 / 256.0),
                   reads=("QR",), writes=("RB",))
            tabs = [(0, VS, MB, QSC[l][:, 0:1]),
                    (512, SQB, MB, QSC[l][:, 1:2]),
                    (1024, VS, RB, KSC[l][:, 0:1]),
                    (1536, SQB, RB, KSC[l][:, 1:2])]
            for (off, t1, t2, sc) in tabs:
                b.emit("dve", lambda e, off=off, t1=t1, t2=t2: e.tensor_mul(
                    RT[:, off:off + 512], t1[:, :], t2[:, :]),
                    reads=("VS", "SQB", "MB", "RB"), writes=("RT",))
                b.emit("dve", lambda e, off=off, sc=sc: e.tensor_scalar_mul(
                    RT[:, off:off + 512], RT[:, off:off + 512], sc),
                    reads=("BI",), writes=("RT",))
            for i in range(4):
                b.dma(CTW, rott[l][i * 128:(i + 1) * 128, t0:t0 + 512],
                      RT[:, i * 512:(i + 1) * 512],
                      reads=("RT",), writes=(f"d:rott{l}",))

    b.barrier()


    def emit_ln_stats_and_apply(src_tile, src_cols, dst_writes, dst_slices,
                                affine=None):
        for kk in range(8):
            sq = SQA if kk % 2 == 0 else SQB
            sqz = "SQA" if kk % 2 == 0 else "SQB"
            src = src_cols(kk)
            b.emit("act", lambda e, sq=sq, src=src: e.activation(sq[:, :], src, AF.Square),
                   reads=(src_tile,), writes=(sqz,))
            b.emit("pe", _mmg([(PS[6][0:1, 0:512], ONES[:, 0:1], src,
                                kk == 0, kk == 7, None)]),
                   reads=(src_tile, "ONES"), writes=("ps6",))
            b.emit("pe", _mmg([(PS[7][0:1, 0:512], ONES[:, 0:1], sq[:, :],
                                kk == 0, kk == 7, None)]),
                   reads=(sqz, "ONES"), writes=("ps7",))
        b.emit("dve", lambda e: e.tensor_scalar_mul(MB[0:1, :], PS[6][0:1, 0:512], 1.0 / 1024.0),
               reads=("ps6",), writes=("MB",))
        b.emit("dve", lambda e: e.tensor_scalar_mul(RB[0:1, :], PS[7][0:1, 0:512], 1.0 / 1024.0),
               reads=("ps7",), writes=("RB",))
        b.emit("act", lambda e: e.activation(RSS[0:1, :], MB[0:1, :], AF.Square),
               reads=("MB",), writes=("RSS",))
        b.emit("dve", lambda e: e.tensor_sub(RB[0:1, :], RB[0:1, :], RSS[0:1, :]),
               reads=("RSS",), writes=("RB",))
        b.emit("act", lambda e: e.activation(RSS[0:1, :], RB[0:1, :], AF.Ln,
                                             bias=CEPS[0:1, 0:1]),
               reads=("RB", "CEPS"), writes=("RSS",))
        b.emit("act", lambda e: e.activation(RB[0:1, :], RSS[0:1, :], AF.Exp, scale=-0.5),
               reads=("RSS",), writes=("RB",))
        b.emit("pe", _mmg([(PS[6][:, 0:512], BONE[0:1, 0:128], MB[0:1, :],
                            True, True, None)]),
               reads=("MB", "BONE"), writes=("ps6",))
        b.emit("pe", _mmg([(PS[7][:, 0:512], BONE[0:1, 0:128], RB[0:1, :],
                            True, True, None)]),
               reads=("RB", "BONE"), writes=("ps7",))
        for kk in range(8):
            src = src_cols(kk)
            dst = dst_slices(kk)
            b.emit("dve", lambda e, d=dst, s=src: e.tensor_sub(d, s, PS[6][:, 0:512]),
                   reads=(src_tile, "ps6"), writes=dst_writes)
            b.emit("dve", lambda e, d=dst: e.tensor_mul(d, d, PS[7][:, 0:512]),
                   reads=("ps7",), writes=dst_writes)
            if affine is not None:
                g_ap, bt_ap = affine
                b.emit("dve", lambda e, d=dst, g=g_ap(kk), bb=bt_ap(kk): e.tensor_scalar(
                    d, d, g, bb, ALU.mult, ALU.add),
                    reads=("BI",), writes=dst_writes)

    def wslot_load(widx, dram_ap, j, nblk):
        s = widx % 2
        b.dma(CW[s], WS[s][:, 0:nblk * 128], dram_ap[j:j + 1, :],
              writes=(f"WS{s}",))
        return s

    def wfus_load_ag(widx, l, j):
        s = widx % 2
        src_ap = bass.AP(wfus_h[l], j * (128 * 24 * 128),
                         [[24 * 128, 128], [1, 16 * 128]])
        b.dma(CW[s], WS[s][:, 0:16 * 128], src_ap, writes=(f"WS{s}",))

    def wfus_load_w2(widx, l, j):
        s = widx % 2
        src_ap = bass.AP(wfus_h[l], j * (128 * 24 * 128) + 16 * 128,
                         [[24 * 128, 128], [1, 8 * 128]])
        b.dma(CW2[s], W2S[s][:, 0:8 * 128], src_ap, writes=(f"W2S{s}",))

    for c in range(NC):
        ccol = c * 512
        for kk in range(8):
            b.dma(CX, X[:, kk * 512:(kk + 1) * 512],
                  xT[kk * 128:(kk + 1) * 128, ccol:ccol + 512], writes=("X",))
        emit_ln_stats_and_apply(
            "X", lambda kk: X[:, kk * 512:(kk + 1) * 512], ("XN",),
            dst_slices=lambda kk: XN[:, kk * 513 + 1:kk * 513 + 513])
        for kk in range(8):
            b.dma(CO, xn_s[kk * 128:(kk + 1) * 128, ccol:ccol + 512],
                  XN[:, kk * 513 + 1:kk * 513 + 513],
                  reads=("XN",), writes=("d:xn",))

        s0 = (2 * c) % 3
        s1 = (2 * c + 1) % 3

        def a1_qkv_group(m):
            s = m % 2
            parts = [(PS[s][:, 0:512], WS[s][:, kk * 128:(kk + 1) * 128],
                      XN[:, kk * 513 + 1:kk * 513 + 513], kk == 0, kk == 7, None)
                     for kk in range(8)]
            b.emit("pe", _mmg(parts), reads=(f"WS{s}", "XN"), writes=(psz[s],))

        def a1_qkv_evac(m):
            s = m % 2
            if m < 8:
                b.emit("act", lambda e, m=m, s=s: e.copy(Qa(m * 512, (m + 1) * 512),
                                                         PS[s][:, 0:512]),
                       reads=(psz[s],), writes=("Q",))
            elif m < 16:
                j = m - 8
                d0 = KR[:, j * 768 + s0 * 256:j * 768 + s0 * 256 + 256]
                d1 = KR[:, j * 768 + s1 * 256:j * 768 + s1 * 256 + 256]
                b.emit("act", lambda e, d=d0, s=s: e.copy(d, PS[s][:, 0:256]),
                       reads=(psz[s],), writes=(f"KR{s0}",))
                b.emit("act", lambda e, d=d1, s=s: e.copy(d, PS[s][:, 256:512]),
                       reads=(psz[s],), writes=(f"KR{s1}",))
            else:
                j = m - 16
                b.emit("act", lambda e, s=s: e.copy(VS[:, :], PS[s][:, 0:512]),
                       reads=(psz[s],), writes=("VS",))
                for u in range(4):
                    b.emit("pe", lambda e, u=u: e.transpose(
                        PS[2][:, u * 128:(u + 1) * 128],
                        VS[:, u * 128:(u + 1) * 128], IDENT[:, :]),
                        reads=("VS", "IDENT"), writes=("ps2",))
                for half in range(2):
                    sl = s0 if half == 0 else s1
                    src = PS[2][:, half * 256:half * 256 + 256]
                    dst = bass.AP(VT, sl * 2048 + j * 128,
                                  [[6144, 128], [1024, 2], [1, 128]])
                    b.emit("act", lambda e, d=dst, sp=src: e.copy(d, sp),
                           reads=("ps2",), writes=(f"VT{sl}",))

        wslot_load(0, wqkvA, 0, 8)
        wslot_load(1, wqkvA, 1, 8)
        a1_qkv_group(0)
        for m in range(1, 24):
            if m + 1 < 24:
                wslot_load(m + 1, wqkvA, m + 1, 8)
            a1_qkv_group(m)
            a1_qkv_evac(m - 1)
        a1_qkv_evac(23)

        aunits = [(b_loc, hp, h) for b_loc in range(2)
                  for hp in range(8) for h in range(2)]
        A1_SBANK = ((3, 4), (0, 1), (5, 7))

        A1_JPACK = {0: (0, 0, 0, 1), 1: (0, 128, 0, 2),
                    2: (1, 0, 0, 2), 3: (1, 256, 1, 1)}
        A1_MOFF = {0: 0, 1: 128, 2: 384, 3: 640}

        def a1_env(u, ui):
            b_loc, hp, h = u
            blk = 2 * c + b_loc
            first = (blk == 0)
            jl = [2, 3] if first else [0, 1, 2, 3]
            return dict(b_loc=b_loc, hp=hp, h=h,
                        spv=(blk - 1) % 3, scv=blk % 3,
                        first=first, jl=jl,
                        qb=hp * 512 + b_loc * 256,
                        sb=A1_SBANK[ui % 3])

        def a1_part1(u, ui):
            v = a1_env(u, ui)
            hp, h = v["hp"], v["h"]
            spv, scv, first, jl = v["spv"], v["scv"], v["first"], v["jl"]
            qb, sb = v["qb"], v["sb"]
            parts = [[], []]
            for j in jl:
                bi, c0, qs0, nq = A1_JPACK[j]
                sl = spv if j < 2 else scv
                kc = hp * 768 + sl * 256 + (j % 2) * 128
                parts[bi].append((PS[sb[bi]][:, c0:c0 + nq * 128],
                                  KR[64 * h:64 * h + 64, kc:kc + 128],
                                  Qa(qb + qs0 * 128, qb + qs0 * 128 + nq * 128,
                                     64 * h, 64 * h + 64),
                                  len(parts[bi]) == 0, False, None))
            allp = []
            for bi in range(2):
                if parts[bi]:
                    parts[bi][-1] = parts[bi][-1][:4] + (True, None)
                    allp += parts[bi]
            rd = ("Q", f"KR{scv}") if first else ("Q", f"KR{spv}", f"KR{scv}")
            wz = (psz[sb[1]],) if first else (psz[sb[0]], psz[sb[1]])
            b.emit("pe", _mmg(allp), reads=rd, writes=wz)
            if not first:
                b.emit("dve", lambda e, bk=sb[0]: e.tensor_add(
                    PS[bk][:, 0:384], PS[bk][:, 0:384], MA[:, 0:384]),
                    reads=("MA",), writes=(psz[sb[0]],))
                b.emit("dve", lambda e, bk=sb[1]: e.tensor_add(
                    PS[bk][:, 0:384], PS[bk][:, 0:384], MA[:, 384:768]),
                    reads=("MA",), writes=(psz[sb[1]],))
            else:
                b.emit("dve", lambda e, bk=sb[1]: e.tensor_add(
                    PS[bk][:, 0:384], PS[bk][:, 0:384], MA[:, 384:768]),
                    reads=("MA",), writes=(psz[sb[1]],))

        def a1_part2(u, ui):
            v = a1_env(u, ui)
            hp, h = v["hp"], v["h"]
            spv, scv, first, jl = v["spv"], v["scv"], v["first"], v["jl"]
            qb, sb = v["qb"], v["sb"]
            PT_ = PTS if ui % 2 == 0 else P
            ptz = "PTS" if ui % 2 == 0 else "P"
            dbk = 6
            if not first:
                b.emit("act", lambda e, bk=sb[0], PT_=PT_: e.activation(
                    PT_[:, 0:384], PS[bk][:, 0:384], AF.Exp,
                    bias=C15[:, 0:1], scale=0.125),
                    reads=(psz[sb[0]], "C15"), writes=(ptz,))
            b.emit("act", lambda e, bk=sb[1], PT_=PT_: e.activation(
                PT_[:, 384:768], PS[bk][:, 0:384], AF.Exp,
                bias=C15[:, 0:1], scale=0.125),
                reads=(psz[sb[1]], "C15"), writes=(ptz,))
            jord = [j for j in (1, 2, 0, 3) if j in jl]
            dparts = []
            for j in jord:
                bi, c0, qs0, nq = A1_JPACK[j]
                pc = bi * 384 + c0
                dparts.append((PS[dbk][0:1, qs0 * 128:qs0 * 128 + nq * 128],
                               ONES[:, 0:1], PT_[:, pc:pc + nq * 128],
                               len(dparts) == 0, j == jord[-1], None))
            b.emit("pe", _mmg(dparts), reads=(ptz, "ONES"), writes=(psz[dbk],))
            b.emit("dve", lambda e, dbk=dbk: e.reciprocal(SMD[0:1, 0:256],
                                                          PS[dbk][0:1, 0:256]),
                   reads=(psz[dbk],), writes=("SMD",))
            b.emit("pe", _mmg([(PS[dbk][64 * h:64 * h + 64, 0:256], BONE[0:1, 0:64],
                                SMD[0:1, 0:256], True, True, (0, 64 * h))]),
                   reads=("SMD", "BONE"), writes=(psz[dbk],))
            b.emit("act", lambda e, h=h, dbk=dbk: e.copy(
                RBC[64 * h:64 * h + 64, 0:256],
                PS[dbk][64 * h:64 * h + 64, 0:256]),
                reads=(psz[dbk],), writes=("RBC",))
            vparts = []
            vread = set()
            for j in jord:
                bi, c0, qs0, nq = A1_JPACK[j]
                sl = spv if j < 2 else scv
                ts = j % 2
                vread.add(f"VT{sl}")
                base = sl * 2048 + ts * 1024 + (2 * hp + h) * 64
                pc = bi * 384 + c0
                vparts.append((PS[2][64 * h:64 * h + 64,
                                     qs0 * 128:qs0 * 128 + nq * 128],
                               VT[:, base:base + 64],
                               PT_[:, pc:pc + nq * 128],
                               len(vparts) == 0, j == jord[-1], (0, 64 * h)))
            b.emit("pe", _mmg(vparts), reads=tuple(vread) + (ptz,),
                   writes=("ps2",))
            b.emit("dve", lambda e, h=h, qb=qb: e.tensor_mul(
                O[64 * h:64 * h + 64, qb:qb + 256],
                PS[2][64 * h:64 * h + 64, 0:256],
                RBC[64 * h:64 * h + 64, 0:256]),
                reads=("ps2", "RBC"), writes=("O",))

        a1_part1(aunits[0], 0)
        a1_part1(aunits[1], 1)
        for ui in range(2, len(aunits)):
            a1_part1(aunits[ui], ui)
            a1_part2(aunits[ui - 2], ui - 2)
        a1_part2(aunits[-2], len(aunits) - 2)
        a1_part2(aunits[-1], len(aunits) - 1)

        def a1_wo_group(fo):
            s = fo % 2
            parts = [(PS[s][:, 0:512], WS[s][:, kk * 128:(kk + 1) * 128],
                      O[:, kk * 512:(kk + 1) * 512], kk == 0, kk == 7, None)
                     for kk in range(8)]
            b.emit("pe", _mmg(parts), reads=(f"WS{s}", "O"), writes=(psz[s],))

        def a1_wo_evac(fo):
            s = fo % 2
            b.emit("dve", lambda e, fo=fo, s=s: e.tensor_add(
                ACC[:, fo * 512:(fo + 1) * 512], PS[s][:, 0:512],
                X[:, fo * 512:(fo + 1) * 512]),
                reads=(psz[s], "X"), writes=(f"ACC{fo}",))
            b.dma(CO, acc_s[fo * 128:(fo + 1) * 128, ccol:ccol + 512],
                  ACC[:, fo * 512:(fo + 1) * 512],
                  reads=(f"ACC{fo}",), writes=("d:acc",))

        wslot_load(0, woutA, 0, 8)
        wslot_load(1, woutA, 1, 8)
        a1_wo_group(0)
        for fo in range(1, 8):
            if fo + 1 < 8:
                wslot_load(fo + 1, woutA, fo + 1, 8)
            a1_wo_group(fo)
            a1_wo_evac(fo - 1)
        a1_wo_evac(7)

    b.barrier()

    def sweep_ffn(l, src, dst):
        b.emit("dve", lambda e: e.memset(GXN[:, :], 0.0), writes=("GXN",))
        b.emit("dve", lambda e: e.memset(GX1[:, :], 0.0), writes=("GX1",))
        for c in range(NC):
            ccol = c * 512
            spv = (c - 1) % 2
            scv = c % 2
            first = (c == 0)
            if src == "x":
                if first:
                    for kk in range(8):
                        b.emit("dve", lambda e, kk=kk: e.memset(
                            XN[:, kk * 513:kk * 513 + 1], 0.0), writes=("XN",))
                        b.dma(CX, XN[:, kk * 513 + 1:kk * 513 + 513],
                              xn_s[kk * 128:(kk + 1) * 128, 0:512],
                              reads=("d:xn",), writes=("XN",))
                else:
                    for kk in range(8):
                        b.dma(CX, XN[:, kk * 513:kk * 513 + 513],
                              xn_s[kk * 128:(kk + 1) * 128, ccol - 1:ccol + 512],
                              reads=("d:xn",), writes=("XN",))
            else:
                for kk in range(8):
                    b.dma(CX, X[:, kk * 512:(kk + 1) * 512],
                          acc_s[kk * 128:(kk + 1) * 128, ccol:ccol + 512],
                          reads=("d:acc",), writes=("X",))
                if first:
                    for kk in range(8):
                        b.emit("dve", lambda e, kk=kk: e.memset(
                            XN[:, kk * 513:kk * 513 + 1], 0.0), writes=("XN",))
                        b.dma(CX, XN[:, kk * 513 + 1:kk * 513 + 513],
                              y1_s[kk * 128:(kk + 1) * 128, 0:512],
                              reads=("d:y1",), writes=("XN",))
                else:
                    for kk in range(8):
                        b.dma(CX, XN[:, kk * 513:kk * 513 + 513],
                              y1_s[kk * 128:(kk + 1) * 128, ccol - 1:ccol + 512],
                              reads=("d:y1",), writes=("XN",))
            for i in range(4):
                b.dma(CRT, RT[:, i * 512:(i + 1) * 512],
                      rott[l][i * 128:(i + 1) * 128, ccol:ccol + 512],
                      reads=(f"d:rott{l}",), writes=("RT",))

            def hshift(kk, off=0, w=512):
                base = kk * 513 + (1 if kk < 4 else 0) + off
                return XN[:, base:base + w]

            def f_qkv_group(m):
                s = m % 2
                parts = [(PS[s][:, 0:512], WS[s][:, kk * 128:(kk + 1) * 128],
                          hshift(kk), kk == 0, kk == 7, None) for kk in range(8)]
                b.emit("pe", _mmg(parts), reads=(f"WS{s}", "XN"), writes=(psz[s],))

            def f_qkv_evac(m):
                s = m % 2
                if m < 4:
                    b.emit("act", lambda e, m=m, s=s: e.copy(
                        Qa(m * 512, (m + 1) * 512), PS[s][:, 0:512]),
                        reads=(psz[s],), writes=("Q",))
                elif m < 8:
                    j = m - 4
                    b.emit("act", lambda e, j=j, s=s: e.copy(
                        X1s(j * 512, (j + 1) * 512), PS[s][:, 0:512]),
                        reads=(psz[s],), writes=("KRAW",))
                else:
                    j = m - 8
                    b.emit("act", lambda e, s=s: e.copy(VS[:, :], PS[s][:, 0:512]),
                           reads=(psz[s],), writes=("VS",))
                    for u in range(4):
                        b.emit("pe", lambda e, u=u: e.transpose(
                            PS[2][:, u * 128:(u + 1) * 128],
                            VS[:, u * 128:(u + 1) * 128], IDENT[:, :]),
                            reads=("VS", "IDENT"), writes=("ps2",))
                    dstv = bass.AP(VT, scv * 2048 + j * 128,
                                   [[6144, 128], [512, 4], [1, 128]])
                    b.emit("act", lambda e, d=dstv: e.copy(d, PS[2][:, 0:512]),
                           reads=("ps2",), writes=(f"VT{scv}",))

            def f_qk_proc(m):
                if m < 4:
                    raw = Qa(m * 512, (m + 1) * 512)
                    rawz = "Q"
                else:
                    raw = X1s((m - 4) * 512, (m - 3) * 512)
                    rawz = "KRAW"
                b.emit("act", lambda e, raw=raw: e.activation(SQA[:, :], raw, AF.Square),
                       reads=(rawz,), writes=("SQA",))
                b.emit("pe", _mmg([(PS[5][0:1, 0:512], ONES2[:, 0:1], SQA[:, :],
                                    True, True, None),
                                   (PS[6][0:1, 0:512], ONES2[:, 1:2], SQA[:, :],
                                    True, True, None)]),
                       reads=("SQA", "ONES2"), writes=("ps5", "ps6"))
                b.emit("dve", lambda e: e.tensor_scalar_max(RSS[0:1, :],
                                                            PS[5][0:1, 0:512], 1e-24),
                       reads=("ps5",), writes=("RSS",))
                b.emit("dve", lambda e: e.tensor_scalar_max(P[0:1, 0:512],
                                                            PS[6][0:1, 0:512], 1e-24),
                       reads=("ps6",), writes=("P",))
                b.emit("act", lambda e: e.activation(RSS[0:1, :], RSS[0:1, :], AF.Ln),
                       reads=(), writes=("RSS",))
                b.emit("act", lambda e: e.activation(RSS[0:1, :], RSS[0:1, :], AF.Exp,
                                                     scale=-0.5),
                       reads=(), writes=("RSS",))
                b.emit("act", lambda e: e.activation(P[0:1, 0:512], P[0:1, 0:512], AF.Ln),
                       reads=(), writes=("P",))
                b.emit("act", lambda e: e.activation(P[0:1, 0:512], P[0:1, 0:512], AF.Exp,
                                                     scale=-0.5),
                       reads=(), writes=("P",))
                b.emit("pe", _mmg([(PS[6][0:64, 0:512], BONE[0:1, 0:64],
                                    RSS[0:1, :], True, True, (0, 0)),
                                   (PS[6][64:128, 0:512], BONE[0:1, 0:64],
                                    P[0:1, 0:512], True, True, (0, 64))]),
                       reads=("RSS", "P", "BONE"), writes=("ps6",))
                b.emit("act", lambda e: e.copy(MB[:, :], PS[6][:, 0:512]),
                       reads=("ps6",), writes=("MB",))
                b.emit("dve", lambda e, raw=raw: e.tensor_mul(QR[:, :], raw, MB[:, :]),
                       reads=(rawz, "MB"), writes=("QR",))
                tb = 0 if m < 4 else 1024
                if m < 4:
                    dzone = "Q"
                    dst = Qa(m * 512, (m + 1) * 512)
                else:
                    j = m - 4
                    dzone = f"KR{scv}"
                    dst = KR[:, j * 1024 + scv * 512:j * 1024 + scv * 512 + 512]
                b.emit("pe", _mmg([(PS[7][:, 0:512], RROT[:, :], QR[:, :],
                                    True, True, None)]),
                       reads=("RROT", "QR"), writes=("ps7",))
                b.emit("dve", lambda e, tb=tb, d=dst: e.tensor_mul(
                    d, QR[:, :], RT[:, tb:tb + 512]),
                    reads=("QR", "RT"), writes=(dzone,))
                b.emit("dve", lambda e, tb=tb: e.tensor_mul(
                    SQB[:, :], PS[7][:, 0:512], RT[:, tb + 512:tb + 1024]),
                    reads=("ps7", "RT"), writes=("SQB",))
                b.emit("dve", lambda e, d=dst: e.tensor_add(d, d, SQB[:, :]),
                       reads=("SQB",), writes=(dzone,))

            wslot_load(0, wqkvF[l], 0, 8)
            wslot_load(1, wqkvF[l], 1, 8)
            f_qkv_group(0)
            for m in range(1, 12):
                if m + 1 < 12:
                    wslot_load(m + 1, wqkvF[l], m + 1, 8)
                f_qkv_group(m)
                f_qkv_evac(m - 1)
                if m >= 4:
                    f_qk_proc(m - 4)
            f_qkv_evac(11)

            funits = [(hp, qs, h) for hp in range(4) for qs in range(4)
                      for h in range(2)]

            def f_env(u, ui):
                hp, qs, h = u
                jl = [j for j in range(qs, qs + 5) if (not first or j >= 4)]
                return dict(hp=hp, qs=qs, h=h, jl=jl,
                            w0=jl[0] * 128, w1=(jl[-1] + 1) * 128,
                            qcols=hp * 512 + qs * 128,
                            scidx=(hp * 4 + qs) * 2 + h,
                            nbh=4 * (ui % 2),
                            bkA=2 + 2 * (ui % 2), bkB=3 + 2 * (ui % 2))

            def f_part1(u, ui):
                v = f_env(u, ui)
                hp, qs, h = v["hp"], v["qs"], v["h"]
                w0, w1 = v["w0"], v["w1"]
                nbh, bkA, bkB = v["nbh"], v["bkA"], v["bkB"]
                aa1 = min(w1, 512)
                hasA = w0 < aa1
                bb0 = max(w0, 512) - 512
                bb1 = w1 - 512
                if hasA:
                    b.emit("pe", _mmg([(PS[bkA][:, w0:aa1],
                                        Qa(v["qcols"], v["qcols"] + 128, 64 * h, 64 * h + 64),
                                        KR[64 * h:64 * h + 64,
                                           hp * 1024 + spv * 512 + w0:
                                           hp * 1024 + spv * 512 + aa1],
                                        True, True, None)]),
                           reads=("Q", f"KR{spv}"), writes=(psz[bkA],))
                b.emit("pe", _mmg([(PS[bkB][:, bb0:bb1],
                                    Qa(v["qcols"], v["qcols"] + 128, 64 * h, 64 * h + 64),
                                    KR[64 * h:64 * h + 64,
                                       hp * 1024 + scv * 512 + bb0:
                                       hp * 1024 + scv * 512 + bb1],
                                    True, True, None)]),
                       reads=("Q", f"KR{scv}"), writes=(psz[bkB],))
                if hasA:
                    b.emit("dve", lambda e, bk=bkA, qs=qs, w0=w0, aa1=aa1: e.tensor_add(
                        PS[bk][:, w0:aa1], PS[bk][:, w0:aa1],
                        MF[:, qs * 1024 + w0:qs * 1024 + aa1]),
                        reads=("MF",), writes=(psz[bkA],))
                b.emit("dve", lambda e, bk=bkB, qs=qs, bb0=bb0, bb1=bb1: e.tensor_add(
                    PS[bk][:, bb0:bb1], PS[bk][:, bb0:bb1],
                    MF[:, qs * 1024 + 512 + bb0:qs * 1024 + 512 + bb1]),
                    reads=("MF",), writes=(psz[bkB],))
                if hasA:
                    b.emit("dve", lambda e, bk=bkA, w0=w0, aa1=aa1, nbh=nbh: e.reduce_max(
                        NMT[:, nbh:nbh + 1], PS[bk][:, w0:aa1], axis=AX.X, negate=True),
                        reads=(psz[bkA],), writes=("NMT",))
                b.emit("dve", lambda e, bk=bkB, bb0=bb0, bb1=bb1, nbh=nbh: e.reduce_max(
                    NMT[:, nbh + 1:nbh + 2], PS[bk][:, bb0:bb1], axis=AX.X, negate=True),
                    reads=(psz[bkB],), writes=("NMT",))
                if hasA:
                    b.emit("dve", lambda e, nbh=nbh: e.tensor_tensor(
                        NMT[:, nbh + 2:nbh + 3], NMT[:, nbh:nbh + 1],
                        NMT[:, nbh + 1:nbh + 2], ALU.min),
                        reads=(), writes=("NMT",))
                    b.emit("dve", lambda e, nbh=nbh: e.tensor_scalar_mul(
                        NMT[:, nbh + 3:nbh + 4], NMT[:, nbh + 2:nbh + 3], 8.0),
                        reads=(), writes=("NMT",))
                else:
                    b.emit("dve", lambda e, nbh=nbh: e.tensor_scalar_mul(
                        NMT[:, nbh + 3:nbh + 4], NMT[:, nbh + 1:nbh + 2], 8.0),
                        reads=(), writes=("NMT",))

            def f_part2(u, ui):
                v = f_env(u, ui)
                hp, qs, h = v["hp"], v["qs"], v["h"]
                jl, w0, w1 = v["jl"], v["w0"], v["w1"]
                nbh, bkA, bkB = v["nbh"], v["bkA"], v["bkB"]
                scidx = v["scidx"]
                qcols = v["qcols"]
                aa1 = min(w1, 512)
                hasA = w0 < aa1
                bb0 = max(w0, 512) - 512
                bb1 = w1 - 512
                if hasA:
                    b.emit("act", lambda e, bk=bkA, si=scidx, nbh=nbh, w0=w0, aa1=aa1:
                           e.activation(
                               P[:, w0:aa1], PS[bk][:, w0:aa1], AF.Exp,
                               bias=NMT[:, nbh + 3:nbh + 4], scale=8.0,
                               accum_out=SM[:, 2 * si:2 * si + 1]),
                           reads=(psz[bkA], "NMT"), writes=("P", "SM"))
                    b.emit("act", lambda e, bk=bkB, si=scidx, nbh=nbh, bb0=bb0, bb1=bb1:
                           e.activation(
                               P[:, 512 + bb0:512 + bb1], PS[bk][:, bb0:bb1], AF.Exp,
                               bias=NMT[:, nbh + 3:nbh + 4], scale=8.0,
                               accum_out=SM[:, 2 * si + 1:2 * si + 2]),
                           reads=(psz[bkB], "NMT"), writes=("P", "SM"))
                    b.emit("dve", lambda e, si=scidx: e.tensor_add(
                        SM[:, 64 + si:65 + si], SM[:, 2 * si:2 * si + 1],
                        SM[:, 2 * si + 1:2 * si + 2]),
                        reads=(), writes=("SM",))
                else:
                    b.emit("act", lambda e, bk=bkB, si=scidx, nbh=nbh, bb0=bb0, bb1=bb1:
                           e.activation(
                               P[:, 512 + bb0:512 + bb1], PS[bk][:, bb0:bb1], AF.Exp,
                               bias=NMT[:, nbh + 3:nbh + 4], scale=8.0,
                               accum_out=SM[:, 64 + si:65 + si]),
                           reads=(psz[bkB], "NMT"), writes=("P", "SM"))
                last_pair = (h == 1)
                if last_pair:
                    si0 = (hp * 4 + qs) * 2
                    b.emit("pe", lambda e, si=si0: [
                        e.transpose(PS[1][0:1, 0:128], SM[:, 64 + si:65 + si],
                                    IDENT[:, :]),
                        e.transpose(PS[1][0:1, 128:256], SM[:, 65 + si:66 + si],
                                    IDENT[:, :])],
                        reads=("SM", "IDENT"), writes=("ps1",))
                    b.emit("dve", lambda e: e.reciprocal(SMD[0:1, :], PS[1][0:1, 0:256]),
                           reads=("ps1",), writes=("SMD",))
                for j in jl:
                    pb = 6 + j // 4
                    b.emit("pe", lambda e, j=j, pb=pb: e.transpose(
                        PS[pb][:, (j % 4) * 128:(j % 4) * 128 + 128],
                        P[:, j * 128:(j + 1) * 128], IDENT[:, :]),
                        reads=("P", "IDENT"), writes=(psz[pb],))
                if last_pair:
                    b.emit("pe", _mmg([(PS[1][0:64, 0:128], BONE[0:1, 0:64],
                                        SMD[0:1, 0:128], True, True, (0, 0)),
                                       (PS[1][64:128, 0:128], BONE[0:1, 0:64],
                                        SMD[0:1, 128:256], True, True, (0, 64))]),
                           reads=("SMD", "BONE"), writes=("ps1",))
                    b.emit("act", lambda e: e.copy(RBC[:, 0:128], PS[1][:, 0:128]),
                           reads=("ps1",), writes=("RBC",))
                jA = [j for j in jl if j < 4]
                jB = [j for j in jl if j >= 4]
                if jA:
                    b.emit("act", lambda e, j0=jA[0], j1=jA[-1]: e.copy(
                        PTS[:, j0 * 128:(j1 + 1) * 128],
                        PS[6][:, (j0 % 4) * 128:((j1 % 4) + 1) * 128]),
                        reads=("ps6",), writes=("PTS",))
                b.emit("act", lambda e, j0=jB[0], j1=jB[-1]: e.copy(
                    PTS[:, j0 * 128:(j1 + 1) * 128],
                    PS[7][:, (j0 % 4) * 128:((j1 % 4) + 1) * 128]),
                    reads=("ps7",), writes=("PTS",))
                parts = []
                vread = set()
                for jj, j in enumerate(jl):
                    sl = spv if j < 4 else scv
                    ts = j % 4
                    vread.add(f"VT{sl}")
                    base = sl * 2048 + ts * 512 + (2 * hp + h) * 64
                    parts.append((PS[0][64 * h:64 * h + 64, 0:128],
                                  VT[:, base:base + 64],
                                  PTS[:, j * 128:(j + 1) * 128],
                                  jj == 0, jj == len(jl) - 1, (0, 64 * h)))
                b.emit("pe", _mmg(parts), reads=tuple(vread) + ("PTS",),
                       writes=("ps0",))
                if last_pair:
                    b.emit("dve", lambda e, qc=qcols: e.tensor_mul(
                        O[:, qc:qc + 128], PS[0][:, 0:128], RBC[:, 0:128]),
                        reads=("ps0", "RBC"), writes=("O",))

            f_part1(funits[0], 0)
            for ui in range(1, len(funits)):
                f_part1(funits[ui], ui)
                f_part2(funits[ui - 1], ui - 1)
            f_part2(funits[-1], len(funits) - 1)

            def f_wo_group(fo):
                s = fo % 2
                parts = [(PS[s][:, 0:512], WS[s][:, kk * 128:(kk + 1) * 128],
                          O[:, kk * 512:(kk + 1) * 512], kk == 0, kk == 3, None)
                         for kk in range(4)]
                b.emit("pe", _mmg(parts), reads=(f"WS{s}", "O"), writes=(psz[s],))

            def f_wo_evac(fo):
                s = fo % 2
                b.emit("dve", lambda e, fo=fo, s=s: e.tensor_add(
                    X1s(fo * 513 + 1, fo * 513 + 513), PS[s][:, 0:512],
                    XN[:, fo * 513 + 1:fo * 513 + 513]),
                    reads=(psz[s], "XN"), writes=("X1",))

            wslot_load(0, woF[l], 0, 4)
            wslot_load(1, woF[l], 1, 4)
            f_wo_group(0)
            for fo in range(1, 8):
                if fo + 1 < 8:
                    wslot_load(fo + 1, woF[l], fo + 1, 4)
                f_wo_group(fo)
                f_wo_evac(fo - 1)
            f_wo_evac(7)
            for kk in range(8):
                b.emit("dve", lambda e, kk=kk: e.tensor_copy(
                    X1s(kk * 513, kk * 513 + 1), GX1[:, kk:kk + 1]),
                    reads=("GX1",), writes=("X1",))
            for kk in range(8):
                b.emit("dve", lambda e, kk=kk: e.tensor_copy(
                    GX1[:, kk:kk + 1], X1s(kk * 513 + 512, kk * 513 + 513)),
                    reads=("X1",), writes=("GX1",))

            def h2slice(kk):
                base = kk * 513 + (1 if kk < 4 else 0)
                return X1s(base, base + 512)

            emit_ln_stats_and_apply(
                "X1", h2slice, ("XN",),
                dst_slices=lambda kk: XN[:, kk * 513 + 1:kk * 513 + 513],
                affine=(lambda kk: LNG[l][:, kk:kk + 1], lambda kk: LNB[l][:, kk:kk + 1]))

            for ts2 in range(2):
                tw = ts2 * 256

                def w1_ag_group(m, tw=tw):
                    s = (ts2 * 22 + m) % 2
                    sa = m % 2
                    parts = [(PS[sa][:, 0:256], WS[s][:, kk * 128:(kk + 1) * 128],
                              XN[:, kk * 513 + 1 + tw:kk * 513 + 1 + tw + 256],
                              kk == 0, False, None) for kk in range(8)]
                    parts += [(PS[sa][:, 256:512], WS[s][:, (8 + kk) * 128:(9 + kk) * 128],
                               XN[:, kk * 513 + 1 + tw:kk * 513 + 1 + tw + 256],
                               False, kk == 7, None) for kk in range(8)]
                    b.emit("pe", _mmg(parts), reads=(f"WS{s}", "XN"), writes=(psz[sa],))

                def w1_w2_chain(m, tw=tw):
                    s = (ts2 * 22 + m) % 2
                    sa = m % 2
                    mz = f"M{m % 2}"
                    mc = (m % 2) * 256
                    b.emit("act", lambda e, sa=sa, m=m: e.activation(
                        MG[:, :], PS[sa][:, 256:512], AF.Gelu,
                        bias=B1G[l][:, m:m + 1]),
                        reads=(psz[sa], "BI"), writes=("MG",))
                    b.emit("dve", lambda e, sa=sa, m=m, mc=mc: e.scalar_tensor_tensor(
                        Mt[:, mc:mc + 256], PS[sa][:, 0:256], B1A[l][:, m:m + 1],
                        MG[:, :], ALU.add, ALU.mult),
                        reads=(psz[sa], "MG", "BI"), writes=(mz,))
                    parts = [(PS[2 + fo // 2][:, (fo % 2) * 256:(fo % 2) * 256 + 256],
                              W2S[s][:, fo * 128:(fo + 1) * 128],
                              Mt[:, mc:mc + 256], (m == 0 and fo % 2 == 0),
                              (m == 21 and fo % 2 == 1), None)
                             for fo in range(8)]
                    b.emit("pe", _mmg(parts), reads=(f"W2S{s}", mz),
                           writes=("ps2", "ps3", "ps4", "ps5"))

                wfus_load_ag(ts2 * 22, l, 0)
                wfus_load_ag(ts2 * 22 + 1, l, 1)
                wfus_load_w2(ts2 * 22, l, 0)
                wfus_load_w2(ts2 * 22 + 1, l, 1)
                w1_ag_group(0)
                for m in range(1, 22):
                    if m + 1 < 22:
                        wfus_load_ag(ts2 * 22 + m + 1, l, m + 1)
                    w1_ag_group(m)
                    w1_w2_chain(m - 1)
                    if m + 1 < 22:
                        wfus_load_w2(ts2 * 22 + m + 1, l, m + 1)
                w1_w2_chain(21)
                for fo in range(8):
                    w2v = PS[2 + fo // 2][:, (fo % 2) * 256:(fo % 2) * 256 + 256]
                    b.emit("dve", lambda e, fo=fo, w2v=w2v, tw=tw: e.scalar_tensor_tensor(
                        ACC[:, fo * 512 + tw:fo * 512 + tw + 256], w2v,
                        B2[l][:, fo:fo + 1],
                        X1s(fo * 513 + 1 + tw, fo * 513 + 1 + tw + 256),
                        ALU.add, ALU.add),
                        reads=("ps" + str(2 + fo // 2), "X1", "BI"),
                        writes=(f"ACC{fo}",))
                    if dst == "out":
                        b.emit("dve", lambda e, fo=fo, tw=tw: e.tensor_add(
                            ACC[:, fo * 512 + tw:fo * 512 + tw + 256],
                            ACC[:, fo * 512 + tw:fo * 512 + tw + 256],
                            X[:, fo * 512 + tw:fo * 512 + tw + 256]),
                            reads=("X",), writes=(f"ACC{fo}",))
            dstd = y1_s if dst == "y1" else outT
            dzone = "d:y1" if dst == "y1" else "d:out"
            for fo in range(8):
                b.dma(CO, dstd[fo * 128:(fo + 1) * 128, ccol:ccol + 512],
                      ACC[:, fo * 512:(fo + 1) * 512],
                      reads=(f"ACC{fo}",), writes=(dzone,))

    sweep_ffn(0, "x", "y1")
    b.barrier()
    sweep_ffn(1, "y1", "out")

    b.wait_all_chans("sp")
    b.replay()
    nc.finalize()
    return nc



def _tile_w(w, nk, nm):
    K, M = w.shape
    assert K == nk * 128 and M == nm * 128
    t = w.reshape(nk, 128, nm, 128).transpose(2, 1, 0, 3)
    return np.ascontiguousarray(t).reshape(nm, 128 * nk * 128)


def _pad_cols(w, M):
    out = np.zeros((w.shape[0], M), dtype=w.dtype)
    out[:, :w.shape[1]] = w
    return out


def _pad_rows(w, K):
    out = np.zeros((K, w.shape[1]), dtype=w.dtype)
    out[:w.shape[0]] = w
    return out


def stage_inputs(L0_attn_w_qkv, L0_attn_w_out, L0_ffn_wqkv, L0_ffn_q_scale,
                 L0_ffn_k_scale, L0_ffn_wo, L0_ffn_ln_g, L0_ffn_ln_b,
                 L0_ffn_w1, L0_ffn_b1, L0_ffn_w2, L0_ffn_b2):
    f = np.float32
    d = {}
    d["wqkvA"] = _tile_w(np.asarray(L0_attn_w_qkv, f), 8, 24)
    d["woutA"] = _tile_w(np.asarray(L0_attn_w_out, f), 8, 8)
    for l in range(2):
        d[f"wqkvF{l}"] = _tile_w(np.asarray(L0_ffn_wqkv[l], f), 8, 12)
        d[f"woF{l}"] = _tile_w(np.asarray(L0_ffn_wo[l], f), 4, 8)
        w1 = np.asarray(L0_ffn_w1[l], f)
        wa = _pad_cols(w1[:, :2730], 2816).reshape(8, 128, 22, 128).transpose(2, 1, 0, 3)
        wg = _pad_cols(w1[:, 2730:], 2816).reshape(8, 128, 22, 128).transpose(2, 1, 0, 3)
        w2 = _pad_rows(np.asarray(L0_ffn_w2[l], f), 2816).reshape(22, 128, 8, 128)
        d[f"wfus{l}"] = np.ascontiguousarray(
            np.concatenate([wa, wg, w2], axis=2)).reshape(22, 128 * 24 * 128)
        b1 = np.asarray(L0_ffn_b1[l], f)
        d[f"b1a{l}"] = np.ascontiguousarray(
            np.concatenate([b1[:2730], np.zeros(86, f)]).reshape(22, 128).T)
        d[f"b1g{l}"] = np.ascontiguousarray(
            np.concatenate([b1[2730:], np.zeros(86, f)]).reshape(22, 128).T)
        d[f"b2{l}"] = np.ascontiguousarray(np.asarray(L0_ffn_b2[l], f).reshape(8, 128).T)
        d[f"lng{l}"] = np.ascontiguousarray(np.asarray(L0_ffn_ln_g[l], f).reshape(8, 128).T)
        d[f"lnb{l}"] = np.ascontiguousarray(np.asarray(L0_ffn_ln_b[l], f).reshape(8, 128).T)
        qs = np.asarray(L0_ffn_q_scale[l], f)
        ks = np.asarray(L0_ffn_k_scale[l], f)
        d[f"qsc{l}"] = np.ascontiguousarray(
            np.stack([np.tile(qs, 2), np.tile(np.roll(qs, -32), 2)], axis=1))
        d[f"ksc{l}"] = np.ascontiguousarray(
            np.stack([np.tile(ks, 2), np.tile(np.roll(ks, -32), 2)], axis=1))
    return d


def kernel(L0_attn_x, L0_attn_w_qkv, L0_attn_w_out, L0_ffn_wqkv, L0_ffn_q_scale,
           L0_ffn_k_scale, L0_ffn_wo, L0_ffn_ln_g, L0_ffn_ln_b, L0_ffn_w1,
           L0_ffn_b1, L0_ffn_w2, L0_ffn_b2):
    x = np.asarray(L0_attn_x, np.float32)
    B, NT, D = x.shape
    nc = build_core_program(NT)
    wd = stage_inputs(L0_attn_w_qkv, L0_attn_w_out, L0_ffn_wqkv, L0_ffn_q_scale,
                      L0_ffn_k_scale, L0_ffn_wo, L0_ffn_ln_g, L0_ffn_ln_b,
                      L0_ffn_w1, L0_ffn_b1, L0_ffn_w2, L0_ffn_b2)
    in_maps = []
    for bb in range(B):
        m = {"xT": np.ascontiguousarray(x[bb].T)}
        m.update(wd)
        in_maps.append(m)
    res = run_bass_kernel_spmd(nc, in_maps, core_ids=list(range(B)))
    out = np.stack([np.asarray(r["outT"]).T for r in res.results], axis=0)
    return out.astype(np.float32)


# version 14
# speedup vs baseline: 1.4968x; 1.0271x over previous
import math

import numpy as np

import concourse.bass as bass
import concourse.mybir as mybir
from concourse.bass_utils import run_bass_kernel_spmd
import concourse.library_config as libcfg

F32 = mybir.dt.float32
BF16 = mybir.dt.bfloat16
AF = mybir.ActivationFunctionType
ALU = mybir.AluOpType
AX = mybir.AxisListType

T = 512
NEG = -1.0e30
PI = math.pi
LN1E4 = math.log(10000.0)

ENGS = ("pe", "dve", "act", "pool", "sp")
_BLOCKSECT = {"pe": "tensor", "dve": "vector", "act": "scalar",
              "pool": "gpsimd", "sp": "sync"}

PROXY_LIB = [l for l in libcfg.all_libraries if l.name == "proxy"][0]


class Chan:
    __slots__ = ("name", "sem", "issued")

    def __init__(self, name, sem):
        self.name = name
        self.sem = sem
        self.issued = 0


class Bld:

    def __init__(self, nc):
        self.nc = nc
        self.ops = {e: [] for e in ENGS}
        self.sem = {e: nc.alloc_semaphore("tk_" + e) for e in ENGS}
        self.n = {e: 0 for e in ENGS}
        self.wm = {e: {} for e in ENGS}
        self.zw = {}
        self.zr = {}
        self.chans = []

    def chan(self, name):
        c = Chan(name, self.nc.alloc_semaphore("ch_" + name))
        self.chans.append(c)
        return c

    def _wait(self, e, key, val):
        if isinstance(key, Chan):
            val = 16 * key.issued
        if val <= 0:
            return
        if self.wm[e].get(key, 0) >= val:
            return
        self.wm[e][key] = val
        semh = key.sem if isinstance(key, Chan) else self.sem[key]
        self.ops[e].append(("w", semh, val))

    def _dep(self, e, zones, war):
        for z in zones:
            lw = self.zw.get(z)
            if lw is not None:
                self._wait(e, lw[0], lw[1])
            if war:
                for k, v in self.zr.get(z, {}).items():
                    self._wait(e, k, v)

    def emit(self, e, fn, reads=(), writes=()):
        self._dep(e, reads, False)
        self._dep(e, writes, True)
        self.ops[e].append(("i", fn, self.sem[e]))
        self.n[e] += 1
        t = self.n[e]
        for z in reads:
            self.zr.setdefault(z, {})[e] = t
        for z in writes:
            self.zw[z] = (e, t)
            self.zr[z] = {}

    def dma(self, ch, out_ap, in_ap, reads=(), writes=(), eng="sp", accum=None):
        self._dep(eng, reads, False)
        self._dep(eng, writes, True)
        self.ops[eng].append(("d", out_ap, in_ap, ch.sem, accum))
        ch.issued += 1
        for z in reads:
            self.zr.setdefault(z, {})[ch] = 0
        for z in writes:
            self.zw[z] = (ch, 0)
            self.zr[z] = {}

    def barrier(self):
        for e1 in ENGS:
            for e2 in ENGS:
                if e1 != e2:
                    self._wait(e1, e2, self.n[e2])

    def wait_all_chans(self, e="sp"):
        for c in self.chans:
            self._wait(e, c, 1)

    def replay(self):
        with self.nc.Block() as block:
            for e in ENGS:
                ops = self.ops[e]

                def body(eng, ops=ops):
                    for op in ops:
                        if op[0] == "w":
                            eng.wait_ge(op[1], op[2])
                        elif op[0] == "i":
                            r = op[1](eng)
                            if isinstance(r, list):
                                r = r[-1]
                            r.then_inc(op[2], 1)
                        else:
                            if op[4] is None:
                                eng.dma_start(op[1], op[2]).then_inc(op[3], 16)
                            else:
                                eng.dma_start(op[1], op[2],
                                              accum_op=op[4]).then_inc(op[3], 16)
                getattr(block, _BLOCKSECT[e])(body)


def _mmg(parts):
    def fn(eng):
        last = None
        for (o, l, r, s, st, tp) in parts:
            if tp is None:
                last = eng.matmul(o, l, r, start=s, stop=st)
            else:
                last = eng.matmul(o, l, r, start=s, stop=st, tile_position=tp)
        return last
    return fn


def build_core_program(NT=4096):
    NC = NT // T
    nc = bass.Bass("TRN2", target_bir_lowering=False)

    xT = nc.dram_tensor("xT", [1024, NT], F32, kind="ExternalInput").ap()
    wqkvA = nc.dram_tensor("wqkvA", [24, 128 * 8 * 128], F32, kind="ExternalInput").ap()
    woutA = nc.dram_tensor("woutA", [8, 128 * 8 * 128], F32, kind="ExternalInput").ap()
    wqkvF = [nc.dram_tensor(f"wqkvF{l}", [12, 128 * 8 * 128], F32, kind="ExternalInput").ap() for l in range(2)]
    woF = [nc.dram_tensor(f"woF{l}", [8, 128 * 4 * 128], F32, kind="ExternalInput").ap() for l in range(2)]
    wfus_h = [nc.dram_tensor(f"wfus{l}", [22, 128 * 24 * 128], F32, kind="ExternalInput") for l in range(2)]
    b1a_d = [nc.dram_tensor(f"b1a{l}", [128, 22], F32, kind="ExternalInput").ap() for l in range(2)]
    b1g_d = [nc.dram_tensor(f"b1g{l}", [128, 22], F32, kind="ExternalInput").ap() for l in range(2)]
    b2_d = [nc.dram_tensor(f"b2{l}", [128, 8], F32, kind="ExternalInput").ap() for l in range(2)]
    lng_d = [nc.dram_tensor(f"lng{l}", [128, 8], F32, kind="ExternalInput").ap() for l in range(2)]
    lnb_d = [nc.dram_tensor(f"lnb{l}", [128, 8], F32, kind="ExternalInput").ap() for l in range(2)]
    qsc_d = [nc.dram_tensor(f"qsc{l}", [128, 2], F32, kind="ExternalInput").ap() for l in range(2)]
    ksc_d = [nc.dram_tensor(f"ksc{l}", [128, 2], F32, kind="ExternalInput").ap() for l in range(2)]
    outT = nc.dram_tensor("outT", [1024, NT], F32, kind="ExternalOutput").ap()
    acc_s = nc.dram_tensor("acc_s", [1024, NT], F32, kind="Internal").ap()
    y1_s = nc.dram_tensor("y1_s", [1024, NT], F32, kind="Internal").ap()
    xn_s = nc.dram_tensor("xn_s", [1024, NT], F32, kind="Internal").ap()
    rott = [nc.dram_tensor(f"rott{l}", [4 * 128, NT], F32, kind="Internal").ap() for l in range(2)]

    A = nc.alloc_sbuf_tensor
    XN = A("XN", [128, 8 * 513], F32)
    X = A("Xt", [128, 8 * 513], F32)
    BIGQ = A("BIGQ", [128, 6152], F32)
    X1OFF = 2048
    ACC = A("ACCt", [128, 8 * 512], F32)
    O = A("Ot", [128, 8 * 512], F32)
    KR = A("KRt", [128, 6144], F32)
    VT = A("VTt", [128, 6144], F32)
    WS = [A("WS0", [128, 16 * 128], F32), A("WS1", [128, 16 * 128], F32)]
    W2S = [A("W2S0", [128, 8 * 128], F32), A("W2S1", [128, 8 * 128], F32)]
    P = A("Pt", [128, 1024], F32)
    PTS = A("PTSt", [128, 1024], F32)
    Mt = A("Mt", [128, 512], F32)
    MG = A("MGt", [128, 256], F32)
    QR = A("QRt", [128, 512], F32)
    VS = A("VSt", [128, 512], F32)
    SQA = A("SQAt", [128, 512], F32)
    SQB = A("SQBt", [128, 512], F32)
    MB = A("MBt", [128, 512], F32)
    RB = A("RBt", [128, 512], F32)
    RT = A("RTt", [128, 2048], F32)
    MA = A("MAt", [128, 768], BF16)
    MF = A("MFt", [128, 4096], BF16)
    SM = A("SMt", [128, 96], F32)
    NMT = A("NMTt", [128, 16], F32)
    SMD = A("SMDt", [1, 256], F32)
    RBC = A("RBCt", [128, 256], F32)
    IDENT = A("IDENTt", [128, 128], F32)
    ONES = A("ONESt", [128, 1], F32)
    ONES2 = A("ONES2t", [128, 2], F32)
    GXN = A("GXNt", [128, 8], F32)
    GX1 = A("GX1t", [128, 8], F32)
    RSS = A("RSSt", [1, 512], F32)
    RROT = A("RROTt", [128, 128], F32)
    BONE = A("BONEt", [1, 128], F32)
    CEPS = A("CEPSt", [128, 1], F32)
    C15 = A("C15t", [128, 1], F32)
    IOD = A("IODt", [128, 4], F32)
    B1A = [A(f"B1A{l}", [128, 22], F32) for l in range(2)]
    B1G = [A(f"B1G{l}", [128, 22], F32) for l in range(2)]
    B2 = [A(f"B2{l}", [128, 8], F32) for l in range(2)]
    LNG = [A(f"LNG{l}", [128, 8], F32) for l in range(2)]
    LNB = [A(f"LNB{l}", [128, 8], F32) for l in range(2)]
    QSC = [A(f"QSC{l}", [128, 2], F32) for l in range(2)]
    KSC = [A(f"KSC{l}", [128, 2], F32) for l in range(2)]

    def Qa(a, b_, p0=0, p1=128):
        return BIGQ[p0:p1, a:b_]

    def X1s(a, b_):
        return BIGQ[:, X1OFF + a:X1OFF + b_]

    PS = [nc.alloc_psum_tensor(f"PS{i}", [128, 512], F32) for i in range(8)]

    b = Bld(nc)
    CX = b.chan("cx")
    CW = [b.chan("cw0"), b.chan("cw1")]
    CW2 = [b.chan("cw2a"), b.chan("cw2b")]
    CRT = b.chan("crt")
    CB = b.chan("cb")
    CO = b.chan("co")
    CTW = b.chan("ctw")
    CPY = b.chan("cpy")
    CACC = b.chan("cacc")

    psz = [f"ps{i}" for i in range(8)]

    b.emit("dve", lambda e: e.memset(BONE[:, :], 1.0), writes=("BONE",))

    b.emit("dve", lambda e: e.memset(IDENT[:, :], 1.0), writes=("IDENT",))
    b.emit("pool", lambda e: e.affine_select(IDENT[:, :], IDENT[:, :], [[1, 128]],
                                             ALU.is_equal, 0.0, base=0,
                                             channel_multiplier=-1),
           reads=("lib",), writes=("IDENT",))
    b.emit("dve", lambda e: e.memset(ONES[:, :], 1.0), writes=("ONES",))
    b.emit("dve", lambda e: e.memset(CEPS[:, :], 1e-5), writes=("CEPS",))
    b.emit("dve", lambda e: e.memset(C15[:, :], -15.0), writes=("C15",))
    b.emit("dve", lambda e: e.memset(ONES2[:, :], 1.0), writes=("ONES2",))
    b.emit("pool", lambda e: e.affine_select(ONES2[:, 0:1], ONES2[:, 0:1], [[1, 1]],
                                             ALU.is_ge, 0.0, base=63,
                                             channel_multiplier=-1),
           reads=("lib",), writes=("ONES2",))
    b.emit("pool", lambda e: e.affine_select(ONES2[:, 1:2], ONES2[:, 1:2], [[1, 1]],
                                             ALU.is_ge, 0.0, base=-64,
                                             channel_multiplier=1),
           reads=("lib",), writes=("ONES2",))
    for (c0, sgn, kb) in ((0, -1.0, -32), (32, 1.0, 0), (64, -1.0, -96), (96, 1.0, -64)):
        sl = RROT[:, c0:c0 + 32]
        b.emit("dve", lambda e, sl=sl, sgn=sgn: e.memset(sl, sgn), writes=("RROT",))
        b.emit("pool", lambda e, sl=sl, kb=kb: e.affine_select(
            sl, sl, [[-1, 32]], ALU.is_equal, 0.0, base=kb,
            channel_multiplier=1), reads=("lib",), writes=("RROT",))
    b.emit("dve", lambda e: e.memset(MA[:, :], 0.0), writes=("MA",))
    b.emit("dve", lambda e: e.memset(MF[:, :], 0.0), writes=("MF",))
    A1_MCOL = {(0, 0): 0, (1, 0): 128, (1, 1): 256, (2, 0): 384,
               (2, 1): 512, (3, 1): 640}
    for (j, qs), c0 in A1_MCOL.items():
        sl = MA[:, c0:c0 + 128]
        dj = (j - qs) * 128
        b.emit("pool", lambda e, sl=sl, dj=dj: e.affine_select(
            sl, sl, [[-1, 128]], ALU.is_ge, NEG, base=dj - 1,
            channel_multiplier=1), reads=("lib",), writes=("MA",))
        b.emit("pool", lambda e, sl=sl, dj=dj: e.affine_select(
            sl, sl, [[1, 128]], ALU.is_ge, NEG, base=256 - dj,
            channel_multiplier=-1), reads=("lib",), writes=("MA",))
    for qs in range(4):
        sl = MF[:, qs * 1024:(qs + 1) * 1024]
        b.emit("pool", lambda e, sl=sl, qs=qs: e.affine_select(
            sl, sl, [[1, 1024]], ALU.is_ge, NEG, base=-(128 * qs),
            channel_multiplier=-1), reads=("lib",), writes=("MF",))
        b.emit("pool", lambda e, sl=sl, qs=qs: e.affine_select(
            sl, sl, [[-1, 1024]], ALU.is_ge, NEG, base=(128 * qs) + 512,
            channel_multiplier=1), reads=("lib",), writes=("MF",))
    for l in range(2):
        b.dma(CB, B1A[l][:, :], b1a_d[l], writes=("BI",))
        b.dma(CB, B1G[l][:, :], b1g_d[l], writes=("BI",))
        b.dma(CB, B2[l][:, :], b2_d[l], writes=("BI",))
        b.dma(CB, LNG[l][:, :], lng_d[l], writes=("BI",))
        b.dma(CB, LNB[l][:, :], lnb_d[l], writes=("BI",))
        b.dma(CB, QSC[l][:, :], qsc_d[l], writes=("BI",))
        b.dma(CB, KSC[l][:, :], ksc_d[l], writes=("BI",))

    for qtr in range(4):
        b.emit("pool", lambda e, q=qtr: e.iota(
            IOD[q * 32:(q + 1) * 32, 0:1], [[1, 1]], base=0, channel_multiplier=1,
            allow_small_or_imprecise_dtypes=True), reads=("lib",), writes=("IOD",))
    b.emit("act", lambda e: e.activation(IOD[:, 1:2], IOD[:, 0:1], AF.Exp,
                                         scale=-2.0 * LN1E4 / 64.0),
           reads=("IOD",), writes=("IOD",))
    b.emit("dve", lambda e: e.tensor_scalar(IOD[:, 2:3], IOD[:, 0:1],
                                            2.0 / 89.6, 25.6 / 89.6,
                                            ALU.mult, ALU.add),
           reads=("IOD",), writes=("IOD",))
    b.emit("act", lambda e: e.activation(IOD[:, 3:4], IOD[:, 2:3], AF.Ln),
           reads=("IOD",), writes=("IOD",))
    for l in range(2):
        for cb in range(NC):
            t0 = cb * 512
            b.emit("pool", lambda e, t0=t0: e.iota(
                MB[:, :], [[1, 512]], base=t0, channel_multiplier=0,
                allow_small_or_imprecise_dtypes=True),
                reads=("lib",), writes=("MB",))
            b.emit("dve", lambda e: e.tensor_scalar_mul(RB[:, :], MB[:, :], IOD[:, 1:2]),
                   reads=("MB", "IOD"), writes=("RB",))
            b.emit("dve", lambda e: e.tensor_scalar_add(P[:, 0:512], RB[:, :], PI),
                   reads=("RB",), writes=("P",))
            for j in range(9, -1, -1):
                cj = 2.0 * PI * (1 << j)
                b.emit("dve", lambda e, cj=cj: e.tensor_single_scalar(
                    VS[:, :], P[:, 0:512], cj, ALU.is_ge),
                    reads=("P",), writes=("VS",))
                b.emit("dve", lambda e, cj=cj: e.scalar_tensor_tensor(
                    P[:, 0:512], VS[:, :], -cj, P[:, 0:512], ALU.mult, ALU.add),
                    reads=("VS",), writes=("P",))
            b.emit("dve", lambda e: e.tensor_scalar_add(P[:, 0:512], P[:, 0:512], -PI),
                   reads=(), writes=("P",))
            b.emit("dve", lambda e: e.tensor_scalar_add(P[:, 512:1024], P[:, 0:512], PI / 2),
                   reads=(), writes=("P",))
            b.emit("dve", lambda e: e.tensor_single_scalar(
                VS[:, :], P[:, 512:1024], PI, ALU.is_ge),
                reads=("P",), writes=("VS",))
            b.emit("dve", lambda e: e.scalar_tensor_tensor(
                P[:, 512:1024], VS[:, :], -2.0 * PI, P[:, 512:1024],
                ALU.mult, ALU.add),
                reads=("VS",), writes=("P",))
            b.emit("act", lambda e: e.activation(SQB[:, :], P[:, 0:512], AF.Sin),
                   reads=("P",), writes=("SQB",))
            b.emit("act", lambda e: e.activation(VS[:, :], P[:, 512:1024], AF.Sin),
                   reads=("P",), writes=("VS",))
            b.emit("pool", lambda e, t0=t0: e.iota(
                QR[:, :], [[1, 512]], base=t0 - 2048, channel_multiplier=0,
                allow_small_or_imprecise_dtypes=True),
                reads=("lib",), writes=("QR",))
            b.emit("dve", lambda e: e.tensor_scalar_mul(QR[:, :], QR[:, :], IOD[:, 3:4]),
                   reads=("IOD",), writes=("QR",))
            b.emit("act", lambda e: e.activation(MB[:, :], QR[:, :], AF.Exp,
                                                 scale=1.0 / 256.0),
                   reads=("QR",), writes=("MB",))
            b.emit("act", lambda e: e.activation(RB[:, :], QR[:, :], AF.Exp,
                                                 scale=-1.0 / 256.0),
                   reads=("QR",), writes=("RB",))
            tabs = [(0, VS, MB, QSC[l][:, 0:1]),
                    (512, SQB, MB, QSC[l][:, 1:2]),
                    (1024, VS, RB, KSC[l][:, 0:1]),
                    (1536, SQB, RB, KSC[l][:, 1:2])]
            for (off, t1, t2, sc) in tabs:
                b.emit("dve", lambda e, off=off, t1=t1, t2=t2: e.tensor_mul(
                    RT[:, off:off + 512], t1[:, :], t2[:, :]),
                    reads=("VS", "SQB", "MB", "RB"), writes=("RT",))
                b.emit("dve", lambda e, off=off, sc=sc: e.tensor_scalar_mul(
                    RT[:, off:off + 512], RT[:, off:off + 512], sc),
                    reads=("BI",), writes=("RT",))
            for i in range(4):
                b.dma(CTW, rott[l][i * 128:(i + 1) * 128, t0:t0 + 512],
                      RT[:, i * 512:(i + 1) * 512],
                      reads=("RT",), writes=(f"d:rott{l}",))

    b.barrier()


    def emit_ln_stats_and_apply(src_tile, src_cols, dst_writes, dst_slices,
                                affine=None):
        for kk in range(8):
            sq = SQA if kk % 2 == 0 else SQB
            sqz = "SQA" if kk % 2 == 0 else "SQB"
            src = src_cols(kk)
            b.emit("act", lambda e, sq=sq, src=src: e.activation(sq[:, :], src, AF.Square),
                   reads=(src_tile,), writes=(sqz,))
            b.emit("pe", _mmg([(PS[6][0:1, 0:512], ONES[:, 0:1], src,
                                kk == 0, kk == 7, None)]),
                   reads=(src_tile, "ONES"), writes=("ps6",))
            b.emit("pe", _mmg([(PS[7][0:1, 0:512], ONES[:, 0:1], sq[:, :],
                                kk == 0, kk == 7, None)]),
                   reads=(sqz, "ONES"), writes=("ps7",))
        b.emit("dve", lambda e: e.tensor_scalar_mul(MB[0:1, :], PS[6][0:1, 0:512], 1.0 / 1024.0),
               reads=("ps6",), writes=("MB",))
        b.emit("dve", lambda e: e.tensor_scalar_mul(RB[0:1, :], PS[7][0:1, 0:512], 1.0 / 1024.0),
               reads=("ps7",), writes=("RB",))
        b.emit("act", lambda e: e.activation(RSS[0:1, :], MB[0:1, :], AF.Square),
               reads=("MB",), writes=("RSS",))
        b.emit("dve", lambda e: e.tensor_sub(RB[0:1, :], RB[0:1, :], RSS[0:1, :]),
               reads=("RSS",), writes=("RB",))
        b.emit("act", lambda e: e.activation(RSS[0:1, :], RB[0:1, :], AF.Ln,
                                             bias=CEPS[0:1, 0:1]),
               reads=("RB", "CEPS"), writes=("RSS",))
        b.emit("act", lambda e: e.activation(RB[0:1, :], RSS[0:1, :], AF.Exp, scale=-0.5),
               reads=("RSS",), writes=("RB",))
        b.emit("pe", _mmg([(PS[6][:, 0:512], BONE[0:1, 0:128], MB[0:1, :],
                            True, True, None)]),
               reads=("MB", "BONE"), writes=("ps6",))
        b.emit("pe", _mmg([(PS[7][:, 0:512], BONE[0:1, 0:128], RB[0:1, :],
                            True, True, None)]),
               reads=("RB", "BONE"), writes=("ps7",))
        for kk in range(8):
            src = src_cols(kk)
            dst = dst_slices(kk)
            b.emit("dve", lambda e, d=dst, s=src: e.tensor_sub(d, s, PS[6][:, 0:512]),
                   reads=(src_tile, "ps6"), writes=dst_writes)
            b.emit("dve", lambda e, d=dst: e.tensor_mul(d, d, PS[7][:, 0:512]),
                   reads=("ps7",), writes=dst_writes)
            if affine is not None:
                g_ap, bt_ap = affine
                b.emit("dve", lambda e, d=dst, g=g_ap(kk), bb=bt_ap(kk): e.tensor_scalar(
                    d, d, g, bb, ALU.mult, ALU.add),
                    reads=("BI",), writes=dst_writes)

    def wslot_load(widx, dram_ap, j, nblk):
        s = widx % 2
        b.dma(CW[s], WS[s][:, 0:nblk * 128], dram_ap[j:j + 1, :],
              writes=(f"WS{s}",))
        return s

    def wfus_load_ag(widx, l, j):
        s = widx % 2
        src_ap = bass.AP(wfus_h[l], j * (128 * 24 * 128),
                         [[24 * 128, 128], [1, 16 * 128]])
        b.dma(CW[s], WS[s][:, 0:16 * 128], src_ap, writes=(f"WS{s}",))

    def wfus_load_w2(widx, l, j):
        s = widx % 2
        src_ap = bass.AP(wfus_h[l], j * (128 * 24 * 128) + 16 * 128,
                         [[24 * 128, 128], [1, 8 * 128]])
        b.dma(CW2[s], W2S[s][:, 0:8 * 128], src_ap, writes=(f"W2S{s}",))

    for c in range(NC):
        ccol = c * 512
        b.dma(CPY, acc_s[:, ccol:ccol + 512], xT[:, ccol:ccol + 512],
              writes=("d:acc",))
        for kk in range(8):
            b.dma(CX, X[:, kk * 512:(kk + 1) * 512],
                  xT[kk * 128:(kk + 1) * 128, ccol:ccol + 512], writes=("X",))
        emit_ln_stats_and_apply(
            "X", lambda kk: X[:, kk * 512:(kk + 1) * 512], ("XN",),
            dst_slices=lambda kk: XN[:, kk * 513 + 1:kk * 513 + 513])
        for kk in range(8):
            b.dma(CO, xn_s[kk * 128:(kk + 1) * 128, ccol:ccol + 512],
                  XN[:, kk * 513 + 1:kk * 513 + 513],
                  reads=("XN",), writes=("d:xn",))

        s0 = (2 * c) % 3
        s1 = (2 * c + 1) % 3

        def a1_qkv_group(m):
            s = m % 2
            parts = [(PS[s][:, 0:512], WS[s][:, kk * 128:(kk + 1) * 128],
                      XN[:, kk * 513 + 1:kk * 513 + 513], kk == 0, kk == 7, None)
                     for kk in range(8)]
            b.emit("pe", _mmg(parts), reads=(f"WS{s}", "XN"), writes=(psz[s],))

        def a1_qkv_evac(m):
            s = m % 2
            if m < 8:
                b.emit("act", lambda e, m=m, s=s: e.copy(Qa(m * 512, (m + 1) * 512),
                                                         PS[s][:, 0:512]),
                       reads=(psz[s],), writes=("Q",))
            elif m < 16:
                j = m - 8
                d0 = KR[:, j * 768 + s0 * 256:j * 768 + s0 * 256 + 256]
                d1 = KR[:, j * 768 + s1 * 256:j * 768 + s1 * 256 + 256]
                b.emit("act", lambda e, d=d0, s=s: e.copy(d, PS[s][:, 0:256]),
                       reads=(psz[s],), writes=(f"KR{s0}",))
                b.emit("act", lambda e, d=d1, s=s: e.copy(d, PS[s][:, 256:512]),
                       reads=(psz[s],), writes=(f"KR{s1}",))
            else:
                j = m - 16
                b.emit("act", lambda e, s=s: e.copy(VS[:, :], PS[s][:, 0:512]),
                       reads=(psz[s],), writes=("VS",))
                for u in range(4):
                    b.emit("pe", lambda e, u=u: e.transpose(
                        PS[2][:, u * 128:(u + 1) * 128],
                        VS[:, u * 128:(u + 1) * 128], IDENT[:, :]),
                        reads=("VS", "IDENT"), writes=("ps2",))
                for half in range(2):
                    sl = s0 if half == 0 else s1
                    src = PS[2][:, half * 256:half * 256 + 256]
                    dst = bass.AP(VT, sl * 2048 + j * 128,
                                  [[6144, 128], [1024, 2], [1, 128]])
                    b.emit("act", lambda e, d=dst, sp=src: e.copy(d, sp),
                           reads=("ps2",), writes=(f"VT{sl}",))

        wslot_load(0, wqkvA, 0, 8)
        wslot_load(1, wqkvA, 1, 8)
        a1_qkv_group(0)
        for m in range(1, 24):
            if m + 1 < 24:
                wslot_load(m + 1, wqkvA, m + 1, 8)
            a1_qkv_group(m)
            a1_qkv_evac(m - 1)
        a1_qkv_evac(23)

        aunits = [(b_loc, hp, h) for b_loc in range(2)
                  for hp in range(8) for h in range(2)]
        A1_SBANK = ((3, 4), (0, 1), (5, 7))

        A1_JPACK = {0: (0, 0, 0, 1), 1: (0, 128, 0, 2),
                    2: (1, 0, 0, 2), 3: (1, 256, 1, 1)}
        A1_MOFF = {0: 0, 1: 128, 2: 384, 3: 640}

        def a1_env(u, ui):
            b_loc, hp, h = u
            blk = 2 * c + b_loc
            first = (blk == 0)
            jl = [2, 3] if first else [0, 1, 2, 3]
            return dict(b_loc=b_loc, hp=hp, h=h,
                        spv=(blk - 1) % 3, scv=blk % 3,
                        first=first, jl=jl,
                        qb=hp * 512 + b_loc * 256,
                        sb=A1_SBANK[ui % 3])

        def a1_part1(u, ui):
            v = a1_env(u, ui)
            hp, h = v["hp"], v["h"]
            spv, scv, first, jl = v["spv"], v["scv"], v["first"], v["jl"]
            qb, sb = v["qb"], v["sb"]
            parts = [[], []]
            for j in jl:
                bi, c0, qs0, nq = A1_JPACK[j]
                sl = spv if j < 2 else scv
                kc = hp * 768 + sl * 256 + (j % 2) * 128
                parts[bi].append((PS[sb[bi]][:, c0:c0 + nq * 128],
                                  KR[64 * h:64 * h + 64, kc:kc + 128],
                                  Qa(qb + qs0 * 128, qb + qs0 * 128 + nq * 128,
                                     64 * h, 64 * h + 64),
                                  len(parts[bi]) == 0, False, None))
            allp = []
            for bi in range(2):
                if parts[bi]:
                    parts[bi][-1] = parts[bi][-1][:4] + (True, None)
                    allp += parts[bi]
            rd = ("Q", f"KR{scv}") if first else ("Q", f"KR{spv}", f"KR{scv}")
            wz = (psz[sb[1]],) if first else (psz[sb[0]], psz[sb[1]])
            b.emit("pe", _mmg(allp), reads=rd, writes=wz)
            if not first:
                b.emit("dve", lambda e, bk=sb[0]: e.tensor_add(
                    PS[bk][:, 0:384], PS[bk][:, 0:384], MA[:, 0:384]),
                    reads=("MA",), writes=(psz[sb[0]],))
                b.emit("dve", lambda e, bk=sb[1]: e.tensor_add(
                    PS[bk][:, 0:384], PS[bk][:, 0:384], MA[:, 384:768]),
                    reads=("MA",), writes=(psz[sb[1]],))
            else:
                b.emit("dve", lambda e, bk=sb[1]: e.tensor_add(
                    PS[bk][:, 0:384], PS[bk][:, 0:384], MA[:, 384:768]),
                    reads=("MA",), writes=(psz[sb[1]],))

        def a1_part2(u, ui):
            v = a1_env(u, ui)
            hp, h = v["hp"], v["h"]
            spv, scv, first, jl = v["spv"], v["scv"], v["first"], v["jl"]
            qb, sb = v["qb"], v["sb"]
            PT_ = PTS if ui % 2 == 0 else P
            ptz = "PTS" if ui % 2 == 0 else "P"
            dbk = 6
            if not first:
                b.emit("act", lambda e, bk=sb[0], PT_=PT_: e.activation(
                    PT_[:, 0:384], PS[bk][:, 0:384], AF.Exp,
                    bias=C15[:, 0:1], scale=0.125),
                    reads=(psz[sb[0]], "C15"), writes=(ptz,))
            b.emit("act", lambda e, bk=sb[1], PT_=PT_: e.activation(
                PT_[:, 384:768], PS[bk][:, 0:384], AF.Exp,
                bias=C15[:, 0:1], scale=0.125),
                reads=(psz[sb[1]], "C15"), writes=(ptz,))
            jord = [j for j in (1, 2, 0, 3) if j in jl]
            dparts = []
            for j in jord:
                bi, c0, qs0, nq = A1_JPACK[j]
                pc = bi * 384 + c0
                dparts.append((PS[dbk][0:1, qs0 * 128:qs0 * 128 + nq * 128],
                               ONES[:, 0:1], PT_[:, pc:pc + nq * 128],
                               len(dparts) == 0, j == jord[-1], None))
            b.emit("pe", _mmg(dparts), reads=(ptz, "ONES"), writes=(psz[dbk],))
            b.emit("dve", lambda e, dbk=dbk: e.reciprocal(SMD[0:1, 0:256],
                                                          PS[dbk][0:1, 0:256]),
                   reads=(psz[dbk],), writes=("SMD",))
            vparts = []
            vread = set()
            for j in jord:
                bi, c0, qs0, nq = A1_JPACK[j]
                sl = spv if j < 2 else scv
                ts = j % 2
                vread.add(f"VT{sl}")
                base = sl * 2048 + ts * 1024 + (2 * hp + h) * 64
                pc = bi * 384 + c0
                vparts.append((PS[2][64 * h:64 * h + 64,
                                     qs0 * 128:qs0 * 128 + nq * 128],
                               VT[:, base:base + 64],
                               PT_[:, pc:pc + nq * 128],
                               len(vparts) == 0, j == jord[-1], (0, 64 * h)))
            b.emit("pe", _mmg(vparts), reads=tuple(vread) + (ptz,),
                   writes=("ps2",))
            b.emit("pe", _mmg([(PS[dbk][64 * h:64 * h + 64, 0:256], BONE[0:1, 0:64],
                                SMD[0:1, 0:256], True, True, (0, 64 * h))]),
                   reads=("SMD", "BONE"), writes=(psz[dbk],))
            b.emit("act", lambda e, h=h, dbk=dbk: e.copy(
                RBC[64 * h:64 * h + 64, 0:256],
                PS[dbk][64 * h:64 * h + 64, 0:256]),
                reads=(psz[dbk],), writes=("RBC",))
            b.emit("dve", lambda e, h=h, qb=qb: e.tensor_mul(
                O[64 * h:64 * h + 64, qb:qb + 256],
                PS[2][64 * h:64 * h + 64, 0:256],
                RBC[64 * h:64 * h + 64, 0:256]),
                reads=("ps2", "RBC"), writes=("O",))

        a1_part1(aunits[0], 0)
        a1_part1(aunits[1], 1)
        for ui in range(2, len(aunits)):
            a1_part1(aunits[ui], ui)
            a1_part2(aunits[ui - 2], ui - 2)
        a1_part2(aunits[-2], len(aunits) - 2)
        a1_part2(aunits[-1], len(aunits) - 1)

        def a1_wo_group(fo):
            s = fo % 2
            parts = [(PS[s][:, 0:512], WS[s][:, kk * 128:(kk + 1) * 128],
                      O[:, kk * 512:(kk + 1) * 512], kk == 0, kk == 7, None)
                     for kk in range(8)]
            b.emit("pe", _mmg(parts), reads=(f"WS{s}", "O"), writes=(psz[s],))

        def a1_wo_evac(fo):
            s = fo % 2
            b.emit("act", lambda e, fo=fo, s=s: e.copy(
                ACC[:, fo * 512:(fo + 1) * 512], PS[s][:, 0:512]),
                reads=(psz[s],), writes=(f"ACC{fo}",))
            b.dma(CACC, acc_s[fo * 128:(fo + 1) * 128, ccol:ccol + 512],
                  ACC[:, fo * 512:(fo + 1) * 512],
                  reads=(f"ACC{fo}", "d:acc"), writes=("d:acc",),
                  eng="pool", accum=ALU.add)

        wslot_load(0, woutA, 0, 8)
        wslot_load(1, woutA, 1, 8)
        a1_wo_group(0)
        for fo in range(1, 8):
            if fo + 1 < 8:
                wslot_load(fo + 1, woutA, fo + 1, 8)
            a1_wo_group(fo)
            a1_wo_evac(fo - 1)
        a1_wo_evac(7)

    b.barrier()

    def sweep_ffn(l, src, dst):
        srcd = xn_s if src == "x" else y1_s
        srcz = "d:xn" if src == "x" else "d:y1"

        def ffn_loads(cc):
            tile = XN if cc % 2 == 0 else X
            tz = "XN" if cc % 2 == 0 else "X"
            col = cc * 512
            if cc == 0:
                for kk in range(8):
                    b.emit("dve", lambda e, kk=kk, tile=tile: e.memset(
                        tile[:, kk * 513:kk * 513 + 1], 0.0), writes=(tz,))
                    b.dma(CX, tile[:, kk * 513 + 1:kk * 513 + 513],
                          srcd[kk * 128:(kk + 1) * 128, 0:512],
                          reads=(srcz,), writes=(tz,))
            else:
                for kk in range(8):
                    b.dma(CX, tile[:, kk * 513:kk * 513 + 513],
                          srcd[kk * 128:(kk + 1) * 128, col - 1:col + 512],
                          reads=(srcz,), writes=(tz,))

        b.emit("dve", lambda e: e.memset(GXN[:, :], 0.0), writes=("GXN",))
        b.emit("dve", lambda e: e.memset(GX1[:, :], 0.0), writes=("GX1",))
        ffn_loads(0)
        for c in range(NC):
            ccol = c * 512
            spv = (c - 1) % 2
            scv = c % 2
            first = (c == 0)
            XNt = XN if c % 2 == 0 else X
            XNz = "XN" if c % 2 == 0 else "X"
            if c + 1 < NC:
                ffn_loads(c + 1)
            for i in range(4):
                b.dma(CRT, RT[:, i * 512:(i + 1) * 512],
                      rott[l][i * 128:(i + 1) * 128, ccol:ccol + 512],
                      reads=(f"d:rott{l}",), writes=("RT",))

            def hshift(kk, off=0, w=512):
                base = kk * 513 + (1 if kk < 4 else 0) + off
                return XNt[:, base:base + w]

            def f_qkv_group(m):
                s = m % 2
                parts = [(PS[s][:, 0:512], WS[s][:, kk * 128:(kk + 1) * 128],
                          hshift(kk), kk == 0, kk == 7, None) for kk in range(8)]
                b.emit("pe", _mmg(parts), reads=(f"WS{s}", XNz), writes=(psz[s],))

            def f_qkv_evac(m):
                s = m % 2
                if m < 4:
                    b.emit("act", lambda e, m=m, s=s: e.copy(
                        Qa(m * 512, (m + 1) * 512), PS[s][:, 0:512]),
                        reads=(psz[s],), writes=("Q",))
                elif m < 8:
                    j = m - 4
                    b.emit("act", lambda e, j=j, s=s: e.copy(
                        X1s(j * 512, (j + 1) * 512), PS[s][:, 0:512]),
                        reads=(psz[s],), writes=("KRAW",))
                else:
                    j = m - 8
                    b.emit("act", lambda e, s=s: e.copy(VS[:, :], PS[s][:, 0:512]),
                           reads=(psz[s],), writes=("VS",))
                    for u in range(4):
                        b.emit("pe", lambda e, u=u: e.transpose(
                            PS[2][:, u * 128:(u + 1) * 128],
                            VS[:, u * 128:(u + 1) * 128], IDENT[:, :]),
                            reads=("VS", "IDENT"), writes=("ps2",))
                    dstv = bass.AP(VT, scv * 2048 + j * 128,
                                   [[6144, 128], [512, 4], [1, 128]])
                    b.emit("act", lambda e, d=dstv: e.copy(d, PS[2][:, 0:512]),
                           reads=("ps2",), writes=(f"VT{scv}",))

            def _rawz(m):
                if m < 4:
                    return Qa(m * 512, (m + 1) * 512), "Q"
                return X1s((m - 4) * 512, (m - 3) * 512), "KRAW"

            def f_qk_proc1(m):
                raw, rawz = _rawz(m)
                b.emit("act", lambda e, raw=raw: e.activation(SQA[:, :], raw, AF.Square),
                       reads=(rawz,), writes=("SQA",))
                b.emit("pe", _mmg([(PS[5][0:1, 0:512], ONES2[:, 0:1], SQA[:, :],
                                    True, True, None),
                                   (PS[6][0:1, 0:512], ONES2[:, 1:2], SQA[:, :],
                                    True, True, None)]),
                       reads=("SQA", "ONES2"), writes=("ps5", "ps6"))
                b.emit("dve", lambda e: e.tensor_scalar_max(RSS[0:1, :],
                                                            PS[5][0:1, 0:512], 1e-24),
                       reads=("ps5",), writes=("RSS",))
                b.emit("dve", lambda e: e.tensor_scalar_max(P[0:1, 0:512],
                                                            PS[6][0:1, 0:512], 1e-24),
                       reads=("ps6",), writes=("P",))
                b.emit("act", lambda e: e.activation(RSS[0:1, :], RSS[0:1, :], AF.Ln),
                       reads=(), writes=("RSS",))
                b.emit("act", lambda e: e.activation(RSS[0:1, :], RSS[0:1, :], AF.Exp,
                                                     scale=-0.5),
                       reads=(), writes=("RSS",))
                b.emit("act", lambda e: e.activation(P[0:1, 0:512], P[0:1, 0:512], AF.Ln),
                       reads=(), writes=("P",))
                b.emit("act", lambda e: e.activation(P[0:1, 0:512], P[0:1, 0:512], AF.Exp,
                                                     scale=-0.5),
                       reads=(), writes=("P",))

            def f_qk_proc2(m):
                raw, rawz = _rawz(m)
                b.emit("pe", _mmg([(PS[6][0:64, 0:512], BONE[0:1, 0:64],
                                    RSS[0:1, :], True, True, (0, 0)),
                                   (PS[6][64:128, 0:512], BONE[0:1, 0:64],
                                    P[0:1, 0:512], True, True, (0, 64))]),
                       reads=("RSS", "P", "BONE"), writes=("ps6",))
                b.emit("act", lambda e: e.copy(MB[:, :], PS[6][:, 0:512]),
                       reads=("ps6",), writes=("MB",))
                b.emit("dve", lambda e, raw=raw: e.tensor_mul(QR[:, :], raw, MB[:, :]),
                       reads=(rawz, "MB"), writes=("QR",))

            def f_qk_proc3(m):
                tb = 0 if m < 4 else 1024
                if m < 4:
                    dzone = "Q"
                    dst = Qa(m * 512, (m + 1) * 512)
                else:
                    j = m - 4
                    dzone = f"KR{scv}"
                    dst = KR[:, j * 1024 + scv * 512:j * 1024 + scv * 512 + 512]
                b.emit("pe", _mmg([(PS[7][:, 0:512], RROT[:, :], QR[:, :],
                                    True, True, None)]),
                       reads=("RROT", "QR"), writes=("ps7",))
                b.emit("dve", lambda e, tb=tb, d=dst: e.tensor_mul(
                    d, QR[:, :], RT[:, tb:tb + 512]),
                    reads=("QR", "RT"), writes=(dzone,))
                b.emit("dve", lambda e, tb=tb: e.tensor_mul(
                    SQB[:, :], PS[7][:, 0:512], RT[:, tb + 512:tb + 1024]),
                    reads=("ps7", "RT"), writes=("SQB",))
                b.emit("dve", lambda e, d=dst: e.tensor_add(d, d, SQB[:, :]),
                       reads=("SQB",), writes=(dzone,))

            wslot_load(0, wqkvF[l], 0, 8)
            wslot_load(1, wqkvF[l], 1, 8)
            f_qkv_group(0)
            for m in range(1, 12):
                if m + 1 < 12:
                    wslot_load(m + 1, wqkvF[l], m + 1, 8)
                f_qkv_group(m)
                f_qkv_evac(m - 1)
                if m >= 5:
                    f_qk_proc3(m - 5)
                if m >= 4:
                    f_qk_proc2(m - 4)
                if 3 <= m <= 10:
                    f_qk_proc1(m - 3)
            f_qk_proc3(7)
            f_qkv_evac(11)

            funits = [(hp, qs, h) for hp in range(4) for qs in range(4)
                      for h in range(2)]

            def f_env(u, ui):
                hp, qs, h = u
                jl = [j for j in range(qs, qs + 5) if (not first or j >= 4)]
                return dict(hp=hp, qs=qs, h=h, jl=jl,
                            w0=jl[0] * 128, w1=(jl[-1] + 1) * 128,
                            qcols=hp * 512 + qs * 128,
                            scidx=(hp * 4 + qs) * 2 + h,
                            nbh=4 * (ui % 2),
                            bkA=2 + 2 * (ui % 2), bkB=3 + 2 * (ui % 2))

            def f_part1(u, ui):
                v = f_env(u, ui)
                hp, qs, h = v["hp"], v["qs"], v["h"]
                w0, w1 = v["w0"], v["w1"]
                nbh, bkA, bkB = v["nbh"], v["bkA"], v["bkB"]
                aa1 = min(w1, 512)
                hasA = w0 < aa1
                bb0 = max(w0, 512) - 512
                bb1 = w1 - 512
                if hasA:
                    b.emit("pe", _mmg([(PS[bkA][:, w0:aa1],
                                        Qa(v["qcols"], v["qcols"] + 128, 64 * h, 64 * h + 64),
                                        KR[64 * h:64 * h + 64,
                                           hp * 1024 + spv * 512 + w0:
                                           hp * 1024 + spv * 512 + aa1],
                                        True, True, None)]),
                           reads=("Q", f"KR{spv}"), writes=(psz[bkA],))
                b.emit("pe", _mmg([(PS[bkB][:, bb0:bb1],
                                    Qa(v["qcols"], v["qcols"] + 128, 64 * h, 64 * h + 64),
                                    KR[64 * h:64 * h + 64,
                                       hp * 1024 + scv * 512 + bb0:
                                       hp * 1024 + scv * 512 + bb1],
                                    True, True, None)]),
                       reads=("Q", f"KR{scv}"), writes=(psz[bkB],))
                if hasA:
                    b.emit("dve", lambda e, bk=bkA, qs=qs, w0=w0, aa1=aa1: e.tensor_add(
                        PS[bk][:, w0:aa1], PS[bk][:, w0:aa1],
                        MF[:, qs * 1024 + w0:qs * 1024 + aa1]),
                        reads=("MF",), writes=(psz[bkA],))
                b.emit("dve", lambda e, bk=bkB, qs=qs, bb0=bb0, bb1=bb1: e.tensor_add(
                    PS[bk][:, bb0:bb1], PS[bk][:, bb0:bb1],
                    MF[:, qs * 1024 + 512 + bb0:qs * 1024 + 512 + bb1]),
                    reads=("MF",), writes=(psz[bkB],))
                if hasA:
                    b.emit("dve", lambda e, bk=bkA, w0=w0, aa1=aa1, nbh=nbh: e.reduce_max(
                        NMT[:, nbh:nbh + 1], PS[bk][:, w0:aa1], axis=AX.X, negate=True),
                        reads=(psz[bkA],), writes=("NMT",))
                b.emit("dve", lambda e, bk=bkB, bb0=bb0, bb1=bb1, nbh=nbh: e.reduce_max(
                    NMT[:, nbh + 1:nbh + 2], PS[bk][:, bb0:bb1], axis=AX.X, negate=True),
                    reads=(psz[bkB],), writes=("NMT",))
                if hasA:
                    b.emit("dve", lambda e, nbh=nbh: e.tensor_tensor(
                        NMT[:, nbh + 2:nbh + 3], NMT[:, nbh:nbh + 1],
                        NMT[:, nbh + 1:nbh + 2], ALU.min),
                        reads=(), writes=("NMT",))
                    b.emit("dve", lambda e, nbh=nbh: e.tensor_scalar_mul(
                        NMT[:, nbh + 3:nbh + 4], NMT[:, nbh + 2:nbh + 3], 8.0),
                        reads=(), writes=("NMT",))
                else:
                    b.emit("dve", lambda e, nbh=nbh: e.tensor_scalar_mul(
                        NMT[:, nbh + 3:nbh + 4], NMT[:, nbh + 1:nbh + 2], 8.0),
                        reads=(), writes=("NMT",))

            def f_part2(u, ui):
                v = f_env(u, ui)
                hp, qs, h = v["hp"], v["qs"], v["h"]
                jl, w0, w1 = v["jl"], v["w0"], v["w1"]
                nbh, bkA, bkB = v["nbh"], v["bkA"], v["bkB"]
                scidx = v["scidx"]
                qcols = v["qcols"]
                aa1 = min(w1, 512)
                hasA = w0 < aa1
                bb0 = max(w0, 512) - 512
                bb1 = w1 - 512
                if hasA:
                    b.emit("act", lambda e, bk=bkA, si=scidx, nbh=nbh, w0=w0, aa1=aa1:
                           e.activation(
                               P[:, w0:aa1], PS[bk][:, w0:aa1], AF.Exp,
                               bias=NMT[:, nbh + 3:nbh + 4], scale=8.0,
                               accum_out=SM[:, 2 * si:2 * si + 1]),
                           reads=(psz[bkA], "NMT"), writes=("P", "SM"))
                    b.emit("act", lambda e, bk=bkB, si=scidx, nbh=nbh, bb0=bb0, bb1=bb1:
                           e.activation(
                               P[:, 512 + bb0:512 + bb1], PS[bk][:, bb0:bb1], AF.Exp,
                               bias=NMT[:, nbh + 3:nbh + 4], scale=8.0,
                               accum_out=SM[:, 2 * si + 1:2 * si + 2]),
                           reads=(psz[bkB], "NMT"), writes=("P", "SM"))
                    b.emit("dve", lambda e, si=scidx: e.tensor_add(
                        SM[:, 64 + si:65 + si], SM[:, 2 * si:2 * si + 1],
                        SM[:, 2 * si + 1:2 * si + 2]),
                        reads=(), writes=("SM",))
                else:
                    b.emit("act", lambda e, bk=bkB, si=scidx, nbh=nbh, bb0=bb0, bb1=bb1:
                           e.activation(
                               P[:, 512 + bb0:512 + bb1], PS[bk][:, bb0:bb1], AF.Exp,
                               bias=NMT[:, nbh + 3:nbh + 4], scale=8.0,
                               accum_out=SM[:, 64 + si:65 + si]),
                           reads=(psz[bkB], "NMT"), writes=("P", "SM"))
                last_pair = (h == 1)
                if last_pair:
                    si0 = (hp * 4 + qs) * 2
                    b.emit("pe", lambda e, si=si0: [
                        e.transpose(PS[1][0:1, 0:128], SM[:, 64 + si:65 + si],
                                    IDENT[:, :]),
                        e.transpose(PS[1][0:1, 128:256], SM[:, 65 + si:66 + si],
                                    IDENT[:, :])],
                        reads=("SM", "IDENT"), writes=("ps1",))
                    b.emit("dve", lambda e: e.reciprocal(SMD[0:1, :], PS[1][0:1, 0:256]),
                           reads=("ps1",), writes=("SMD",))
                for j in jl:
                    pb = 6 + j // 4
                    b.emit("pe", lambda e, j=j, pb=pb: e.transpose(
                        PS[pb][:, (j % 4) * 128:(j % 4) * 128 + 128],
                        P[:, j * 128:(j + 1) * 128], IDENT[:, :]),
                        reads=("P", "IDENT"), writes=(psz[pb],))
                if last_pair:
                    b.emit("pe", _mmg([(PS[1][0:64, 0:128], BONE[0:1, 0:64],
                                        SMD[0:1, 0:128], True, True, (0, 0)),
                                       (PS[1][64:128, 0:128], BONE[0:1, 0:64],
                                        SMD[0:1, 128:256], True, True, (0, 64))]),
                           reads=("SMD", "BONE"), writes=("ps1",))
                    b.emit("act", lambda e: e.copy(RBC[:, 0:128], PS[1][:, 0:128]),
                           reads=("ps1",), writes=("RBC",))
                jA = [j for j in jl if j < 4]
                jB = [j for j in jl if j >= 4]
                if jA:
                    b.emit("act", lambda e, j0=jA[0], j1=jA[-1]: e.copy(
                        PTS[:, j0 * 128:(j1 + 1) * 128],
                        PS[6][:, (j0 % 4) * 128:((j1 % 4) + 1) * 128]),
                        reads=("ps6",), writes=("PTS",))
                b.emit("act", lambda e, j0=jB[0], j1=jB[-1]: e.copy(
                    PTS[:, j0 * 128:(j1 + 1) * 128],
                    PS[7][:, (j0 % 4) * 128:((j1 % 4) + 1) * 128]),
                    reads=("ps7",), writes=("PTS",))
                parts = []
                vread = set()
                for jj, j in enumerate(jl):
                    sl = spv if j < 4 else scv
                    ts = j % 4
                    vread.add(f"VT{sl}")
                    base = sl * 2048 + ts * 512 + (2 * hp + h) * 64
                    parts.append((PS[0][64 * h:64 * h + 64, 0:128],
                                  VT[:, base:base + 64],
                                  PTS[:, j * 128:(j + 1) * 128],
                                  jj == 0, jj == len(jl) - 1, (0, 64 * h)))
                b.emit("pe", _mmg(parts), reads=tuple(vread) + ("PTS",),
                       writes=("ps0",))
                if last_pair:
                    b.emit("dve", lambda e, qc=qcols: e.tensor_mul(
                        O[:, qc:qc + 128], PS[0][:, 0:128], RBC[:, 0:128]),
                        reads=("ps0", "RBC"), writes=("O",))

            f_part1(funits[0], 0)
            for ui in range(1, len(funits)):
                f_part1(funits[ui], ui)
                f_part2(funits[ui - 1], ui - 1)
            f_part2(funits[-1], len(funits) - 1)

            def f_wo_group(fo):
                s = fo % 2
                parts = [(PS[s][:, 0:512], WS[s][:, kk * 128:(kk + 1) * 128],
                          O[:, kk * 512:(kk + 1) * 512], kk == 0, kk == 3, None)
                         for kk in range(4)]
                b.emit("pe", _mmg(parts), reads=(f"WS{s}", "O"), writes=(psz[s],))

            def f_wo_evac(fo):
                s = fo % 2
                b.emit("dve", lambda e, fo=fo, s=s, XNt=XNt: e.tensor_add(
                    X1s(fo * 513 + 1, fo * 513 + 513), PS[s][:, 0:512],
                    XNt[:, fo * 513 + 1:fo * 513 + 513]),
                    reads=(psz[s], XNz), writes=("X1",))

            wslot_load(0, woF[l], 0, 4)
            wslot_load(1, woF[l], 1, 4)
            f_wo_group(0)
            for fo in range(1, 8):
                if fo + 1 < 8:
                    wslot_load(fo + 1, woF[l], fo + 1, 4)
                f_wo_group(fo)
                f_wo_evac(fo - 1)
            f_wo_evac(7)
            for kk in range(8):
                b.emit("dve", lambda e, kk=kk: e.tensor_copy(
                    X1s(kk * 513, kk * 513 + 1), GX1[:, kk:kk + 1]),
                    reads=("GX1",), writes=("X1",))
            for kk in range(8):
                b.emit("dve", lambda e, kk=kk: e.tensor_copy(
                    GX1[:, kk:kk + 1], X1s(kk * 513 + 512, kk * 513 + 513)),
                    reads=("X1",), writes=("GX1",))

            def h2slice(kk):
                base = kk * 513 + (1 if kk < 4 else 0)
                return X1s(base, base + 512)

            emit_ln_stats_and_apply(
                "X1", h2slice, (XNz,),
                dst_slices=lambda kk: XNt[:, kk * 513 + 1:kk * 513 + 513],
                affine=(lambda kk: LNG[l][:, kk:kk + 1], lambda kk: LNB[l][:, kk:kk + 1]))

            for ts2 in range(2):
                tw = ts2 * 256

                def w1_ag_group(m, tw=tw):
                    s = (ts2 * 22 + m) % 2
                    sa = m % 2
                    parts = [(PS[sa][:, 0:256], WS[s][:, kk * 128:(kk + 1) * 128],
                              XNt[:, kk * 513 + 1 + tw:kk * 513 + 1 + tw + 256],
                              kk == 0, False, None) for kk in range(8)]
                    parts += [(PS[sa][:, 256:512], WS[s][:, (8 + kk) * 128:(9 + kk) * 128],
                               XNt[:, kk * 513 + 1 + tw:kk * 513 + 1 + tw + 256],
                               False, kk == 7, None) for kk in range(8)]
                    b.emit("pe", _mmg(parts), reads=(f"WS{s}", XNz), writes=(psz[sa],))

                def w1_w2_chain(m, tw=tw):
                    s = (ts2 * 22 + m) % 2
                    sa = m % 2
                    mz = f"M{m % 2}"
                    mc = (m % 2) * 256
                    b.emit("act", lambda e, sa=sa, m=m: e.activation(
                        MG[:, :], PS[sa][:, 256:512], AF.Gelu,
                        bias=B1G[l][:, m:m + 1]),
                        reads=(psz[sa], "BI"), writes=("MG",))
                    b.emit("dve", lambda e, sa=sa, m=m, mc=mc: e.scalar_tensor_tensor(
                        Mt[:, mc:mc + 256], PS[sa][:, 0:256], B1A[l][:, m:m + 1],
                        MG[:, :], ALU.add, ALU.mult),
                        reads=(psz[sa], "MG", "BI"), writes=(mz,))
                    parts = [(PS[2 + fo // 2][:, (fo % 2) * 256:(fo % 2) * 256 + 256],
                              W2S[s][:, fo * 128:(fo + 1) * 128],
                              Mt[:, mc:mc + 256], (m == 0 and fo % 2 == 0),
                              (m == 21 and fo % 2 == 1), None)
                             for fo in range(8)]
                    b.emit("pe", _mmg(parts), reads=(f"W2S{s}", mz),
                           writes=("ps2", "ps3", "ps4", "ps5"))

                wfus_load_ag(ts2 * 22, l, 0)
                wfus_load_ag(ts2 * 22 + 1, l, 1)
                wfus_load_w2(ts2 * 22, l, 0)
                wfus_load_w2(ts2 * 22 + 1, l, 1)
                w1_ag_group(0)
                for m in range(1, 22):
                    if m + 1 < 22:
                        wfus_load_ag(ts2 * 22 + m + 1, l, m + 1)
                    w1_ag_group(m)
                    w1_w2_chain(m - 1)
                    if m + 1 < 22:
                        wfus_load_w2(ts2 * 22 + m + 1, l, m + 1)
                w1_w2_chain(21)
                for fo in range(8):
                    w2v = PS[2 + fo // 2][:, (fo % 2) * 256:(fo % 2) * 256 + 256]
                    b.emit("dve", lambda e, fo=fo, w2v=w2v, tw=tw: e.scalar_tensor_tensor(
                        ACC[:, fo * 512 + tw:fo * 512 + tw + 256], w2v,
                        B2[l][:, fo:fo + 1],
                        X1s(fo * 513 + 1 + tw, fo * 513 + 1 + tw + 256),
                        ALU.add, ALU.add),
                        reads=("ps" + str(2 + fo // 2), "X1", "BI"),
                        writes=(f"ACC{fo}",))
            dstd = y1_s if dst == "y1" else outT
            dzone = "d:y1" if dst == "y1" else "d:out"
            for fo in range(8):
                b.dma(CO, dstd[fo * 128:(fo + 1) * 128, ccol:ccol + 512],
                      ACC[:, fo * 512:(fo + 1) * 512],
                      reads=(f"ACC{fo}",), writes=(dzone,))
            if dst == "out":
                b.dma(CACC, outT[:, ccol:ccol + 512], acc_s[:, ccol:ccol + 512],
                      reads=("d:acc", "d:out"), writes=("d:out",),
                      eng="pool", accum=ALU.add)

    sweep_ffn(0, "x", "y1")
    b.barrier()
    sweep_ffn(1, "y1", "out")

    b.wait_all_chans("sp")
    b.replay()
    nc.finalize()
    return nc



def _tile_w(w, nk, nm):
    K, M = w.shape
    assert K == nk * 128 and M == nm * 128
    t = w.reshape(nk, 128, nm, 128).transpose(2, 1, 0, 3)
    return np.ascontiguousarray(t).reshape(nm, 128 * nk * 128)


def _pad_cols(w, M):
    out = np.zeros((w.shape[0], M), dtype=w.dtype)
    out[:, :w.shape[1]] = w
    return out


def _pad_rows(w, K):
    out = np.zeros((K, w.shape[1]), dtype=w.dtype)
    out[:w.shape[0]] = w
    return out


def stage_inputs(L0_attn_w_qkv, L0_attn_w_out, L0_ffn_wqkv, L0_ffn_q_scale,
                 L0_ffn_k_scale, L0_ffn_wo, L0_ffn_ln_g, L0_ffn_ln_b,
                 L0_ffn_w1, L0_ffn_b1, L0_ffn_w2, L0_ffn_b2):
    f = np.float32
    d = {}
    d["wqkvA"] = _tile_w(np.asarray(L0_attn_w_qkv, f), 8, 24)
    d["woutA"] = _tile_w(np.asarray(L0_attn_w_out, f), 8, 8)
    for l in range(2):
        d[f"wqkvF{l}"] = _tile_w(np.asarray(L0_ffn_wqkv[l], f), 8, 12)
        d[f"woF{l}"] = _tile_w(np.asarray(L0_ffn_wo[l], f), 4, 8)
        w1 = np.asarray(L0_ffn_w1[l], f)
        wa = _pad_cols(w1[:, :2730], 2816).reshape(8, 128, 22, 128).transpose(2, 1, 0, 3)
        wg = _pad_cols(w1[:, 2730:], 2816).reshape(8, 128, 22, 128).transpose(2, 1, 0, 3)
        w2 = _pad_rows(np.asarray(L0_ffn_w2[l], f), 2816).reshape(22, 128, 8, 128)
        d[f"wfus{l}"] = np.ascontiguousarray(
            np.concatenate([wa, wg, w2], axis=2)).reshape(22, 128 * 24 * 128)
        b1 = np.asarray(L0_ffn_b1[l], f)
        d[f"b1a{l}"] = np.ascontiguousarray(
            np.concatenate([b1[:2730], np.zeros(86, f)]).reshape(22, 128).T)
        d[f"b1g{l}"] = np.ascontiguousarray(
            np.concatenate([b1[2730:], np.zeros(86, f)]).reshape(22, 128).T)
        d[f"b2{l}"] = np.ascontiguousarray(np.asarray(L0_ffn_b2[l], f).reshape(8, 128).T)
        d[f"lng{l}"] = np.ascontiguousarray(np.asarray(L0_ffn_ln_g[l], f).reshape(8, 128).T)
        d[f"lnb{l}"] = np.ascontiguousarray(np.asarray(L0_ffn_ln_b[l], f).reshape(8, 128).T)
        qs = np.asarray(L0_ffn_q_scale[l], f)
        ks = np.asarray(L0_ffn_k_scale[l], f)
        d[f"qsc{l}"] = np.ascontiguousarray(
            np.stack([np.tile(qs, 2), np.tile(np.roll(qs, -32), 2)], axis=1))
        d[f"ksc{l}"] = np.ascontiguousarray(
            np.stack([np.tile(ks, 2), np.tile(np.roll(ks, -32), 2)], axis=1))
    return d


def kernel(L0_attn_x, L0_attn_w_qkv, L0_attn_w_out, L0_ffn_wqkv, L0_ffn_q_scale,
           L0_ffn_k_scale, L0_ffn_wo, L0_ffn_ln_g, L0_ffn_ln_b, L0_ffn_w1,
           L0_ffn_b1, L0_ffn_w2, L0_ffn_b2):
    x = np.asarray(L0_attn_x, np.float32)
    B, NT, D = x.shape
    nc = build_core_program(NT)
    wd = stage_inputs(L0_attn_w_qkv, L0_attn_w_out, L0_ffn_wqkv, L0_ffn_q_scale,
                      L0_ffn_k_scale, L0_ffn_wo, L0_ffn_ln_g, L0_ffn_ln_b,
                      L0_ffn_w1, L0_ffn_b1, L0_ffn_w2, L0_ffn_b2)
    in_maps = []
    for bb in range(B):
        m = {"xT": np.ascontiguousarray(x[bb].T)}
        m.update(wd)
        in_maps.append(m)
    res = run_bass_kernel_spmd(nc, in_maps, core_ids=list(range(B)))
    out = np.stack([np.asarray(r["outT"]).T for r in res.results], axis=0)
    return out.astype(np.float32)


# version 15
# speedup vs baseline: 1.5489x; 1.0348x over previous
import math

import numpy as np

import concourse.bass as bass
import concourse.mybir as mybir
from concourse.bass_utils import run_bass_kernel_spmd
import concourse.library_config as libcfg

F32 = mybir.dt.float32
BF16 = mybir.dt.bfloat16
AF = mybir.ActivationFunctionType
ALU = mybir.AluOpType
AX = mybir.AxisListType

T = 512
NEG = -1.0e30
PI = math.pi
LN1E4 = math.log(10000.0)

ENGS = ("pe", "dve", "act", "pool", "sp")
_BLOCKSECT = {"pe": "tensor", "dve": "vector", "act": "scalar",
              "pool": "gpsimd", "sp": "sync"}

PROXY_LIB = [l for l in libcfg.all_libraries if l.name == "proxy"][0]


class Chan:
    __slots__ = ("name", "sem", "issued")

    def __init__(self, name, sem):
        self.name = name
        self.sem = sem
        self.issued = 0


class Bld:

    def __init__(self, nc):
        self.nc = nc
        self.ops = {e: [] for e in ENGS}
        self.sem = {e: nc.alloc_semaphore("tk_" + e) for e in ENGS}
        self.n = {e: 0 for e in ENGS}
        self.wm = {e: {} for e in ENGS}
        self.zw = {}
        self.zr = {}
        self.chans = []

    def chan(self, name):
        c = Chan(name, self.nc.alloc_semaphore("ch_" + name))
        self.chans.append(c)
        return c

    def _wait(self, e, key, val):
        if isinstance(key, Chan):
            val = 16 * key.issued
        if val <= 0:
            return
        if self.wm[e].get(key, 0) >= val:
            return
        self.wm[e][key] = val
        semh = key.sem if isinstance(key, Chan) else self.sem[key]
        self.ops[e].append(("w", semh, val))

    def _dep(self, e, zones, war):
        for z in zones:
            lw = self.zw.get(z)
            if lw is not None:
                self._wait(e, lw[0], lw[1])
            if war:
                for k, v in self.zr.get(z, {}).items():
                    self._wait(e, k, v)

    def emit(self, e, fn, reads=(), writes=()):
        self._dep(e, reads, False)
        self._dep(e, writes, True)
        self.ops[e].append(("i", fn, self.sem[e]))
        self.n[e] += 1
        t = self.n[e]
        for z in reads:
            self.zr.setdefault(z, {})[e] = t
        for z in writes:
            self.zw[z] = (e, t)
            self.zr[z] = {}

    def dma(self, ch, out_ap, in_ap, reads=(), writes=(), eng="sp", accum=None):
        self._dep(eng, reads, False)
        self._dep(eng, writes, True)
        self.ops[eng].append(("d", out_ap, in_ap, ch.sem, accum))
        ch.issued += 1
        for z in reads:
            self.zr.setdefault(z, {})[ch] = 0
        for z in writes:
            self.zw[z] = (ch, 0)
            self.zr[z] = {}

    def barrier(self):
        for e1 in ENGS:
            for e2 in ENGS:
                if e1 != e2:
                    self._wait(e1, e2, self.n[e2])

    def wait_all_chans(self, e="sp"):
        for c in self.chans:
            self._wait(e, c, 1)

    def replay(self):
        with self.nc.Block() as block:
            for e in ENGS:
                ops = self.ops[e]

                def body(eng, ops=ops):
                    for op in ops:
                        if op[0] == "w":
                            eng.wait_ge(op[1], op[2])
                        elif op[0] == "i":
                            r = op[1](eng)
                            if isinstance(r, list):
                                r = r[-1]
                            r.then_inc(op[2], 1)
                        else:
                            if op[4] is None:
                                eng.dma_start(op[1], op[2]).then_inc(op[3], 16)
                            else:
                                eng.dma_start(op[1], op[2],
                                              accum_op=op[4]).then_inc(op[3], 16)
                getattr(block, _BLOCKSECT[e])(body)


def _mmg(parts):
    def fn(eng):
        last = None
        for (o, l, r, s, st, tp) in parts:
            if tp is None:
                last = eng.matmul(o, l, r, start=s, stop=st)
            else:
                last = eng.matmul(o, l, r, start=s, stop=st, tile_position=tp)
        return last
    return fn


def build_core_program(NT=4096):
    NC = NT // T
    nc = bass.Bass("TRN2", target_bir_lowering=False)

    xT = nc.dram_tensor("xT", [1024, NT], F32, kind="ExternalInput").ap()
    wqkvA = nc.dram_tensor("wqkvA", [24, 128 * 8 * 128], F32, kind="ExternalInput").ap()
    woutA = nc.dram_tensor("woutA", [8, 128 * 8 * 128], F32, kind="ExternalInput").ap()
    wqkvF = [nc.dram_tensor(f"wqkvF{l}", [12, 128 * 8 * 128], F32, kind="ExternalInput").ap() for l in range(2)]
    woF = [nc.dram_tensor(f"woF{l}", [8, 128 * 4 * 128], F32, kind="ExternalInput").ap() for l in range(2)]
    wfus_h = [nc.dram_tensor(f"wfus{l}", [22, 128 * 24 * 128], F32, kind="ExternalInput") for l in range(2)]
    b1a_d = [nc.dram_tensor(f"b1a{l}", [128, 22], F32, kind="ExternalInput").ap() for l in range(2)]
    b1g_d = [nc.dram_tensor(f"b1g{l}", [128, 22], F32, kind="ExternalInput").ap() for l in range(2)]
    b2_d = [nc.dram_tensor(f"b2{l}", [128, 8], F32, kind="ExternalInput").ap() for l in range(2)]
    lng_d = [nc.dram_tensor(f"lng{l}", [128, 8], F32, kind="ExternalInput").ap() for l in range(2)]
    lnb_d = [nc.dram_tensor(f"lnb{l}", [128, 8], F32, kind="ExternalInput").ap() for l in range(2)]
    qsc_d = [nc.dram_tensor(f"qsc{l}", [128, 2], F32, kind="ExternalInput").ap() for l in range(2)]
    ksc_d = [nc.dram_tensor(f"ksc{l}", [128, 2], F32, kind="ExternalInput").ap() for l in range(2)]
    outT = nc.dram_tensor("outT", [1024, NT], F32, kind="ExternalOutput").ap()
    acc_s = nc.dram_tensor("acc_s", [1024, NT], F32, kind="Internal").ap()
    y1_s = nc.dram_tensor("y1_s", [1024, NT], F32, kind="Internal").ap()
    xn_s = nc.dram_tensor("xn_s", [1024, NT], F32, kind="Internal").ap()
    rott = [nc.dram_tensor(f"rott{l}", [4 * 128, NT], F32, kind="Internal").ap() for l in range(2)]

    A = nc.alloc_sbuf_tensor
    XN = A("XN", [128, 8 * 513], F32)
    X = A("Xt", [128, 8 * 513], F32)
    BIGQ = A("BIGQ", [128, 6152], F32)
    X1OFF = 2048
    ACC = A("ACCt", [128, 8 * 512], F32)
    O = A("Ot", [128, 8 * 512], F32)
    KR = A("KRt", [128, 6144], F32)
    VT = A("VTt", [128, 6144], F32)
    WS = [A("WS0", [128, 16 * 128], F32), A("WS1", [128, 16 * 128], F32)]
    W2S = [A("W2S0", [128, 8 * 128], F32), A("W2S1", [128, 8 * 128], F32)]
    P = A("Pt", [128, 1024], F32)
    PTS = A("PTSt", [128, 1024], F32)
    Mt = A("Mt", [128, 512], F32)
    MG = A("MGt", [128, 256], F32)
    QR = A("QRt", [128, 512], F32)
    VS = A("VSt", [128, 512], F32)
    SQA = A("SQAt", [128, 512], F32)
    SQB = A("SQBt", [128, 512], F32)
    MB = A("MBt", [128, 512], F32)
    RB = A("RBt", [128, 512], F32)
    RT = A("RTt", [128, 2048], F32)
    MA = A("MAt", [128, 768], BF16)
    MF = A("MFt", [128, 4096], BF16)
    SM = A("SMt", [128, 96], F32)
    NMT = A("NMTt", [128, 16], F32)
    SMD = A("SMDt", [1, 256], F32)
    RBC = A("RBCt", [128, 256], F32)
    IDENT = A("IDENTt", [128, 128], F32)
    ONES = A("ONESt", [128, 1], F32)
    ONES2 = A("ONES2t", [128, 2], F32)
    GXN = A("GXNt", [128, 8], F32)
    GX1 = A("GX1t", [128, 8], F32)
    RSS = A("RSSt", [1, 512], F32)
    RROT = A("RROTt", [128, 128], F32)
    BONE = A("BONEt", [1, 128], F32)
    CEPS = A("CEPSt", [128, 1], F32)
    C15 = A("C15t", [128, 1], F32)
    IOD = A("IODt", [128, 4], F32)
    B1A = [A(f"B1A{l}", [128, 22], F32) for l in range(2)]
    B1G = [A(f"B1G{l}", [128, 22], F32) for l in range(2)]
    B2 = [A(f"B2{l}", [128, 8], F32) for l in range(2)]
    LNG = [A(f"LNG{l}", [128, 8], F32) for l in range(2)]
    LNB = [A(f"LNB{l}", [128, 8], F32) for l in range(2)]
    QSC = [A(f"QSC{l}", [128, 2], F32) for l in range(2)]
    KSC = [A(f"KSC{l}", [128, 2], F32) for l in range(2)]

    def Qa(a, b_, p0=0, p1=128):
        return BIGQ[p0:p1, a:b_]

    def X1s(a, b_):
        return BIGQ[:, X1OFF + a:X1OFF + b_]

    PS = [nc.alloc_psum_tensor(f"PS{i}", [128, 512], F32) for i in range(8)]

    b = Bld(nc)
    CX = b.chan("cx")
    CW = [b.chan("cw0"), b.chan("cw1")]
    CW2 = [b.chan("cw2a"), b.chan("cw2b")]
    CRT = b.chan("crt")
    CB = b.chan("cb")
    CO = b.chan("co")
    CTW = b.chan("ctw")
    CPY = b.chan("cpy")
    CX1 = b.chan("cx1")
    CACC = b.chan("cacc")

    psz = [f"ps{i}" for i in range(8)]

    b.emit("dve", lambda e: e.memset(BONE[:, :], 1.0), writes=("BONE",))

    b.emit("dve", lambda e: e.memset(IDENT[:, :], 1.0), writes=("IDENT",))
    b.emit("pool", lambda e: e.affine_select(IDENT[:, :], IDENT[:, :], [[1, 128]],
                                             ALU.is_equal, 0.0, base=0,
                                             channel_multiplier=-1),
           reads=("lib",), writes=("IDENT",))
    b.emit("dve", lambda e: e.memset(ONES[:, :], 1.0), writes=("ONES",))
    b.emit("dve", lambda e: e.memset(CEPS[:, :], 1e-5), writes=("CEPS",))
    b.emit("dve", lambda e: e.memset(C15[:, :], -15.0), writes=("C15",))
    b.emit("dve", lambda e: e.memset(ONES2[:, :], 1.0), writes=("ONES2",))
    b.emit("pool", lambda e: e.affine_select(ONES2[:, 0:1], ONES2[:, 0:1], [[1, 1]],
                                             ALU.is_ge, 0.0, base=63,
                                             channel_multiplier=-1),
           reads=("lib",), writes=("ONES2",))
    b.emit("pool", lambda e: e.affine_select(ONES2[:, 1:2], ONES2[:, 1:2], [[1, 1]],
                                             ALU.is_ge, 0.0, base=-64,
                                             channel_multiplier=1),
           reads=("lib",), writes=("ONES2",))
    for (c0, sgn, kb) in ((0, -1.0, -32), (32, 1.0, 0), (64, -1.0, -96), (96, 1.0, -64)):
        sl = RROT[:, c0:c0 + 32]
        b.emit("dve", lambda e, sl=sl, sgn=sgn: e.memset(sl, sgn), writes=("RROT",))
        b.emit("pool", lambda e, sl=sl, kb=kb: e.affine_select(
            sl, sl, [[-1, 32]], ALU.is_equal, 0.0, base=kb,
            channel_multiplier=1), reads=("lib",), writes=("RROT",))
    b.emit("dve", lambda e: e.memset(MA[:, :], 0.0), writes=("MA",))
    b.emit("dve", lambda e: e.memset(MF[:, :], 0.0), writes=("MF",))
    A1_MCOL = {(0, 0): 0, (1, 0): 128, (1, 1): 256, (2, 0): 384,
               (2, 1): 512, (3, 1): 640}
    for (j, qs), c0 in A1_MCOL.items():
        sl = MA[:, c0:c0 + 128]
        dj = (j - qs) * 128
        b.emit("pool", lambda e, sl=sl, dj=dj: e.affine_select(
            sl, sl, [[-1, 128]], ALU.is_ge, NEG, base=dj - 1,
            channel_multiplier=1), reads=("lib",), writes=("MA",))
        b.emit("pool", lambda e, sl=sl, dj=dj: e.affine_select(
            sl, sl, [[1, 128]], ALU.is_ge, NEG, base=256 - dj,
            channel_multiplier=-1), reads=("lib",), writes=("MA",))
    for qs in range(4):
        sl = MF[:, qs * 1024:(qs + 1) * 1024]
        b.emit("pool", lambda e, sl=sl, qs=qs: e.affine_select(
            sl, sl, [[1, 1024]], ALU.is_ge, NEG, base=-(128 * qs),
            channel_multiplier=-1), reads=("lib",), writes=("MF",))
        b.emit("pool", lambda e, sl=sl, qs=qs: e.affine_select(
            sl, sl, [[-1, 1024]], ALU.is_ge, NEG, base=(128 * qs) + 512,
            channel_multiplier=1), reads=("lib",), writes=("MF",))
    for l in range(2):
        b.dma(CB, B1A[l][:, :], b1a_d[l], writes=("BI",))
        b.dma(CB, B1G[l][:, :], b1g_d[l], writes=("BI",))
        b.dma(CB, B2[l][:, :], b2_d[l], writes=("BI",))
        b.dma(CB, LNG[l][:, :], lng_d[l], writes=("BI",))
        b.dma(CB, LNB[l][:, :], lnb_d[l], writes=("BI",))
        b.dma(CB, QSC[l][:, :], qsc_d[l], writes=("BI",))
        b.dma(CB, KSC[l][:, :], ksc_d[l], writes=("BI",))

    for qtr in range(4):
        b.emit("pool", lambda e, q=qtr: e.iota(
            IOD[q * 32:(q + 1) * 32, 0:1], [[1, 1]], base=0, channel_multiplier=1,
            allow_small_or_imprecise_dtypes=True), reads=("lib",), writes=("IOD",))
    b.emit("act", lambda e: e.activation(IOD[:, 1:2], IOD[:, 0:1], AF.Exp,
                                         scale=-2.0 * LN1E4 / 64.0),
           reads=("IOD",), writes=("IOD",))
    b.emit("dve", lambda e: e.tensor_scalar(IOD[:, 2:3], IOD[:, 0:1],
                                            2.0 / 89.6, 25.6 / 89.6,
                                            ALU.mult, ALU.add),
           reads=("IOD",), writes=("IOD",))
    b.emit("act", lambda e: e.activation(IOD[:, 3:4], IOD[:, 2:3], AF.Ln),
           reads=("IOD",), writes=("IOD",))
    for l in range(2):
        for cb in range(NC):
            t0 = cb * 512
            b.emit("pool", lambda e, t0=t0: e.iota(
                MB[:, :], [[1, 512]], base=t0, channel_multiplier=0,
                allow_small_or_imprecise_dtypes=True),
                reads=("lib",), writes=("MB",))
            b.emit("dve", lambda e: e.tensor_scalar_mul(RB[:, :], MB[:, :], IOD[:, 1:2]),
                   reads=("MB", "IOD"), writes=("RB",))
            b.emit("dve", lambda e: e.tensor_scalar_add(P[:, 0:512], RB[:, :], PI),
                   reads=("RB",), writes=("P",))
            for j in range(9, -1, -1):
                cj = 2.0 * PI * (1 << j)
                b.emit("dve", lambda e, cj=cj: e.tensor_single_scalar(
                    VS[:, :], P[:, 0:512], cj, ALU.is_ge),
                    reads=("P",), writes=("VS",))
                b.emit("dve", lambda e, cj=cj: e.scalar_tensor_tensor(
                    P[:, 0:512], VS[:, :], -cj, P[:, 0:512], ALU.mult, ALU.add),
                    reads=("VS",), writes=("P",))
            b.emit("dve", lambda e: e.tensor_scalar_add(P[:, 0:512], P[:, 0:512], -PI),
                   reads=(), writes=("P",))
            b.emit("dve", lambda e: e.tensor_scalar_add(P[:, 512:1024], P[:, 0:512], PI / 2),
                   reads=(), writes=("P",))
            b.emit("dve", lambda e: e.tensor_single_scalar(
                VS[:, :], P[:, 512:1024], PI, ALU.is_ge),
                reads=("P",), writes=("VS",))
            b.emit("dve", lambda e: e.scalar_tensor_tensor(
                P[:, 512:1024], VS[:, :], -2.0 * PI, P[:, 512:1024],
                ALU.mult, ALU.add),
                reads=("VS",), writes=("P",))
            b.emit("act", lambda e: e.activation(SQB[:, :], P[:, 0:512], AF.Sin),
                   reads=("P",), writes=("SQB",))
            b.emit("act", lambda e: e.activation(VS[:, :], P[:, 512:1024], AF.Sin),
                   reads=("P",), writes=("VS",))
            b.emit("pool", lambda e, t0=t0: e.iota(
                QR[:, :], [[1, 512]], base=t0 - 2048, channel_multiplier=0,
                allow_small_or_imprecise_dtypes=True),
                reads=("lib",), writes=("QR",))
            b.emit("dve", lambda e: e.tensor_scalar_mul(QR[:, :], QR[:, :], IOD[:, 3:4]),
                   reads=("IOD",), writes=("QR",))
            b.emit("act", lambda e: e.activation(MB[:, :], QR[:, :], AF.Exp,
                                                 scale=1.0 / 256.0),
                   reads=("QR",), writes=("MB",))
            b.emit("act", lambda e: e.activation(RB[:, :], QR[:, :], AF.Exp,
                                                 scale=-1.0 / 256.0),
                   reads=("QR",), writes=("RB",))
            tabs = [(0, VS, MB, QSC[l][:, 0:1]),
                    (512, SQB, MB, QSC[l][:, 1:2]),
                    (1024, VS, RB, KSC[l][:, 0:1]),
                    (1536, SQB, RB, KSC[l][:, 1:2])]
            for (off, t1, t2, sc) in tabs:
                b.emit("dve", lambda e, off=off, t1=t1, t2=t2: e.tensor_mul(
                    RT[:, off:off + 512], t1[:, :], t2[:, :]),
                    reads=("VS", "SQB", "MB", "RB"), writes=("RT",))
                b.emit("dve", lambda e, off=off, sc=sc: e.tensor_scalar_mul(
                    RT[:, off:off + 512], RT[:, off:off + 512], sc),
                    reads=("BI",), writes=("RT",))
            for i in range(4):
                b.dma(CTW, rott[l][i * 128:(i + 1) * 128, t0:t0 + 512],
                      RT[:, i * 512:(i + 1) * 512],
                      reads=("RT",), writes=(f"d:rott{l}",))

    b.barrier()


    def emit_ln_stats_and_apply(src_tile, src_cols, dst_writes, dst_slices,
                                affine=None):
        for kk in range(8):
            sq = SQA if kk % 2 == 0 else SQB
            sqz = "SQA" if kk % 2 == 0 else "SQB"
            src = src_cols(kk)
            b.emit("act", lambda e, sq=sq, src=src: e.activation(sq[:, :], src, AF.Square),
                   reads=(src_tile,), writes=(sqz,))
            b.emit("pe", _mmg([(PS[6][0:1, 0:512], ONES[:, 0:1], src,
                                kk == 0, kk == 7, None)]),
                   reads=(src_tile, "ONES"), writes=("ps6",))
            b.emit("pe", _mmg([(PS[7][0:1, 0:512], ONES[:, 0:1], sq[:, :],
                                kk == 0, kk == 7, None)]),
                   reads=(sqz, "ONES"), writes=("ps7",))
        b.emit("dve", lambda e: e.tensor_scalar_mul(MB[0:1, :], PS[6][0:1, 0:512], 1.0 / 1024.0),
               reads=("ps6",), writes=("MB",))
        b.emit("dve", lambda e: e.tensor_scalar_mul(RB[0:1, :], PS[7][0:1, 0:512], 1.0 / 1024.0),
               reads=("ps7",), writes=("RB",))
        b.emit("act", lambda e: e.activation(RSS[0:1, :], MB[0:1, :], AF.Square),
               reads=("MB",), writes=("RSS",))
        b.emit("dve", lambda e: e.tensor_sub(RB[0:1, :], RB[0:1, :], RSS[0:1, :]),
               reads=("RSS",), writes=("RB",))
        b.emit("act", lambda e: e.activation(RSS[0:1, :], RB[0:1, :], AF.Ln,
                                             bias=CEPS[0:1, 0:1]),
               reads=("RB", "CEPS"), writes=("RSS",))
        b.emit("act", lambda e: e.activation(RB[0:1, :], RSS[0:1, :], AF.Exp, scale=-0.5),
               reads=("RSS",), writes=("RB",))
        b.emit("pe", _mmg([(PS[6][:, 0:512], BONE[0:1, 0:128], MB[0:1, :],
                            True, True, None)]),
               reads=("MB", "BONE"), writes=("ps6",))
        b.emit("pe", _mmg([(PS[7][:, 0:512], BONE[0:1, 0:128], RB[0:1, :],
                            True, True, None)]),
               reads=("RB", "BONE"), writes=("ps7",))
        for kk in range(8):
            src = src_cols(kk)
            dst = dst_slices(kk)
            b.emit("dve", lambda e, d=dst, s=src: e.tensor_sub(d, s, PS[6][:, 0:512]),
                   reads=(src_tile, "ps6"), writes=dst_writes)
            b.emit("dve", lambda e, d=dst: e.tensor_mul(d, d, PS[7][:, 0:512]),
                   reads=("ps7",), writes=dst_writes)
            if affine is not None:
                g_ap, bt_ap = affine
                b.emit("dve", lambda e, d=dst, g=g_ap(kk), bb=bt_ap(kk): e.tensor_scalar(
                    d, d, g, bb, ALU.mult, ALU.add),
                    reads=("BI",), writes=dst_writes)

    def wslot_load(widx, dram_ap, j, nblk):
        s = widx % 2
        b.dma(CW[s], WS[s][:, 0:nblk * 128], dram_ap[j:j + 1, :],
              writes=(f"WS{s}",))
        return s

    def wfus_load_ag(widx, l, j):
        s = widx % 2
        src_ap = bass.AP(wfus_h[l], j * (128 * 24 * 128),
                         [[24 * 128, 128], [1, 16 * 128]])
        b.dma(CW[s], WS[s][:, 0:16 * 128], src_ap, writes=(f"WS{s}",))

    def wfus_load_w2(widx, l, j):
        s = widx % 2
        src_ap = bass.AP(wfus_h[l], j * (128 * 24 * 128) + 16 * 128,
                         [[24 * 128, 128], [1, 8 * 128]])
        b.dma(CW2[s], W2S[s][:, 0:8 * 128], src_ap, writes=(f"W2S{s}",))

    def a1_ln_block(c):
        ccol = c * 512
        b.dma(CPY, acc_s[:, ccol:ccol + 512], xT[:, ccol:ccol + 512],
              writes=("d:acc",))
        for kk in range(8):
            b.dma(CX, X[:, kk * 512:(kk + 1) * 512],
                  xT[kk * 128:(kk + 1) * 128, ccol:ccol + 512], writes=("X",))
        emit_ln_stats_and_apply(
            "X", lambda kk: X[:, kk * 512:(kk + 1) * 512], ("XN",),
            dst_slices=lambda kk: XN[:, kk * 513 + 1:kk * 513 + 513])
        for kk in range(8):
            b.dma(CO, xn_s[kk * 128:(kk + 1) * 128, ccol:ccol + 512],
                  XN[:, kk * 513 + 1:kk * 513 + 513],
                  reads=("XN",), writes=("d:xn",))

    a1_ln_block(0)
    for c in range(NC):
        ccol = c * 512

        s0 = (2 * c) % 3
        s1 = (2 * c + 1) % 3

        def a1_qkv_group(m):
            s = m % 2
            parts = [(PS[s][:, 0:512], WS[s][:, kk * 128:(kk + 1) * 128],
                      XN[:, kk * 513 + 1:kk * 513 + 513], kk == 0, kk == 7, None)
                     for kk in range(8)]
            b.emit("pe", _mmg(parts), reads=(f"WS{s}", "XN"), writes=(psz[s],))

        def a1_qkv_evac(m):
            s = m % 2
            if m < 8:
                b.emit("act", lambda e, m=m, s=s: e.copy(Qa(m * 512, (m + 1) * 512),
                                                         PS[s][:, 0:512]),
                       reads=(psz[s],), writes=("Q",))
            elif m < 16:
                j = m - 8
                d0 = KR[:, j * 768 + s0 * 256:j * 768 + s0 * 256 + 256]
                d1 = KR[:, j * 768 + s1 * 256:j * 768 + s1 * 256 + 256]
                b.emit("act", lambda e, d=d0, s=s: e.copy(d, PS[s][:, 0:256]),
                       reads=(psz[s],), writes=(f"KR{s0}",))
                b.emit("act", lambda e, d=d1, s=s: e.copy(d, PS[s][:, 256:512]),
                       reads=(psz[s],), writes=(f"KR{s1}",))
            else:
                j = m - 16
                b.emit("act", lambda e, s=s: e.copy(VS[:, :], PS[s][:, 0:512]),
                       reads=(psz[s],), writes=("VS",))
                for u in range(4):
                    b.emit("pe", lambda e, u=u: e.transpose(
                        PS[2][:, u * 128:(u + 1) * 128],
                        VS[:, u * 128:(u + 1) * 128], IDENT[:, :]),
                        reads=("VS", "IDENT"), writes=("ps2",))
                for half in range(2):
                    sl = s0 if half == 0 else s1
                    src = PS[2][:, half * 256:half * 256 + 256]
                    dst = bass.AP(VT, sl * 2048 + j * 128,
                                  [[6144, 128], [1024, 2], [1, 128]])
                    b.emit("act", lambda e, d=dst, sp=src: e.copy(d, sp),
                           reads=("ps2",), writes=(f"VT{sl}",))

        wslot_load(0, wqkvA, 0, 8)
        wslot_load(1, wqkvA, 1, 8)
        a1_qkv_group(0)
        for m in range(1, 24):
            if m + 1 < 24:
                wslot_load(m + 1, wqkvA, m + 1, 8)
            a1_qkv_group(m)
            a1_qkv_evac(m - 1)
        a1_qkv_evac(23)
        if c + 1 < NC:
            a1_ln_block(c + 1)

        aunits = [(b_loc, hp, h) for b_loc in range(2)
                  for hp in range(8) for h in range(2)]
        A1_SBANK = ((3, 4), (0, 1), (5, 7))

        A1_JPACK = {0: (0, 0, 0, 1), 1: (0, 128, 0, 2),
                    2: (1, 0, 0, 2), 3: (1, 256, 1, 1)}
        A1_MOFF = {0: 0, 1: 128, 2: 384, 3: 640}

        def a1_env(u, ui):
            b_loc, hp, h = u
            blk = 2 * c + b_loc
            first = (blk == 0)
            jl = [2, 3] if first else [0, 1, 2, 3]
            return dict(b_loc=b_loc, hp=hp, h=h,
                        spv=(blk - 1) % 3, scv=blk % 3,
                        first=first, jl=jl,
                        qb=hp * 512 + b_loc * 256,
                        sb=A1_SBANK[ui % 3])

        def a1_part1(u, ui):
            v = a1_env(u, ui)
            hp, h = v["hp"], v["h"]
            spv, scv, first, jl = v["spv"], v["scv"], v["first"], v["jl"]
            qb, sb = v["qb"], v["sb"]
            parts = [[], []]
            for j in jl:
                bi, c0, qs0, nq = A1_JPACK[j]
                sl = spv if j < 2 else scv
                kc = hp * 768 + sl * 256 + (j % 2) * 128
                parts[bi].append((PS[sb[bi]][:, c0:c0 + nq * 128],
                                  KR[64 * h:64 * h + 64, kc:kc + 128],
                                  Qa(qb + qs0 * 128, qb + qs0 * 128 + nq * 128,
                                     64 * h, 64 * h + 64),
                                  len(parts[bi]) == 0, False, None))
            allp = []
            for bi in range(2):
                if parts[bi]:
                    parts[bi][-1] = parts[bi][-1][:4] + (True, None)
                    allp += parts[bi]
            rd = ("Q", f"KR{scv}") if first else ("Q", f"KR{spv}", f"KR{scv}")
            wz = (psz[sb[1]],) if first else (psz[sb[0]], psz[sb[1]])
            b.emit("pe", _mmg(allp), reads=rd, writes=wz)
            if not first:
                b.emit("dve", lambda e, bk=sb[0]: e.tensor_add(
                    PS[bk][:, 0:384], PS[bk][:, 0:384], MA[:, 0:384]),
                    reads=("MA",), writes=(psz[sb[0]],))
                b.emit("dve", lambda e, bk=sb[1]: e.tensor_add(
                    PS[bk][:, 0:384], PS[bk][:, 0:384], MA[:, 384:768]),
                    reads=("MA",), writes=(psz[sb[1]],))
            else:
                b.emit("dve", lambda e, bk=sb[1]: e.tensor_add(
                    PS[bk][:, 0:384], PS[bk][:, 0:384], MA[:, 384:768]),
                    reads=("MA",), writes=(psz[sb[1]],))

        def a1_part2(u, ui):
            v = a1_env(u, ui)
            hp, h = v["hp"], v["h"]
            spv, scv, first, jl = v["spv"], v["scv"], v["first"], v["jl"]
            qb, sb = v["qb"], v["sb"]
            PT_ = PTS if ui % 2 == 0 else P
            ptz = "PTS" if ui % 2 == 0 else "P"
            dbk = 6
            if not first:
                b.emit("act", lambda e, bk=sb[0], PT_=PT_: e.activation(
                    PT_[:, 0:384], PS[bk][:, 0:384], AF.Exp,
                    bias=C15[:, 0:1], scale=0.125),
                    reads=(psz[sb[0]], "C15"), writes=(ptz,))
            b.emit("act", lambda e, bk=sb[1], PT_=PT_: e.activation(
                PT_[:, 384:768], PS[bk][:, 0:384], AF.Exp,
                bias=C15[:, 0:1], scale=0.125),
                reads=(psz[sb[1]], "C15"), writes=(ptz,))
            jord = [j for j in (1, 2, 0, 3) if j in jl]
            dparts = []
            for j in jord:
                bi, c0, qs0, nq = A1_JPACK[j]
                pc = bi * 384 + c0
                dparts.append((PS[dbk][0:1, qs0 * 128:qs0 * 128 + nq * 128],
                               ONES[:, 0:1], PT_[:, pc:pc + nq * 128],
                               len(dparts) == 0, j == jord[-1], None))
            b.emit("pe", _mmg(dparts), reads=(ptz, "ONES"), writes=(psz[dbk],))
            b.emit("dve", lambda e, dbk=dbk: e.reciprocal(SMD[0:1, 0:256],
                                                          PS[dbk][0:1, 0:256]),
                   reads=(psz[dbk],), writes=("SMD",))
            vparts = []
            vread = set()
            for j in jord:
                bi, c0, qs0, nq = A1_JPACK[j]
                sl = spv if j < 2 else scv
                ts = j % 2
                vread.add(f"VT{sl}")
                base = sl * 2048 + ts * 1024 + (2 * hp + h) * 64
                pc = bi * 384 + c0
                vparts.append((PS[2][64 * h:64 * h + 64,
                                     qs0 * 128:qs0 * 128 + nq * 128],
                               VT[:, base:base + 64],
                               PT_[:, pc:pc + nq * 128],
                               len(vparts) == 0, j == jord[-1], (0, 64 * h)))
            b.emit("pe", _mmg(vparts), reads=tuple(vread) + (ptz,),
                   writes=("ps2",))
            b.emit("pe", _mmg([(PS[dbk][64 * h:64 * h + 64, 0:256], BONE[0:1, 0:64],
                                SMD[0:1, 0:256], True, True, (0, 64 * h))]),
                   reads=("SMD", "BONE"), writes=(psz[dbk],))
            b.emit("act", lambda e, h=h, dbk=dbk: e.copy(
                RBC[64 * h:64 * h + 64, 0:256],
                PS[dbk][64 * h:64 * h + 64, 0:256]),
                reads=(psz[dbk],), writes=("RBC",))
            b.emit("dve", lambda e, h=h, qb=qb: e.tensor_mul(
                O[64 * h:64 * h + 64, qb:qb + 256],
                PS[2][64 * h:64 * h + 64, 0:256],
                RBC[64 * h:64 * h + 64, 0:256]),
                reads=("ps2", "RBC"), writes=("O",))

        a1_part1(aunits[0], 0)
        a1_part1(aunits[1], 1)
        for ui in range(2, len(aunits)):
            a1_part1(aunits[ui], ui)
            a1_part2(aunits[ui - 2], ui - 2)
        a1_part2(aunits[-2], len(aunits) - 2)
        a1_part2(aunits[-1], len(aunits) - 1)

        def a1_wo_group(fo):
            s = fo % 2
            parts = [(PS[s][:, 0:512], WS[s][:, kk * 128:(kk + 1) * 128],
                      O[:, kk * 512:(kk + 1) * 512], kk == 0, kk == 7, None)
                     for kk in range(8)]
            b.emit("pe", _mmg(parts), reads=(f"WS{s}", "O"), writes=(psz[s],))

        def a1_wo_evac(fo):
            s = fo % 2
            b.emit("act", lambda e, fo=fo, s=s: e.copy(
                ACC[:, fo * 512:(fo + 1) * 512], PS[s][:, 0:512]),
                reads=(psz[s],), writes=(f"ACC{fo}",))
            b.dma(CACC, acc_s[fo * 128:(fo + 1) * 128, ccol:ccol + 512],
                  ACC[:, fo * 512:(fo + 1) * 512],
                  reads=(f"ACC{fo}", "d:acc"), writes=("d:acc",),
                  eng="pool", accum=ALU.add)

        wslot_load(0, woutA, 0, 8)
        wslot_load(1, woutA, 1, 8)
        a1_wo_group(0)
        for fo in range(1, 8):
            if fo + 1 < 8:
                wslot_load(fo + 1, woutA, fo + 1, 8)
            a1_wo_group(fo)
            a1_wo_evac(fo - 1)
        a1_wo_evac(7)

    b.barrier()

    def sweep_ffn(l, src, dst):
        srcd = xn_s if src == "x" else y1_s
        srcz = "d:xn" if src == "x" else "d:y1"

        def ffn_loads(cc):
            tile = XN if cc % 2 == 0 else X
            tz = "XN" if cc % 2 == 0 else "X"
            ch = CX if cc % 2 == 0 else CX1
            col = cc * 512
            if cc == 0:
                for kk in range(8):
                    b.emit("dve", lambda e, kk=kk, tile=tile: e.memset(
                        tile[:, kk * 513:kk * 513 + 1], 0.0), writes=(tz,))
                    b.dma(ch, tile[:, kk * 513 + 1:kk * 513 + 513],
                          srcd[kk * 128:(kk + 1) * 128, 0:512],
                          reads=(srcz,), writes=(tz,))
            else:
                for kk in range(8):
                    b.dma(ch, tile[:, kk * 513:kk * 513 + 513],
                          srcd[kk * 128:(kk + 1) * 128, col - 1:col + 512],
                          reads=(srcz,), writes=(tz,))

        b.emit("dve", lambda e: e.memset(GXN[:, :], 0.0), writes=("GXN",))
        b.emit("dve", lambda e: e.memset(GX1[:, :], 0.0), writes=("GX1",))
        ffn_loads(0)
        for c in range(NC):
            ccol = c * 512
            spv = (c - 1) % 2
            scv = c % 2
            first = (c == 0)
            XNt = XN if c % 2 == 0 else X
            XNz = "XN" if c % 2 == 0 else "X"
            if c + 1 < NC:
                ffn_loads(c + 1)
            for i in range(4):
                b.dma(CRT, RT[:, i * 512:(i + 1) * 512],
                      rott[l][i * 128:(i + 1) * 128, ccol:ccol + 512],
                      reads=(f"d:rott{l}",), writes=("RT",))

            def hshift(kk, off=0, w=512):
                base = kk * 513 + (1 if kk < 4 else 0) + off
                return XNt[:, base:base + w]

            def f_qkv_group(m):
                s = m % 2
                parts = [(PS[s][:, 0:512], WS[s][:, kk * 128:(kk + 1) * 128],
                          hshift(kk), kk == 0, kk == 7, None) for kk in range(8)]
                b.emit("pe", _mmg(parts), reads=(f"WS{s}", XNz), writes=(psz[s],))

            def f_qkv_evac(m):
                s = m % 2
                if m < 4:
                    b.emit("act", lambda e, m=m, s=s: e.copy(
                        Qa(m * 512, (m + 1) * 512), PS[s][:, 0:512]),
                        reads=(psz[s],), writes=("Q",))
                elif m < 8:
                    j = m - 4
                    b.emit("act", lambda e, j=j, s=s: e.copy(
                        X1s(j * 512, (j + 1) * 512), PS[s][:, 0:512]),
                        reads=(psz[s],), writes=("KRAW",))
                else:
                    j = m - 8
                    b.emit("act", lambda e, s=s: e.copy(VS[:, :], PS[s][:, 0:512]),
                           reads=(psz[s],), writes=("VS",))
                    for u in range(4):
                        b.emit("pe", lambda e, u=u: e.transpose(
                            PS[2][:, u * 128:(u + 1) * 128],
                            VS[:, u * 128:(u + 1) * 128], IDENT[:, :]),
                            reads=("VS", "IDENT"), writes=("ps2",))
                    dstv = bass.AP(VT, scv * 2048 + j * 128,
                                   [[6144, 128], [512, 4], [1, 128]])
                    b.emit("act", lambda e, d=dstv: e.copy(d, PS[2][:, 0:512]),
                           reads=("ps2",), writes=(f"VT{scv}",))

            def _rawz(m):
                if m < 4:
                    return Qa(m * 512, (m + 1) * 512), "Q"
                return X1s((m - 4) * 512, (m - 3) * 512), "KRAW"

            def f_qk_proc1(m):
                raw, rawz = _rawz(m)
                b.emit("act", lambda e, raw=raw: e.activation(SQA[:, :], raw, AF.Square),
                       reads=(rawz,), writes=("SQA",))
                b.emit("pe", _mmg([(PS[5][0:1, 0:512], ONES2[:, 0:1], SQA[:, :],
                                    True, True, None),
                                   (PS[6][0:1, 0:512], ONES2[:, 1:2], SQA[:, :],
                                    True, True, None)]),
                       reads=("SQA", "ONES2"), writes=("ps5", "ps6"))
                b.emit("dve", lambda e: e.tensor_scalar_max(RSS[0:1, :],
                                                            PS[5][0:1, 0:512], 1e-24),
                       reads=("ps5",), writes=("RSS",))
                b.emit("dve", lambda e: e.tensor_scalar_max(P[0:1, 0:512],
                                                            PS[6][0:1, 0:512], 1e-24),
                       reads=("ps6",), writes=("P",))
                b.emit("act", lambda e: e.activation(RSS[0:1, :], RSS[0:1, :], AF.Ln),
                       reads=(), writes=("RSS",))
                b.emit("act", lambda e: e.activation(RSS[0:1, :], RSS[0:1, :], AF.Exp,
                                                     scale=-0.5),
                       reads=(), writes=("RSS",))
                b.emit("act", lambda e: e.activation(P[0:1, 0:512], P[0:1, 0:512], AF.Ln),
                       reads=(), writes=("P",))
                b.emit("act", lambda e: e.activation(P[0:1, 0:512], P[0:1, 0:512], AF.Exp,
                                                     scale=-0.5),
                       reads=(), writes=("P",))

            def f_qk_proc2(m):
                raw, rawz = _rawz(m)
                b.emit("pe", _mmg([(PS[6][0:64, 0:512], BONE[0:1, 0:64],
                                    RSS[0:1, :], True, True, (0, 0)),
                                   (PS[6][64:128, 0:512], BONE[0:1, 0:64],
                                    P[0:1, 0:512], True, True, (0, 64))]),
                       reads=("RSS", "P", "BONE"), writes=("ps6",))
                b.emit("act", lambda e: e.copy(MB[:, :], PS[6][:, 0:512]),
                       reads=("ps6",), writes=("MB",))
                b.emit("dve", lambda e, raw=raw: e.tensor_mul(QR[:, :], raw, MB[:, :]),
                       reads=(rawz, "MB"), writes=("QR",))

            def f_qk_proc3(m):
                tb = 0 if m < 4 else 1024
                if m < 4:
                    dzone = "Q"
                    dst = Qa(m * 512, (m + 1) * 512)
                else:
                    j = m - 4
                    dzone = f"KR{scv}"
                    dst = KR[:, j * 1024 + scv * 512:j * 1024 + scv * 512 + 512]
                b.emit("pe", _mmg([(PS[7][:, 0:512], RROT[:, :], QR[:, :],
                                    True, True, None)]),
                       reads=("RROT", "QR"), writes=("ps7",))
                b.emit("dve", lambda e, tb=tb, d=dst: e.tensor_mul(
                    d, QR[:, :], RT[:, tb:tb + 512]),
                    reads=("QR", "RT"), writes=(dzone,))
                b.emit("dve", lambda e, tb=tb: e.tensor_mul(
                    SQB[:, :], PS[7][:, 0:512], RT[:, tb + 512:tb + 1024]),
                    reads=("ps7", "RT"), writes=("SQB",))
                b.emit("dve", lambda e, d=dst: e.tensor_add(d, d, SQB[:, :]),
                       reads=("SQB",), writes=(dzone,))

            wslot_load(0, wqkvF[l], 0, 8)
            wslot_load(1, wqkvF[l], 1, 8)
            f_qkv_group(0)
            for m in range(1, 12):
                if m + 1 < 12:
                    wslot_load(m + 1, wqkvF[l], m + 1, 8)
                f_qkv_group(m)
                f_qkv_evac(m - 1)
                if m >= 5:
                    f_qk_proc3(m - 5)
                if m >= 4:
                    f_qk_proc2(m - 4)
                if 3 <= m <= 10:
                    f_qk_proc1(m - 3)
            f_qk_proc3(7)
            f_qkv_evac(11)

            funits = [(hp, qs, h) for hp in range(4) for qs in range(4)
                      for h in range(2)]

            def f_env(u, ui):
                hp, qs, h = u
                jl = [j for j in range(qs, qs + 5) if (not first or j >= 4)]
                return dict(hp=hp, qs=qs, h=h, jl=jl,
                            w0=jl[0] * 128, w1=(jl[-1] + 1) * 128,
                            qcols=hp * 512 + qs * 128,
                            scidx=(hp * 4 + qs) * 2 + h,
                            nbh=4 * (ui % 2),
                            bkA=2 + 2 * (ui % 2), bkB=3 + 2 * (ui % 2))

            def f_part1(u, ui):
                v = f_env(u, ui)
                hp, qs, h = v["hp"], v["qs"], v["h"]
                w0, w1 = v["w0"], v["w1"]
                nbh, bkA, bkB = v["nbh"], v["bkA"], v["bkB"]
                aa1 = min(w1, 512)
                hasA = w0 < aa1
                bb0 = max(w0, 512) - 512
                bb1 = w1 - 512
                if hasA:
                    b.emit("pe", _mmg([(PS[bkA][:, w0:aa1],
                                        Qa(v["qcols"], v["qcols"] + 128, 64 * h, 64 * h + 64),
                                        KR[64 * h:64 * h + 64,
                                           hp * 1024 + spv * 512 + w0:
                                           hp * 1024 + spv * 512 + aa1],
                                        True, True, None)]),
                           reads=("Q", f"KR{spv}"), writes=(psz[bkA],))
                b.emit("pe", _mmg([(PS[bkB][:, bb0:bb1],
                                    Qa(v["qcols"], v["qcols"] + 128, 64 * h, 64 * h + 64),
                                    KR[64 * h:64 * h + 64,
                                       hp * 1024 + scv * 512 + bb0:
                                       hp * 1024 + scv * 512 + bb1],
                                    True, True, None)]),
                       reads=("Q", f"KR{scv}"), writes=(psz[bkB],))
                if hasA:
                    b.emit("dve", lambda e, bk=bkA, qs=qs, w0=w0, aa1=aa1: e.tensor_add(
                        PS[bk][:, w0:aa1], PS[bk][:, w0:aa1],
                        MF[:, qs * 1024 + w0:qs * 1024 + aa1]),
                        reads=("MF",), writes=(psz[bkA],))
                b.emit("dve", lambda e, bk=bkB, qs=qs, bb0=bb0, bb1=bb1: e.tensor_add(
                    PS[bk][:, bb0:bb1], PS[bk][:, bb0:bb1],
                    MF[:, qs * 1024 + 512 + bb0:qs * 1024 + 512 + bb1]),
                    reads=("MF",), writes=(psz[bkB],))
                if hasA:
                    b.emit("dve", lambda e, bk=bkA, w0=w0, aa1=aa1, nbh=nbh: e.reduce_max(
                        NMT[:, nbh:nbh + 1], PS[bk][:, w0:aa1], axis=AX.X, negate=True),
                        reads=(psz[bkA],), writes=("NMT",))
                b.emit("dve", lambda e, bk=bkB, bb0=bb0, bb1=bb1, nbh=nbh: e.reduce_max(
                    NMT[:, nbh + 1:nbh + 2], PS[bk][:, bb0:bb1], axis=AX.X, negate=True),
                    reads=(psz[bkB],), writes=("NMT",))
                if hasA:
                    b.emit("dve", lambda e, nbh=nbh: e.tensor_tensor(
                        NMT[:, nbh + 2:nbh + 3], NMT[:, nbh:nbh + 1],
                        NMT[:, nbh + 1:nbh + 2], ALU.min),
                        reads=(), writes=("NMT",))
                    b.emit("dve", lambda e, nbh=nbh: e.tensor_scalar_mul(
                        NMT[:, nbh + 3:nbh + 4], NMT[:, nbh + 2:nbh + 3], 8.0),
                        reads=(), writes=("NMT",))
                else:
                    b.emit("dve", lambda e, nbh=nbh: e.tensor_scalar_mul(
                        NMT[:, nbh + 3:nbh + 4], NMT[:, nbh + 1:nbh + 2], 8.0),
                        reads=(), writes=("NMT",))

            def f_part2(u, ui):
                v = f_env(u, ui)
                hp, qs, h = v["hp"], v["qs"], v["h"]
                jl, w0, w1 = v["jl"], v["w0"], v["w1"]
                nbh, bkA, bkB = v["nbh"], v["bkA"], v["bkB"]
                scidx = v["scidx"]
                qcols = v["qcols"]
                aa1 = min(w1, 512)
                hasA = w0 < aa1
                bb0 = max(w0, 512) - 512
                bb1 = w1 - 512
                if hasA:
                    b.emit("act", lambda e, bk=bkA, si=scidx, nbh=nbh, w0=w0, aa1=aa1:
                           e.activation(
                               P[:, w0:aa1], PS[bk][:, w0:aa1], AF.Exp,
                               bias=NMT[:, nbh + 3:nbh + 4], scale=8.0,
                               accum_out=SM[:, 2 * si:2 * si + 1]),
                           reads=(psz[bkA], "NMT"), writes=("P", "SM"))
                    b.emit("act", lambda e, bk=bkB, si=scidx, nbh=nbh, bb0=bb0, bb1=bb1:
                           e.activation(
                               P[:, 512 + bb0:512 + bb1], PS[bk][:, bb0:bb1], AF.Exp,
                               bias=NMT[:, nbh + 3:nbh + 4], scale=8.0,
                               accum_out=SM[:, 2 * si + 1:2 * si + 2]),
                           reads=(psz[bkB], "NMT"), writes=("P", "SM"))
                    b.emit("dve", lambda e, si=scidx: e.tensor_add(
                        SM[:, 64 + si:65 + si], SM[:, 2 * si:2 * si + 1],
                        SM[:, 2 * si + 1:2 * si + 2]),
                        reads=(), writes=("SM",))
                else:
                    b.emit("act", lambda e, bk=bkB, si=scidx, nbh=nbh, bb0=bb0, bb1=bb1:
                           e.activation(
                               P[:, 512 + bb0:512 + bb1], PS[bk][:, bb0:bb1], AF.Exp,
                               bias=NMT[:, nbh + 3:nbh + 4], scale=8.0,
                               accum_out=SM[:, 64 + si:65 + si]),
                           reads=(psz[bkB], "NMT"), writes=("P", "SM"))
                last_pair = (h == 1)
                if last_pair:
                    si0 = (hp * 4 + qs) * 2
                    b.emit("pe", lambda e, si=si0: [
                        e.transpose(PS[1][0:1, 0:128], SM[:, 64 + si:65 + si],
                                    IDENT[:, :]),
                        e.transpose(PS[1][0:1, 128:256], SM[:, 65 + si:66 + si],
                                    IDENT[:, :])],
                        reads=("SM", "IDENT"), writes=("ps1",))
                    b.emit("dve", lambda e: e.reciprocal(SMD[0:1, :], PS[1][0:1, 0:256]),
                           reads=("ps1",), writes=("SMD",))
                for j in jl:
                    pb = 6 + j // 4
                    b.emit("pe", lambda e, j=j, pb=pb: e.transpose(
                        PS[pb][:, (j % 4) * 128:(j % 4) * 128 + 128],
                        P[:, j * 128:(j + 1) * 128], IDENT[:, :]),
                        reads=("P", "IDENT"), writes=(psz[pb],))
                if last_pair:
                    b.emit("pe", _mmg([(PS[1][0:64, 0:128], BONE[0:1, 0:64],
                                        SMD[0:1, 0:128], True, True, (0, 0)),
                                       (PS[1][64:128, 0:128], BONE[0:1, 0:64],
                                        SMD[0:1, 128:256], True, True, (0, 64))]),
                           reads=("SMD", "BONE"), writes=("ps1",))
                    b.emit("act", lambda e: e.copy(RBC[:, 0:128], PS[1][:, 0:128]),
                           reads=("ps1",), writes=("RBC",))
                jA = [j for j in jl if j < 4]
                jB = [j for j in jl if j >= 4]
                if jA:
                    b.emit("act", lambda e, j0=jA[0], j1=jA[-1]: e.copy(
                        PTS[:, j0 * 128:(j1 + 1) * 128],
                        PS[6][:, (j0 % 4) * 128:((j1 % 4) + 1) * 128]),
                        reads=("ps6",), writes=("PTS",))
                b.emit("act", lambda e, j0=jB[0], j1=jB[-1]: e.copy(
                    PTS[:, j0 * 128:(j1 + 1) * 128],
                    PS[7][:, (j0 % 4) * 128:((j1 % 4) + 1) * 128]),
                    reads=("ps7",), writes=("PTS",))
                parts = []
                vread = set()
                for jj, j in enumerate(jl):
                    sl = spv if j < 4 else scv
                    ts = j % 4
                    vread.add(f"VT{sl}")
                    base = sl * 2048 + ts * 512 + (2 * hp + h) * 64
                    parts.append((PS[0][64 * h:64 * h + 64, 0:128],
                                  VT[:, base:base + 64],
                                  PTS[:, j * 128:(j + 1) * 128],
                                  jj == 0, jj == len(jl) - 1, (0, 64 * h)))
                b.emit("pe", _mmg(parts), reads=tuple(vread) + ("PTS",),
                       writes=("ps0",))
                if last_pair:
                    b.emit("dve", lambda e, qc=qcols: e.tensor_mul(
                        O[:, qc:qc + 128], PS[0][:, 0:128], RBC[:, 0:128]),
                        reads=("ps0", "RBC"), writes=("O",))

            f_part1(funits[0], 0)
            for ui in range(1, len(funits)):
                f_part1(funits[ui], ui)
                f_part2(funits[ui - 1], ui - 1)
            f_part2(funits[-1], len(funits) - 1)

            def f_wo_group(fo):
                s = fo % 2
                parts = [(PS[s][:, 0:512], WS[s][:, kk * 128:(kk + 1) * 128],
                          O[:, kk * 512:(kk + 1) * 512], kk == 0, kk == 3, None)
                         for kk in range(4)]
                b.emit("pe", _mmg(parts), reads=(f"WS{s}", "O"), writes=(psz[s],))

            def f_wo_evac(fo):
                s = fo % 2
                b.emit("dve", lambda e, fo=fo, s=s, XNt=XNt: e.tensor_add(
                    X1s(fo * 513 + 1, fo * 513 + 513), PS[s][:, 0:512],
                    XNt[:, fo * 513 + 1:fo * 513 + 513]),
                    reads=(psz[s], XNz), writes=("X1",))

            wslot_load(0, woF[l], 0, 4)
            wslot_load(1, woF[l], 1, 4)
            f_wo_group(0)
            for fo in range(1, 8):
                if fo + 1 < 8:
                    wslot_load(fo + 1, woF[l], fo + 1, 4)
                f_wo_group(fo)
                f_wo_evac(fo - 1)
            f_wo_evac(7)
            for kk in range(8):
                b.emit("dve", lambda e, kk=kk: e.tensor_copy(
                    X1s(kk * 513, kk * 513 + 1), GX1[:, kk:kk + 1]),
                    reads=("GX1",), writes=("X1",))
            for kk in range(8):
                b.emit("dve", lambda e, kk=kk: e.tensor_copy(
                    GX1[:, kk:kk + 1], X1s(kk * 513 + 512, kk * 513 + 513)),
                    reads=("X1",), writes=("GX1",))

            def h2slice(kk):
                base = kk * 513 + (1 if kk < 4 else 0)
                return X1s(base, base + 512)

            emit_ln_stats_and_apply(
                "X1", h2slice, (XNz,),
                dst_slices=lambda kk: XNt[:, kk * 513 + 1:kk * 513 + 513],
                affine=(lambda kk: LNG[l][:, kk:kk + 1], lambda kk: LNB[l][:, kk:kk + 1]))

            for ts2 in range(2):
                tw = ts2 * 256

                def w1_ag_group(m, tw=tw):
                    s = (ts2 * 22 + m) % 2
                    sa = m % 2
                    parts = [(PS[sa][:, 0:256], WS[s][:, kk * 128:(kk + 1) * 128],
                              XNt[:, kk * 513 + 1 + tw:kk * 513 + 1 + tw + 256],
                              kk == 0, False, None) for kk in range(8)]
                    parts += [(PS[sa][:, 256:512], WS[s][:, (8 + kk) * 128:(9 + kk) * 128],
                               XNt[:, kk * 513 + 1 + tw:kk * 513 + 1 + tw + 256],
                               False, kk == 7, None) for kk in range(8)]
                    b.emit("pe", _mmg(parts), reads=(f"WS{s}", XNz), writes=(psz[sa],))

                def w1_w2_chain(m, tw=tw):
                    s = (ts2 * 22 + m) % 2
                    sa = m % 2
                    mz = f"M{m % 2}"
                    mc = (m % 2) * 256
                    b.emit("act", lambda e, sa=sa, m=m: e.activation(
                        MG[:, :], PS[sa][:, 256:512], AF.Gelu,
                        bias=B1G[l][:, m:m + 1]),
                        reads=(psz[sa], "BI"), writes=("MG",))
                    b.emit("dve", lambda e, sa=sa, m=m, mc=mc: e.scalar_tensor_tensor(
                        Mt[:, mc:mc + 256], PS[sa][:, 0:256], B1A[l][:, m:m + 1],
                        MG[:, :], ALU.add, ALU.mult),
                        reads=(psz[sa], "MG", "BI"), writes=(mz,))
                    parts = [(PS[2 + fo // 2][:, (fo % 2) * 256:(fo % 2) * 256 + 256],
                              W2S[s][:, fo * 128:(fo + 1) * 128],
                              Mt[:, mc:mc + 256], (m == 0 and fo % 2 == 0),
                              (m == 21 and fo % 2 == 1), None)
                             for fo in range(8)]
                    b.emit("pe", _mmg(parts), reads=(f"W2S{s}", mz),
                           writes=("ps2", "ps3", "ps4", "ps5"))

                wfus_load_ag(ts2 * 22, l, 0)
                wfus_load_ag(ts2 * 22 + 1, l, 1)
                wfus_load_w2(ts2 * 22, l, 0)
                wfus_load_w2(ts2 * 22 + 1, l, 1)
                w1_ag_group(0)
                for m in range(1, 22):
                    if m + 1 < 22:
                        wfus_load_ag(ts2 * 22 + m + 1, l, m + 1)
                    w1_ag_group(m)
                    w1_w2_chain(m - 1)
                    if m + 1 < 22:
                        wfus_load_w2(ts2 * 22 + m + 1, l, m + 1)
                w1_w2_chain(21)
                for fo in range(8):
                    w2v = PS[2 + fo // 2][:, (fo % 2) * 256:(fo % 2) * 256 + 256]
                    b.emit("dve", lambda e, fo=fo, w2v=w2v, tw=tw: e.scalar_tensor_tensor(
                        ACC[:, fo * 512 + tw:fo * 512 + tw + 256], w2v,
                        B2[l][:, fo:fo + 1],
                        X1s(fo * 513 + 1 + tw, fo * 513 + 1 + tw + 256),
                        ALU.add, ALU.add),
                        reads=("ps" + str(2 + fo // 2), "X1", "BI"),
                        writes=(f"ACC{fo}",))
            dstd = y1_s if dst == "y1" else outT
            dzone = "d:y1" if dst == "y1" else "d:out"
            for fo in range(8):
                b.dma(CO, dstd[fo * 128:(fo + 1) * 128, ccol:ccol + 512],
                      ACC[:, fo * 512:(fo + 1) * 512],
                      reads=(f"ACC{fo}",), writes=(dzone,))
            if dst == "out":
                b.dma(CACC, outT[:, ccol:ccol + 512], acc_s[:, ccol:ccol + 512],
                      reads=("d:acc", "d:out"), writes=("d:out",),
                      eng="pool", accum=ALU.add)

    sweep_ffn(0, "x", "y1")
    b.barrier()
    sweep_ffn(1, "y1", "out")

    b.wait_all_chans("sp")
    b.replay()
    nc.finalize()
    return nc



def _tile_w(w, nk, nm):
    K, M = w.shape
    assert K == nk * 128 and M == nm * 128
    t = w.reshape(nk, 128, nm, 128).transpose(2, 1, 0, 3)
    return np.ascontiguousarray(t).reshape(nm, 128 * nk * 128)


def _pad_cols(w, M):
    out = np.zeros((w.shape[0], M), dtype=w.dtype)
    out[:, :w.shape[1]] = w
    return out


def _pad_rows(w, K):
    out = np.zeros((K, w.shape[1]), dtype=w.dtype)
    out[:w.shape[0]] = w
    return out


def stage_inputs(L0_attn_w_qkv, L0_attn_w_out, L0_ffn_wqkv, L0_ffn_q_scale,
                 L0_ffn_k_scale, L0_ffn_wo, L0_ffn_ln_g, L0_ffn_ln_b,
                 L0_ffn_w1, L0_ffn_b1, L0_ffn_w2, L0_ffn_b2):
    f = np.float32
    d = {}
    d["wqkvA"] = _tile_w(np.asarray(L0_attn_w_qkv, f), 8, 24)
    d["woutA"] = _tile_w(np.asarray(L0_attn_w_out, f), 8, 8)
    for l in range(2):
        d[f"wqkvF{l}"] = _tile_w(np.asarray(L0_ffn_wqkv[l], f), 8, 12)
        d[f"woF{l}"] = _tile_w(np.asarray(L0_ffn_wo[l], f), 4, 8)
        w1 = np.asarray(L0_ffn_w1[l], f)
        wa = _pad_cols(w1[:, :2730], 2816).reshape(8, 128, 22, 128).transpose(2, 1, 0, 3)
        wg = _pad_cols(w1[:, 2730:], 2816).reshape(8, 128, 22, 128).transpose(2, 1, 0, 3)
        w2 = _pad_rows(np.asarray(L0_ffn_w2[l], f), 2816).reshape(22, 128, 8, 128)
        d[f"wfus{l}"] = np.ascontiguousarray(
            np.concatenate([wa, wg, w2], axis=2)).reshape(22, 128 * 24 * 128)
        b1 = np.asarray(L0_ffn_b1[l], f)
        d[f"b1a{l}"] = np.ascontiguousarray(
            np.concatenate([b1[:2730], np.zeros(86, f)]).reshape(22, 128).T)
        d[f"b1g{l}"] = np.ascontiguousarray(
            np.concatenate([b1[2730:], np.zeros(86, f)]).reshape(22, 128).T)
        d[f"b2{l}"] = np.ascontiguousarray(np.asarray(L0_ffn_b2[l], f).reshape(8, 128).T)
        d[f"lng{l}"] = np.ascontiguousarray(np.asarray(L0_ffn_ln_g[l], f).reshape(8, 128).T)
        d[f"lnb{l}"] = np.ascontiguousarray(np.asarray(L0_ffn_ln_b[l], f).reshape(8, 128).T)
        qs = np.asarray(L0_ffn_q_scale[l], f)
        ks = np.asarray(L0_ffn_k_scale[l], f)
        d[f"qsc{l}"] = np.ascontiguousarray(
            np.stack([np.tile(qs, 2), np.tile(np.roll(qs, -32), 2)], axis=1))
        d[f"ksc{l}"] = np.ascontiguousarray(
            np.stack([np.tile(ks, 2), np.tile(np.roll(ks, -32), 2)], axis=1))
    return d


def kernel(L0_attn_x, L0_attn_w_qkv, L0_attn_w_out, L0_ffn_wqkv, L0_ffn_q_scale,
           L0_ffn_k_scale, L0_ffn_wo, L0_ffn_ln_g, L0_ffn_ln_b, L0_ffn_w1,
           L0_ffn_b1, L0_ffn_w2, L0_ffn_b2):
    x = np.asarray(L0_attn_x, np.float32)
    B, NT, D = x.shape
    nc = build_core_program(NT)
    wd = stage_inputs(L0_attn_w_qkv, L0_attn_w_out, L0_ffn_wqkv, L0_ffn_q_scale,
                      L0_ffn_k_scale, L0_ffn_wo, L0_ffn_ln_g, L0_ffn_ln_b,
                      L0_ffn_w1, L0_ffn_b1, L0_ffn_w2, L0_ffn_b2)
    in_maps = []
    for bb in range(B):
        m = {"xT": np.ascontiguousarray(x[bb].T)}
        m.update(wd)
        in_maps.append(m)
    res = run_bass_kernel_spmd(nc, in_maps, core_ids=list(range(B)))
    out = np.stack([np.asarray(r["outT"]).T for r in res.results], axis=0)
    return out.astype(np.float32)
